# version 02
# baseline (speedup 1.0000x reference)
import sys
for _p in ("/opt/trn_rl_repo",):
    if _p not in sys.path:
        sys.path.insert(0, _p)
"""Framework for raw-Bass sublayer emitters (mk_hybrid_d4).

Layout conventions:
- Residual x: token-major fp32, NT tiles [128, D] (tile i = tokens i*128..+128).
- ln: bf16 token-major tiles; lnT: bf16 feature-major tiles [128, T] (chunk c =
  features c*128..+128).
- Weights are staged in DRAM pre-transposed host-side so the device rhs is
  [feature-chunk 128 x out_dim], contraction dim on partitions.

Sync model:
- Per-engine semaphores: t, v, s, g (compute, +1 per instruction we track),
  qd, gd, sd (DMA completions on sync/gpsimd/scalar queues, +16), cc (+1).
- Builder tracks Python-side counts; barrier() makes every engine wait for
  every other sem's current count at its next emitted instruction.
"""
import numpy as np
import concourse.bass as bass
import concourse.mybir as mybir

F32 = mybir.dt.float32
BF16 = mybir.dt.bfloat16
I32 = mybir.dt.int32
U32 = mybir.dt.uint32
FP8 = mybir.dt.float8e4
AF = mybir.ActivationFunctionType
ALU = mybir.AluOpType
AX = mybir.AxisListType

P = 128
D = 1024
NT = 16
TT = 2048
NF = 8

ENGS = ("t", "v", "s", "g", "q")
SEMS = ("t", "v", "s", "g", "qd", "gd", "sd", "cc")


class Prog:
    def __init__(self, nc):
        self.nc = nc
        self.count = {s: 0 for s in SEMS}
        self.waited = {}
        self.sems = {}
        self._exit = []

    def open_sems(self, stack):
        for s in SEMS:
            self.sems[s] = stack.enter_context(self.nc.semaphore("sem_" + s))

    def inc(self, inst, sem, n=None):
        if n is None:
            n = 16 if sem in ("qd", "gd", "sd") else 1
        inst.then_inc(self.sems[sem], n)
        self.count[sem] += n
        return (sem, self.count[sem])

    def tok(self, sem):
        return (sem, self.count[sem])

    def wait(self, eng, engname, *toks):
        best = {}
        for t in toks:
            if t is None:
                continue
            s, v = t
            if v > best.get(s, -1):
                best[s] = v
        for s, v in sorted(best.items()):
            key = (engname, s)
            if self.waited.get(key, -1) >= v:
                continue
            eng.wait_ge(self.sems[s], v)
            self.waited[key] = v

    def barrier_tokens(self):
        return [(s, self.count[s]) for s in SEMS]

    def blk(self, **bodies):
        nc = self.nc
        with nc.Block() as block:
            if "q" in bodies:
                @block.sync
                def _(sync):
                    bodies["q"](sync)
            if "g" in bodies:
                @block.gpsimd
                def _(g):
                    bodies["g"](g)
            if "t" in bodies:
                @block.tensor
                def _(t):
                    bodies["t"](t)
            if "s" in bodies:
                @block.scalar
                def _(s):
                    bodies["s"](s)
            if "v" in bodies:
                @block.vector
                def _(v):
                    bodies["v"](v)


def np_dt(dtype):
    return mybir.dt.np(dtype)



def emit_layernorm(pr, x_tiles, x_toks, ln_tiles, sq_scratch, stat, eps=1e-5,
                   out_extra=None, gamma=None, beta=None):
    nc = pr.nc
    out_toks = [None] * len(x_tiles)

    for i, (xt, xtok) in enumerate(zip(x_tiles, x_toks)):
        def body_v(v, i=i, xt=xt, xtok=xtok):
            pr.wait(v, "v", xtok)
            t0 = pr.inc(v.reduce_sum(stat[:, 0:1], xt[:, :], AX.X), "v")
        def body_s(s, i=i, xt=xt, xtok=xtok):
            pr.wait(s, "s", xtok)
            pr.inc(s.activation(sq_scratch[:, :], xt[:, :], AF.Square,
                                accum_out=stat[:, 1:2]), "s")
        pr.blk(v=body_v, s=body_s)
        vtok = pr.tok("v")
        stok = pr.tok("s")

        def body_v2(v, i=i):
            pr.wait(v, "v", vtok, stok)
            pr.inc(v.tensor_scalar(stat[:, 2:4], stat[:, 0:2], 1.0 / D, None, ALU.mult), "v")
            vt = pr.tok("v")
            pr.wait(v, "v", vt)
            pr.inc(v.tensor_tensor(stat[:, 5:6], stat[:, 2:3], stat[:, 2:3], ALU.mult), "v")
            vt = pr.tok("v")
            pr.wait(v, "v", vt)
            pr.inc(v.tensor_tensor(stat[:, 4:5], stat[:, 3:4], stat[:, 5:6], ALU.subtract), "v")
        pr.blk(v=body_v2)
        vtok2 = pr.tok("v")

        def body_s2(s, i=i, xt=xt):
            pr.wait(s, "s", vtok2)
            pr.inc(s.activation(stat[:, 5:6], stat[:, 4:5], AF.Ln, bias=eps), "s")
            st = pr.tok("s")
            pr.wait(s, "s", st)
            pr.inc(s.activation(stat[:, 6:7], stat[:, 5:6], AF.Exp, scale=-0.5), "s")
            st = pr.tok("s")
            pr.wait(s, "s", st)
        pr.blk(s=body_s2)
        stok2 = pr.tok("s")

        def body_v3(v, i=i):
            pr.wait(v, "v", stok2)
            pr.inc(v.tensor_tensor(stat[:, 7:8], stat[:, 2:3], stat[:, 6:7], ALU.mult), "v")
            vt = pr.tok("v")
            pr.wait(v, "v", vt)
            pr.inc(v.tensor_scalar(stat[:, 7:8], stat[:, 7:8], -1.0, None, ALU.mult), "v")
        pr.blk(v=body_v3)
        vtok3 = pr.tok("v")

        def body_s3(s, i=i, xt=xt):
            pr.wait(s, "s", vtok3)
            pr.inc(s.activation(ln_tiles[i][:, :], xt[:, :], AF.Copy,
                                bias=stat[:, 7:8], scale=stat[:, 6:7]), "s")
            if out_extra is not None:
                st = pr.tok("s")
                pr.wait(s, "s", st)
                pr.inc(s.activation(out_extra[i][:, :], xt[:, :], AF.Copy,
                                    bias=stat[:, 7:8], scale=stat[:, 6:7]), "s")
        pr.blk(s=body_s3)
        out_toks[i] = pr.tok("s")
    return out_toks


def emit_transpose_fm(pr, src_tiles, src_toks, dst_tiles, ident_bf16, psA, psB,
                      n_src=NT, n_chunk=NF, src_col0=0):
    nc = pr.nc
    ps = [psA, psB]
    groups = []
    for c in range(n_chunk):
        for i0 in range(0, n_src, 4):
            groups.append((c, i0))
    for gi, (c, i0) in enumerate(groups):
        bank = ps[gi % 2]
        n_in_group = min(4, n_src - i0)

        def body_t(t, c=c, i0=i0, bank=bank, n_in_group=n_in_group, gi=gi):
            if gi >= 2:
                pr.wait(t, "t", evac_toks[gi - 2])
            for j in range(n_in_group):
                pr.wait(t, "t", src_toks[i0 + j])
                pr.inc(t.transpose(bank[:, j * 128:(j + 1) * 128],
                                   src_tiles[i0 + j][:, src_col0 + c * 128:src_col0 + (c + 1) * 128],
                                   ident_bf16[:, :]), "t")

        pr.blk(t=body_t)
        ttok = pr.tok("t")

        def body_s(s, c=c, i0=i0, bank=bank, n_in_group=n_in_group, ttok=ttok):
            pr.wait(s, "s", ttok)
            pr.inc(s.copy(dst_tiles[c][:, i0 * 128:(i0 + n_in_group) * 128],
                          bank[:, 0:n_in_group * 128]), "s")

        pr.blk(s=body_s)
        if gi == 0:
            evac_toks = []
        evac_toks.append(pr.tok("s"))
    return pr.tok("s")


def emit_matmul(pr, lhsT_tiles, lhsT_tok, rhs_chunks, rhs_toks, psum_pair,
                out_writer, n_k, out_dim, free_per_mm=512, t_tiles=NT,
                lhsT_of=None):
    nc = pr.nc
    nmm = (out_dim + free_per_mm - 1) // free_per_mm
    evac_toks = [None, None]
    for i in range(t_tiles):
        bank = psum_pair[i % 2]

        def body_t(t, i=i, bank=bank):
            pr.wait(t, "t", lhsT_tok)
            pr.wait(t, "t", *rhs_toks)
            if evac_toks[i % 2] is not None:
                pr.wait(t, "t", evac_toks[i % 2])
            for m in range(nmm):
                o0 = m * free_per_mm
                o1 = min(out_dim, o0 + free_per_mm)
                for c in range(n_k):
                    lhs_ap = (lhsT_of(c, i) if lhsT_of is not None
                              else lhsT_tiles[c][:, i * 128:(i + 1) * 128])
                    t.matmul(bank[:, o0:o1], lhs_ap, rhs_chunks[c][:, o0:o1],
                             start=(c == 0), stop=(c == n_k - 1))
            pr.inc(t.engine_nop(), "t")

        pr.blk(t=body_t)
        mmtok = pr.tok("t")
        evac_toks[i % 2] = out_writer(i, bank, mmtok)
    return evac_toks

"""Sublayer emitters + single-sublayer test driver.

Each sublayer computes y = x + f(LN(x)) for this core's [2048, 1024] slice.
x lives in DRAM between sublayers in unfused mode (XIO handles get/add_store).
"""
import numpy as np
from contextlib import ExitStack
import concourse.bass as bass
import concourse.mybir as mybir
from concourse.bass_utils import run_bass_kernel_spmd

CONST_VALUES = [1e-5, 1e-6, 1.0, -1.0, 0.5, -0.5, 1.0 / 127.0, 127.0,
                1e30, -1e30, 0.0, 2048.0, 240.0, 1e9, 20.0, -20.0]


def setup_consts(pr, stack):
    nc = pr.nc
    arr = np.tile(np.asarray(CONST_VALUES, np.float32)[None, :], (P, 1))
    dram = nc.inline_tensor(arr, "constpool")
    pool = stack.enter_context(nc.sbuf_tensor("constpool_sb", [P, len(CONST_VALUES)], F32))
    def body(q):
        pr.inc(q.dma_start(out=pool[:, :], in_=dram.ap()), "qd")
    pr.blk(q=body)
    pr.const_pool = pool
    pr.const_tok = pr.tok("qd")
    pr.const_idx = {v: i for i, v in enumerate(CONST_VALUES)}
    for v, i in pr.const_idx.items():
        nc.const_aps.register(F32, v, pool[:, i:i + 1])


def cval(pr, v):
    return pr.const_pool[:, pr.const_idx[v]:pr.const_idx[v] + 1]


def setup_identity(pr, stack):
    nc = pr.nc
    import ml_dtypes
    ident_np = np.eye(P, dtype=ml_dtypes.bfloat16)
    dram = nc.inline_tensor(ident_np, "identconst")
    ident = stack.enter_context(nc.sbuf_tensor("ident_sb", [P, P], BF16))
    identf_np = np.eye(P, dtype=np.float32)
    dramf = nc.inline_tensor(identf_np, "identconstf")
    identf = stack.enter_context(nc.sbuf_tensor("identf_sb", [P, P], F32))
    def body(q):
        pr.inc(q.dma_start(out=ident[:, :], in_=dram.ap()), "qd")
        pr.inc(q.dma_start(out=identf[:, :], in_=dramf.ap()), "qd")
    pr.blk(q=body)
    pr.ident = ident
    pr.identf = identf
    pr.ident_tok = pr.tok("qd")


class XIOUnfused:

    def __init__(self, pr, stack, x_in_ap, x_out_ap):
        self.pr = pr
        self.x_in = x_in_ap
        self.x_out = x_out_ap
        nc = pr.nc
        self.tmp = [stack.enter_context(nc.sbuf_tensor(f"xio_tmp{j}", [P, D], F32))
                    for j in range(2)]
        self.tmp_tok = [None] * 2
        self.tmp_load = [None] * 2
        self.rr = 0

    def get(self, i):
        pr = self.pr
        slot = self.rr % 2
        self.rr += 1
        war = self.tmp_tok[slot]

        def body(q, i=i, slot=slot, war=war):
            pr.wait(q, "q", war)
            pr.inc(q.dma_start(out=self.tmp[slot][:, :],
                               in_=self.x_in[i * 128:(i + 1) * 128, :]), "qd")
        pr.blk(q=body)
        self.tmp_load[slot] = pr.tok("qd")
        return self.tmp[slot], self.tmp_load[slot], slot

    def done(self, slot, tok):
        self.tmp_tok[slot] = tok

    def add_store(self, i, src_ap, src_tok, scale=None, scale_ap=None):
        pr = self.pr
        xap, ltok, slot = self.get(i)

        def body_v(v, i=i, xap=xap, ltok=ltok):
            pr.wait(v, "v", ltok, src_tok)
            if scale_ap is not None:
                pr.inc(v.scalar_tensor_tensor(xap[:, :], src_ap, scale_ap, xap[:, :],
                                              ALU.mult, ALU.add), "v")
            elif scale is not None:
                pr.inc(v.scalar_tensor_tensor(xap[:, :], src_ap, float(scale), xap[:, :],
                                              ALU.mult, ALU.add), "v")
            else:
                pr.inc(v.tensor_tensor(xap[:, :], xap[:, :], src_ap, ALU.add), "v")
        pr.blk(v=body_v)
        vtok = pr.tok("v")

        def body_q(q, i=i, xap=xap, vtok=vtok):
            pr.wait(q, "q", vtok)
            pr.inc(q.dma_start(out=self.x_out[i * 128:(i + 1) * 128, :],
                               in_=xap[:, :]), "qd")
        pr.blk(q=body_q)
        stok = pr.tok("qd")
        self.done(slot, stok)
        return stok


def emit_ln_from_xio(pr, stack, xio, ln_tiles, n_tiles=NT, eps=1e-5,
                     post_tile=None, tile_war=None):
    nc = pr.nc
    sq = stack.enter_context(nc.sbuf_tensor("ln_sq", [P, D], F32))
    stat = stack.enter_context(nc.sbuf_tensor("ln_stat", [P, 8 * n_tiles], F32))
    out_toks = []
    for i in range(n_tiles):
        xt, xtok, slot = xio.get(i)
        st = stat[:, 8 * i:8 * i + 8]

        def b1(v, xt=xt, xtok=xtok, st=st):
            pr.wait(v, "v", xtok)
            pr.inc(v.reduce_sum(st[:, 0:1], xt[:, :], AX.X), "v")
        def b1s(s, xt=xt, xtok=xtok, st=st, i=i):
            pr.wait(s, "s", xtok)
            if i > 0:
                pr.wait(s, "s", sq_done[0])
            pr.inc(s.activation(sq[:, :], xt[:, :], AF.Square, accum_out=st[:, 1:2]), "s")
        pr.blk(v=b1, s=b1s)
        sq_done = [pr.tok("s")]
        vtok, stok = pr.tok("v"), pr.tok("s")

        def b2(v, st=st, vtok=vtok, stok=stok):
            pr.wait(v, "v", vtok, stok)
            pr.inc(v.tensor_scalar(st[:, 2:4], st[:, 0:2], 1.0 / D, None, ALU.mult), "v")
            pr.wait(v, "v", pr.tok("v"))
            pr.inc(v.tensor_tensor(st[:, 4:5], st[:, 2:3], st[:, 2:3], ALU.mult), "v")
            pr.wait(v, "v", pr.tok("v"))
            pr.inc(v.tensor_tensor(st[:, 4:5], st[:, 3:4], st[:, 4:5], ALU.subtract), "v")
        pr.blk(v=b2)
        vtok2 = pr.tok("v")

        def b3(s, st=st, vtok2=vtok2):
            pr.wait(s, "s", vtok2, pr.const_tok)
            pr.inc(s.activation(st[:, 5:6], st[:, 4:5], AF.Ln, bias=cval(pr, eps)), "s")
            pr.wait(s, "s", pr.tok("s"))
            pr.inc(s.activation(st[:, 6:7], st[:, 5:6], AF.Exp, scale=-0.5), "s")
        pr.blk(s=b3)
        stok2 = pr.tok("s")

        def b4(v, st=st, stok2=stok2):
            pr.wait(v, "v", stok2)
            pr.inc(v.scalar_tensor_tensor(st[:, 7:8], st[:, 2:3], -1.0, st[:, 6:7],
                                          ALU.mult, ALU.mult), "v")
        pr.blk(v=b4)
        vtok3 = pr.tok("v")

        def b5(v, xt=xt, st=st, vtok3=vtok3, i=i):
            pr.wait(v, "v", vtok3)
            if tile_war is not None:
                pr.wait(v, "v", tile_war(i))
            pr.inc(v.tensor_scalar(ln_tiles[i][:, :], xt[:, :], st[:, 6:7],
                                   st[:, 7:8], ALU.mult, ALU.add), "v")
        pr.blk(v=b5)
        tok = pr.tok("v")
        if post_tile is not None:
            tok = post_tile(i, tok)
        out_toks.append(tok)
        xio.done(slot, tok)
    return out_toks


def emit_transpose_fm(pr, stack, src_tiles, src_toks, dst_tiles, n_src=NT,
                      n_chunk=NF, src_col0=0, dst_row0=0, tag="tr", dtype=BF16):
    nc = pr.nc
    psA = stack.enter_context(nc.psum_tensor(tag + "_psA", [P, 512], dtype))
    psB = stack.enter_context(nc.psum_tensor(tag + "_psB", [P, 512], dtype))
    ps = [psA, psB]
    evac_toks = []
    groups = [(c, i0) for c in range(n_chunk) for i0 in range(0, n_src, 4)]
    for gi, (c, i0) in enumerate(groups):
        bank = ps[gi % 2]
        n_in = min(4, n_src - i0)

        def body_t(t, c=c, i0=i0, bank=bank, n_in=n_in, gi=gi):
            pr.wait(t, "t", pr.ident_tok)
            if gi >= 2:
                pr.wait(t, "t", evac_toks[gi - 2])
            last = None
            idn = pr.ident if dtype == BF16 else pr.identf
            for j in range(n_in):
                pr.wait(t, "t", src_toks[i0 + j])
                last = t.transpose(bank[:, j * 128:(j + 1) * 128],
                                   src_tiles[i0 + j][:, src_col0 + c * 128:src_col0 + (c + 1) * 128],
                                   idn[:, :])
            pr.inc(last, "t")
        pr.blk(t=body_t)
        ttok = pr.tok("t")

        def body_s(s, c=c, i0=i0, bank=bank, n_in=n_in, ttok=ttok):
            pr.wait(s, "s", ttok)
            pr.inc(s.copy(dst_tiles[c][:, dst_row0 + i0 * 128:dst_row0 + (i0 + n_in) * 128],
                          bank[:, 0:n_in * 128]), "s")
        pr.blk(s=body_s)
        evac_toks.append(pr.tok("s"))
    return pr.tok("s")



def stage_L1_ffn(inputs, pre):
    wg, wu, wd = inputs["L1_ffn_w_gate"], inputs["L1_ffn_w_up"], inputs["L1_ffn_w_down"]
    pre["L1_ffn_wgT"] = np.ascontiguousarray(
        wg.reshape(24, 128, 8, 128).transpose(0, 3, 2, 1)).reshape(24, 128, 1024)
    pre["L1_ffn_wuT"] = np.ascontiguousarray(
        wu.reshape(24, 128, 8, 128).transpose(0, 3, 2, 1)).reshape(24, 128, 1024)
    pre["L1_ffn_wdT"] = np.ascontiguousarray(wd.T).reshape(24, 128, 1024)


def decl_L1_ffn(nc):
    return {
        "wgT": nc.dram_tensor("L1_ffn_wgT", [24, 1024, 128], I32, kind="ExternalInput").ap(),
        "wuT": nc.dram_tensor("L1_ffn_wuT", [24, 1024, 128], I32, kind="ExternalInput").ap(),
        "wdT": nc.dram_tensor("L1_ffn_wdT", [24, 128, 1024], I32, kind="ExternalInput").ap(),
    }


def emit_L1_ffn(pr, stack, xio, lnT_tiles, lnT_tok, w):
    nc = pr.nc
    HO = 24
    ctx = stack.enter_context
    with ExitStack() as ls:
        lctx = ls.enter_context
        hT = [lctx(nc.sbuf_tensor(f"l1f_hT{o}", [P, TT], BF16)) for o in range(HO)]
        wg_b = lctx(nc.sbuf_tensor("l1f_wgb", [P, 2 * 1024], BF16))
        wu_b = lctx(nc.sbuf_tensor("l1f_wub", [P, 2 * 1024], BF16))
        wd_b = [lctx(nc.sbuf_tensor(f"l1f_wdb{o}", [P, 1024], BF16)) for o in range(HO)]
        sg = lctx(nc.sbuf_tensor("l1f_sg", [P, 512], F32))
        psA = lctx(nc.psum_tensor("l1f_psA", [P, 512], F32))
        psB = lctx(nc.psum_tensor("l1f_psB", [P, 512], F32))
        psC = lctx(nc.psum_tensor("l1f_psC", [P, 1024], F32))
        psD = lctx(nc.psum_tensor("l1f_psD", [P, 1024], F32))

        def ldd(g):
            for o in range(HO):
                pr.inc(g.dma_start(out=wd_b[o][:, :], in_=w["wdT"][o]), "gd")
        pr.blk(g=ldd)
        wd_conv = [pr.tok("gd")]

        silu_done = {}
        mul_done = {}
        mm_done = {}
        wgu_conv = [None] * HO
        for o in range(HO):
            buf = o % 2
            wslc = slice(buf * 1024, (buf + 1) * 1024)

            def ldw(g, o=o, wslc=wslc):
                if o >= 2:
                    pr.wait(g, "g", mm_done.get(o - 2))
                pr.inc(g.dma_start(out=wg_b[:, wslc], in_=w["wgT"][o]), "gd")
                pr.inc(g.dma_start(out=wu_b[:, wslc], in_=w["wuT"][o]), "gd")
            pr.blk(g=ldw)
            wgu_conv[o] = pr.tok("gd")

            for tg in range(4):
                tsl = slice(tg * 512, (tg + 1) * 512)

                def mm(t, o=o, tg=tg, tsl=tsl, wslc=wslc):
                    pr.wait(t, "t", wgu_conv[o], lnT_tok)
                    pr.wait(t, "t", silu_done.get((o - 1, tg)))
                    pr.wait(t, "t", mul_done.get((o - 1, tg)))
                    last = None
                    for c in range(8):
                        last = t.matmul(psA[:, :], wg_b[:, wslc][:, c * 128:(c + 1) * 128],
                                        lnT_tiles[c][:, tsl], start=(c == 0), stop=(c == 7))
                    for c in range(8):
                        last = t.matmul(psB[:, :], wu_b[:, wslc][:, c * 128:(c + 1) * 128],
                                        lnT_tiles[c][:, tsl], start=(c == 0), stop=(c == 7))
                    pr.inc(last, "t")
                pr.blk(t=mm)
                mmtok = pr.tok("t")

                def ev1(s, o=o, tg=tg, mmtok=mmtok):
                    pr.wait(s, "s", mmtok)
                    pr.inc(s.activation(sg[:, :], psA[:, :], AF.Silu, scale=1.0 / 127.0), "s")
                pr.blk(s=ev1)
                silu_done[(o, tg)] = pr.tok("s")

                def ev2(v, o=o, tg=tg, tsl=tsl):
                    pr.wait(v, "v", silu_done[(o, tg)])
                    pr.inc(v.scalar_tensor_tensor(hT[o][:, tsl], psB[:, :], 1.0 / 127.0,
                                                  sg[:, :], ALU.mult, ALU.mult), "v")
                pr.blk(v=ev2)
                mul_done[(o, tg)] = pr.tok("v")
            mm_done[o] = pr.tok("v")

        hT_tok = pr.tok("v")

        out_toks = [None] * NT
        ev_done = {}
        for i in range(NT):
            bank = psC if i % 2 == 0 else psD

            def mmd(t, i=i, bank=bank):
                pr.wait(t, "t", hT_tok, *wd_conv)
                pr.wait(t, "t", ev_done.get(i - 2))
                last = None
                for o in range(HO):
                    lhs = hT[o][:, i * 128:(i + 1) * 128]
                    last = t.matmul(bank[:, 0:512], lhs, wd_b[o][:, 0:512],
                                    start=(o == 0), stop=(o == HO - 1))
                for o in range(HO):
                    lhs = hT[o][:, i * 128:(i + 1) * 128]
                    last = t.matmul(bank[:, 512:1024], lhs, wd_b[o][:, 512:1024],
                                    start=(o == 0), stop=(o == HO - 1))
                pr.inc(last, "t")
            pr.blk(t=mmd)
            mmtok = pr.tok("t")
            out_toks[i] = xio.add_store(i, bank[:, :], mmtok, scale=1.0 / 127.0)
            ev_done[i] = out_toks[i]
    return out_toks


def setup_ones(pr, stack):
    nc = pr.nc
    import ml_dtypes
    arr = np.ones((P, 128), np.float32)
    dram = nc.inline_tensor(arr, "onesf32")
    arrb = np.ones((P, 128), ml_dtypes.bfloat16)
    dramb = nc.inline_tensor(arrb, "onesbf16")
    onf = stack.enter_context(nc.sbuf_tensor("ones_f32", [P, 128], F32))
    onb = stack.enter_context(nc.sbuf_tensor("ones_bf16", [P, 128], BF16))
    def body(q):
        pr.inc(q.dma_start(out=onf[:, :], in_=dram.ap()), "qd")
        pr.inc(q.dma_start(out=onb[:, :], in_=dramb.ap()), "qd")
    pr.blk(q=body)
    pr.ones_f32 = onf
    pr.ones_bf16 = onb
    pr.ones_tok = pr.tok("qd")


def emit_row_broadcast(pr, ps_out, row_ap, row_tok, width, tag="rb"):
    def body(t):
        pr.wait(t, "t", row_tok, pr.ones_tok)
        last = None
        dt_is_bf16 = row_ap.dtype == BF16
        ones = pr.ones_bf16 if dt_is_bf16 else pr.ones_f32
        for o0 in range(0, width, 512):
            o1 = min(width, o0 + 512)
            last = t.matmul(ps_out[:, o0:o1], ones[0:1, 0:128],
                            row_ap[0:1, o0:o1], start=True, stop=True)
        pr.inc(last, "t")
    pr.blk(t=body)
    return pr.tok("t")



def stage_L3_ffn(inputs, pre):
    k = inputs["L3_ffn_kernel"]
    pre["L3_ffn_kernel_r"] = np.ascontiguousarray(np.asarray(k).reshape(8, 128, 1024))


def decl_L3_ffn(nc):
    return {
        "kernel": nc.dram_tensor("L3_ffn_kernel_r", [8, 128, 1024], BF16, kind="ExternalInput").ap(),
        "s_in": nc.dram_tensor("L3_ffn_input_scale", [1, 1], F32, kind="ExternalInput").ap(),
        "s_k": nc.dram_tensor("L3_ffn_kernel_scale", [1, 1], F32, kind="ExternalInput").ap(),
        "bias": nc.dram_tensor("L3_ffn_bias", [1, 1024], BF16, kind="ExternalInput").ap(),
    }


def emit_fp8_quant(pr, dst_bf16, src_f32, src_tok, f8a, f8b, sc_f32, mask_t):
    def body(v):
        pr.wait(v, "v", src_tok)
        pr.inc(v.tensor_copy(f8a[:, :], src_f32[:, :]), "v")
        pr.inc(v.tensor_tensor(sc_f32[:, :], src_f32[:, :], src_f32[:, :], ALU.mult), "v")
        pr.wait(v, "v", pr.tok("v"))
        pr.inc(v.tensor_scalar(mask_t[:, :], sc_f32[:, :], 61504.0, None, ALU.is_ge), "v")
        pr.inc(v.tensor_scalar(sc_f32[:, :], src_f32[:, :], 0.5, None, ALU.mult), "v")
        pr.wait(v, "v", pr.tok("v"))
        pr.inc(v.tensor_copy(f8b[:, :], sc_f32[:, :]), "v")
        pr.wait(v, "v", pr.tok("v"))
        pr.inc(v.tensor_scalar(sc_f32[:, :], f8b[:, :], 2.0, None, ALU.mult), "v")
        pr.wait(v, "v", pr.tok("v"))
        pr.inc(v.select(dst_bf16[:, :], mask_t[:, :], sc_f32[:, :], f8a[:, :]), "v")
        pr.wait(v, "v", pr.tok("v"))
    pr.blk(v=body)
    return pr.tok("v")


def emit_scalar_bcast(pr, ps, col_out, src_11, src_tok):
    def body_t(t):
        pr.wait(t, "t", src_tok, pr.ones_tok)
        pr.inc(t.matmul(ps[:, 0:1], pr.ones_f32[0:1, 0:128], src_11[0:1, 0:1],
                        start=True, stop=True), "t")
    pr.blk(t=body_t)
    ttok = pr.tok("t")

    def body_s(s):
        pr.wait(s, "s", ttok)
        pr.inc(s.copy(col_out[:, :], ps[:, 0:1]), "s")
    pr.blk(s=body_s)
    return pr.tok("s")


def emit_L3_ffn(pr, stack, xio, lnT_tiles, lnT_tok, w):
    nc = pr.nc
    with ExitStack() as ls:
        lctx = ls.enter_context
        sc = lctx(nc.sbuf_tensor("l3f_sc", [P, 8], F32))
        s11 = lctx(nc.sbuf_tensor("l3f_s11", [1, 2], F32))
        bias_b = lctx(nc.sbuf_tensor("l3f_biasb", [P, 1024], BF16))
        kq = [lctx(nc.sbuf_tensor(f"l3f_kq{c}", [P, 1024], BF16)) for c in range(8)]
        xq = [lctx(nc.sbuf_tensor(f"l3f_xq{c}", [P, TT], BF16)) for c in range(8)]
        kf = lctx(nc.sbuf_tensor("l3f_kf", [P, TT], F32))
        f8a = lctx(nc.sbuf_tensor("l3f_f8a", [P, TT], FP8))
        f8b = lctx(nc.sbuf_tensor("l3f_f8b", [P, TT], FP8))
        scf = lctx(nc.sbuf_tensor("l3f_scf", [P, TT], F32))
        mskt = lctx(nc.sbuf_tensor("l3f_msk", [P, TT], mybir.dt.uint8))
        yb = lctx(nc.sbuf_tensor("l3f_yb", [P, 1024], BF16))
        kraw = lctx(nc.sbuf_tensor("l3f_kraw", [P, 2 * 1024], BF16))
        psS = lctx(nc.psum_tensor("l3f_psS", [P, 8], F32))
        psBC = lctx(nc.psum_tensor("l3f_psBC", [P, 1024], F32))
        psC = lctx(nc.psum_tensor("l3f_psC", [P, 1024], F32))
        psD = lctx(nc.psum_tensor("l3f_psD", [P, 1024], F32))

        def ld_s(q):
            pr.inc(q.dma_start(out=s11[:, 0:1], in_=w["s_in"]), "qd")
            pr.inc(q.dma_start(out=s11[:, 1:2], in_=w["s_k"]), "qd")
        pr.blk(q=ld_s)
        s11_tok = pr.tok("qd")
        emit_scalar_bcast(pr, psS, sc[:, 0:1], s11[:, 0:1], s11_tok)
        btok = emit_scalar_bcast(pr, psS, sc[:, 1:2], s11[:, 1:2], s11_tok)

        def mk_scales(v):
            pr.wait(v, "v", btok)
            pr.inc(v.reciprocal(sc[:, 2:3], sc[:, 0:1]), "v")
            pr.inc(v.reciprocal(sc[:, 3:4], sc[:, 1:2]), "v")
            pr.inc(v.tensor_tensor(sc[:, 4:5], sc[:, 0:1], sc[:, 1:2], ALU.mult), "v")
            pr.wait(v, "v", pr.tok("v"))
        pr.blk(v=mk_scales)
        sc_tok = pr.tok("v")

        def ld_b(q):
            pr.inc(q.dma_start(out=kraw[0:1, 0:1024], in_=w["bias"]), "qd")
        pr.blk(q=ld_b)
        brow_tok = pr.tok("qd")
        bc_tok = emit_row_broadcast(pr, psBC, kraw[0:1, 0:1024], brow_tok, 1024)

        def ev_b(s):
            pr.wait(s, "s", bc_tok)
            pr.inc(s.copy(bias_b[:, :], psBC[:, :]), "s")
        pr.blk(s=ev_b)
        bias_tok = pr.tok("s")

        kq_toks = []
        for c in range(8):
            buf = c % 2
            kslc = slice(buf * 1024, (buf + 1) * 1024)

            def ldk(q, c=c, kslc=kslc):
                if c >= 2:
                    pr.wait(q, "q", kq_toks[c - 2])
                pr.inc(q.dma_start(out=kraw[:, kslc], in_=w["kernel"][c]), "qd")
            pr.blk(q=ldk)
            ktok = pr.tok("qd")

            def mulk(v, c=c, kslc=kslc, ktok=ktok):
                pr.wait(v, "v", ktok, sc_tok)
                pr.inc(v.tensor_scalar(kf[:, 0:1024], kraw[:, kslc], sc[:, 3:4],
                                       None, ALU.mult), "v")
            pr.blk(v=mulk)
            mtok = pr.tok("v")
            kq_toks.append(emit_fp8_quant(pr, kq[c], kf[:, 0:1024], mtok,
                                          f8a[:, 0:1024], f8b[:, 0:1024],
                                          scf[:, 0:1024], mskt[:, 0:1024]))

        xq_toks = []
        for c in range(8):
            def mulx(v, c=c):
                pr.wait(v, "v", lnT_tok, sc_tok)
                if c >= 1:
                    pr.wait(v, "v", xq_toks[c - 1])
                pr.inc(v.tensor_scalar(kf[:, :], lnT_tiles[c][:, :], sc[:, 2:3],
                                       None, ALU.mult), "v")
            pr.blk(v=mulx)
            mtok = pr.tok("v")
            xq_toks.append(emit_fp8_quant(pr, xq[c], kf, mtok, f8a, f8b, scf, mskt))
        xq_tok = xq_toks[-1]

        prev = {}
        for i in range(NT):
            bank = psC if i % 2 == 0 else psD

            def mmx(t, i=i, bank=bank):
                pr.wait(t, "t", xq_tok, *kq_toks)
                pr.wait(t, "t", prev.get(i - 2))
                last = None
                for mh in range(2):
                    for c in range(8):
                        last = t.matmul(bank[:, mh * 512:(mh + 1) * 512],
                                        xq[c][:, i * 128:(i + 1) * 128],
                                        kq[c][:, mh * 512:(mh + 1) * 512],
                                        start=(c == 0), stop=(c == 7))
                pr.inc(last, "t")
            pr.blk(t=mmx)
            mmtok = pr.tok("t")

            def evy(v, i=i, bank=bank, mmtok=mmtok):
                pr.wait(v, "v", mmtok, bias_tok)
                pr.inc(v.tensor_scalar(yb[:, :], bank[:, :], sc[:, 4:5], None,
                                       ALU.mult), "v")
                pr.wait(v, "v", pr.tok("v"))
                pr.inc(v.tensor_tensor(yb[:, :], yb[:, :], bias_b[:, :], ALU.add), "v")
            pr.blk(v=evy)
            ytok = pr.tok("v")
            prev[i] = xio.add_store(i, yb[:, :], ytok)
        return [prev[i] for i in range(NT)]



def stage_L0_attn_b(inputs, pre):
    WQ, WK, WV = (np.asarray(inputs["L0_attn_b_WQ"]), np.asarray(inputs["L0_attn_b_WK"]),
                  np.asarray(inputs["L0_attn_b_WV"]))
    pre["L0ab_WQt"] = np.ascontiguousarray(
        WQ.reshape(8, 128, 8, 128).transpose(0, 2, 3, 1)).reshape(8, 128 * 8, 128)
    pre["L0ab_WKt"] = np.ascontiguousarray(
        WK.reshape(8, 128, 8, 128).transpose(0, 2, 3, 1)).reshape(8, 128 * 8, 128)
    pre["L0ab_WVt"] = np.ascontiguousarray(WV.T).reshape(8, 128, 1024)


def decl_L0_attn_b(nc):
    return {
        "WQt": nc.dram_tensor("L0ab_WQt", [8, 128 * 8, 128], F32, kind="ExternalInput").ap(),
        "WKt": nc.dram_tensor("L0ab_WKt", [8, 128 * 8, 128], F32, kind="ExternalInput").ap(),
        "WVt": nc.dram_tensor("L0ab_WVt", [8, 128, 1024], F32, kind="ExternalInput").ap(),
    }


def emit_proj_T(pr, lnT_tiles, lnT_tok, wdram, wsl, qT, tag, psum, n_kt=8,
                evac_bf16=True, war_tok=None, act=None):
    nc = pr.nc
    ev_done = {}
    ld_done = {}
    for kt in range(n_kt):
        buf = kt % 2
        wslc = slice(buf * 1024, (buf + 1) * 1024)

        def ldw(g, kt=kt, wslc=wslc):
            if kt >= 2:
                pr.wait(g, "g", ev_done.get((kt - 2, 3)))
            else:
                pr.wait(g, "g", war_tok)
            pr.inc(g.dma_start(
                out=wsl[:, wslc].rearrange("p (c k) -> p c k", c=8),
                in_=wdram[kt].rearrange("(c p) k -> p c k", p=128)), "gd")
        pr.blk(g=ldw)
        ld_done[kt] = pr.tok("gd")

        for tg in range(4):
            bank = psum[tg % 2]

            def mm(t, kt=kt, tg=tg, wslc=wslc, bank=bank):
                pr.wait(t, "t", ld_done[kt], lnT_tok)
                prev = ev_done.get((kt, tg - 2)) or ev_done.get((kt - 1, tg + 2))
                pr.wait(t, "t", prev if prev is not None else war_tok)
                last = None
                for c in range(8):
                    last = t.matmul(bank[:, :], wsl[:, wslc][:, c * 128:(c + 1) * 128],
                                    lnT_tiles[c][:, tg * 512:(tg + 1) * 512],
                                    start=(c == 0), stop=(c == 7))
                pr.inc(last, "t")
            pr.blk(t=mm)
            mmtok = pr.tok("t")

            def ev(s, kt=kt, tg=tg, bank=bank, mmtok=mmtok):
                pr.wait(s, "s", mmtok)
                if act is not None:
                    pr.inc(s.activation(qT[kt][:, tg * 512:(tg + 1) * 512], bank[:, :],
                                        act), "s")
                else:
                    pr.inc(s.copy(qT[kt][:, tg * 512:(tg + 1) * 512], bank[:, :]), "s")
            pr.blk(s=ev)
            ev_done[(kt, tg)] = pr.tok("s")
    return pr.tok("s")


def emit_proj_tok(pr, lnT_tiles, lnT_tok, wdram_chunks, wbuf, out_tiles, psum,
                  tag, out_dim=1024, n_c=8, scale=None, war_tok=None,
                  bias_ap=None, bias_tok=None):
    nc = pr.nc
    def ldall(g):
        for c in range(n_c):
            pr.inc(g.dma_start(out=wbuf[:, c * out_dim:(c + 1) * out_dim],
                               in_=wdram_chunks[c]), "gd")
    pr.blk(g=ldall)
    wtok = pr.tok("gd")

    ev_done = {-2: war_tok, -1: war_tok}
    for i in range(NT):
        bank = psum[i % 2]

        def mm(t, i=i, bank=bank):
            pr.wait(t, "t", wtok, lnT_tok)
            pr.wait(t, "t", ev_done.get(i - 2))
            last = None
            for mh in range((out_dim + 511) // 512):
                o0, o1 = mh * 512, min(out_dim, (mh + 1) * 512)
                for c in range(n_c):
                    last = t.matmul(bank[:, o0:o1],
                                    lnT_tiles[c][:, i * 128:(i + 1) * 128],
                                    wbuf[:, c * out_dim + o0:c * out_dim + o1],
                                    start=(c == 0), stop=(c == n_c - 1))
            pr.inc(last, "t")
        pr.blk(t=mm)
        mmtok = pr.tok("t")

        if bias_ap is None:
            def ev(s, i=i, bank=bank, mmtok=mmtok):
                pr.wait(s, "s", mmtok)
                if scale is not None:
                    pr.inc(s.activation(out_tiles[i][:, :], bank[:, :], AF.Copy,
                                        scale=float(scale)), "s")
                else:
                    pr.inc(s.copy(out_tiles[i][:, :], bank[:, :]), "s")
            pr.blk(s=ev)
            ev_done[i] = pr.tok("s")
        else:
            def ev(v, i=i, bank=bank, mmtok=mmtok):
                pr.wait(v, "v", mmtok, bias_tok)
                pr.inc(v.tensor_add(out_tiles[i][:, :], bank[:, :], bias_ap), "v")
            pr.blk(v=ev)
            ev_done[i] = pr.tok("v")
    return ev_done[NT - 1]


def emit_L0_attn_b(pr, stack, xio, lnT_tiles, lnT_tok, w):
    nc = pr.nc
    with ExitStack() as ls:
        lctx = ls.enter_context
        qT = [lctx(nc.sbuf_tensor(f"l0b_qT{c}", [P, TT], BF16)) for c in range(8)]
        kT = [lctx(nc.sbuf_tensor(f"l0b_kT{c}", [P, TT], BF16)) for c in range(8)]
        vt = [lctx(nc.sbuf_tensor(f"l0b_v{i}", [P, 1024], BF16)) for i in range(NT)]
        wsl = lctx(nc.sbuf_tensor("l0b_wsl", [P, 2048], BF16))
        wv = lctx(nc.sbuf_tensor("l0b_wv", [P, 8 * 1024], BF16))
        ws = lctx(nc.sbuf_tensor("l0b_ws", [P, 2 * 256], BF16))
        rz = lctx(nc.sbuf_tensor("l0b_rz", [P, 4], F32))
        with ExitStack() as ps_stack:
            psA = ps_stack.enter_context(nc.psum_tensor("l0b_psA", [P, 512], F32))
            psB = ps_stack.enter_context(nc.psum_tensor("l0b_psB", [P, 512], F32))
            qT_tok = emit_proj_T(pr, lnT_tiles, lnT_tok, w["WQt"], wsl, qT, "q", [psA, psB])
            kT_tok = emit_proj_T(pr, lnT_tiles, lnT_tok, w["WKt"], wsl, kT, "k", [psA, psB],
                                 war_tok=qT_tok)
            psV = [ps_stack.enter_context(nc.psum_tensor(f"l0b_psV{j}", [P, 1024], F32))
                   for j in range(2)]
            v_tok = emit_proj_tok(pr, lnT_tiles, lnT_tok,
                                  [w["WVt"][c] for c in range(8)], wv, vt, psV, "v",
                                  war_tok=kT_tok)
        psS = [lctx(nc.psum_tensor(f"l0b_psS{j}", [P, 256], F32)) for j in range(2)]
        psO = [lctx(nc.psum_tensor(f"l0b_psO{j}", [P, 1536], F32)) for j in range(2)]

        out_toks = [None] * NT
        prev_group = {}
        prev_last_exp = {}
        for ig in range(8):
            i0 = ig * 256
            jt_max = (i0 + 255) // 128
            exp_done = {}
            attn_done = {-2: prev_group.get(ig - 2), -1: prev_group.get(ig - 2)}
            for jt in range(jt_max + 1):
                sbank = psS[jt % 2]
                wslc = slice((jt % 2) * 256, (jt % 2 + 1) * 256)

                def mmsc(t, ig=ig, jt=jt, sbank=sbank, i0=i0):
                    pr.wait(t, "t", qT_tok, kT_tok)
                    pr.wait(t, "t", exp_done.get(jt - 2) if jt >= 2
                            else prev_last_exp.get(ig - 1))
                    last = None
                    for c in range(8):
                        last = t.matmul(sbank[:, :], kT[c][:, jt * 128:(jt + 1) * 128],
                                        qT[c][:, i0:i0 + 256], start=(c == 0), stop=(c == 7))
                    pr.inc(last, "t")
                pr.blk(t=mmsc)
                sctok = pr.tok("t")

                def evexp(s, jt=jt, sbank=sbank, wslc=wslc, sctok=sctok, ig=ig):
                    pr.wait(s, "s", sctok)
                    pr.wait(s, "s", attn_done.get(jt - 2))
                    pr.inc(s.activation(ws[:, wslc], sbank[:, :], AF.Exp,
                                        scale=1.0 / 32.0), "s")
                pr.blk(s=evexp)
                etok = pr.tok("s")

                if jt * 128 + 127 > i0:
                    def msk(g, jt=jt, wslc=wslc, etok=etok, i0=i0):
                        pr.wait(g, "g", etok)
                        pr.inc(g.affine_select(ws[:, wslc], ws[:, wslc], [[1, 256]],
                                               ALU.is_ge, 0.0, base=i0 - jt * 128,
                                               channel_multiplier=-1), "g")
                    pr.blk(g=msk)
                    etok = pr.tok("g")

                def mmav(t, ig=ig, jt=jt, wslc=wslc, etok=etok, jt_max=jt_max):
                    pr.wait(t, "t", etok, v_tok)
                    if jt == 0:
                        pr.wait(t, "t", prev_group.get(ig - 2))
                    last = None
                    for isub in range(2):
                        lhs = ws[:, wslc][:, isub * 128:(isub + 1) * 128]
                        for mh in range(2):
                            last = t.matmul(psO[isub][:, mh * 512:(mh + 1) * 512],
                                            lhs, vt[jt][:, mh * 512:(mh + 1) * 512],
                                            start=(jt == 0), stop=(jt == jt_max))
                        last = t.matmul(psO[isub][:, 1024:1025],
                                        lhs, pr.ones_bf16[:, 0:1],
                                        start=(jt == 0), stop=(jt == jt_max))
                    pr.inc(last, "t")
                pr.blk(t=mmav)
                attn_done[jt] = pr.tok("t")
                exp_done[jt] = attn_done[jt]

            avtok = pr.tok("t")
            prev_last_exp[ig] = pr.tok("s")

            def rzc(v, ig=ig, avtok=avtok):
                pr.wait(v, "v", avtok)
                pr.inc(v.reciprocal(rz[:, (ig % 2) * 2:(ig % 2) * 2 + 1],
                                    psO[0][:, 1024:1025]), "v")
                pr.inc(v.reciprocal(rz[:, (ig % 2) * 2 + 1:(ig % 2) * 2 + 2],
                                    psO[1][:, 1024:1025]), "v")
                pr.wait(v, "v", pr.tok("v"))
            pr.blk(v=rzc)
            rztok = pr.tok("v")

            for isub in range(2):
                i = ig * 2 + isub
                out_toks[i] = xio.add_store(
                    i, psO[isub][:, 0:1024], rztok,
                    scale_ap=rz[:, (ig % 2) * 2 + isub:(ig % 2) * 2 + isub + 1])
            prev_group[ig] = out_toks[ig * 2 + 1]
        return out_toks



def stage_L1_attn_a(inputs, pre):
    for nm in ("wq", "wk", "wv"):
        W = np.asarray(inputs[f"L1_attn_a_{nm}"])
        pre[f"L1aa_{nm}T"] = np.ascontiguousarray(W.T).reshape(8, 128, 1024)


def rope_tables_L1():
    HD = 64
    inv = 1.0 / (10000.0 ** (np.arange(0, HD, 2, dtype=np.float64) / HD))
    t = np.arange(TT, dtype=np.float64)
    fr = np.outer(t, inv)
    cos = np.cos(fr); sin = np.sin(fr)
    cos2 = np.repeat(cos, 2, axis=1)
    sin2 = np.repeat(sin, 2, axis=1)
    return (np.ascontiguousarray(cos2.reshape(16, 128, 64).astype(np.float32)),
            np.ascontiguousarray(sin2.reshape(16, 128, 64).astype(np.float32)))


def decl_L1_attn_a(nc):
    return {
        "wqT": nc.dram_tensor("L1aa_wqT", [8, 128, 1024], F32, kind="ExternalInput").ap(),
        "wkT": nc.dram_tensor("L1aa_wkT", [8, 128, 1024], F32, kind="ExternalInput").ap(),
        "wvT": nc.dram_tensor("L1aa_wvT", [8, 128, 1024], F32, kind="ExternalInput").ap(),
    }


def emit_rope_inplace(pr, tiles, toks, cs_sb, cs_tok, rot, t1):
    new_toks = []
    for i, (q, qtok) in enumerate(zip(tiles, toks)):
        cos2 = cs_sb[:, i * 128:i * 128 + 64]
        sin2 = cs_sb[:, i * 128 + 64:i * 128 + 128]

        def body(v, q=q, qtok=qtok, cos2=cos2, sin2=sin2):
            pr.wait(v, "v", qtok, cs_tok)
            qv = q[:, :].rearrange("p (x two) -> p x two", two=2)
            rv = rot[:, :].rearrange("p (x two) -> p x two", two=2)
            pr.inc(v.tensor_scalar(rv[:, :, 0:1], qv[:, :, 1:2], -1.0, None, ALU.mult), "v")
            pr.inc(v.tensor_copy(rv[:, :, 1:2], qv[:, :, 0:1]), "v")
            pr.wait(v, "v", pr.tok("v"))
            q3 = q[:, :].rearrange("p (h d) -> p h d", d=64)
            r3 = rot[:, :].rearrange("p (h d) -> p h d", d=64)
            t3 = t1[:, :].rearrange("p (h d) -> p h d", d=64)
            cb = cos2.unsqueeze(1).broadcast_to([P, 16, 64])
            sb = sin2.unsqueeze(1).broadcast_to([P, 16, 64])
            pr.inc(v.tensor_tensor(t3, q3, cb, ALU.mult), "v")
            pr.inc(v.tensor_tensor(r3, r3, sb, ALU.mult), "v")
            pr.wait(v, "v", pr.tok("v"))
            pr.inc(v.tensor_tensor(q[:, :], t1[:, :], rot[:, :], ALU.add), "v")
        pr.blk(v=body)
        new_toks.append(pr.tok("v"))
    return new_toks


def emit_L1_attn_a(pr, stack, xio, lnT_tiles, lnT_tok, w, rope_consts):
    nc = pr.nc
    cos2_np, sin2_np = rope_consts
    cs_np = np.concatenate([cos2_np, sin2_np], axis=2).reshape(16, 128, 128)
    cs_np = np.ascontiguousarray(cs_np.transpose(1, 0, 2)).reshape(128, 16 * 128)
    cs_dram = nc.inline_tensor(cs_np, "l1aa_cs")

    def ptok(dct, ig, h, back=2, nh=16, default=None):
        hh = h - back
        igg = ig
        while hh < 0:
            hh += nh
            igg -= 1
        return dct.get((igg, hh), default)

    with ExitStack() as ls:
        lctx = ls.enter_context
        qT = [lctx(nc.sbuf_tensor(f"l1aa_qT{c}", [P, TT], BF16)) for c in range(8)]
        kT = [lctx(nc.sbuf_tensor(f"l1aa_kT{c}", [P, TT], BF16)) for c in range(8)]
        wbuf = lctx(nc.sbuf_tensor("l1aa_wbuf", [P, 8 * 1024], BF16))

        with ExitStack() as qk_stack:
            qctx = qk_stack.enter_context
            cs_sb = qctx(nc.sbuf_tensor("l1aa_cs_sb", [P, 16 * 128], F32))
            rot = qctx(nc.sbuf_tensor("l1aa_rot", [P, 1024], F32))
            t1s = qctx(nc.sbuf_tensor("l1aa_t1", [P, 1024], F32))

            def ldcs(q):
                pr.inc(q.dma_start(out=cs_sb[:, :], in_=cs_dram.ap()), "qd")
            pr.blk(q=ldcs)
            cs_tok = pr.tok("qd")
            qtl = [qctx(nc.sbuf_tensor(f"l1aa_q{i}", [P, D], BF16)) for i in range(NT)]
            ktl = [qctx(nc.sbuf_tensor(f"l1aa_k{i}", [P, D], BF16)) for i in range(NT)]
            psP = [qctx(nc.psum_tensor(f"l1aa_psP{j}", [P, 1024], F32)) for j in range(2)]
            q_tok = emit_proj_tok(pr, lnT_tiles, lnT_tok, [w["wqT"][c] for c in range(8)],
                                  wbuf, qtl, psP, "q")
            k_tok = emit_proj_tok(pr, lnT_tiles, lnT_tok, [w["wkT"][c] for c in range(8)],
                                  wbuf, ktl, psP, "k", war_tok=q_tok)
            q_toks = emit_rope_inplace(pr, qtl, [q_tok] * NT, cs_sb, cs_tok, rot, t1s)
            k_toks = emit_rope_inplace(pr, ktl, [k_tok] * NT, cs_sb, cs_tok, rot, t1s)
            qT_tok = emit_transpose_fm(pr, qk_stack, qtl, q_toks, qT, tag="l1aaq")
            kT_tok = emit_transpose_fm(pr, qk_stack, ktl, k_toks, kT, tag="l1aak")

        vx = [lctx(nc.sbuf_tensor(f"l1aa_vx{i}", [P, 16 * 65], BF16)) for i in range(NT)]
        onT = [lctx(nc.sbuf_tensor(f"l1aa_onT{c}", [P, TT], BF16)) for c in range(8)]
        vp_stack = ExitStack()
        psP2 = [vp_stack.enter_context(nc.psum_tensor(f"l1aa_psP2{j}", [P, 1024], F32)) for j in range(2)]

        def ldvw(g):
            pr.wait(g, "g", kT_tok)
            for c in range(8):
                pr.inc(g.dma_start(out=wbuf[:, c * 1024:(c + 1) * 1024],
                                   in_=w["wvT"][c]), "gd")
        pr.blk(g=ldvw)
        vw_tok = pr.tok("gd")

        def vones(v):
            for i in range(NT):
                pr.inc(v.memset(vx[i][:, :].rearrange("p (h e) -> p h e", e=65)[:, :, 64:65],
                                1.0), "v")
        pr.blk(v=vones)
        vones_tok = pr.tok("v")

        ev_done = {}
        for i in range(NT):
            bank = psP2[i % 2]

            def mmv(t, i=i, bank=bank):
                pr.wait(t, "t", lnT_tok, vw_tok)
                pr.wait(t, "t", ev_done.get(i - 2, kT_tok))
                last = None
                for mh in range(2):
                    for c in range(8):
                        last = t.matmul(bank[:, mh * 512:(mh + 1) * 512],
                                        lnT_tiles[c][:, i * 128:(i + 1) * 128],
                                        wbuf[:, c * 1024 + mh * 512:c * 1024 + (mh + 1) * 512],
                                        start=(c == 0), stop=(c == 7))
                pr.inc(last, "t")
            pr.blk(t=mmv)
            mmtok = pr.tok("t")

            def evv(s, i=i, bank=bank, mmtok=mmtok):
                pr.wait(s, "s", mmtok, vones_tok)
                dst = vx[i][:, :].rearrange("p (h e) -> p h e", e=65)[:, :, 0:64]
                pr.inc(s.copy(dst, bank[:, :].rearrange("p (h d) -> p h d", d=64)), "s")
            pr.blk(s=evv)
            ev_done[i] = pr.tok("s")
        v_tok = pr.tok("s")
        vp_stack.close()

        ws = lctx(nc.sbuf_tensor("l1aa_ws", [P, 2 * 512], BF16))
        rzrow = lctx(nc.sbuf_tensor("l1aa_rzrow", [1, 512], F32))
        rzb = lctx(nc.sbuf_tensor("l1aa_rzb", [P, 512], F32))
        att_stack = ExitStack()
        psS = [att_stack.enter_context(nc.psum_tensor(f"l1aa_psS{j}", [P, 512], F32)) for j in range(2)]
        psO = [att_stack.enter_context(nc.psum_tensor(f"l1aa_psO{j}", [P, 512], F32)) for j in range(2)]
        psB2 = att_stack.enter_context(nc.psum_tensor("l1aa_psB2", [P, 512], F32))

        last_exp = {}
        norm_done = {}
        rz_done = {}
        for ig in range(4):
            i0 = ig * 512
            for h in range(16):
                cchunk, coff = h // 2, (h % 2) * 64
                for jt in range(NT):
                    sb = psS[jt % 2]
                    wslc = slice((jt % 2) * 512, (jt % 2 + 1) * 512)

                    def mmsc(t, ig=ig, h=h, jt=jt, sb=sb, i0=i0, cchunk=cchunk, coff=coff):
                        pr.wait(t, "t", qT_tok, kT_tok)
                        if jt < 2:
                            pr.wait(t, "t", ptok(last_exp, ig, h, 1))
                        else:
                            pr.wait(t, "t", exp_done[jt - 2])
                        pr.inc(t.matmul(sb[:, :],
                                        kT[cchunk][coff:coff + 64, jt * 128:(jt + 1) * 128],
                                        qT[cchunk][coff:coff + 64, i0:i0 + 512],
                                        start=True, stop=True), "t")
                    pr.blk(t=mmsc)
                    sctok = pr.tok("t")

                    def evexp(s, jt=jt, sb=sb, wslc=wslc, sctok=sctok, ig=ig, h=h):
                        pr.wait(s, "s", sctok)
                        if jt >= 2:
                            pr.wait(s, "s", attn_done[jt - 2])
                        elif jt < 2 and (ig + h) > 0:
                            pr.wait(s, "s", ptok(last_attn, ig, h, 1))
                        pr.inc(s.activation(ws[:, wslc], sb[:, :], AF.Exp,
                                            scale=0.125), "s")
                    pr.blk(s=evexp)
                    if jt == 0:
                        exp_done = {}
                        attn_done = {}
                    exp_done[jt] = pr.tok("s")

                    def mmav(t, ig=ig, h=h, jt=jt, wslc=wslc):
                        pr.wait(t, "t", exp_done[jt], v_tok)
                        if jt == 0:
                            pr.wait(t, "t", ptok(norm_done, ig, h, 2))
                        pr.inc(t.matmul(psO[h % 2][0:65, :],
                                        vx[jt][:, h * 65:(h + 1) * 65],
                                        ws[:, wslc],
                                        start=(jt == 0), stop=(jt == NT - 1)), "t")
                    pr.blk(t=mmav)
                    attn_done[jt] = pr.tok("t")
                last_exp[(ig, h)] = exp_done[NT - 1]
                last_attn = dict() if (ig == 0 and h == 0) else last_attn
                last_attn[(ig, h)] = attn_done[NT - 1]
                avtok = attn_done[NT - 1]

                def rzc(v, h=h, avtok=avtok, ig=ig):
                    pr.wait(v, "v", avtok)
                    pr.wait(v, "v", rz_done.get("bc"))
                    pr.inc(v.reciprocal(rzrow[0:1, :], psO[h % 2][64:65, :]), "v")
                pr.blk(v=rzc)
                rztok = pr.tok("v")

                def bcz(t, rztok=rztok):
                    pr.wait(t, "t", rztok, pr.ones_tok)
                    pr.wait(t, "t", rz_done.get("cp"))
                    pr.inc(t.matmul(psB2[:, :], pr.ones_f32[0:1, 0:128],
                                    rzrow[0:1, :], start=True, stop=True), "t")
                pr.blk(t=bcz)
                rz_done["bc"] = pr.tok("t")

                def cpz(s):
                    pr.wait(s, "s", rz_done["bc"])
                    pr.wait(s, "s", rz_done.get("mul"))
                    pr.inc(s.copy(rzb[:, :], psB2[:, :]), "s")
                pr.blk(s=cpz)
                rz_done["cp"] = pr.tok("s")

                def nrm(v, ig=ig, h=h, cchunk=cchunk, coff=coff):
                    pr.wait(v, "v", rz_done["cp"])
                    pr.inc(v.tensor_tensor(onT[cchunk][coff:coff + 64, i0:i0 + 512],
                                           psO[h % 2][0:64, :], rzb[0:64, :],
                                           ALU.mult), "v")
                pr.blk(v=nrm)
                norm_done[(ig, h)] = pr.tok("v")
                rz_done["mul"] = norm_done[(ig, h)]

        onT_tok = pr.tok("v")
        att_stack.close()

        ytl = [lctx(nc.sbuf_tensor(f"l1aa_y{j}", [P, D], BF16)) for j in range(2)]
        out_toks = [None] * NT
        with ExitStack() as trs:
            psT = [trs.enter_context(nc.psum_tensor(f"l1aa_psT{j}", [P, 1024], BF16))
                   for j in range(2)]
            ytok = {}
            for i in range(NT):
                bank = psT[i % 2]
                yt = ytl[i % 2]

                def trb(t, i=i, bank=bank):
                    pr.wait(t, "t", onT_tok, pr.ident_tok)
                    pr.wait(t, "t", ytok.get(i - 2))
                    last = None
                    for c in range(8):
                        last = t.transpose(bank[:, c * 128:(c + 1) * 128],
                                           onT[c][:, i * 128:(i + 1) * 128],
                                           pr.ident[:, :])
                    pr.inc(last, "t")
                pr.blk(t=trb)
                ttok = pr.tok("t")

                def evy(s, i=i, bank=bank, yt=yt, ttok=ttok):
                    pr.wait(s, "s", ttok)
                    pr.inc(s.copy(yt[:, :], bank[:, :]), "s")
                pr.blk(s=evy)
                sy = pr.tok("s")
                out_toks[i] = xio.add_store(i, yt[:, :], sy)
                ytok[i] = out_toks[i]
        return out_toks



def stage_L0_attn_a(inputs, pre):
    for nm in ("Wq", "Wk", "Wv", "Wo"):
        W = np.asarray(inputs[f"L0_attn_a_{nm}"])
        pre[f"L0aa_{nm}T"] = np.ascontiguousarray(W.T).reshape(8, 128, 1024)
    wfT = np.ascontiguousarray(np.asarray(inputs["L0_attn_a_Wf"]).T)
    pre["L0aa_WfT"] = np.ascontiguousarray(np.concatenate([wfT, wfT], axis=0))


def decl_L0_attn_a(nc):
    d = {}
    for nm in ("WqT", "WkT", "WvT", "WoT"):
        d[nm] = nc.dram_tensor(f"L0aa_{nm}", [8, 128, 1024], F32, kind="ExternalInput").ap()
    d["WfT"] = nc.dram_tensor("L0aa_WfT", [128, 256], F32, kind="ExternalInput").ap()
    return d


def emit_elu1_evac(pr, dst, psrc, mmtok, tmp_r, tmp_m, war=None):
    def b1(s):
        pr.wait(s, "s", mmtok)
        if war is not None:
            pr.wait(s, "s", war)
        pr.inc(s.activation(tmp_r[:, :], psrc, AF.Relu), "s")
    pr.blk(s=b1)
    rtok = pr.tok("s")

    def b2(v):
        pr.wait(v, "v", rtok)
        pr.inc(v.tensor_tensor(tmp_m[:, :], psrc, tmp_r[:, :], ALU.subtract), "v")
    pr.blk(v=b2)
    vtok = pr.tok("v")

    def b3(s):
        pr.wait(s, "s", vtok)
        pr.inc(s.activation(tmp_m[:, :], tmp_m[:, :], AF.Exp), "s")
    pr.blk(s=b3)
    etok = pr.tok("s")

    def b4(v):
        pr.wait(v, "v", etok)
        pr.inc(v.tensor_add(dst, tmp_r[:, :], tmp_m[:, :]), "v")
    pr.blk(v=b4)
    return pr.tok("v")


def emit_L0_attn_a(pr, stack, xio, lnT_tiles, lnT_tok, w):
    nc = pr.nc
    with ExitStack() as ls:
        lctx = ls.enter_context
        qT = [lctx(nc.sbuf_tensor(f"l0aa_qT{c}", [P, TT], BF16)) for c in range(8)]
        kT = [lctx(nc.sbuf_tensor(f"l0aa_kT{c}", [P, TT], BF16)) for c in range(8)]
        wbuf = lctx(nc.sbuf_tensor("l0aa_wbuf", [P, 8 * 1024], BF16))
        wfT = lctx(nc.sbuf_tensor("l0aa_wfT", [128, 256], BF16))

        def ldwf(g):
            pr.inc(g.dma_start(out=wfT[:, :], in_=w["WfT"]), "gd")
        pr.blk(g=ldwf)
        wf_tok = pr.tok("gd")

        with ExitStack() as qk_stack:
            qctx = qk_stack.enter_context
            qtl = [qctx(nc.sbuf_tensor(f"l0aa_q{i}", [P, D], BF16)) for i in range(NT)]
            ktl = [qctx(nc.sbuf_tensor(f"l0aa_k{i}", [P, D], BF16)) for i in range(NT)]
            psP = [qctx(nc.psum_tensor(f"l0aa_psP{j}", [P, 1024], F32)) for j in range(2)]
            q_tok = emit_proj_tok(pr, lnT_tiles, lnT_tok, [w["WqT"][c] for c in range(8)],
                                  wbuf, qtl, psP, "q")
            k_tok = emit_proj_tok(pr, lnT_tiles, lnT_tok, [w["WkT"][c] for c in range(8)],
                                  wbuf, ktl, psP, "k", war_tok=q_tok)
            qT_tok = emit_transpose_fm(pr, qk_stack, qtl, [q_tok] * NT, qT, tag="l0aaq")
            kT_tok = emit_transpose_fm(pr, qk_stack, ktl, [k_tok] * NT, kT, tag="l0aak")

        vx = [lctx(nc.sbuf_tensor(f"l0aa_vx{i}", [P, 16 * 65], BF16)) for i in range(NT)]
        att = [lctx(nc.sbuf_tensor(f"l0aa_att{i}", [P, 1024], BF16)) for i in range(NT)]
        vp_stack = ExitStack()
        psP2 = [vp_stack.enter_context(nc.psum_tensor(f"l0aa_psP2{j}", [P, 1024], F32))
                for j in range(2)]

        def ldvw(g):
            pr.wait(g, "g", kT_tok)
            for c in range(8):
                pr.inc(g.dma_start(out=wbuf[:, c * 1024:(c + 1) * 1024],
                                   in_=w["WvT"][c]), "gd")
        pr.blk(g=ldvw)
        vw_tok = pr.tok("gd")

        def vones(v):
            for i in range(NT):
                pr.inc(v.memset(vx[i][:, :].rearrange("p (h e) -> p h e", e=65)[:, :, 64:65],
                                1.0), "v")
        pr.blk(v=vones)
        vones_tok = pr.tok("v")

        ev_done = {}
        for i in range(NT):
            bank = psP2[i % 2]

            def mmv(t, i=i, bank=bank):
                pr.wait(t, "t", lnT_tok, vw_tok)
                pr.wait(t, "t", ev_done.get(i - 2, kT_tok))
                last = None
                for mh in range(2):
                    for c in range(8):
                        last = t.matmul(bank[:, mh * 512:(mh + 1) * 512],
                                        lnT_tiles[c][:, i * 128:(i + 1) * 128],
                                        wbuf[:, c * 1024 + mh * 512:c * 1024 + (mh + 1) * 512],
                                        start=(c == 0), stop=(c == 7))
                pr.inc(last, "t")
            pr.blk(t=mmv)
            mmtok = pr.tok("t")

            def evv(s, i=i, bank=bank, mmtok=mmtok):
                pr.wait(s, "s", mmtok, vones_tok)
                dst = vx[i][:, :].rearrange("p (h e) -> p h e", e=65)[:, :, 0:64]
                pr.inc(s.copy(dst, bank[:, :].rearrange("p (h d) -> p h d", d=64)), "s")
            pr.blk(s=evv)
            ev_done[i] = pr.tok("s")
        v_tok = pr.tok("s")
        vp_stack.close()

        hd_stack = ExitStack()
        hctx = hd_stack.enter_context
        kf1 = hctx(nc.sbuf_tensor("l0aa_kf1", [P, 16 * 256], BF16))
        qf1 = [hctx(nc.sbuf_tensor(f"l0aa_qf1{j}", [P, TT], BF16)) for j in range(2)]
        kvb = hctx(nc.sbuf_tensor("l0aa_kvb", [P, 2 * 65], BF16))
        tr1 = hctx(nc.sbuf_tensor("l0aa_tr1", [P, 512], F32))
        tr2 = hctx(nc.sbuf_tensor("l0aa_tr2", [P, 512], F32))
        rn = hctx(nc.sbuf_tensor("l0aa_rn", [P, 2], F32))
        psF = [hctx(nc.psum_tensor(f"l0aa_psF{j}", [P, 512], F32)) for j in range(2)]
        psKV = hctx(nc.psum_tensor("l0aa_psKV", [P, 130], F32))
        psA = [hctx(nc.psum_tensor(f"l0aa_psA{j}", [P, 65], F32)) for j in range(2)]

        prev_head = {}
        att_done = {}
        for h in range(16):
            cchunk, coff = h // 2, (h % 2) * 64
            kf_toks = []
            for i in range(NT):
                bank = psF[i % 2]

                def mmkf(t, h=h, i=i, bank=bank, cchunk=cchunk, coff=coff):
                    pr.wait(t, "t", kT_tok, wf_tok)
                    pr.wait(t, "t", kf_toks[i - 2] if i >= 2 else prev_head.get(h - 1))
                    pr.inc(t.matmul(bank[:, 0:256],
                                    kT[cchunk][coff:coff + 64, i * 128:(i + 1) * 128],
                                    wfT[coff:coff + 64, :], start=True, stop=True), "t")
                pr.blk(t=mmkf)
                mmtok = pr.tok("t")
                kf_toks.append(emit_elu1_evac(
                    pr, kf1[:, i * 256:(i + 1) * 256], bank[:, 0:256], mmtok,
                    tr1[:, 0:256], tr2[:, 0:256],
                    war=kf_toks[i - 1] if i >= 1 else prev_head.get(h - 1)))
            kf_tok = kf_toks[-1]

            def mmkv(t, h=h, kf_tok=kf_tok):
                pr.wait(t, "t", kf_tok, v_tok)
                pr.wait(t, "t", prev_head.get(h - 1))
                last = None
                for fh in range(2):
                    for i in range(NT):
                        last = t.matmul(psKV[:, fh * 65:(fh + 1) * 65],
                                        kf1[:, i * 256 + fh * 128:i * 256 + (fh + 1) * 128],
                                        vx[i][:, h * 65:(h + 1) * 65],
                                        start=(i == 0), stop=(i == NT - 1))
                pr.inc(last, "t")
            pr.blk(t=mmkv)
            kv_mm = pr.tok("t")

            def evkv(s, kv_mm=kv_mm):
                pr.wait(s, "s", kv_mm)
                pr.inc(s.copy(kvb[:, :], psKV[:, :]), "s")
            pr.blk(s=evkv)
            kv_tok = pr.tok("s")

            qf_toks = []
            for fh in range(2):
                for tg in range(4):
                    bank = psF[tg % 2]

                    def mmqf(t, h=h, fh=fh, tg=tg, bank=bank, cchunk=cchunk, coff=coff):
                        pr.wait(t, "t", qT_tok, wf_tok)
                        pr.wait(t, "t", qf_toks[-2] if len(qf_toks) >= 2 else kv_tok)
                        pr.inc(t.matmul(bank[:, :],
                                        wfT[coff:coff + 64, fh * 128:(fh + 1) * 128],
                                        qT[cchunk][coff:coff + 64, tg * 512:(tg + 1) * 512],
                                        start=True, stop=True), "t")
                    pr.blk(t=mmqf)
                    mmtok = pr.tok("t")
                    qf_toks.append(emit_elu1_evac(
                        pr, qf1[fh][:, tg * 512:(tg + 1) * 512], bank[:, :], mmtok,
                        tr1[:, :], tr2[:, :],
                        war=qf_toks[-1] if qf_toks else kf_tok))
            qf_tok = qf_toks[-1]

            for i in range(NT):
                bank = psA[i % 2]

                def mmo(t, h=h, i=i, bank=bank, qf_tok=qf_tok, kv_tok=kv_tok):
                    pr.wait(t, "t", qf_tok, kv_tok)
                    pr.wait(t, "t", att_done.get((h, i - 2), prev_head.get(h - 1)))
                    last = None
                    for fh in range(2):
                        last = t.matmul(bank[:, :],
                                        qf1[fh][:, i * 128:(i + 1) * 128],
                                        kvb[:, fh * 65:(fh + 1) * 65],
                                        start=(fh == 0), stop=(fh == 1))
                    pr.inc(last, "t")
                pr.blk(t=mmo)
                mmtok = pr.tok("t")

                def evo(v, h=h, i=i, bank=bank, mmtok=mmtok):
                    pr.wait(v, "v", mmtok)
                    pr.inc(v.tensor_scalar(rn[:, i % 2:i % 2 + 1], bank[:, 64:65],
                                           1e-6, None, ALU.add), "v")
                    pr.wait(v, "v", pr.tok("v"))
                    pr.inc(v.reciprocal(rn[:, i % 2:i % 2 + 1], rn[:, i % 2:i % 2 + 1]), "v")
                    pr.wait(v, "v", pr.tok("v"))
                    pr.inc(v.tensor_scalar(att[i][:, h * 64:(h + 1) * 64],
                                           bank[:, 0:64], rn[:, i % 2:i % 2 + 1],
                                           None, ALU.mult), "v")
                pr.blk(v=evo)
                att_done[(h, i)] = pr.tok("v")
            prev_head[h] = pr.tok("v")
        att_tok = pr.tok("v")
        hd_stack.close()

        attT = lnT_tiles
        with ExitStack() as fs:
            attT_tok = emit_transpose_fm(pr, fs, att, [att_tok] * NT, attT, tag="l0aat")

            def ldow(g):
                pr.wait(g, "g", v_tok)
                for c in range(8):
                    pr.inc(g.dma_start(out=wbuf[:, c * 1024:(c + 1) * 1024],
                                       in_=w["WoT"][c]), "gd")
            pr.blk(g=ldow)
            ow_tok = pr.tok("gd")

            psO = [fs.enter_context(nc.psum_tensor(f"l0aa_psO{j}", [P, 1024], F32))
                   for j in range(2)]
            out_toks = [None] * NT
            ev2 = {}
            for i in range(NT):
                bank = psO[i % 2]

                def mmo2(t, i=i, bank=bank):
                    pr.wait(t, "t", attT_tok, ow_tok)
                    pr.wait(t, "t", ev2.get(i - 2))
                    last = None
                    for mh in range(2):
                        for c in range(8):
                            last = t.matmul(bank[:, mh * 512:(mh + 1) * 512],
                                            attT[c][:, i * 128:(i + 1) * 128],
                                            wbuf[:, c * 1024 + mh * 512:c * 1024 + (mh + 1) * 512],
                                            start=(c == 0), stop=(c == 7))
                    pr.inc(last, "t")
                pr.blk(t=mmo2)
                mmtok = pr.tok("t")
                out_toks[i] = xio.add_store(i, bank[:, :], mmtok)
                ev2[i] = out_toks[i]
        return out_toks



def stage_L3_attn_b(inputs, pre):
    for nm in ("Wq", "Wk", "Wv"):
        W = np.asarray(inputs[f"L3_attn_b_{nm}"])
        pre[f"L3ab_{nm}T"] = np.ascontiguousarray(W.T).reshape(8, 128, 1024)
    for nm in ("bq", "bk", "bv"):
        pre[f"L3ab_{nm}"] = np.asarray(inputs[f"L3_attn_b_{nm}"]).reshape(1, 1024)
    pre["L3ab_gamma"] = np.asarray(inputs["L3_attn_b_gamma"]).reshape(1, 1)


def decl_L3_attn_b(nc):
    d = {}
    for nm in ("WqT", "WkT", "WvT"):
        d[nm] = nc.dram_tensor(f"L3ab_{nm}", [8, 128, 1024], F32, kind="ExternalInput").ap()
    for nm in ("bq", "bk", "bv"):
        d[nm] = nc.dram_tensor(f"L3ab_{nm}", [1, 1024], F32, kind="ExternalInput").ap()
    d["gamma"] = nc.dram_tensor("L3ab_gamma", [1, 1], F32, kind="ExternalInput").ap()
    return d


def emit_colnorm_inv(pr, tiles, tok, q2, psN, invn, sq_is_sum=False, eps_add=None):
    sq_toks = []
    for i in range(NT):
        def bsq(s, i=i):
            pr.wait(s, "s", tok)
            if i > 0:
                pr.wait(s, "s", mm_tok[0])
            pr.inc(s.activation(q2[:, :], tiles[i][:, :], AF.Square), "s")
        pr.blk(s=bsq)
        sqtok = pr.tok("s")

        def bmm(t, i=i, sqtok=sqtok):
            pr.wait(t, "t", sqtok, pr.ones_tok)
            last = None
            for sl in range(8):
                last = t.matmul(psN[:, sl:sl + 1], q2[:, sl * 128:(sl + 1) * 128],
                                pr.ones_f32[:, 0:1], start=(i == 0), stop=(i == NT - 1))
            pr.inc(last, "t")
        pr.blk(t=bmm)
        if i == 0:
            mm_tok = [None]
        mm_tok[0] = pr.tok("t")
    mmtok = pr.tok("t")

    def binv(s, mmtok=mmtok):
        pr.wait(s, "s", mmtok, pr.const_tok)
        pr.inc(s.activation(invn[:, :], psN[:, :], AF.Ln), "s")
        pr.wait(s, "s", pr.tok("s"))
        pr.inc(s.activation(invn[:, :], invn[:, :], AF.Exp, scale=-0.5), "s")
        pr.wait(s, "s", pr.tok("s"))
    pr.blk(s=binv)
    return pr.tok("s")


def emit_col_bcast(pr, stack, col8, col_tok, dram_bounce, psB, out_b, tag):
    nc = pr.nc
    with nc.sbuf_tensor(tag + "_t8", [8, 128], F32) as t8, \
         nc.sbuf_tensor(tag + "_row", [1, 1024], F32) as row, \
         nc.psum_tensor(tag + "_pst", [P, 128], F32) as pst:
        def btr(t):
            pr.wait(t, "t", col_tok, pr.ident_tok)
            pr.inc(t.transpose(pst[0:8, 0:128], col8[:, 0:8], pr.identf[:, :]), "t")
        pr.blk(t=btr)
        ttok = pr.tok("t")

        def bev(s, ttok=ttok):
            pr.wait(s, "s", ttok)
            pr.inc(s.copy(t8[:, :], pst[0:8, 0:128]), "s")
        pr.blk(s=bev)
        etok = pr.tok("s")

        def bdma(g, etok=etok):
            pr.wait(g, "g", etok)
            pr.inc(g.dma_start(out=dram_bounce, in_=t8[:, :]), "gd")
        pr.blk(g=bdma)
        d1 = pr.tok("gd")

        def bdma2(g, d1=d1):
            pr.wait(g, "g", d1)
            pr.inc(g.dma_start(out=row[:, :], in_=dram_bounce.rearrange("a b -> (a b)").unsqueeze(0)), "gd")
        pr.blk(g=bdma2)
        d2 = pr.tok("gd")

        rb_tok = emit_row_broadcast(pr, psB, row, d2, 1024)

        def bev2(s, rb_tok=rb_tok):
            pr.wait(s, "s", rb_tok)
            pr.inc(s.copy(out_b[:, :], psB[:, :]), "s")
        pr.blk(s=bev2)
    return pr.tok("s")


def emit_L3_attn_b(pr, stack, xio, lnT_tiles, lnT_tok, w):
    nc = pr.nc
    L = 1024.0
    with ExitStack() as ls:
        lctx = ls.enter_context
        Q = [lctx(nc.sbuf_tensor(f"l3b_Q{i}", [P, D], BF16)) for i in range(NT)]
        K = [lctx(nc.sbuf_tensor(f"l3b_K{i}", [P, D], BF16)) for i in range(NT)]
        V = [lctx(nc.sbuf_tensor(f"l3b_V{i}", [P, D], BF16)) for i in range(NT)]
        VT = [lctx(nc.sbuf_tensor(f"l3b_VT{c}", [P, TT], BF16)) for c in range(8)]
        q2 = lctx(nc.sbuf_tensor("l3b_q2", [P, D], F32))
        invq = lctx(nc.sbuf_tensor("l3b_invq", [P, 8], F32))
        invk = lctx(nc.sbuf_tensor("l3b_invk", [P, 8], F32))
        ksums = lctx(nc.sbuf_tensor("l3b_ksums", [P, 16], BF16))
        vsums = lctx(nc.sbuf_tensor("l3b_vsums", [P, 16], F32))
        ts8 = lctx(nc.sbuf_tensor("l3b_ts8", [P, 8], F32))
        gcol = lctx(nc.sbuf_tensor("l3b_gcol", [P, 1], F32))
        g11 = lctx(nc.sbuf_tensor("l3b_g11", [1, 1], F32))
        bounce = nc.dram_tensor("l3b_bounce", [8, 128], F32, kind="Internal").ap()
        psW = [lctx(nc.psum_tensor(f"l3b_psW{j}", [P, 1024], F32)) for j in range(2)]
        bc_stack = ExitStack()
        psN = bc_stack.enter_context(nc.psum_tensor("l3b_psN", [P, 8], F32))
        psB = bc_stack.enter_context(nc.psum_tensor("l3b_psB", [P, 1024], F32))

        def ldg(q):
            pr.inc(q.dma_start(out=g11[:, :], in_=w["gamma"]), "qd")
        pr.blk(q=ldg)
        g_tok = emit_scalar_bcast(pr, psN, gcol, g11, pr.tok("qd"))

        proj_stack = ExitStack()
        wbuf = proj_stack.enter_context(nc.sbuf_tensor("l3b_wbuf", [P, 8 * 1024], BF16))
        bias_b = proj_stack.enter_context(nc.sbuf_tensor("l3b_biasb", [P, 1024], F32))
        row1 = proj_stack.enter_context(nc.sbuf_tensor("l3b_row1", [1, 1024], F32))

        def load_bias(nm):
            def bb(q, nm=nm):
                pr.inc(q.dma_start(out=row1[:, :], in_=w[nm]), "qd")
            pr.blk(q=bb)
            rtok = pr.tok("qd")
            rb = emit_row_broadcast(pr, psB, row1, rtok, 1024)

            def ev(s, rb=rb):
                pr.wait(s, "s", rb)
                pr.inc(s.copy(bias_b[:, :], psB[:, :]), "s")
            pr.blk(s=ev)
            return pr.tok("s")

        btok = load_bias("bq")
        q_tok = emit_proj_tok(pr, lnT_tiles, lnT_tok, [w["WqT"][c] for c in range(8)],
                              wbuf, Q, psW, "q", war_tok=g_tok,
                              bias_ap=bias_b[:, :], bias_tok=btok)
        btok = load_bias("bk")
        k_tok = emit_proj_tok(pr, lnT_tiles, lnT_tok, [w["WkT"][c] for c in range(8)],
                              wbuf, K, psW, "k", war_tok=q_tok,
                              bias_ap=bias_b[:, :], bias_tok=btok)
        btok = load_bias("bv")
        v_tok = emit_proj_tok(pr, lnT_tiles, lnT_tok, [w["WvT"][c] for c in range(8)],
                              wbuf, V, psW, "v", war_tok=k_tok,
                              bias_ap=bias_b[:, :], bias_tok=btok)
        proj_stack.close()
        invqb = lctx(nc.sbuf_tensor("l3b_invqb", [P, 1024], F32))
        invkb = lctx(nc.sbuf_tensor("l3b_invkb", [P, 1024], F32))
        tsb = lctx(nc.sbuf_tensor("l3b_tsb", [P, 1024], F32))

        invq_tok = emit_colnorm_inv(pr, Q, q_tok, q2, psN, invq)
        invqb_tok = emit_col_bcast(pr, ls, invq, invq_tok, bounce, psB, invqb, "l3bq")
        invk_tok = emit_colnorm_inv(pr, K, k_tok, q2, psN, invk)
        invkb_tok = emit_col_bcast(pr, ls, invk, invk_tok, bounce, psB, invkb, "l3bk")

        def normqk(v):
            pr.wait(v, "v", invqb_tok, invkb_tok)
            for i in range(NT):
                pr.inc(v.tensor_tensor(Q[i][:, :], Q[i][:, :], invqb[:, :], ALU.mult), "v")
                pr.inc(v.tensor_tensor(K[i][:, :], K[i][:, :], invkb[:, :], ALU.mult), "v")
        pr.blk(v=normqk)
        qk_tok = pr.tok("v")

        def sums(v):
            pr.wait(v, "v", qk_tok, v_tok)
            for i in range(NT):
                pr.inc(v.reduce_sum(vsums[:, i:i + 1], V[i][:, :], AX.X), "v")
                pr.inc(v.reduce_sum(q2[:, i:i + 1], K[i][:, :], AX.X), "v")
            pr.wait(v, "v", pr.tok("v"))
            pr.inc(v.tensor_scalar(ksums[:, :], q2[:, 0:16], 1e-6, None, ALU.add), "v")
            pr.wait(v, "v", pr.tok("v"))
        pr.blk(v=sums)
        sums_tok = pr.tok("v")

        def mmts(t):
            pr.wait(t, "t", sums_tok)
            last = None
            for i in range(NT):
                for sl in range(8):
                    last = t.matmul(psN[:, sl:sl + 1], Q[i][:, sl * 128:(sl + 1) * 128],
                                    ksums[:, i:i + 1], start=(i == 0), stop=(i == NT - 1))
            pr.inc(last, "t")
        pr.blk(t=mmts)
        ts_mm = pr.tok("t")

        def tsinv(v, ts_mm=ts_mm):
            pr.wait(v, "v", ts_mm)
            pr.inc(v.tensor_scalar(ts8[:, :], psN[:, :], L, None, ALU.add), "v")
            pr.wait(v, "v", pr.tok("v"))
            pr.inc(v.reciprocal(ts8[:, :], ts8[:, :]), "v")
            pr.wait(v, "v", pr.tok("v"))
        pr.blk(v=tsinv)
        ts_tok = pr.tok("v")
        tsb_tok = emit_col_bcast(pr, ls, ts8, ts_tok, bounce, psB, tsb, "l3bt")
        bc_stack.close()

        with ExitStack() as vts:
            vt_tok = emit_transpose_fm(pr, vts, V, [v_tok] * NT, VT, tag="l3bvt")

        GT = lnT_tiles
        gt_toks = []
        for c in range(8):
            def mmg(t, c=c):
                pr.wait(t, "t", qk_tok)
                if len(gt_toks) >= 2:
                    pr.wait(t, "t", gt_toks[-2])
                last = None
                for mh in range(2):
                    for i in range(NT):
                        last = t.matmul(psW[c % 2][:, mh * 512:(mh + 1) * 512],
                                        K[i][:, c * 128:(c + 1) * 128],
                                        Q[i][:, mh * 512:(mh + 1) * 512],
                                        start=(i == 0), stop=(i == NT - 1))
                pr.inc(last, "t")
            pr.blk(t=mmg)
            gmm = pr.tok("t")

            def evg(s, c=c, gmm=gmm):
                pr.wait(s, "s", gmm)
                pr.inc(s.copy(GT[c][:, 0:1024], psW[c % 2][:, :]), "s")
            pr.blk(s=evg)
            gt_toks.append(pr.tok("s"))
        gt_tok = gt_toks[-1]

        o2 = q2
        out_toks = [None] * NT
        prev = {}
        for i in range(NT):
            bank = psW[i % 2]

            def mmo(t, i=i, bank=bank):
                pr.wait(t, "t", gt_tok, vt_tok)
                pr.wait(t, "t", prev.get(i - 2, gt_tok))
                last = None
                for mh in range(2):
                    for c in range(8):
                        last = t.matmul(bank[:, mh * 512:(mh + 1) * 512],
                                        VT[c][:, i * 128:(i + 1) * 128],
                                        GT[c][:, mh * 512:(mh + 1) * 512],
                                        start=(c == 0), stop=(c == 7))
                pr.inc(last, "t")
            pr.blk(t=mmo)
            mmtok = pr.tok("t")

            def evo(v, i=i, bank=bank, mmtok=mmtok):
                pr.wait(v, "v", mmtok, sums_tok, tsb_tok, g_tok)
                pr.inc(v.tensor_scalar(o2[:, :], bank[:, :], vsums[:, i:i + 1],
                                       None, ALU.add), "v")
                pr.wait(v, "v", pr.tok("v"))
                pr.inc(v.tensor_tensor(o2[:, :], o2[:, :], tsb[:, :], ALU.mult), "v")
            pr.blk(v=evo)
            otok = pr.tok("v")
            out_toks[i] = xio.add_store(i, o2[:, :], otok, scale_ap=gcol[:, 0:1])
            prev[i] = out_toks[i]
        return out_toks



def stage_L2_attn_a(inputs, pre):
    for nm in ("Wq", "Wk", "Wv"):
        pre[f"L2aa_{nm}"] = np.ascontiguousarray(np.asarray(inputs[f"L2_attn_a_{nm}"])).reshape(8, 128, 512)
    for nm in ("bq", "bk", "bv"):
        pre[f"L2aa_{nm}"] = np.asarray(inputs[f"L2_attn_a_{nm}"]).reshape(1, 512)
    pre["L2aa_Wo"] = np.ascontiguousarray(np.asarray(inputs["L2_attn_a_Wo"])).reshape(4, 128, 1024)
    pre["L2aa_bo"] = np.asarray(inputs["L2_attn_a_bo"]).reshape(1, 1024)
    al = np.asarray(inputs["L2_attn_a_alpha"]).reshape(512)
    pre["L2aa_alpha_row"] = al.reshape(1, 512)
    pre["L2aa_alpha_col"] = np.ascontiguousarray(al.reshape(4, 128).T)
    ag = np.asarray(inputs["L2_attn_a_aug_param"]).reshape(512)
    pre["L2aa_aug_col"] = np.ascontiguousarray(ag.reshape(4, 128).T)
    pre["L2aa_ln_scale"] = np.asarray(inputs["L2_attn_a_ln_scale"]).reshape(1, 1024)
    pre["L2aa_ln_bias"] = np.asarray(inputs["L2_attn_a_ln_bias"]).reshape(1, 1024)


def decl_L2_attn_a(nc):
    d = {}
    for nm in ("Wq", "Wk", "Wv"):
        d[nm] = nc.dram_tensor(f"L2aa_{nm}", [8, 128, 512], F32, kind="ExternalInput").ap()
    for nm in ("bq", "bk", "bv"):
        d[nm] = nc.dram_tensor(f"L2aa_{nm}", [1, 512], F32, kind="ExternalInput").ap()
    d["Wo"] = nc.dram_tensor("L2aa_Wo", [4, 128, 1024], F32, kind="ExternalInput").ap()
    d["bo"] = nc.dram_tensor("L2aa_bo", [1, 1024], F32, kind="ExternalInput").ap()
    d["alpha_row"] = nc.dram_tensor("L2aa_alpha_row", [1, 512], F32, kind="ExternalInput").ap()
    d["alpha_col"] = nc.dram_tensor("L2aa_alpha_col", [128, 4], F32, kind="ExternalInput").ap()
    d["aug_col"] = nc.dram_tensor("L2aa_aug_col", [128, 4], F32, kind="ExternalInput").ap()
    d["ln_scale"] = nc.dram_tensor("L2aa_ln_scale", [1, 1024], F32, kind="ExternalInput").ap()
    d["ln_bias"] = nc.dram_tensor("L2aa_ln_bias", [1, 1024], F32, kind="ExternalInput").ap()
    return d


def emit_L2_attn_a(pr, stack, xio, lnT_tiles, lnT_tok, w):
    nc = pr.nc
    with ExitStack() as ls:
        lctx = ls.enter_context
        qt = [lctx(nc.sbuf_tensor(f"l2aa_q{i}", [P, 512], BF16)) for i in range(NT)]
        kt = [lctx(nc.sbuf_tensor(f"l2aa_k{i}", [P, 512], BF16)) for i in range(NT)]
        vt = [lctx(nc.sbuf_tensor(f"l2aa_v{i}", [P, 512], BF16)) for i in range(NT)]
        ut = [lctx(nc.sbuf_tensor(f"l2aa_u{i}", [P, 512], BF16)) for i in range(NT)]
        uT = [lctx(nc.sbuf_tensor(f"l2aa_uT{c}", [P, TT], BF16)) for c in range(4)]
        qT = [lctx(nc.sbuf_tensor(f"l2aa_qT{c}", [P, TT], BF16)) for c in range(4)]
        stT = [lctx(nc.sbuf_tensor(f"l2aa_stT{c}", [P, TT], F32)) for c in range(4)]
        wbuf = lctx(nc.sbuf_tensor("l2aa_wbuf", [P, 8 * 512], BF16))
        row1 = lctx(nc.sbuf_tensor("l2aa_row1", [1, 1024], F32))
        bias_b = lctx(nc.sbuf_tensor("l2aa_biasb", [P, 512], F32))
        dec_b = lctx(nc.sbuf_tensor("l2aa_decb", [P, 512], F32))
        dcol = lctx(nc.sbuf_tensor("l2aa_dcol", [P, 4], F32))
        acol = lctx(nc.sbuf_tensor("l2aa_acol", [P, 4], F32))
        zt = lctx(nc.sbuf_tensor("l2aa_zt", [P, 8], F32))
        zs = lctx(nc.sbuf_tensor("l2aa_zs", [P, 4], F32))
        prod = lctx(nc.sbuf_tensor("l2aa_prod", [P, 1024], F32))
        psW = [lctx(nc.psum_tensor(f"l2aa_psW{j}", [P, 512], F32)) for j in range(2)]
        psB = lctx(nc.psum_tensor("l2aa_psB", [P, 1024], F32))

        def ldc(q):
            pr.inc(q.dma_start(out=dcol[:, 0:4], in_=w["alpha_col"]), "qd")
            pr.inc(q.dma_start(out=acol[:, 0:4], in_=w["aug_col"]), "qd")
            pr.inc(q.dma_start(out=row1[:, 0:512], in_=w["alpha_row"]), "qd")
        pr.blk(q=ldc)
        c_tok = pr.tok("qd")

        def sgc(s):
            pr.wait(s, "s", c_tok)
            pr.inc(s.activation(dcol[:, 0:4], dcol[:, 0:4], AF.Sigmoid), "s")
        pr.blk(s=sgc)
        dcol_tok = pr.tok("s")

        rb = emit_row_broadcast(pr, psB, row1[:, 0:512], c_tok, 512)

        def sgb(s, rb=rb):
            pr.wait(s, "s", rb)
            pr.inc(s.activation(dec_b[:, :], psB[:, 0:512], AF.Sigmoid), "s")
        pr.blk(s=sgb)

        def omd(v):
            pr.wait(v, "v", pr.tok("s"))
            pr.inc(v.tensor_scalar(dec_b[:, :], dec_b[:, :], -1.0, 1.0, ALU.mult, ALU.add), "v")
        pr.blk(v=omd)
        decb_tok = pr.tok("v")

        def load_bias(nm):
            def bb(q, nm=nm):
                pr.inc(q.dma_start(out=row1[:, 0:512], in_=w[nm]), "qd")
            pr.blk(q=bb)
            rtok = pr.tok("qd")
            rbl = emit_row_broadcast(pr, psB, row1[:, 0:512], rtok, 512)

            def ev(s, rbl=rbl):
                pr.wait(s, "s", rbl)
                pr.inc(s.copy(bias_b[:, :], psB[:, 0:512]), "s")
            pr.blk(s=ev)
            return pr.tok("s")

        btok = load_bias("bq")
        q_tok = emit_proj_tok(pr, lnT_tiles, lnT_tok, [w["Wq"][c] for c in range(8)],
                              wbuf, qt, psW, "q", out_dim=512, war_tok=decb_tok,
                              bias_ap=bias_b[:, :], bias_tok=btok)
        btok = load_bias("bk")
        k_tok = emit_proj_tok(pr, lnT_tiles, lnT_tok, [w["Wk"][c] for c in range(8)],
                              wbuf, kt, psW, "k", out_dim=512, war_tok=q_tok,
                              bias_ap=bias_b[:, :], bias_tok=btok)
        btok = load_bias("bv")
        v_tok = emit_proj_tok(pr, lnT_tiles, lnT_tok, [w["Wv"][c] for c in range(8)],
                              wbuf, vt, psW, "v", out_dim=512, war_tok=k_tok,
                              bias_ap=bias_b[:, :], bias_tok=btok)

        u_toks = []
        be_toks = []
        for i in range(NT):
            def bu(v, i=i):
                pr.wait(v, "v", q_tok, k_tok, v_tok, decb_tok)
                if i > 0:
                    pr.wait(v, "v", be_toks[i - 1])
                pr.inc(v.tensor_mul(prod[:, 0:512], qt[i][:, :], kt[i][:, :]), "v")
                pr.wait(v, "v", pr.tok("v"))
                p3 = prod[:, 0:512].rearrange("p (h d) -> p h d", d=64)
                pr.inc(v.tensor_reduce(zt[:, :], p3, AX.X, ALU.add), "v")
                pr.wait(v, "v", pr.tok("v"))
                pr.inc(v.tensor_scalar(zt[:, :], zt[:, :], 0.125, None, ALU.mult), "v")
                pr.wait(v, "v", pr.tok("v"))
                pr.inc(v.reduce_max(zs[:, 0:1], zt[:, :], AX.X, negate=True), "v")
                pr.wait(v, "v", pr.tok("v"))
            pr.blk(v=bu)
            vt1 = pr.tok("v")

            def be(s, i=i, vt1=vt1):
                pr.wait(s, "s", vt1)
                pr.inc(s.activation(zt[:, :], zt[:, :], AF.Exp, bias=zs[:, 0:1],
                                    accum_out=zs[:, 1:2]), "s")
            pr.blk(s=be)
            st1 = pr.tok("s")
            be_toks.append(st1)

            def bn(v, i=i, st1=st1):
                pr.wait(v, "v", st1)
                pr.inc(v.reciprocal(zs[:, 2:3], zs[:, 1:2]), "v")
                pr.wait(v, "v", pr.tok("v"))
                pr.inc(v.tensor_scalar(zt[:, :], zt[:, :], zs[:, 2:3], None, ALU.mult), "v")
                pr.wait(v, "v", pr.tok("v"))
                a3 = zt[:, :].unsqueeze(2).broadcast_to([P, 8, 64])
                v3 = vt[i][:, :].rearrange("p (h d) -> p h d", d=64)
                u3 = ut[i][:, :].rearrange("p (h d) -> p h d", d=64)
                pr.inc(v.tensor_tensor(u3, v3, a3, ALU.mult), "v")
                pr.wait(v, "v", pr.tok("v"))
                pr.inc(v.tensor_mul(ut[i][:, :], ut[i][:, :], dec_b[:, :]), "v")
            pr.blk(v=bn)
            u_toks.append(pr.tok("v"))

        with ExitStack() as ts1:
            psTa = ts1.enter_context(nc.psum_tensor("l2aa_psTa", [P, 512], BF16))
            psTb = ts1.enter_context(nc.psum_tensor("l2aa_psTb", [P, 512], BF16))
            evt = []
            gi = 0
            for c in range(4):
                for i0 in range(0, NT, 4):
                    bank = psTa if gi % 2 == 0 else psTb

                    def btr(t, c=c, i0=i0, bank=bank, gi=gi):
                        pr.wait(t, "t", pr.ident_tok)
                        if gi >= 2:
                            pr.wait(t, "t", evt[gi - 2])
                        last = None
                        for j in range(4):
                            pr.wait(t, "t", u_toks[i0 + j])
                            last = t.transpose(bank[:, j * 128:(j + 1) * 128],
                                               ut[i0 + j][:, c * 128:(c + 1) * 128],
                                               pr.ident[:, :])
                        pr.inc(last, "t")
                    pr.blk(t=btr)
                    ttok = pr.tok("t")

                    def bev(s, c=c, i0=i0, bank=bank, ttok=ttok):
                        pr.wait(s, "s", ttok)
                        pr.inc(s.copy(uT[c][:, i0 * 128:(i0 + 4) * 128], bank[:, :]), "s")
                    pr.blk(s=bev)
                    evt.append(pr.tok("s"))
                    gi += 1
            uT_tok = pr.tok("s")
        with ExitStack() as ts2:
            qT_tok = emit_transpose_fm(pr, ts2, qt, [q_tok] * NT, qT, n_chunk=4,
                                       tag="l2aaqt")

        scan_toks = []
        for c in range(4):
            def bscan(v, c=c):
                pr.wait(v, "v", uT_tok, dcol_tok)
                pr.inc(v.tensor_tensor_scan(stT[c][:, :],
                                            dcol[:, c:c + 1].broadcast_to([P, TT]),
                                            uT[c][:, :], 0.0,
                                            ALU.mult, ALU.add), "v")
            pr.blk(v=bscan)
            scan_toks.append(pr.tok("v"))

        outT = uT
        o_toks = []
        for c in range(4):
            def bo1(s, c=c):
                pr.wait(s, "s", scan_toks[c], qT_tok, c_tok)
                pr.inc(s.activation(prod[:, 0:TT] if False else stT[c][:, :],
                                    stT[c][:, :], AF.Copy) if False else None, "s") \
                    if False else None
                pr.inc(s.activation(outT[c][:, :], qT[c][:, :], AF.Sigmoid,
                                    scale=acol[:, c:c + 1]), "s")
            pr.blk(s=bo1)
            s1 = pr.tok("s")

            def bo2(v, c=c, s1=s1):
                pr.wait(v, "v", s1)
                pr.inc(v.scalar_tensor_tensor(stT[c][:, :], qT[c][:, :], 1.0,
                                              stT[c][:, :], ALU.mult, ALU.mult), "v")
                pr.wait(v, "v", pr.tok("v"))
                pr.inc(v.tensor_add(outT[c][:, :], outT[c][:, :], stT[c][:, :]), "v")
            pr.blk(v=bo2)
            o_toks.append(pr.tok("v"))
        obT = qT
        ob_toks = []
        for c in range(4):
            def bcv(v, c=c):
                pr.wait(v, "v", o_toks[c])
                pr.inc(v.tensor_copy(obT[c][:, :], outT[c][:, :]), "v")
            pr.blk(v=bcv)
            ob_toks.append(pr.tok("v"))
        obT_tok = pr.tok("v")

        def ldow(g):
            pr.wait(g, "g", v_tok)
            for c in range(4):
                pr.inc(g.dma_start(out=wbuf[:, c * 1024:(c + 1) * 1024],
                                   in_=w["Wo"][c]), "gd")
        pr.blk(g=ldow)
        ow_tok = pr.tok("gd")

        def ldlb(q):
            pr.inc(q.dma_start(out=row1[:, :], in_=w["bo"]), "qd")
        pr.blk(q=ldlb)
        bo_rtok = pr.tok("qd")
        bo_rb = emit_row_broadcast(pr, psB, row1, bo_rtok, 1024)
        bo_b = lctx(nc.sbuf_tensor("l2aa_bob", [P, 1024], F32))

        def evbo(s):
            pr.wait(s, "s", bo_rb)
            pr.inc(s.copy(bo_b[:, :], psB[:, :]), "s")
        pr.blk(s=evbo)
        bo_tok = pr.tok("s")

        def ldls(q):
            pr.wait(q, "q", bo_rb)
            pr.inc(q.dma_start(out=row1[:, :], in_=w["ln_scale"]), "qd")
        pr.blk(q=ldls)
        ls_rtok = pr.tok("qd")
        ls_rb = emit_row_broadcast(pr, psB, row1, ls_rtok, 1024)
        lsc_b = lctx(nc.sbuf_tensor("l2aa_lscb", [P, 1024], F32))

        def evls(s):
            pr.wait(s, "s", ls_rb)
            pr.inc(s.copy(lsc_b[:, :], psB[:, :]), "s")
        pr.blk(s=evls)
        lsc_tok = pr.tok("s")

        def ldlb2(q):
            pr.wait(q, "q", ls_rb)
            pr.inc(q.dma_start(out=row1[:, :], in_=w["ln_bias"]), "qd")
        pr.blk(q=ldlb2)
        lb_rtok = pr.tok("qd")
        lb_rb = emit_row_broadcast(pr, psB, row1, lb_rtok, 1024)
        lnb_b = lctx(nc.sbuf_tensor("l2aa_lnbb", [P, 1024], F32))

        def evlb(s):
            pr.wait(s, "s", lb_rb)
            pr.inc(s.copy(lnb_b[:, :], psB[:, :]), "s")
        pr.blk(s=evlb)
        lnb_tok = pr.tok("s")

        yout = lctx(nc.sbuf_tensor("l2aa_yout", [P, 1024], F32))
        stt = lctx(nc.sbuf_tensor("l2aa_stt", [P, 8], F32))
        psO = [lctx(nc.psum_tensor(f"l2aa_psO{j}", [P, 1024], F32)) for j in range(2)]
        out_toks = [None] * NT
        prev = {}
        for i in range(NT):
            bank = psO[i % 2]

            def mmo(t, i=i, bank=bank):
                pr.wait(t, "t", obT_tok, ow_tok)
                pr.wait(t, "t", prev.get(i - 2))
                last = None
                for mh in range(2):
                    for c in range(4):
                        last = t.matmul(bank[:, mh * 512:(mh + 1) * 512],
                                        obT[c][:, i * 128:(i + 1) * 128],
                                        wbuf[:, c * 1024 + mh * 512:c * 1024 + (mh + 1) * 512],
                                        start=(c == 0), stop=(c == 3))
                pr.inc(last, "t")
            pr.blk(t=mmo)
            mmtok = pr.tok("t")

            def ev1(v, i=i, bank=bank, mmtok=mmtok):
                pr.wait(v, "v", mmtok, bo_tok)
                pr.inc(v.tensor_add(yout[:, :], bank[:, :], bo_b[:, :]), "v")
                pr.wait(v, "v", pr.tok("v"))
                pr.inc(v.reduce_sum(stt[:, 0:1], yout[:, :], AX.X), "v")
            pr.blk(v=ev1)
            v1 = pr.tok("v")

            def ev2(s, i=i, v1=v1):
                pr.wait(s, "s", v1)
                pr.inc(s.activation(prod[:, 0:1024], yout[:, :], AF.Square,
                                    accum_out=stt[:, 1:2]), "s")
            pr.blk(s=ev2)
            s1 = pr.tok("s")

            def ev3(v, i=i, s1=s1):
                pr.wait(v, "v", s1)
                pr.inc(v.tensor_scalar(stt[:, 2:4], stt[:, 0:2], 1.0 / 1024, None, ALU.mult), "v")
                pr.wait(v, "v", pr.tok("v"))
                pr.inc(v.tensor_tensor(stt[:, 4:5], stt[:, 2:3], stt[:, 2:3], ALU.mult), "v")
                pr.wait(v, "v", pr.tok("v"))
                pr.inc(v.tensor_tensor(stt[:, 4:5], stt[:, 3:4], stt[:, 4:5], ALU.subtract), "v")
            pr.blk(v=ev3)
            v2 = pr.tok("v")

            def ev4(s, i=i, v2=v2):
                pr.wait(s, "s", v2, pr.const_tok)
                pr.inc(s.activation(stt[:, 5:6], stt[:, 4:5], AF.Ln, bias=cval(pr, 1e-6)), "s")
                pr.wait(s, "s", pr.tok("s"))
                pr.inc(s.activation(stt[:, 6:7], stt[:, 5:6], AF.Exp, scale=-0.5), "s")
            pr.blk(s=ev4)
            s2 = pr.tok("s")

            def ev5(v, i=i, s2=s2):
                pr.wait(v, "v", s2, lsc_tok, lnb_tok)
                pr.inc(v.scalar_tensor_tensor(stt[:, 7:8], stt[:, 2:3], -1.0,
                                              stt[:, 6:7], ALU.mult, ALU.mult), "v")
                pr.wait(v, "v", pr.tok("v"))
                pr.inc(v.tensor_scalar(yout[:, :], yout[:, :], stt[:, 6:7],
                                       stt[:, 7:8], ALU.mult, ALU.add), "v")
                pr.wait(v, "v", pr.tok("v"))
                pr.inc(v.tensor_mul(yout[:, :], yout[:, :], lsc_b[:, :]), "v")
                pr.wait(v, "v", pr.tok("v"))
                pr.inc(v.tensor_add(yout[:, :], yout[:, :], lnb_b[:, :]), "v")
            pr.blk(v=ev5)
            ytok = pr.tok("v")
            out_toks[i] = xio.add_store(i, yout[:, :], ytok)
            prev[i] = out_toks[i]
        return out_toks



def stage_L3_attn_a(inputs, pre):
    Wxp = np.asarray(inputs["L3_attn_a_W_xproj"])
    pre["L3aa2_WxpT"] = np.ascontiguousarray(Wxp.T).reshape(8, 128, 96)
    Wdt = np.asarray(inputs["L3_attn_a_W_dt"])
    pre["L3aa2_WdtT"] = np.ascontiguousarray(Wdt.T)
    pre["L3aa2_bdt"] = np.ascontiguousarray(
        np.asarray(inputs["L3_attn_a_b_dt"]).reshape(8, 128).T)
    Alog = np.asarray(inputs["L3_attn_a_A_log"])
    pre["L3aa2_AlogT"] = np.ascontiguousarray(Alog.reshape(8, 128, 16)).reshape(8, 128 * 16) \
        .reshape(8, 128, 16).transpose(1, 0, 2).reshape(128, 128)
    pre["L3aa2_Dcol"] = np.ascontiguousarray(
        np.asarray(inputs["L3_attn_a_D"]).reshape(8, 128).T)


def decl_L3_attn_a(nc):
    return {
        "WxpT": nc.dram_tensor("L3aa2_WxpT", [8, 128, 96], F32, kind="ExternalInput").ap(),
        "WdtT": nc.dram_tensor("L3aa2_WdtT", [64, 1024], F32, kind="ExternalInput").ap(),
        "bdt": nc.dram_tensor("L3aa2_bdt", [128, 8], F32, kind="ExternalInput").ap(),
        "AlogT": nc.dram_tensor("L3aa2_AlogT", [128, 128], F32, kind="ExternalInput").ap(),
        "Dcol": nc.dram_tensor("L3aa2_Dcol", [128, 8], F32, kind="ExternalInput").ap(),
    }


def emit_L3_attn_a(pr, stack, xio, lnT_tiles, lnT_tok, w):
    nc = pr.nc
    import ml_dtypes
    sel_np = np.zeros((16, 16 * 128), ml_dtypes.bfloat16)
    for n in range(16):
        sel_np[n, n * 128:(n + 1) * 128] = 1.0
    sel_dram = nc.inline_tensor(sel_np, "l3aa2_sel")
    with ExitStack() as ls:
        lctx = ls.enter_context
        xp = [lctx(nc.sbuf_tensor(f"m_xp{i}", [P, 96], BF16)) for i in range(NT)]
        xpTdt = lctx(nc.sbuf_tensor("m_xpTdt", [64, TT], BF16))
        Bt = lctx(nc.sbuf_tensor("m_Bt", [16, TT], BF16))
        Ct = lctx(nc.sbuf_tensor("m_Ct", [16, TT], BF16))
        sel = lctx(nc.sbuf_tensor("m_sel", [16, 16 * 128], BF16))
        wdt = lctx(nc.sbuf_tensor("m_wdt", [64, 1024], BF16))
        wxp = lctx(nc.sbuf_tensor("m_wxp", [P, 8 * 96], BF16))
        bdt = lctxnc = lctx(nc.sbuf_tensor("m_bdt", [P, 8], F32))
        Acols = lctx(nc.sbuf_tensor("m_Acol", [P, 128], F32))
        Dcol = lctx(nc.sbuf_tensor("m_Dcol", [P, 8], F32))
        dT = lctx(nc.sbuf_tensor("m_dT", [P, TT], BF16))
        dxT = lctx(nc.sbuf_tensor("m_dxT", [P, TT], BF16))
        yTc = lctx(nc.sbuf_tensor("m_yTc", [P, TT], F32))
        da = lctx(nc.sbuf_tensor("m_da", [P, TT], F32))
        ub = lctx(nc.sbuf_tensor("m_ub", [P, TT], F32))
        hb = lctx(nc.sbuf_tensor("m_hb", [P, TT], BF16))
        ytok = [lctx(nc.sbuf_tensor(f"m_y{i}", [P, D], BF16)) for i in range(NT)]
        sp1 = lctx(nc.sbuf_tensor("m_sp1", [P, 512], F32))
        sp2 = lctx(nc.sbuf_tensor("m_sp2", [P, 512], F32))
        psA = [lctx(nc.psum_tensor(f"m_psA{j}", [P, 512], F32)) for j in range(2)]
        psTB = lctx(nc.psum_tensor("m_psTB", [P, 384], BF16))
        psR = lctx(nc.psum_tensor("m_psR", [P, TT], F32))

        def ldc(q):
            pr.inc(q.dma_start(out=sel[:, :], in_=sel_dram.ap()), "qd")
            pr.inc(q.dma_start(out=bdt[:, :], in_=w["bdt"]), "qd")
            pr.inc(q.dma_start(out=Acols[:, :], in_=w["AlogT"]), "qd")
            pr.inc(q.dma_start(out=Dcol[:, :], in_=w["Dcol"]), "qd")
        pr.blk(q=ldc)
        c_tok = pr.tok("qd")

        def ldw(g):
            pr.inc(g.dma_start(out=wdt[:, :], in_=w["WdtT"]), "gd")
            for c in range(8):
                pr.inc(g.dma_start(out=wxp[:, c * 96:(c + 1) * 96], in_=w["WxpT"][c]), "gd")
        pr.blk(g=ldw)
        w_tok = pr.tok("gd")

        def mkA(s):
            pr.wait(s, "s", c_tok)
            pr.inc(s.activation(Acols[:, :], Acols[:, :], AF.Exp), "s")
        pr.blk(s=mkA)
        sA = pr.tok("s")

        def mkA2(v):
            pr.wait(v, "v", sA)
            pr.inc(v.tensor_scalar(Acols[:, :], Acols[:, :], -1.0, None, ALU.mult), "v")
        pr.blk(v=mkA2)
        A_tok = pr.tok("v")

        xp_done = {}
        for i in range(NT):
            bank = psA[i % 2]

            def mmxp(t, i=i, bank=bank):
                pr.wait(t, "t", lnT_tok, w_tok)
                pr.wait(t, "t", xp_done.get(i - 2))
                last = None
                for c in range(8):
                    last = t.matmul(bank[:, 0:96],
                                    lnT_tiles[c][:, i * 128:(i + 1) * 128],
                                    wxp[:, c * 96:(c + 1) * 96],
                                    start=(c == 0), stop=(c == 7))
                pr.inc(last, "t")
            pr.blk(t=mmxp)
            mmtok = pr.tok("t")

            def evxp(s, i=i, bank=bank, mmtok=mmtok):
                pr.wait(s, "s", mmtok)
                pr.inc(s.copy(xp[i][:, :], bank[:, 0:96]), "s")
            pr.blk(s=evxp)
            xp_done[i] = pr.tok("s")
        xp_tok = pr.tok("s")

        tr_done = {}
        for i in range(NT):
            bank = psA[i % 2]

            def btr(t, i=i):
                pr.wait(t, "t", xp_done[i], pr.ident_tok)
                pr.wait(t, "t", tr_done.get(i - 1))
                t.transpose(psTB[0:64, 0:128], xp[i][:, 0:64], pr.ident[:, :])
                t.transpose(psTB[0:16, 128:256], xp[i][:, 64:80], pr.ident[:, :])
                last = t.transpose(psTB[0:16, 256:384], xp[i][:, 80:96], pr.ident[:, :])
                pr.inc(last, "t")
            pr.blk(t=btr)
            ttok = pr.tok("t")

            def bev(s, i=i, ttok=ttok):
                pr.wait(s, "s", ttok)
                pr.inc(s.copy(xpTdt[:, i * 128:(i + 1) * 128], psTB[0:64, 0:128]), "s")
                pr.inc(s.copy(Bt[:, i * 128:(i + 1) * 128], psTB[0:16, 128:256]), "s")
                pr.inc(s.copy(Ct[:, i * 128:(i + 1) * 128], psTB[0:16, 256:384]), "s")
            pr.blk(s=bev)
            tr_done[i] = pr.tok("s")
        tr_tok = pr.tok("s")

        out_toks = [None] * NT
        psY = [lctx(nc.psum_tensor("m_psY0", [P, 128], BF16))] * 2
        prev_c = {}
        ytok_done = {}
        for c in range(8):
            dsp_done = {}
            for tg in range(4):
                bank = psA[tg % 2]

                def mmd(t, c=c, tg=tg, bank=bank):
                    pr.wait(t, "t", tr_tok, w_tok)
                    pr.wait(t, "t", dsp_done.get(tg - 2, prev_c.get(c - 1)))
                    pr.inc(t.matmul(bank[:, :], wdt[:, c * 128:(c + 1) * 128],
                                    xpTdt[:, tg * 512:(tg + 1) * 512],
                                    start=True, stop=True), "t")
                pr.blk(t=mmd)
                mmtok = pr.tok("t")

                def sp_a(v, c=c, tg=tg, bank=bank, mmtok=mmtok):
                    pr.wait(v, "v", mmtok)
                    pr.inc(v.tensor_scalar(sp1[:, :], bank[:, :], bdt[:, c:c + 1],
                                           None, ALU.add), "v")
                pr.blk(v=sp_a)
                v1 = pr.tok("v")

                def sp_b(s, v1=v1):
                    pr.wait(s, "s", v1)
                    pr.inc(s.activation(sp2[:, :], sp1[:, :], AF.Abs), "s")
                    pr.wait(s, "s", pr.tok("s"))
                    pr.inc(s.activation(sp2[:, :], sp2[:, :], AF.Exp, scale=-1.0), "s")
                    pr.wait(s, "s", pr.tok("s"))
                    pr.inc(s.activation(sp2[:, :], sp2[:, :], AF.Ln, bias=cval(pr, 1.0)), "s")
                    pr.wait(s, "s", pr.tok("s"))
                    pr.inc(s.activation(sp1[:, :], sp1[:, :], AF.Relu), "s")
                pr.blk(s=sp_b)
                s1 = pr.tok("s")

                def sp_c(v, c=c, tg=tg, s1=s1):
                    pr.wait(v, "v", s1)
                    pr.inc(v.tensor_add(dT[:, tg * 512:(tg + 1) * 512], sp1[:, :],
                                        sp2[:, :]), "v")
                pr.blk(v=sp_c)
                dsp_done[tg] = pr.tok("v")
            dT_tok = pr.tok("v")

            def mkdx(v, c=c, dT_tok=dT_tok):
                pr.wait(v, "v", dT_tok, lnT_tok)
                pr.inc(v.tensor_mul(dxT[:, :], dT[:, :], lnT_tiles[c][:, :]), "v")
                pr.wait(v, "v", pr.tok("v"))
                pr.inc(v.tensor_scalar(yTc[:, :], lnT_tiles[c][:, :], Dcol[:, c:c + 1],
                                       None, ALU.mult), "v")
            pr.blk(v=mkdx)
            dx_tok = pr.tok("v")

            for n in range(16):
                def mmb(t, n=n, dx_tok=dx_tok):
                    pr.wait(t, "t", tr_tok, dx_tok)
                    pr.wait(t, "t", crep_used.get((c, n - 1)) if n > 0 else dx_tok)
                    last = None
                    for tg in range(4):
                        last = t.matmul(psR[:, tg * 512:(tg + 1) * 512],
                                        sel[:, n * 128:(n + 1) * 128],
                                        Bt[:, tg * 512:(tg + 1) * 512],
                                        start=True, stop=True)
                    pr.inc(last, "t")
                pr.blk(t=mmb)
                if c == 0 and n == 0:
                    crep_used = {}
                brep_tok = pr.tok("t")

                def mku(v, c=c, n=n, brep_tok=brep_tok):
                    pr.wait(v, "v", brep_tok)
                    pr.inc(v.tensor_mul(ub[:, :], dxT[:, :], psR[:, :]), "v")
                    pr.wait(v, "v", pr.tok("v"))
                    pr.inc(v.tensor_scalar(da[:, :], dT[:, :], Acols[:, c * 16 + n:c * 16 + n + 1],
                                           None, ALU.mult), "v")
                pr.blk(v=mku)
                u_tok = pr.tok("v")

                def mkda(s, u_tok=u_tok):
                    pr.wait(s, "s", u_tok)
                    pr.inc(s.activation(da[:, :], da[:, :], AF.Exp), "s")
                pr.blk(s=mkda)
                da_tok = pr.tok("s")

                def mkscan(v, da_tok=da_tok):
                    pr.wait(v, "v", da_tok)
                    pr.inc(v.tensor_tensor_scan(hb[:, :], da[:, :], ub[:, :], 0.0,
                                                ALU.mult, ALU.add), "v")
                pr.blk(v=mkscan)
                h_tok = pr.tok("v")

                def mmc(t, n=n, h_tok=h_tok):
                    pr.wait(t, "t", u_tok)
                    last = None
                    for tg in range(4):
                        last = t.matmul(psR[:, tg * 512:(tg + 1) * 512],
                                        sel[:, n * 128:(n + 1) * 128],
                                        Ct[:, tg * 512:(tg + 1) * 512],
                                        start=True, stop=True)
                    pr.inc(last, "t")
                pr.blk(t=mmc)
                crep_tok = pr.tok("t")

                def acc(v, c=c, n=n, crep_tok=crep_tok, h_tok=h_tok):
                    pr.wait(v, "v", crep_tok, h_tok)
                    pr.inc(v.tensor_mul(da[:, :], hb[:, :], psR[:, :]), "v")
                    pr.wait(v, "v", pr.tok("v"))
                    pr.inc(v.tensor_add(yTc[:, :], yTc[:, :], da[:, :]), "v")
                pr.blk(v=acc)
                crep_used[(c, n)] = pr.tok("v")
            yc_tok = pr.tok("v")

            def ycast(v, c=c, yc_tok=yc_tok):
                pr.wait(v, "v", yc_tok)
                pr.inc(v.tensor_copy(dxT[:, :], yTc[:, :]), "v")
            pr.blk(v=ycast)
            ycb_tok = pr.tok("v")

            for i in range(NT):
                bank = psY[i % 2]

                def ytr(t, c=c, i=i, bank=bank, ycb_tok=ycb_tok):
                    pr.wait(t, "t", ycb_tok, pr.ident_tok)
                    pr.wait(t, "t", ytok_done.get((c, i - 1), ytok_done.get((c - 1, NT - 1))))
                    pr.inc(t.transpose(bank[:, :], dxT[:, i * 128:(i + 1) * 128],
                                       pr.ident[:, :]), "t")
                pr.blk(t=ytr)
                ttok = pr.tok("t")

                def yev(s, c=c, i=i, bank=bank, ttok=ttok):
                    pr.wait(s, "s", ttok)
                    pr.inc(s.copy(ytok[i][:, c * 128:(c + 1) * 128], bank[:, :]), "s")
                pr.blk(s=yev)
                ytok_done[(c, i)] = pr.tok("s")
            prev_c[c] = pr.tok("s")
        y_tok = pr.tok("s")

        for i in range(NT):
            out_toks[i] = xio.add_store(i, ytok[i][:, :], y_tok)
        return out_toks



def stage_L2_ffn(inputs, pre):
    wg = np.asarray(inputs["L2_ffn_w_gate"])
    wu = np.asarray(inputs["L2_ffn_w_up"])
    wd = np.asarray(inputs["L2_ffn_w_down"])
    pre["L2f_wgT"] = np.ascontiguousarray(wg.T).reshape(8, 128, 4096)
    pre["L2f_wuT"] = np.ascontiguousarray(wu.T).reshape(8, 128, 4096)
    pre["L2f_wdT"] = np.ascontiguousarray(wd.T).reshape(32, 128, 1024)


def decl_L2_ffn(nc):
    return {
        "wgT": nc.dram_tensor("L2f_wgT", [8, 128, 4096], F32, kind="ExternalInput").ap(),
        "wuT": nc.dram_tensor("L2f_wuT", [8, 128, 4096], F32, kind="ExternalInput").ap(),
        "wdT": nc.dram_tensor("L2f_wdT", [32, 128, 1024], F32, kind="ExternalInput").ap(),
    }


def emit_ternarize(pr, stack, wdram, n_chunks, width, outs_dram, mwcol, tag, uid=[0]):
    nc = pr.nc
    uid[0] += 1
    u = f"{tag}{uid[0]}"
    with ExitStack() as ls:
        lctx = ls.enter_context
        wbufA = lctx(nc.sbuf_tensor(u + "_wA", [P, width], F32))
        wbufB = lctx(nc.sbuf_tensor(u + "_wB", [P, width], F32))
        absb = lctx(nc.sbuf_tensor(u + "_abs", [P, width], BF16))
        acc = lctx(nc.sbuf_tensor(u + "_acc", [P, n_chunks], F32))
        tot = lctx(nc.sbuf_tensor(u + "_tot", [P, 2], F32))
        sgn = lctx(nc.sbuf_tensor(u + "_sgn", [P, width], BF16))
        msk = lctx(nc.sbuf_tensor(u + "_msk", [P, width], BF16))
        ps1 = lctx(nc.psum_tensor(u + "_ps1", [P, 8], F32))
        bufs = [wbufA, wbufB]
        ld = {}
        ab = {}
        for c in range(n_chunks):
            b = bufs[c % 2]

            def bl(g, c=c, b=b):
                pr.wait(g, "g", ab.get(c - 2))
                pr.inc(g.dma_start(out=b[:, :], in_=wdram[c]), "gd")
            pr.blk(g=bl)
            ld[c] = pr.tok("gd")

            def ba(s, c=c, b=b):
                pr.wait(s, "s", ld[c])
                pr.inc(s.activation(absb[:, :], b[:, :], AF.Abs,
                                    accum_out=acc[:, c:c + 1]), "s")
            pr.blk(s=ba)
            ab[c] = pr.tok("s")
        abs_tok = pr.tok("s")

        def red(v):
            pr.wait(v, "v", abs_tok)
            pr.inc(v.reduce_sum(tot[:, 0:1], acc[:, 0:n_chunks], AX.X), "v")
        pr.blk(v=red)
        rtok = pr.tok("v")

        def mmt(t, rtok=rtok):
            pr.wait(t, "t", rtok, pr.ones_tok)
            pr.inc(t.matmul(ps1[0:1, 0:1], tot[:, 0:1], pr.ones_f32[:, 0:1],
                            start=True, stop=True), "t")
        pr.blk(t=mmt)
        mtok = pr.tok("t")

        def evt(s, mtok=mtok):
            pr.wait(s, "s", mtok)
            pr.inc(s.copy(tot[0:1, 1:2], ps1[0:1, 0:1]), "s")
        pr.blk(s=evt)
        etok = pr.tok("s")
        rb = emit_row_broadcast(pr, ps1, tot[0:1, 1:2], etok, 1)

        def mkmw(v, rb=rb):
            pr.wait(v, "v", rb)
            pr.inc(v.tensor_scalar(mwcol[:, 0:1], ps1[:, 0:1],
                                   1.0 / (n_chunks * 128 * width), 1e-5,
                                   ALU.mult, ALU.max), "v")
            pr.wait(v, "v", pr.tok("v"))
            pr.inc(v.tensor_scalar(mwcol[:, 1:2], mwcol[:, 0:1], 0.5, None, ALU.mult), "v")
        pr.blk(v=mkmw)
        mw_tok = pr.tok("v")

        st = {}
        for c in range(n_chunks):
            b = bufs[c % 2]

            def bl2(g, c=c, b=b):
                pr.wait(g, "g", st.get(c - 2, mw_tok))
                pr.inc(g.dma_start(out=b[:, :], in_=wdram[c]), "gd")
            pr.blk(g=bl2)
            l2 = pr.tok("gd")

            def bs(s, c=c, b=b, l2=l2):
                pr.wait(s, "s", l2)
                pr.inc(s.activation(sgn[:, :], b[:, :], AF.Sign), "s")
                pr.wait(s, "s", pr.tok("s"))
                pr.inc(s.activation(absb[:, :], b[:, :], AF.Abs), "s")
            pr.blk(s=bs)
            stok = pr.tok("s")

            def bm(v, c=c, stok=stok):
                pr.wait(v, "v", stok, mw_tok)
                pr.inc(v.tensor_scalar(msk[:, :], absb[:, :], mwcol[:, 1:2],
                                       None, ALU.is_gt), "v")
                pr.wait(v, "v", pr.tok("v"))
                pr.inc(v.tensor_mul(msk[:, :], msk[:, :], sgn[:, :]), "v")
            pr.blk(v=bm)
            vtok = pr.tok("v")

            def bst(g, c=c, vtok=vtok):
                pr.wait(g, "g", vtok)
                pr.inc(g.dma_start(out=outs_dram[c], in_=msk[:, :]), "gd")
            pr.blk(g=bst)
            st[c] = pr.tok("gd")
    return pr.tok("gd"), mw_tok


def emit_L2_ffn(pr, stack, xio, lnT_tiles, lnT_tok, ln_tiles, ln_toks, w, uid):
    nc = pr.nc
    wgq_d = nc.dram_tensor(f"l2f_wgq{uid}", [8, 128, 4096], BF16, kind="Internal").ap()
    wuq_d = nc.dram_tensor(f"l2f_wuq{uid}", [8, 128, 4096], BF16, kind="Internal").ap()
    wdq_d = nc.dram_tensor(f"l2f_wdq{uid}", [32, 128, 1024], BF16, kind="Internal").ap()
    with ExitStack() as ls:
        lctx = ls.enter_context
        mws = lctx(nc.sbuf_tensor("l2f_mws", [P, 6], F32))
        t1, _ = emit_ternarize(pr, ls, [w["wgT"][c] for c in range(8)], 8, 4096,
                               [wgq_d[c] for c in range(8)], mws[:, 0:2], "l2f_tg")
        t2, _ = emit_ternarize(pr, ls, [w["wuT"][c] for c in range(8)], 8, 4096,
                               [wuq_d[c] for c in range(8)], mws[:, 2:4], "l2f_tu")
        t3, _ = emit_ternarize(pr, ls, [w["wdT"][c] for c in range(32)], 32, 1024,
                               [wdq_d[c] for c in range(32)], mws[:, 4:6], "l2f_td")

        xqT = [lctx(nc.sbuf_tensor(f"l2f_xqT{c}", [P, TT], BF16)) for c in range(8)]
        scol = lctx(nc.sbuf_tensor("l2f_scol", [P, 64], F32))
        qs = lctx(nc.sbuf_tensor("l2f_qs", [P, 1024], F32))
        qi = lctx(nc.sbuf_tensor("l2f_qi", [P, 1024], I32))

        with ExitStack() as xs:
            xq = [xs.enter_context(nc.sbuf_tensor(f"l2f_xq{i}", [P, D], BF16))
                  for i in range(NT)]
            xq_toks = []
            for i in range(NT):
                def aq(v, i=i):
                    pr.wait(v, "v", ln_toks[i])
                    if xq_toks:
                        pr.wait(v, "v", xq_toks[-1])
                    pr.inc(v.tensor_reduce(scol[:, i:i + 1], ln_tiles[i][:, :], AX.X,
                                           ALU.max, apply_absolute_value=True), "v")
                    pr.wait(v, "v", pr.tok("v"))
                    pr.inc(v.tensor_scalar(scol[:, i:i + 1], scol[:, i:i + 1], 1e-5,
                                           None, ALU.max), "v")
                    pr.wait(v, "v", pr.tok("v"))
                    pr.inc(v.tensor_scalar(scol[:, 16 + i:17 + i], scol[:, i:i + 1],
                                           1.0 / 127.0, None, ALU.mult), "v")
                    pr.inc(v.reciprocal(scol[:, 32 + i:33 + i], scol[:, i:i + 1]), "v")
                    pr.wait(v, "v", pr.tok("v"))
                    pr.inc(v.tensor_scalar(scol[:, 32 + i:33 + i], scol[:, 32 + i:33 + i],
                                           127.0, None, ALU.mult), "v")
                    pr.wait(v, "v", pr.tok("v"))
                    pr.inc(v.tensor_scalar(qs[:, 0:D], ln_tiles[i][:, :],
                                           scol[:, 32 + i:33 + i], None, ALU.mult), "v")
                    pr.wait(v, "v", pr.tok("v"))
                    pr.inc(v.tensor_copy(qi[:, 0:D], qs[:, 0:D]), "v")
                    pr.wait(v, "v", pr.tok("v"))
                    pr.inc(v.tensor_copy(xq[i][:, :], qi[:, 0:D]), "v")
                pr.blk(v=aq)
                xq_toks.append(pr.tok("v"))
            xqT_tok = emit_transpose_fm(pr, xs, xq, xq_toks, xqT, tag="l2fxq" + str(uid))

        def mkc(v):
            pr.wait(v, "v", xq_toks[-1], t1, t2, t3)
            pr.inc(v.tensor_scalar(scol[:, 48:64], scol[:, 16:32], mws[:, 0:1],
                                   None, ALU.mult), "v")
        pr.blk(v=mkc)
        cg_tok = pr.tok("v")

        hq = [lctx(nc.sbuf_tensor(f"l2f_h{j}", [P, 4096], BF16)) for j in range(4)]
        hqT = [lctx(nc.sbuf_tensor(f"l2f_hqT{o}", [P, 512], BF16)) for o in range(32)]
        wsl = lctx(nc.sbuf_tensor("l2f_wsl", [P, 8192], BF16))
        wdsl = lctx(nc.sbuf_tensor("l2f_wdsl", [P, 2 * 1024], BF16))
        sgb = lctx(nc.sbuf_tensor("l2f_sgb", [P, 512], F32))
        hs = lctx(nc.sbuf_tensor("l2f_hs", [P, 16], F32))
        psA = lctx(nc.psum_tensor("l2f_psA", [P, 512], F32))
        psB = lctx(nc.psum_tensor("l2f_psB", [P, 512], F32))
        psT = lctx(nc.psum_tensor("l2f_psT", [P, 512], BF16))
        psD = [lctx(nc.psum_tensor(f"l2f_psD{j}", [P, 1024], F32)) for j in range(2)]

        out_toks = [None] * NT
        prev_q = {}
        dprev = {}
        for qt in range(4):
            hdone = {}
            for og in range(8):
                w0 = 0

                def lw(g, qt=qt, og=og, w0=w0):
                    if og == 0 and qt == 0:
                        pr.wait(g, "g", t1, t2)
                    pr.wait(g, "g", hdone.get(og - 1, prev_q.get(qt - 1)))
                    for c in range(8):
                        pr.inc(g.dma_start(out=wsl[:, w0 + c * 512:w0 + (c + 1) * 512],
                                           in_=wgq_d[c][:, og * 512:(og + 1) * 512]), "gd")
                        pr.inc(g.dma_start(out=wsl[:, w0 + 4096 + c * 512:w0 + 4096 + (c + 1) * 512],
                                           in_=wuq_d[c][:, og * 512:(og + 1) * 512]), "gd")
                pr.blk(g=lw)
                wtok = pr.tok("gd")

                for j in range(4):
                    i = qt * 4 + j

                    def mmgu(t, i=i, og=og, w0=w0, wtok=wtok, j=j):
                        pr.wait(t, "t", wtok, xqT_tok)
                        pr.wait(t, "t", hdone.get((og - 1, j)))
                        last = None
                        for c in range(8):
                            last = t.matmul(psA[:, :], xqT[c][:, i * 128:(i + 1) * 128],
                                            wsl[:, w0 + c * 512:w0 + (c + 1) * 512],
                                            start=(c == 0), stop=(c == 7))
                        for c in range(8):
                            last = t.matmul(psB[:, :], xqT[c][:, i * 128:(i + 1) * 128],
                                            wsl[:, w0 + 4096 + c * 512:w0 + 4096 + (c + 1) * 512],
                                            start=(c == 0), stop=(c == 7))
                        pr.inc(last, "t")
                    pr.blk(t=mmgu)
                    mmtok = pr.tok("t")

                    def evs(s, i=i, mmtok=mmtok):
                        pr.wait(s, "s", mmtok, cg_tok)
                        pr.inc(s.activation(sgb[:, :], psA[:, :], AF.Silu,
                                            scale=scol[:, 48 + i:49 + i]), "s")
                    pr.blk(s=evs)
                    sil = pr.tok("s")

                    def evh(v, i=i, j=j, og=og, sil=sil):
                        pr.wait(v, "v", sil)
                        pr.inc(v.scalar_tensor_tensor(qs[:, 0:512], psB[:, :],
                                                      scol[:, 16 + i:17 + i], sgb[:, :],
                                                      ALU.mult, ALU.mult), "v")
                        pr.wait(v, "v", pr.tok("v"))
                        pr.inc(v.tensor_scalar(hq[j][:, og * 512:(og + 1) * 512],
                                               qs[:, 0:512], mws[:, 2:3], None,
                                               ALU.mult), "v")
                    pr.blk(v=evh)
                    hdone[(og, j)] = pr.tok("v")
                hdone[og] = pr.tok("v")

            trq = {}
            for j in range(4):
                i = qt * 4 + j

                def qh(v, i=i, j=j):
                    pr.wait(v, "v", hdone[(7, j)])
                    if trq:
                        pr.wait(v, "v", list(trq.values())[-1])
                    pr.inc(v.tensor_reduce(hs[:, j:j + 1], hq[j][:, :], AX.X,
                                           ALU.max, apply_absolute_value=True), "v")
                    pr.wait(v, "v", pr.tok("v"))
                    pr.inc(v.tensor_scalar(hs[:, j:j + 1], hs[:, j:j + 1], 1e-5,
                                           None, ALU.max), "v")
                    pr.wait(v, "v", pr.tok("v"))
                    pr.inc(v.tensor_scalar(hs[:, 8 + j:9 + j], hs[:, j:j + 1],
                                           1.0 / 127.0, None, ALU.mult), "v")
                    pr.inc(v.reciprocal(hs[:, 4 + j:5 + j], hs[:, j:j + 1]), "v")
                    pr.wait(v, "v", pr.tok("v"))
                    pr.inc(v.tensor_scalar(hs[:, 4 + j:5 + j], hs[:, 4 + j:5 + j],
                                           127.0, None, ALU.mult), "v")
                    pr.wait(v, "v", pr.tok("v"))
                    for rr in range(4):
                        sl = slice(rr * 1024, (rr + 1) * 1024)
                        pr.inc(v.tensor_scalar(qs[:, :], hq[j][:, sl], hs[:, 4 + j:5 + j],
                                               None, ALU.mult), "v")
                        pr.wait(v, "v", pr.tok("v"))
                        pr.inc(v.tensor_copy(qi[:, :], qs[:, :]), "v")
                        pr.wait(v, "v", pr.tok("v"))
                        pr.inc(v.tensor_copy(hq[j][:, sl], qi[:, :]), "v")
                        pr.wait(v, "v", pr.tok("v"))
                pr.blk(v=qh)
                qtok = pr.tok("v")

                for o in range(32):
                    def htr(t, j=j, o=o, qtok=qtok):
                        pr.wait(t, "t", qtok, pr.ident_tok)
                        pr.wait(t, "t", trq.get((j, o - 1), trq.get((j - 1, 31))))
                        pr.inc(t.transpose(psT[:, 0:128],
                                           hq[j][:, o * 128:(o + 1) * 128],
                                           pr.ident[:, :]), "t")
                    pr.blk(t=htr)
                    ttk = pr.tok("t")

                    def hev(s, j=j, o=o, ttk=ttk):
                        pr.wait(s, "s", ttk)
                        pr.inc(s.copy(hqT[o // 4 * 4 + o % 4][:, j * 128:(j + 1) * 128]
                                      if False else hqT[o][:, j * 128:(j + 1) * 128],
                                      psT[:, 0:128]), "s")
                    pr.blk(s=hev)
                    trq[(j, o)] = pr.tok("s")
            hqT_tok = pr.tok("s")

            for hp in range(2):
                jj = [hp * 2, hp * 2 + 1]
                ddone = {}
                for o in range(32):
                    buf = o % 2
                    wslc = slice(buf * 1024, (buf + 1) * 1024)

                    def ldd(g, o=o, qt=qt, hp=hp, wslc=wslc):
                        if o == 0 and qt == 0 and hp == 0:
                            pr.wait(g, "g", t3)
                        pr.wait(g, "g", ddone.get(o - 2, dprev.get((qt, hp - 1), prev_q.get(qt - 1))))
                        pr.inc(g.dma_start(out=wdsl[:, wslc], in_=wdq_d[o]), "gd")
                    pr.blk(g=ldd)
                    dtok = pr.tok("gd")

                    def mmd(t, o=o, qt=qt, hp=hp, wslc=wslc, dtok=dtok):
                        pr.wait(t, "t", dtok, hqT_tok)
                        if o == 0:
                            pr.wait(t, "t", dprev.get((qt, hp - 1), prev_q.get(qt - 1)))
                        last = None
                        for jx in range(2):
                            j = hp * 2 + jx
                            for mh in range(2):
                                last = t.matmul(psD[jx][:, mh * 512:(mh + 1) * 512],
                                                hqT[o][:, j * 128:(j + 1) * 128],
                                                wdsl[:, wslc][:, mh * 512:(mh + 1) * 512],
                                                start=(o == 0), stop=(o == 31))
                        pr.inc(last, "t")
                    pr.blk(t=mmd)
                    ddone[o] = pr.tok("t")
                dmm = pr.tok("t")

                for jx in range(2):
                    j = hp * 2 + jx
                    i = qt * 4 + j

                    def mkcd(v, j=j, dmm=dmm):
                        pr.wait(v, "v", dmm)
                        pr.inc(v.tensor_scalar(hs[:, 12 + j:13 + j], hs[:, 8 + j:9 + j],
                                               mws[:, 4:5], None, ALU.mult), "v")
                    pr.blk(v=mkcd)
                    cdk = pr.tok("v")
                    out_toks[i] = xio.add_store(i, psD[jx][:, :], cdk,
                                                scale_ap=hs[:, 12 + j:13 + j])
                dprev[(qt, hp)] = out_toks[qt * 4 + hp * 2 + 1]
            prev_q[qt] = out_toks[qt * 4 + 3]
        return out_toks



def stage_L0_ffn(inputs, pre):
    gw = np.asarray(inputs["L0_ffn_gate_w"])
    pre["L0f_gwT"] = np.ascontiguousarray(gw.T).reshape(8, 128, 8)
    w1 = np.asarray(inputs["L0_ffn_w1"])
    pre["L0f_w1s"] = np.ascontiguousarray(
        w1.reshape(8, 8, 128, 8, 128).transpose(0, 1, 3, 4, 2)).reshape(8, 8, 1024, 128)
    w2 = np.asarray(inputs["L0_ffn_w2"])
    pre["L0f_w2s"] = np.ascontiguousarray(
        w2.transpose(0, 2, 1)).reshape(8, 8, 128, 1024)


def decl_L0_ffn(nc):
    return {
        "gwT": nc.dram_tensor("L0f_gwT", [8, 128, 8], F32, kind="ExternalInput").ap(),
        "w1s": nc.dram_tensor("L0f_w1s", [8, 8, 1024, 128], F32, kind="ExternalInput").ap(),
        "w2s": nc.dram_tensor("L0f_w2s", [8, 8, 128, 1024], F32, kind="ExternalInput").ap(),
    }


def emit_L0_ffn(pr, stack, xio, lnT_tiles, lnT_tok, w, uid):
    nc = pr.nc
    import ml_dtypes
    NTOK_G = 16384
    CAP = 2048
    sel64 = np.zeros((64, 8), np.float32)
    for p_ in range(64):
        sel64[p_, p_ % 8] = 1.0
    sel64_d = nc.inline_tensor(sel64, f"l0f_sel64_{uid}")
    sel8x64 = np.zeros((8, 64), np.float32)
    for p_ in range(64):
        sel8x64[p_ % 8, p_] = 1.0
    sel8x64_d = nc.inline_tensor(sel8x64, f"l0f_sel8x64_{uid}")
    cc_in = nc.dram_tensor(f"l0f_ccin{uid}", [8, TT], F32, kind="Internal")
    cc_out = nc.dram_tensor(f"l0f_ccout{uid}", [64, TT], F32, kind="Internal")

    with ExitStack() as ls:
        lctx = ls.enter_context
        probs = lctx(nc.sbuf_tensor("l0f_probs", [P, NT * 8], F32))
        Rcol = lctx(nc.sbuf_tensor("l0f_R", [P, NT * 8], F32))
        scr = lctx(nc.sbuf_tensor("l0f_scr", [P, NT * 8], F32))
        cnt = lctx(nc.sbuf_tensor("l0f_cnt", [P, 8], F32))
        ssum = lctx(nc.sbuf_tensor("l0f_ssum", [P, 8], F32))
        selm = lctx(nc.sbuf_tensor("l0f_selm", [P, NT * 8], mybir.dt.uint8))
        neg9 = lctx(nc.sbuf_tensor("l0f_neg9", [P, 8], F32))
        tmp8 = lctx(nc.sbuf_tensor("l0f_tmp8", [P, 8], F32))
        gw_sb = lctx(nc.sbuf_tensor("l0f_gw", [P, 64], BF16))
        sel64_sb = lctx(nc.sbuf_tensor("l0f_s64", [64, 8], F32))
        sel8_sb = lctx(nc.sbuf_tensor("l0f_s8", [8, 64], F32))
        Rg = lctx(nc.sbuf_tensor("l0f_Rg", [64, TT], F32))
        lo = lctx(nc.sbuf_tensor("l0f_lo", [64, 8], F32))
        lou = lctx(nc.sbuf_tensor("l0f_lou", [64, 1], mybir.dt.uint8))
        ind = lctx(nc.sbuf_tensor("l0f_ind", [64, TT], BF16))
        c8 = lctx(nc.sbuf_tensor("l0f_c8", [8, 1], F32))
        lob = lctx(nc.sbuf_tensor("l0f_lob", [P, 8], F32))
        psG = lctx(nc.psum_tensor("l0f_psG", [P, 8], F32))
        psT8 = lctx(nc.psum_tensor("l0f_psT8", [P, 128], F32))

        def ldg(g):
            for c in range(8):
                pr.inc(g.dma_start(out=gw_sb[:, c * 8:(c + 1) * 8], in_=w["gwT"][c]), "gd")
        pr.blk(g=ldg)
        def ldg2(q):
            pr.inc(q.dma_start(out=sel64_sb[:, :], in_=sel64_d.ap()), "qd")
            pr.inc(q.dma_start(out=sel8_sb[:, :], in_=sel8x64_d.ap()), "qd")
        pr.blk(q=ldg2)
        gw_tok = pr.tok("gd")
        sel_tok = pr.tok("qd")

        prb_toks = []
        for i in range(NT):
            def mmg(t, i=i):
                pr.wait(t, "t", gw_tok, lnT_tok)
                if prb_toks:
                    pr.wait(t, "t", prb_toks[-1])
                last = None
                for c in range(8):
                    last = t.matmul(psG[:, :], lnT_tiles[c][:, i * 128:(i + 1) * 128],
                                    gw_sb[:, c * 8:(c + 1) * 8],
                                    start=(c == 0), stop=(c == 7))
                pr.inc(last, "t")
            pr.blk(t=mmg)
            gmt = pr.tok("t")

            def sfm(v, i=i, gmt=gmt):
                pr.wait(v, "v", gmt)
                pr.inc(v.reduce_max(tmp8[:, 0:1], psG[:, :], AX.X, negate=True), "v")
            pr.blk(v=sfm)
            v1 = pr.tok("v")

            def sfe(s, i=i, v1=v1):
                pr.wait(s, "s", v1)
                pr.inc(s.activation(probs[:, i * 8:(i + 1) * 8], psG[:, :], AF.Exp,
                                    bias=tmp8[:, 0:1], accum_out=tmp8[:, 1:2]), "s")
            pr.blk(s=sfe)
            s1 = pr.tok("s")

            def sfn(v, i=i, s1=s1):
                pr.wait(v, "v", s1)
                pr.inc(v.reciprocal(tmp8[:, 2:3], tmp8[:, 1:2]), "v")
                pr.wait(v, "v", pr.tok("v"))
                pr.inc(v.tensor_scalar(probs[:, i * 8:(i + 1) * 8],
                                       probs[:, i * 8:(i + 1) * 8],
                                       tmp8[:, 2:3], None, ALU.mult), "v")
            pr.blk(v=sfn)
            prb_toks.append(pr.tok("v"))

        def mkneg(g):
            pr.inc(g.memset(neg9[:, :], -1e9), "g")
        pr.blk(g=mkneg)
        n9_tok = pr.tok("g")

        r_toks = []
        for i in range(NT):
            sl = slice(i * 8, (i + 1) * 8)

            def mkr(v, i=i, sl=sl):
                pr.wait(v, "v", prb_toks[i], n9_tok)
                if r_toks:
                    pr.wait(v, "v", r_toks[-1])
                pr.inc(v.memset(cnt[:, :], 0.0), "v")
                pr.inc(v.memset(ssum[:, :], 0.0), "v")
                pr.wait(v, "v", pr.tok("v"))
                for e2 in range(8):
                    pcol = probs[:, i * 8 + e2:i * 8 + e2 + 1]
                    pr.inc(v.tensor_scalar(tmp8[:, :], probs[:, sl], pcol, None,
                                           ALU.is_lt), "v")
                    pr.wait(v, "v", pr.tok("v"))
                    pr.inc(v.tensor_add(cnt[:, :], cnt[:, :], tmp8[:, :]), "v")
                    pr.inc(v.scalar_tensor_tensor(ssum[:, :], tmp8[:, :], pcol,
                                                  ssum[:, :], ALU.mult, ALU.add), "v")
                    pr.wait(v, "v", pr.tok("v"))
                pr.inc(v.tensor_scalar(selm[:, sl], ssum[:, :], 0.9, None, ALU.is_lt), "v")
                pr.inc(v.tensor_sub(tmp8[:, :], probs[:, sl], cnt[:, :]), "v")
                pr.wait(v, "v", pr.tok("v"))
                pr.inc(v.tensor_scalar(tmp8[:, :], tmp8[:, :], 1.0, None, ALU.subtract), "v")
                pr.wait(v, "v", pr.tok("v"))
                pr.inc(v.select(Rcol[:, sl], selm[:, sl], tmp8[:, :], neg9[:, :]), "v")
            pr.blk(v=mkr)
            r_toks.append(pr.tok("v"))

        rt_done = {}
        for i in range(NT):
            def rtr(t, i=i):
                pr.wait(t, "t", r_toks[i], pr.ident_tok)
                pr.wait(t, "t", rt_done.get(i - 1))
                pr.inc(t.transpose(psT8[0:8, 0:128], Rcol[:, i * 8:(i + 1) * 8],
                                   pr.identf[:, :]), "t")
            pr.blk(t=rtr)
            tt = pr.tok("t")

            def rev(s, i=i, tt=tt):
                pr.wait(s, "s", tt)
                pr.inc(s.copy(Rg[0:8, i * 128:(i + 1) * 128], psT8[0:8, 0:128]), "s")
            pr.blk(s=rev)
            rt_done[i] = pr.tok("s")
        rt_tok = pr.tok("s")

        def dmain(g):
            pr.wait(g, "g", rt_tok)
            pr.inc(g.dma_start(out=cc_in.ap(), in_=Rg[0:8, :]), "gd")
        pr.blk(g=dmain)
        ccin_tok = pr.tok("gd")

        def docc(g):
            pr.wait(g, "g", ccin_tok)
            pr.inc(g.collective_compute(
                "AllGather", ALU.bypass,
                replica_groups=[list(range(8))],
                ins=[cc_in.ap().opt()],
                outs=[cc_out.ap().opt()],
            ), "cc")
        pr.blk(g=docc)
        cc_tok = pr.tok("cc")

        def ldrg(g):
            pr.wait(g, "g", cc_tok)
            pr.inc(g.dma_start(out=Rg[:, :], in_=cc_out.ap()), "gd")
        pr.blk(g=ldrg)
        rg_tok = pr.tok("gd")

        def binit(g):
            pr.wait(g, "g", rg_tok)
            pr.inc(g.memset(lo[:, 0:1], -9.5), "g")
            pr.inc(g.memset(lo[:, 1:2], 1.0), "g")
        pr.blk(g=binit)
        btok = pr.tok("g")

        for it in range(30):
            def bmid(v, it=it, btok=btok):
                pr.wait(v, "v", btok)
                pr.inc(v.tensor_add(lo[:, 2:3], lo[:, 0:1], lo[:, 1:2]), "v")
                pr.wait(v, "v", pr.tok("v"))
                pr.inc(v.tensor_scalar(lo[:, 2:3], lo[:, 2:3], 0.5, None, ALU.mult), "v")
                pr.wait(v, "v", pr.tok("v"))
                pr.inc(v.tensor_scalar(ind[:, :], Rg[:, :], lo[:, 2:3], None,
                                       ALU.is_gt), "v")
                pr.wait(v, "v", pr.tok("v"))
                pr.inc(v.reduce_sum(lo[:, 3:4], ind[:, :], AX.X), "v")
            pr.blk(v=bmid)
            v1 = pr.tok("v")

            def bmm(t, v1=v1):
                pr.wait(t, "t", v1, sel_tok)
                pr.inc(t.matmul(psT8[0:8, 0:1], sel64_sb[:, :], lo[0:64, 3:4],
                                start=True, stop=True), "t")
            pr.blk(t=bmm)
            t1 = pr.tok("t")

            def bev(s, t1=t1):
                pr.wait(s, "s", t1)
                pr.inc(s.copy(c8[:, :], psT8[0:8, 0:1]), "s")
            pr.blk(s=bev)
            s1 = pr.tok("s")

            def bmm2(t, s1=s1):
                pr.wait(t, "t", s1)
                pr.inc(t.matmul(psT8[0:64, 1:2], sel8_sb[:, :], c8[:, :],
                                start=True, stop=True), "t")
            pr.blk(t=bmm2)
            t2 = pr.tok("t")

            def bupd(v, t2=t2):
                pr.wait(v, "v", t2)
                pr.inc(v.tensor_scalar(lou[:, 0:1], psT8[0:64, 1:2], float(CAP),
                                       None, ALU.is_ge), "v")
                pr.wait(v, "v", pr.tok("v"))
                pr.inc(v.select(lo[:, 0:1], lou[:, 0:1], lo[:, 2:3], lo[:, 0:1]), "v")
                pr.inc(v.select(lo[:, 1:2], lou[:, 0:1], lo[:, 1:2], lo[:, 2:3]), "v")
            pr.blk(v=bupd)
            btok = pr.tok("v")

        def lotr(t, btok=btok):
            pr.wait(t, "t", btok)
            pr.inc(t.transpose(psT8[0:1, 0:8], lo[0:8, 0:1], pr.identf[0:8, 0:8]), "t")
        pr.blk(t=lotr)
        lt = pr.tok("t")

        def loev(s, lt=lt):
            pr.wait(s, "s", lt)
            pr.inc(s.copy(tmp8[0:1, 0:8], psT8[0:1, 0:8]), "s")
        pr.blk(s=loev)
        le = pr.tok("s")
        lob_tok = emit_row_broadcast(pr, psT8, tmp8[0:1, 0:8], le, 8)

        def loev2(s, lob_tok=lob_tok):
            pr.wait(s, "s", lob_tok)
            pr.inc(s.copy(lob[:, :], psT8[:, 0:8]), "s")
        pr.blk(s=loev2)
        lob_tok = pr.tok("s")

        scr_toks = []
        for i in range(NT):
            sl = slice(i * 8, (i + 1) * 8)

            def mks(v, i=i, sl=sl):
                pr.wait(v, "v", lob_tok, r_toks[i])
                if scr_toks:
                    pr.wait(v, "v", scr_toks[-1])
                pr.inc(v.tensor_tensor(tmp8[:, :], Rcol[:, sl], lob[:, :], ALU.is_gt), "v")
                pr.wait(v, "v", pr.tok("v"))
                pr.inc(v.tensor_mul(scr[:, sl], tmp8[:, :], probs[:, sl]), "v")
            pr.blk(v=mks)
            scr_toks.append(pr.tok("v"))
        scr_tok = scr_toks[-1]

        hT = [lctx(nc.sbuf_tensor(f"l0f_hT{c}", [P, TT], BF16)) for c in range(8)]
        yt = [lctx(nc.sbuf_tensor(f"l0f_y{i}", [P, D], F32)) for i in range(NT)]
        wsl = lctx(nc.sbuf_tensor("l0f_wsl", [P, 2048], BF16))
        w2b = lctx(nc.sbuf_tensor("l0f_w2b", [P, 8 * 1024], BF16))
        psE = [lctx(nc.psum_tensor(f"l0f_psE{j}", [P, 512], F32)) for j in range(2)]
        psY = [lctx(nc.psum_tensor(f"l0f_psY{j}", [P, 1024], F32)) for j in range(2)]

        def zy(v):
            for i in range(NT):
                pr.inc(v.memset(yt[i][:, :], 0.0), "v")
        pr.blk(v=zy)
        y0_tok = pr.tok("v")

        out_toks = [None] * NT
        prev_e = {}
        for e in range(8):
            hT_tok = emit_proj_T(pr, lnT_tiles, lnT_tok, w["w1s"][e], wsl, hT,
                                 f"l0fh{e}", psE, war_tok=prev_e.get(e - 1, y0_tok),
                                 act=AF.Gelu)
            yacc = {}

            def ldw2(g, e=e):
                pr.wait(g, "g", prev_e.get(e - 1))
                for c in range(8):
                    pr.inc(g.dma_start(out=w2b[:, c * 1024:(c + 1) * 1024],
                                       in_=w["w2s"][e][c]), "gd")
            pr.blk(g=ldw2)
            w2_tok = pr.tok("gd")
            for i in range(NT):
                bank = psY[i % 2]

                def mmy(t, e=e, i=i, bank=bank, w2_tok=w2_tok):
                    pr.wait(t, "t", hT_tok, w2_tok)
                    pr.wait(t, "t", yacc.get(i - 2))
                    last = None
                    for mh in range(2):
                        for c in range(8):
                            last = t.matmul(bank[:, mh * 512:(mh + 1) * 512],
                                            hT[c][:, i * 128:(i + 1) * 128],
                                            w2b[:, c * 1024 + mh * 512:c * 1024 + (mh + 1) * 512],
                                            start=(c == 0), stop=(c == 7))
                    pr.inc(last, "t")
                pr.blk(t=mmy)
                mmtok = pr.tok("t")

                def evy(v, e=e, i=i, bank=bank, mmtok=mmtok):
                    pr.wait(v, "v", mmtok, scr_tok, y0_tok)
                    pr.inc(v.scalar_tensor_tensor(yt[i][:, :], bank[:, :],
                                                  scr[:, i * 8 + e:i * 8 + e + 1],
                                                  yt[i][:, :], ALU.mult, ALU.add), "v")
                pr.blk(v=evy)
                yacc[i] = pr.tok("v")
            prev_e[e] = pr.tok("v")

        if getattr(pr, "l0f_debug", None) is not None:
            dbg = pr.l0f_debug

            def dmp(g):
                pr.wait(g, "g", scr_tok, pr.tok("v"))
                pr.inc(g.dma_start(out=dbg["scr"], in_=scr[:, :]), "gd")
                pr.inc(g.dma_start(out=dbg["lo"], in_=lo[:, :]), "gd")
                pr.inc(g.dma_start(out=dbg["Rg"], in_=Rg[:, 0:256]), "gd")
                pr.inc(g.dma_start(out=dbg["probs"], in_=probs[:, :]), "gd")
            pr.blk(g=dmp)
        for i in range(NT):
            out_toks[i] = xio.add_store(i, yt[i][:, :], prev_e[7])
        return out_toks



def stage_L2_attn_b(inputs, pre):
    W = np.asarray(inputs["L2_attn_b_W_attn"])
    pre["L2ab_W"] = np.ascontiguousarray(W).reshape(8, 128, 3072)
    pre["L2ab_b"] = np.asarray(inputs["L2_attn_b_b_attn"]).reshape(1, 3072)
    Wp = np.asarray(inputs["L2_attn_b_W_proj"])
    pre["L2ab_Wp"] = np.ascontiguousarray(Wp).reshape(8, 128, 1024)
    pre["L2ab_bp"] = np.asarray(inputs["L2_attn_b_b_proj"]).reshape(1, 1024)
    pre["L2ab_R0"] = np.ascontiguousarray(np.asarray(inputs["L2_attn_b_R"])[:, 0, :])


def rope_tables_L2():
    HD = 64
    inv = 1.0 / (10000.0 ** (np.arange(0, HD, 2, dtype=np.float64) / HD))
    t = np.arange(TT, dtype=np.float64)
    emb = np.concatenate([np.outer(t, inv), np.outer(t, inv)], 1)
    cos = np.cos(emb); sin = np.sin(emb)
    return (np.ascontiguousarray(cos.reshape(16, 128, 64).astype(np.float32)),
            np.ascontiguousarray(sin.reshape(16, 128, 64).astype(np.float32)))


def decl_L2_attn_b(nc):
    return {
        "W": nc.dram_tensor("L2ab_W", [8, 128, 3072], F32, kind="ExternalInput").ap(),
        "b": nc.dram_tensor("L2ab_b", [1, 3072], F32, kind="ExternalInput").ap(),
        "Wp": nc.dram_tensor("L2ab_Wp", [8, 128, 1024], F32, kind="ExternalInput").ap(),
        "bp": nc.dram_tensor("L2ab_bp", [1, 1024], F32, kind="ExternalInput").ap(),
        "R0": nc.dram_tensor("L2ab_R0", [64, 32], F32, kind="ExternalInput").ap(),
    }


def _l2b_ldcs(pr, cs2, cs_used, cs_dram, i, key):
    sl = 0 if key == "q" else 1

    def body(q, i=i, sl=sl):
        pr.wait(q, "q", cs_used.get((key, i - 1)))
        pr.inc(q.dma_start(out=cs2[:, sl * 128:(sl + 1) * 128],
                           in_=cs_dram.ap().rearrange("p (i c) -> p i c", c=128)[:, i, :]), "qd")
    pr.blk(q=body)
    return pr.tok("qd"), sl


def emit_L2_attn_b(pr, stack, xio, w, rope_consts, uid):
    nc = pr.nc
    import ml_dtypes
    cos2_np, sin2_np = rope_consts
    cs_np = np.concatenate([cos2_np, sin2_np], axis=2).reshape(16, 128, 128)
    cs_np = np.ascontiguousarray(cs_np.transpose(1, 0, 2)).reshape(128, 16 * 128)
    cs_dram = nc.inline_tensor(cs_np, f"l2ab_cs{uid}")
    iota64_np = np.tile(np.arange(64, dtype=np.float32)[None, :], (P, 1))
    iota64_d = nc.inline_tensor(iota64_np, f"l2ab_io64{uid}")
    iota32_np = np.tile(np.tile(np.arange(32, dtype=np.float32), 16)[None, :], (P, 1))
    iota32_d = nc.inline_tensor(iota32_np, f"l2ab_io32{uid}")
    u128 = np.ascontiguousarray(np.triu(np.ones((128, 128)), 1)).astype(ml_dtypes.bfloat16)
    u128_d = nc.inline_tensor(u128, f"l2ab_u128{uid}")
    u64 = np.ascontiguousarray(np.triu(np.ones((64, 64), np.float32), 1))
    u64_d = nc.inline_tensor(u64, f"l2ab_u64{uid}")
    qsq_d = nc.dram_tensor(f"l2ab_qsq{uid}", [16 * TT, 64], BF16, kind="Internal").ap()
    qsk_d = nc.dram_tensor(f"l2ab_qsk{uid}", [16 * TT, 64], BF16, kind="Internal").ap()
    qsv_d = nc.dram_tensor(f"l2ab_qsv{uid}", [16 * TT, 64], BF16, kind="Internal").ap()
    outl_d = nc.dram_tensor(f"l2ab_ol{uid}", [16 * TT, 64], BF16, kind="Internal").ap()

    with ExitStack() as ls:
        lctx = ls.enter_context
        lnT16 = [lctx(nc.sbuf_tensor(f"l2b_lnT{c}", [P, TT], BF16)) for c in range(8)]
        poscol = lctx(nc.sbuf_tensor("l2b_pos", [P, 16 * 16], I32))
        bcol = lctx(nc.sbuf_tensor("l2b_bcol", [P, 16 * 16], F32))
        cs2 = lctx(nc.sbuf_tensor("l2b_cs2", [P, 2 * 128], F32))
        cs_used = {}
        io64 = lctx(nc.sbuf_tensor("l2b_io64", [P, 64], F32))
        io32 = lctx(nc.sbuf_tensor("l2b_io32", [P, 512], F32))
        u128s = lctx(nc.sbuf_tensor("l2b_u128", [P, 128], BF16))
        u64s = lctx(nc.sbuf_tensor("l2b_u64", [64, 64], F32))
        r0 = lctx(nc.sbuf_tensor("l2b_r0", [64, 32], F32))
        bias1 = lctx(nc.sbuf_tensor("l2b_bias1", [P, 1024], F32))
        row1 = lctx(nc.sbuf_tensor("l2b_row1", [1, 1024], F32))
        qr = lctx(nc.sbuf_tensor("l2b_qr", [P, 1024], F32))
        rot = lctx(nc.sbuf_tensor("l2b_rot", [P, 1024], F32))
        t1b = lctx(nc.sbuf_tensor("l2b_t1", [P, 1024], F32))

        def ldc(q):
            pr.inc(q.dma_start(out=io64[:, :], in_=iota64_d.ap()), "qd")
            pr.inc(q.dma_start(out=io32[:, :], in_=iota32_d.ap()), "qd")
            pr.inc(q.dma_start(out=u128s[:, :], in_=u128_d.ap()), "qd")
            pr.inc(q.dma_start(out=u64s[:, :], in_=u64_d.ap()), "qd")
            pr.inc(q.dma_start(out=r0[:, :], in_=w["R0"]), "qd")
        pr.blk(q=ldc)
        c_tok = pr.tok("qd")

        def load_bias_seg(seg, psB, war):
            def ldb(q, seg=seg):
                pr.wait(q, "q", war)
                pr.inc(q.dma_start(out=row1[:, :],
                                   in_=w["b"][0:1, seg * 1024:(seg + 1) * 1024]), "qd")
            pr.blk(q=ldb)
            rt = pr.tok("qd")
            rb = emit_row_broadcast(pr, psB, row1, rt, 1024)

            def evb(s, rb=rb):
                pr.wait(s, "s", rb)
                pr.inc(s.copy(bias1[:, :], psB[:, :]), "s")
            pr.blk(s=evb)
            return pr.tok("s")

        with ExitStack() as phq:
            pctx = phq.enter_context
            lnTlo = [pctx(nc.sbuf_tensor(f"l2b_lnTlo{c}", [P, TT], BF16)) for c in range(8)]
            tmpln = pctx(nc.sbuf_tensor("l2b_tmpln0", [P, D], F32))
            tmphl = pctx(nc.sbuf_tensor("l2b_tmphl0", [P, 2 * D], BF16))
            psLT = pctx(nc.psum_tensor("l2b_psLT0", [P, 1024], BF16))
            tr_toks = {}

            def post_tile(i, tok):
                def hilo(v, i=i, tok=tok):
                    pr.wait(v, "v", tok)
                    pr.inc(v.tensor_copy(tmphl[:, 0:D], tmpln[:, :]), "v")
                    pr.wait(v, "v", pr.tok("v"))
                    pr.inc(v.tensor_sub(tmphl[:, D:2 * D], tmpln[:, :], tmphl[:, 0:D]), "v")
                pr.blk(v=hilo)
                hl_tok = pr.tok("v")

                for grp in range(2):
                    dstset = lnT16 if grp == 0 else lnTlo

                    def trt(t, i=i, grp=grp, hl_tok=hl_tok):
                        pr.wait(t, "t", hl_tok, pr.ident_tok)
                        pr.wait(t, "t", tr_toks.get((i, grp - 1), tr_toks.get((i - 1, 1))))
                        last = None
                        for c in range(8):
                            last = t.transpose(psLT[:, c * 128:(c + 1) * 128],
                                               tmphl[:, grp * D + c * 128:grp * D + (c + 1) * 128],
                                               pr.ident[:, :])
                        pr.inc(last, "t")
                    pr.blk(t=trt)
                    ttk = pr.tok("t")

                    def evt(s, i=i, grp=grp, ttk=ttk, dstset=dstset):
                        pr.wait(s, "s", ttk)
                        for c in range(8):
                            pr.inc(s.copy(dstset[c][:, i * 128:(i + 1) * 128],
                                          psLT[:, c * 128:(c + 1) * 128]), "s")
                    pr.blk(s=evt)
                    tr_toks[(i, grp)] = pr.tok("s")
                return tr_toks[(i, 1)]

            ln_list = [tmpln] * NT
            emit_ln_from_xio(pr, phq, xio, ln_list, post_tile=post_tile,
                             tile_war=lambda i: tr_toks.get((i - 1, 1)))
            lnT16_tok = pr.tok("s")

            wqh = pctx(nc.sbuf_tensor("l2b_wqh", [P, 8 * 1024], BF16))
            wql = pctx(nc.sbuf_tensor("l2b_wql", [P, 8 * 1024], BF16))
            wtmp = pctx(nc.sbuf_tensor("l2b_wtmp", [P, 1024], F32))
            qtl = [pctx(nc.sbuf_tensor(f"l2b_q{i}", [P, D], BF16)) for i in range(NT)]
            qrT = pctx(nc.sbuf_tensor("l2b_qrT", [64, 2048], F32))
            ptile = pctx(nc.sbuf_tensor("l2b_pt", [P, 512], F32))
            mx16 = pctx(nc.sbuf_tensor("l2b_mx", [P, 16 * 4], F32))
            psQ = [pctx(nc.psum_tensor(f"l2b_psQ{j}", [P, 1024], F32)) for j in range(2)]
            psP = pctx(nc.psum_tensor("l2b_psP", [P, 512], F32))
            psB3 = pctx(nc.psum_tensor("l2b_psB3", [P, 1024], F32))
            psTq = psB3

            wq_cv = []
            for c in range(8):
                def ldwq(g, c=c):
                    if wq_cv:
                        pr.wait(g, "g", wq_cv[-1])
                    pr.inc(g.dma_start(out=wtmp[:, :], in_=w["W"][c][:, 0:1024]), "gd")
                pr.blk(g=ldwq)
                lt = pr.tok("gd")

                def cvw(v, c=c, lt=lt):
                    pr.wait(v, "v", lt)
                    pr.inc(v.tensor_copy(wqh[:, c * 1024:(c + 1) * 1024], wtmp[:, :]), "v")
                    pr.wait(v, "v", pr.tok("v"))
                    pr.inc(v.tensor_sub(wql[:, c * 1024:(c + 1) * 1024], wtmp[:, :],
                                        wqh[:, c * 1024:(c + 1) * 1024]), "v")
                pr.blk(v=cvw)
                wq_cv.append(pr.tok("v"))
            wq_tok = wq_cv[-1]
            bias_q = load_bias_seg(0, psB3, None)

            b_done = {}
            bk_done = {}
            for i in range(NT):
                bank = psQ[i % 2]

                def mmq(t, i=i, bank=bank):
                    pr.wait(t, "t", wq_tok, lnT16_tok)
                    pr.wait(t, "t", b_done.get(i - 2))
                    last = None
                    for mh in range(2):
                        ng = 0
                        for aa, bb in ((lnT16, wqh), (lnT16, wql), (lnTlo, wqh)):
                            for c in range(8):
                                last = t.matmul(bank[:, mh * 512:(mh + 1) * 512],
                                                aa[c][:, i * 128:(i + 1) * 128],
                                                bb[:, c * 1024 + mh * 512:c * 1024 + (mh + 1) * 512],
                                                start=(ng == 0), stop=(ng == 23))
                                ng += 1
                    pr.inc(last, "t")
                pr.blk(t=mmq)
                mmtok = pr.tok("t")
                cstok, csl = _l2b_ldcs(pr, cs2, cs_used, cs_dram, i, "q")

                def rope_q(v, i=i, bank=bank, mmtok=mmtok, cstok=cstok, csl=csl):
                    pr.wait(v, "v", mmtok, bias_q, cstok)
                    pr.inc(v.tensor_add(qr[:, :], bank[:, :], bias1[:, :]), "v")
                    pr.wait(v, "v", pr.tok("v"))
                    qv = qr[:, :].rearrange("p (x two) -> p x two", two=2)
                    rv = rot[:, :].rearrange("p (x two) -> p x two", two=2)
                    pr.inc(v.tensor_scalar(rv[:, :, 0:1], qv[:, :, 1:2], -1.0, None, ALU.mult), "v")
                    pr.inc(v.tensor_copy(rv[:, :, 1:2], qv[:, :, 0:1]), "v")
                    pr.wait(v, "v", pr.tok("v"))
                    cos2 = cs2[:, csl * 128:csl * 128 + 64]
                    sin2 = cs2[:, csl * 128 + 64:csl * 128 + 128]
                    q3 = qr[:, :].rearrange("p (h d) -> p h d", d=64)
                    r3 = rot[:, :].rearrange("p (h d) -> p h d", d=64)
                    t3 = t1b[:, :].rearrange("p (h d) -> p h d", d=64)
                    cb = cos2.unsqueeze(1).broadcast_to([P, 16, 64])
                    sbx = sin2.unsqueeze(1).broadcast_to([P, 16, 64])
                    pr.inc(v.tensor_tensor(t3, q3, cb, ALU.mult), "v")
                    pr.inc(v.tensor_tensor(r3, r3, sbx, ALU.mult), "v")
                    pr.wait(v, "v", pr.tok("v"))
                    pr.inc(v.tensor_add(qr[:, :], t1b[:, :], rot[:, :]), "v")
                    pr.wait(v, "v", pr.tok("v"))
                    pr.inc(v.tensor_copy(qtl[i][:, :], qr[:, :]), "v")
                pr.blk(v=rope_q)
                vtok = pr.tok("v")
                cs_used[("q", i)] = vtok

                def trq(t, i=i, vtok=vtok, half=0):
                    pass
                for hh in range(2):
                    def trqh(t, i=i, vtok=vtok, hh=hh):
                        pr.wait(t, "t", vtok, pr.ident_tok)
                        if hh == 0:
                            pr.wait(t, "t", bk_done.get(i - 1))
                        else:
                            pr.wait(t, "t", ev_tr[0])
                        last = None
                        for hx in range(8):
                            h = hh * 8 + hx
                            last = t.transpose(psTq[0:64, hx * 128:(hx + 1) * 128],
                                               qr[:, h * 64:(h + 1) * 64], pr.identf[:, :])
                        pr.inc(last, "t")
                    pr.blk(t=trqh)
                    ttk = pr.tok("t")

                    def evqh(s, hh=hh, ttk=ttk):
                        pr.wait(s, "s", ttk)
                        pr.inc(s.copy(qrT[0:64, hh * 1024:(hh + 1) * 1024],
                                      psTq[0:64, 0:1024]), "s")
                    pr.blk(s=evqh)
                    if hh == 0:
                        ev_tr = [pr.tok("s")]
                    else:
                        ev_tr[0] = pr.tok("s")
                etok = pr.tok("s")

                def mmp(t, i=i, etok=etok):
                    pr.wait(t, "t", etok, c_tok)
                    pr.wait(t, "t", bk_done.get(i - 1))
                    last = None
                    for h in range(16):
                        last = t.matmul(psP[:, h * 32:(h + 1) * 32],
                                        qrT[0:64, h * 128:(h + 1) * 128],
                                        r0[:, :], start=True, stop=True)
                    pr.inc(last, "t")
                pr.blk(t=mmp)
                ptok = pr.tok("t")

                def bkt(v, i=i, ptok=ptok):
                    pr.wait(v, "v", ptok)
                    pr.inc(v.tensor_copy(ptile[:, :], psP[:, :]), "v")
                    pr.wait(v, "v", pr.tok("v"))
                    p3 = ptile[:, :].rearrange("p (h m) -> p h m", m=32)
                    pr.inc(v.tensor_reduce(mx16[:, 0:16], p3, AX.X, ALU.max), "v")
                    pr.inc(v.tensor_reduce(mx16[:, 16:32], p3, AX.X, ALU.min), "v")
                    pr.wait(v, "v", pr.tok("v"))
                    mxb = mx16[:, 0:16].unsqueeze(2).broadcast_to([P, 16, 32])
                    mnb = mx16[:, 16:32].unsqueeze(2).broadcast_to([P, 16, 32])
                    t3 = t1b[:, 0:512].rearrange("p (h m) -> p h m", m=32)
                    pr.inc(v.tensor_tensor(t3, p3, mxb, ALU.is_lt), "v")
                    r3 = rot[:, 0:512].rearrange("p (h m) -> p h m", m=32)
                    pr.inc(v.tensor_tensor(r3, p3, mnb, ALU.is_gt), "v")
                    pr.wait(v, "v", pr.tok("v"))
                    pr.inc(v.scalar_tensor_tensor(t1b[:, 0:512], t1b[:, 0:512], 64.0,
                                                  io32[:, :], ALU.mult, ALU.add), "v")
                    pr.inc(v.scalar_tensor_tensor(rot[:, 0:512], rot[:, 0:512], 64.0,
                                                  io32[:, :], ALU.mult, ALU.add), "v")
                    pr.wait(v, "v", pr.tok("v"))
                    t3b = t1b[:, 0:512].rearrange("p (h m) -> p h m", m=32)
                    r3b = rot[:, 0:512].rearrange("p (h m) -> p h m", m=32)
                    pr.inc(v.tensor_reduce(mx16[:, 32:48], t3b, AX.X, ALU.min), "v")
                    pr.inc(v.tensor_reduce(mx16[:, 48:64], r3b, AX.X, ALU.min), "v")
                    pr.wait(v, "v", pr.tok("v"))
                    pr.inc(v.tensor_scalar(mx16[:, 16:32], mx16[:, 16:32], -1.0, None,
                                           ALU.mult), "v")
                    pr.inc(v.tensor_scalar(mx16[:, 48:64], mx16[:, 48:64], 32.0, None,
                                           ALU.add), "v")
                    pr.wait(v, "v", pr.tok("v"))
                    pr.inc(v.tensor_tensor(t1b[:, 512:528], mx16[:, 0:16], mx16[:, 16:32],
                                           ALU.is_ge), "v")
                    pr.wait(v, "v", pr.tok("v"))
                    pr.inc(v.tensor_copy(t1b[:, 528:544].bitcast(mybir.dt.uint8)[:, 0:16]
                                         if False else rot[:, 512:528], t1b[:, 512:528]), "v")
                    pr.inc(v.tensor_tensor(rot[:, 528:544], mx16[:, 32:48], mx16[:, 48:64],
                                           ALU.subtract), "v")
                    pr.wait(v, "v", pr.tok("v"))
                    pr.inc(v.scalar_tensor_tensor(bcol[:, i * 16:(i + 1) * 16],
                                                  t1b[:, 512:528], 1.0, rot[:, 528:544],
                                                  ALU.mult, ALU.mult), "v")
                    pr.wait(v, "v", pr.tok("v"))
                    pr.inc(v.tensor_add(bcol[:, i * 16:(i + 1) * 16],
                                        bcol[:, i * 16:(i + 1) * 16], mx16[:, 48:64]), "v")
                pr.blk(v=bkt)
                b_done[i] = pr.tok("v")
                bk_done[i] = b_done[i]
            q_phase_tok = pr.tok("v")
            sort_done = {}

            Ot = pctx(nc.sbuf_tensor("l2b_O", [P, 64], BF16))
            hrow = pctx(nc.sbuf_tensor("l2b_hrow", [1, 64], F32))
            pfrow = pctx(nc.sbuf_tensor("l2b_pfrow", [1, 64], F32))
            hcol = pctx(nc.sbuf_tensor("l2b_hcol", [64, 1], F32))
            rsum = pctx(nc.sbuf_tensor("l2b_rsum", [P, 8], F32))
            psH = psQ[0][:, 0:64]
            psC = psQ[0][:, 512:576]
            psHB = psQ[1][:, 0:64]

            for h in range(16):
                o_done = []
                for i in range(NT):
                    def mko(v, h=h, i=i):
                        pr.wait(v, "v", b_done[i] if h == 0 else q_phase_tok)
                        if o_done:
                            pr.wait(v, "v", o_done[-1])
                        elif h > 0:
                            pr.wait(v, "v", sort_done[(h - 1, NT - 1)])
                        pr.inc(v.tensor_scalar(Ot[:, :], io64[:, :],
                                               bcol[:, i * 16 + h:i * 16 + h + 1],
                                               None, ALU.is_equal), "v")
                    pr.blk(v=mko)
                    otok = pr.tok("v")

                    def mmh(t, h=h, i=i, otok=otok):
                        pr.wait(t, "t", otok, pr.ones_tok)
                        pr.inc(t.matmul(psH[0:64, 0:1], Ot[:, :], pr.ones_bf16[:, 0:1],
                                        start=(i == 0), stop=(i == NT - 1)), "t")
                    pr.blk(t=mmh)
                    o_done.append(pr.tok("t"))
                h_tok = pr.tok("t")

                def evh(s, h_tok=h_tok):
                    pr.wait(s, "s", h_tok)
                    pr.inc(s.copy(hcol[:, :], psH[0:64, 0:1]), "s")
                pr.blk(s=evh)
                hc_tok = pr.tok("s")

                def mmpf(t, hc_tok=hc_tok):
                    pr.wait(t, "t", hc_tok)
                    pr.inc(t.matmul(psH[0:64, 1:2], u64s[:, :], hcol[:, :],
                                    start=True, stop=True), "t")
                pr.blk(t=mmpf)
                pf_tok = pr.tok("t")

                def trpf(t, pf_tok=pf_tok):
                    pr.wait(t, "t", pf_tok)
                    pr.inc(t.transpose(psH[0:1, 2:66], psH[0:64, 1:2], pr.identf[0:64, 0:64])
                           if False else
                           t.transpose(psC[0:1, 0:64], hcol[:, :], pr.identf[0:64, 0:64]), "t")
                pr.blk(t=trpf)
                def evpf(s):
                    pr.wait(s, "s", pr.tok("t"))
                    pr.inc(s.copy(hrow[0:1, :], psC[0:1, 0:64]), "s")
                    pr.inc(s.copy(rsum[0:64, 0:1], psH[0:64, 1:2]), "s")
                pr.blk(s=evpf)
                ev1 = pr.tok("s")

                def trpf2(t, ev1=ev1):
                    pr.wait(t, "t", ev1)
                    pr.inc(t.transpose(psC[0:1, 0:64], rsum[0:64, 0:1],
                                       pr.identf[0:64, 0:64]), "t")
                pr.blk(t=trpf2)
                tr2 = pr.tok("t")

                def evpf2(s, tr2=tr2):
                    pr.wait(s, "s", tr2)
                    pr.inc(s.copy(pfrow[0:1, :], psC[0:1, 0:64]), "s")
                pr.blk(s=evpf2)
                pf_row_tok = pr.tok("s")

                for i in range(NT):
                    def mko2(v, h=h, i=i, pf_row_tok=pf_row_tok):
                        pr.wait(v, "v", pf_row_tok)
                        if sort_done and (h, i - 1) in sort_done:
                            pr.wait(v, "v", sort_done[(h, i - 1)])
                        pr.inc(v.tensor_scalar(Ot[:, :], io64[:, :],
                                               bcol[:, i * 16 + h:i * 16 + h + 1],
                                               None, ALU.is_equal), "v")
                    pr.blk(v=mko2)
                    otok = pr.tok("v")

                    def mmc(t, otok=otok, i=i, h=h):
                        pr.wait(t, "t", otok)
                        pr.inc(t.matmul(psC[:, 0:64], u128s[:, :], Ot[:, :],
                                        start=True, stop=True), "t")
                        pr.inc(t.matmul(psHB[:, 0:64], pr.ones_f32[0:1, 0:128],
                                        pfrow[0:1, :], start=True, stop=True), "t")
                    pr.blk(t=mmc)
                    ctok = pr.tok("t")

                    def mkpos(v, i=i, h=h, ctok=ctok):
                        pr.wait(v, "v", ctok)
                        pr.inc(v.tensor_mul(t1b[:, 0:64], Ot[:, :], psC[:, 0:64]), "v")
                        pr.inc(v.tensor_mul(rot[:, 0:64], Ot[:, :], psHB[:, 0:64]), "v")
                        pr.wait(v, "v", pr.tok("v"))
                        pr.inc(v.reduce_sum(rsum[:, 0:1], t1b[:, 0:64], AX.X), "v")
                        pr.inc(v.reduce_sum(rsum[:, 1:2], rot[:, 0:64], AX.X), "v")
                        pr.wait(v, "v", pr.tok("v"))
                        pr.inc(v.tensor_add(rsum[:, 2:3], rsum[:, 0:1], rsum[:, 1:2]), "v")
                        pr.wait(v, "v", pr.tok("v"))
                        pr.inc(v.tensor_scalar(rsum[:, 2:3], rsum[:, 2:3],
                                               float(h * TT), None, ALU.add), "v")
                        pr.wait(v, "v", pr.tok("v"))
                        pr.inc(v.tensor_copy(poscol[:, i * 16 + h:i * 16 + h + 1],
                                             rsum[:, 2:3]), "v")
                    pr.blk(v=mkpos)
                    postok = pr.tok("v")

                    def mmu(t, otok=otok, postok=postok):
                        pr.wait(t, "t", postok, pr.ones_tok)
                        pr.inc(t.matmul(psC[0:1, 0:64], pr.ones_bf16[:, 0:1], Ot[:, :],
                                        start=True, stop=True), "t")
                    pr.blk(t=mmu)
                    utok = pr.tok("t")

                    def evu(v, utok=utok):
                        pr.wait(v, "v", utok)
                        pr.inc(v.tensor_add(pfrow[0:1, :], pfrow[0:1, :], psC[0:1, 0:64]), "v")
                    pr.blk(v=evu)
                    uvtok = pr.tok("v")

                    def sct(g, h=h, i=i, postok=postok):
                        pr.wait(g, "g", postok)
                        pr.inc(g.indirect_dma_start(
                            qsq_d, bass.IndirectOffsetOnAxis(
                                ap=poscol[:, i * 16 + h:i * 16 + h + 1], axis=0),
                            qtl[i][:, h * 64:(h + 1) * 64], None), "gd")
                    pr.blk(g=sct)
                    sort_done[(h, i)] = pr.tok("gd")
            sort_tok = pr.tok("gd")
        with ExitStack() as phkv:
            kctx = phkv.enter_context
            kvtl = [kctx(nc.sbuf_tensor(f"l2b_kv{i}", [P, 2048], BF16)) for i in range(NT)]
            wkv = kctx(nc.sbuf_tensor("l2b_wkv", [P, 8 * 2048], BF16))
            psKV = [kctx(nc.psum_tensor(f"l2b_psKV{j}", [P, 1024], F32)) for j in range(2)]
            psB4 = kctx(nc.psum_tensor("l2b_psB4", [P, 1024], F32))
            bias_kv = kctx(nc.sbuf_tensor("l2b_biaskv", [P, 2048], F32))
            bias_k = load_bias_seg(1, psB4, None)

            def cpk(v):
                pr.wait(v, "v", bias_k)
                pr.inc(v.tensor_copy(bias_kv[:, 0:1024], bias1[:, :]), "v")
            pr.blk(v=cpk)
            bias_k = pr.tok("v")
            bias_v0 = load_bias_seg(2, psB4, bias_k)

            def cpv(v):
                pr.wait(v, "v", bias_v0)
                pr.inc(v.tensor_copy(bias_kv[:, 1024:2048], bias1[:, :]), "v")
            pr.blk(v=cpv)
            bias_v = pr.tok("v")

            def ldwkv(g):
                pr.wait(g, "g", sort_tok)
                for c in range(8):
                    pr.inc(g.dma_start(out=wkv[:, c * 2048:(c + 1) * 2048],
                                       in_=w["W"][c][:, 1024:3072]), "gd")
            pr.blk(g=ldwkv)
            wkv_tok = pr.tok("gd")

            kv_done = {}
            for i in range(NT):
                for half in range(2):
                    bank = psKV[(i * 2 + half) % 2]

                    def mmkv(t, i=i, half=half, bank=bank):
                        pr.wait(t, "t", wkv_tok, lnT16_tok)
                        pr.wait(t, "t", kv_done.get((i, half - 2), kv_done.get((i - 1, half))))
                        last = None
                        for mh in range(2):
                            for c in range(8):
                                last = t.matmul(bank[:, mh * 512:(mh + 1) * 512],
                                                lnT16[c][:, i * 128:(i + 1) * 128],
                                                wkv[:, c * 2048 + half * 1024 + mh * 512:
                                                    c * 2048 + half * 1024 + (mh + 1) * 512],
                                                start=(c == 0), stop=(c == 7))
                        pr.inc(last, "t")
                    pr.blk(t=mmkv)
                    mmtok = pr.tok("t")
                    cstok, csl = (None, 1)
                    if half == 0:
                        cstok, csl = _l2b_ldcs(pr, cs2, cs_used, cs_dram, i, "k")

                    def evkv(v, i=i, half=half, bank=bank, mmtok=mmtok, cstok=cstok, csl=csl):
                        pr.wait(v, "v", mmtok, bias_v, cstok)
                        dst = kvtl[i][:, half * 1024:(half + 1) * 1024]
                        pr.inc(v.tensor_add(qr[:, :], bank[:, :],
                                            bias_kv[:, half * 1024:(half + 1) * 1024]), "v")
                        pr.wait(v, "v", pr.tok("v"))
                        if half == 0:
                            qv = qr[:, :].rearrange("p (x two) -> p x two", two=2)
                            rv = rot[:, :].rearrange("p (x two) -> p x two", two=2)
                            pr.inc(v.tensor_scalar(rv[:, :, 0:1], qv[:, :, 1:2], -1.0, None, ALU.mult), "v")
                            pr.inc(v.tensor_copy(rv[:, :, 1:2], qv[:, :, 0:1]), "v")
                            pr.wait(v, "v", pr.tok("v"))
                            cos2 = cs2[:, csl * 128:csl * 128 + 64]
                            sin2 = cs2[:, csl * 128 + 64:csl * 128 + 128]
                            q3 = qr[:, :].rearrange("p (h d) -> p h d", d=64)
                            r3 = rot[:, :].rearrange("p (h d) -> p h d", d=64)
                            t3 = t1b[:, :].rearrange("p (h d) -> p h d", d=64)
                            cb = cos2.unsqueeze(1).broadcast_to([P, 16, 64])
                            sbx = sin2.unsqueeze(1).broadcast_to([P, 16, 64])
                            pr.inc(v.tensor_tensor(t3, q3, cb, ALU.mult), "v")
                            pr.inc(v.tensor_tensor(r3, r3, sbx, ALU.mult), "v")
                            pr.wait(v, "v", pr.tok("v"))
                            pr.inc(v.tensor_add(dst, t1b[:, :], rot[:, :]), "v")
                        else:
                            pr.inc(v.tensor_copy(dst, qr[:, :]), "v")
                    pr.blk(v=evkv)
                    kv_done[(i, half)] = pr.tok("v")
                    if half == 0:
                        cs_used[("k", i)] = kv_done[(i, half)]

            kvs_done = {}
            for h in range(16):
                for i in range(NT):
                    def skv(g, h=h, i=i):
                        pr.wait(g, "g", kv_done[(i, 1)])
                        pr.inc(g.indirect_dma_start(
                            qsk_d, bass.IndirectOffsetOnAxis(
                                ap=poscol[:, i * 16 + h:i * 16 + h + 1], axis=0),
                            kvtl[i][:, h * 64:(h + 1) * 64], None), "gd")
                        pr.inc(g.indirect_dma_start(
                            qsv_d, bass.IndirectOffsetOnAxis(
                                ap=poscol[:, i * 16 + h:i * 16 + h + 1], axis=0),
                            kvtl[i][:, 1024 + h * 64:1024 + (h + 1) * 64], None), "gd")
                    pr.blk(g=skv)
                    kvs_done[(h, i)] = pr.tok("gd")
            sort_tok = pr.tok("gd")

        att = [lctx(nc.sbuf_tensor(f"l2b_att{i}", [P, 1024], BF16)) for i in range(NT)]
        latt = ExitStack()
        chnk = [latt.enter_context(nc.sbuf_tensor(f"l2b_ch{j}", [P, 192], BF16)) for j in range(2)]
        kqT = latt.enter_context(nc.sbuf_tensor("l2b_kqT", [64, 2 * 256], BF16))
        wsx = latt.enter_context(nc.sbuf_tensor("l2b_wsx", [P, 2 * 128], BF16))
        olz = latt.enter_context(nc.sbuf_tensor("l2b_olz", [P, 2 * 66], F32))
        psS2 = [latt.enter_context(nc.psum_tensor(f"l2b_psS2{j}", [P, 128], F32)) for j in range(2)]
        psO2 = [latt.enter_context(nc.psum_tensor(f"l2b_psO2{j}", [P, 66], F32)) for j in range(2)]
        psT2 = [latt.enter_context(nc.psum_tensor(f"l2b_psT2{j}", [64, 256], BF16)) for j in range(2)]

        la_done = {}
        ld_done = {}
        tr_done2 = {}
        for h in range(16):
            for ck in range(16):
                sl2 = (h * 16 + ck) % 2
                c0 = sl2 * 0

                def ldch(g, h=h, ck=ck, sl2=sl2):
                    pr.wait(g, "g", sort_tok)
                    pr.wait(g, "g", la_done.get((h, ck - 2), la_done.get((h - 1, 14 + ck))))
                    r0_ = h * TT + ck * 128
                    pr.inc(g.dma_start(out=chnk[sl2][:, 0:64],
                                       in_=qsq_d[r0_:r0_ + 128, :]), "gd")
                    pr.inc(g.dma_start(out=chnk[sl2][:, 64:128],
                                       in_=qsk_d[r0_:r0_ + 128, :]), "gd")
                    pr.inc(g.dma_start(out=chnk[sl2][:, 128:192],
                                       in_=qsv_d[r0_:r0_ + 128, :]), "gd")
                pr.blk(g=ldch)
                ld_done[(h, ck)] = pr.tok("gd")

                def trc(t, h=h, ck=ck, sl2=sl2):
                    pr.wait(t, "t", ld_done[(h, ck)], pr.ident_tok)
                    pr.wait(t, "t", la_done.get((h, ck - 2), la_done.get((h - 1, 14 + ck))))
                    t.transpose(psT2[sl2][0:64, 0:128], chnk[sl2][:, 0:64], pr.ident[:, :])
                    pr.inc(t.transpose(psT2[sl2][0:64, 128:256], chnk[sl2][:, 64:128],
                                       pr.ident[:, :]), "t")
                pr.blk(t=trc)
                ttok = pr.tok("t")

                def evc(s, sl2=sl2, ttok=ttok):
                    pr.wait(s, "s", ttok)
                    pr.inc(s.copy(kqT[0:64, sl2 * 256:(sl2 + 1) * 256],
                                  psT2[sl2][0:64, :]), "s")
                pr.blk(s=evc)
                etok = pr.tok("s")
                tr_done2[(h, ck)] = etok

                def mms(t, sl2=sl2, etok=etok):
                    pr.wait(t, "t", etok)
                    pr.inc(t.matmul(psS2[sl2][:, :],
                                    kqT[0:64, sl2 * 256 + 128:sl2 * 256 + 256],
                                    kqT[0:64, sl2 * 256:sl2 * 256 + 128],
                                    start=True, stop=True), "t")
                pr.blk(t=mms)
                stok = pr.tok("t")

                def eve(s, sl2=sl2, stok=stok):
                    pr.wait(s, "s", stok)
                    pr.inc(s.activation(wsx[:, sl2 * 128:(sl2 + 1) * 128], psS2[sl2][:, :],
                                        AF.Exp, scale=0.125), "s")
                pr.blk(s=eve)
                extok = pr.tok("s")

                def mmav(t, h=h, ck=ck, sl2=sl2, extok=extok):
                    pr.wait(t, "t", extok)
                    last = t.matmul(psO2[sl2][:, 0:64],
                                    wsx[:, sl2 * 128:(sl2 + 1) * 128],
                                    chnk[sl2][:, 128:192], start=True, stop=True)
                    pr.inc(t.matmul(psO2[sl2][:, 64:65],
                                    wsx[:, sl2 * 128:(sl2 + 1) * 128],
                                    pr.ones_bf16[:, 0:1], start=True, stop=True), "t")
                pr.blk(t=mmav)
                avtok = pr.tok("t")

                def evo(v, h=h, ck=ck, sl2=sl2, avtok=avtok):
                    pr.wait(v, "v", avtok)
                    pr.inc(v.reciprocal(olz[:, sl2 * 66 + 65:sl2 * 66 + 66],
                                        psO2[sl2][:, 64:65]), "v")
                    pr.wait(v, "v", pr.tok("v"))
                    pr.inc(v.tensor_scalar(olz[:, sl2 * 66:sl2 * 66 + 64],
                                           psO2[sl2][:, 0:64],
                                           olz[:, sl2 * 66 + 65:sl2 * 66 + 66],
                                           None, ALU.mult), "v")
                pr.blk(v=evo)
                otok = pr.tok("v")

                def sto(g, h=h, ck=ck, sl2=sl2, otok=otok):
                    pr.wait(g, "g", otok)
                    r0_ = h * TT + ck * 128
                    pr.inc(g.dma_start(out=outl_d[r0_:r0_ + 128, :],
                                       in_=olz[:, sl2 * 66:sl2 * 66 + 64]), "gd")
                pr.blk(g=sto)
                la_done[(h, ck)] = pr.tok("gd")
        la_tok = pr.tok("gd")

        ug_done = {}
        for h in range(16):
            for i in range(NT):
                def ug(g, h=h, i=i):
                    pr.wait(g, "g", la_tok)
                    pr.inc(g.indirect_dma_start(
                        att[i][:, h * 64:(h + 1) * 64], None,
                        outl_d, bass.IndirectOffsetOnAxis(
                            ap=poscol[:, i * 16 + h:i * 16 + h + 1], axis=0)), "gd")
                pr.blk(g=ug)
                ug_done[(h, i)] = pr.tok("gd")
        att_tok = pr.tok("gd")
        latt.close()
        if getattr(pr, "l2b_debug", None) is not None:
            dbg = pr.l2b_debug

            def dmp(g):
                pr.wait(g, "g", att_tok)
                pr.inc(g.dma_start(out=dbg["pos"], in_=poscol[:, :]), "gd")
                pr.inc(g.dma_start(out=dbg["bc"], in_=bcol[:, :]), "gd")
                pr.inc(g.dma_start(out=dbg["att"], in_=att[0][:, :]), "gd")
                pr.inc(g.dma_start(out=dbg["qsq"], in_=qsq_d[0:128, :]), "gd")
                pr.inc(g.dma_start(out=dbg["ol"], in_=outl_d[0:128, :]), "gd")
                pr.inc(g.dma_start(out=dbg["kv"][:, 0:64], in_=qsk_d[0:128, :]), "gd")
                pr.inc(g.dma_start(out=dbg["kv"][:, 64:128], in_=qsv_d[0:128, :]), "gd")
            pr.blk(g=dmp)

        attT = lnT16
        with ExitStack() as fs:
            attT_tok = emit_transpose_fm(pr, fs, att, [att_tok] * NT, attT,
                                         tag=f"l2bat{uid}")
            fin = fs.enter_context(nc.sbuf_tensor("l2b_fin", [P, 1024], F32))
            wpb = fs.enter_context(nc.sbuf_tensor("l2b_wpb", [P, 8 * 1024], BF16))
            bpb = fs.enter_context(nc.sbuf_tensor("l2b_bpb", [P, 1024], F32))
            rowp = fs.enter_context(nc.sbuf_tensor("l2b_rowp", [1, 1024], F32))
            psF = [fs.enter_context(nc.psum_tensor(f"l2b_psF{j}", [P, 1024], F32))
                   for j in range(2)]

            def ldwp(g):
                pr.wait(g, "g", att_tok)
                for c in range(8):
                    pr.inc(g.dma_start(out=wpb[:, c * 1024:(c + 1) * 1024],
                                       in_=w["Wp"][c]), "gd")
            pr.blk(g=ldwp)
            wp_tok = pr.tok("gd")

            def ldbp(q):
                pr.inc(q.dma_start(out=rowp[:, :], in_=w["bp"]), "qd")
            pr.blk(q=ldbp)
            bp_rtok = pr.tok("qd")
            bp_rb = emit_row_broadcast(pr, psF[0], rowp, bp_rtok, 1024)

            def evbp(s):
                pr.wait(s, "s", bp_rb)
                pr.inc(s.copy(bpb[:, :], psF[0][:, :]), "s")
            pr.blk(s=evbp)
            bp_tok = pr.tok("s")

            out_toks = [None] * NT
            prev = {}
            for i in range(NT):
                bank = psF[i % 2]

                def mmo(t, i=i, bank=bank):
                    pr.wait(t, "t", attT_tok, wp_tok, bp_tok)
                    pr.wait(t, "t", prev.get(i - 2, bp_tok))
                    last = None
                    for mh in range(2):
                        for c in range(8):
                            last = t.matmul(bank[:, mh * 512:(mh + 1) * 512],
                                            attT[c][:, i * 128:(i + 1) * 128],
                                            wpb[:, c * 1024 + mh * 512:c * 1024 + (mh + 1) * 512],
                                            start=(c == 0), stop=(c == 7))
                    pr.inc(last, "t")
                pr.blk(t=mmo)
                mmtok = pr.tok("t")

                def evf(v, i=i, bank=bank, mmtok=mmtok):
                    pr.wait(v, "v", mmtok)
                    pr.inc(v.tensor_add(fin[:, :], bank[:, :], bpb[:, :]), "v")
                pr.blk(v=evf)
                ftok = pr.tok("v")
                out_toks[i] = xio.add_store(i, fin[:, :], ftok)
                prev[i] = out_toks[i]
        return out_toks



def stage_L1_attn_b(inputs, pre):
    for nm in ("wq", "wk", "wv", "wo"):
        W = np.asarray(inputs[f"L1_attn_b_{nm}"])
        pre[f"L1ab_{nm}T"] = np.ascontiguousarray(W.T).reshape(8, 128, 1024)


def decl_L1_attn_b(nc):
    d = {}
    for nm in ("wqT", "wkT", "wvT", "woT"):
        d[nm] = nc.dram_tensor(f"L1ab_{nm}", [8, 128, 1024], F32, kind="ExternalInput").ap()
    return d


def emit_sparsify_top2(pr, tiles, tok0, absb, t2b, m1, war_prev=None):
    toks = []
    for i, q in enumerate(tiles):
        def body_s(s, q=q, i=i):
            pr.wait(s, "s", tok0)
            pr.wait(s, "s", toks[-1] if toks else war_prev)
            pr.inc(s.activation(absb[:, :], q[:, :], AF.Abs), "s")
        pr.blk(s=body_s)
        stok = pr.tok("s")

        def body_v(v, q=q, i=i, stok=stok):
            pr.wait(v, "v", stok)
            a3 = absb[:, :].rearrange("p (h d) -> p h d", d=64)
            t3 = t2b[:, :].rearrange("p (h d) -> p h d", d=64)
            pr.inc(v.tensor_reduce(m1[:, 0:16], a3, AX.X, ALU.max), "v")
            pr.wait(v, "v", pr.tok("v"))
            m1b = m1[:, 0:16].unsqueeze(2).broadcast_to([P, 16, 64])
            pr.inc(v.tensor_tensor(t3, a3, m1b, ALU.is_lt), "v")
            pr.wait(v, "v", pr.tok("v"))
            pr.inc(v.tensor_mul(t2b[:, :], t2b[:, :], absb[:, :]), "v")
            pr.wait(v, "v", pr.tok("v"))
            pr.inc(v.tensor_reduce(m1[:, 16:32], t3, AX.X, ALU.max), "v")
            pr.wait(v, "v", pr.tok("v"))
            m2b = m1[:, 16:32].unsqueeze(2).broadcast_to([P, 16, 64])
            pr.inc(v.tensor_tensor(t3, a3, m2b, ALU.is_ge), "v")
            pr.wait(v, "v", pr.tok("v"))
            pr.inc(v.tensor_mul(q[:, :], q[:, :], t2b[:, :]), "v")
        pr.blk(v=body_v)
        toks.append(pr.tok("v"))
    return toks


def emit_L1_attn_b(pr, stack, xio, w, uid):
    nc = pr.nc
    NBIS = 3
    NITER = 16
    with ExitStack() as ls:
        lctx = ls.enter_context
        qtl = [lctx(nc.sbuf_tensor(f"l1b_q{i}", [P, D], BF16)) for i in range(NT)]
        ktl = [lctx(nc.sbuf_tensor(f"l1b_k{i}", [P, D], BF16)) for i in range(NT)]
        vtl = [lctx(nc.sbuf_tensor(f"l1b_v{i}", [P, D], BF16)) for i in range(NT)]
        wexp = lctx(nc.sbuf_tensor("l1b_wexp", [P, TT], BF16))
        wTb = lctx(nc.sbuf_tensor("l1b_wT", [P, TT], BF16))
        indf = lctx(nc.sbuf_tensor("l1b_indf", [P, TT], BF16))
        scrap = lctx(nc.sbuf_tensor("l1b_scrap", [P, TT], BF16))
        zbA = lctx(nc.sbuf_tensor("l1b_zbA", [P, 256], F32))
        zbB = lctx(nc.sbuf_tensor("l1b_zbB", [P, 256], F32))
        cols = lctx(nc.sbuf_tensor("l1b_cols", [P, 16], F32))
        colu = lctx(nc.sbuf_tensor("l1b_colu", [P, 2], mybir.dt.uint8))

        with ExitStack() as ph1:
            pctx = ph1.enter_context
            lnT = [pctx(nc.sbuf_tensor(f"l1b_lnT{c}", [P, TT], BF16)) for c in range(8)]
            lntmp = [pctx(nc.sbuf_tensor(f"l1b_ln{j}", [P, D], BF16)) for j in range(2)]
            psLT = pctx(nc.psum_tensor("l1b_psLT", [P, 1024], BF16))
            tr_toks = {}

            def post_tile(i, tok):
                def trt(t, i=i, tok=tok):
                    pr.wait(t, "t", tok, pr.ident_tok)
                    pr.wait(t, "t", tr_toks.get(i - 2))
                    last = None
                    for c in range(8):
                        last = t.transpose(psLT[:, c * 128:(c + 1) * 128],
                                           lntmp[i % 2][:, c * 128:(c + 1) * 128],
                                           pr.ident[:, :])
                    pr.inc(last, "t")
                pr.blk(t=trt)
                ttk = pr.tok("t")

                def evt(s, i=i, ttk=ttk):
                    pr.wait(s, "s", ttk)
                    for c in range(8):
                        pr.inc(s.copy(lnT[c][:, i * 128:(i + 1) * 128],
                                      psLT[:, c * 128:(c + 1) * 128]), "s")
                pr.blk(s=evt)
                tr_toks[i] = pr.tok("s")
                return tr_toks[i]

            emit_ln_from_xio(pr, ph1, xio, [lntmp[i % 2] for i in range(NT)],
                             post_tile=post_tile,
                             tile_war=lambda i: tr_toks.get(i - 2))
            lnT_tok = pr.tok("s")

            wbuf = pctx(nc.sbuf_tensor("l1b_wbuf", [P, 8 * 1024], BF16))
            psW = [pctx(nc.psum_tensor(f"l1b_psW{j}", [P, 1024], F32)) for j in range(2)]
            q_tok = emit_proj_tok(pr, lnT, lnT_tok, [w["wqT"][c] for c in range(8)],
                                  wbuf, qtl, psW, "q")
            k_tok = emit_proj_tok(pr, lnT, lnT_tok, [w["wkT"][c] for c in range(8)],
                                  wbuf, ktl, psW, "k", war_tok=q_tok)
            v_tok = emit_proj_tok(pr, lnT, lnT_tok, [w["wvT"][c] for c in range(8)],
                                  wbuf, vtl, psW, "v", war_tok=k_tok)

        with ExitStack() as sps:
            absb = sps.enter_context(nc.sbuf_tensor("l1b_absb", [P, 1024], F32))
            t2b = sps.enter_context(nc.sbuf_tensor("l1b_t2b", [P, 1024], F32))
            m1s = sps.enter_context(nc.sbuf_tensor("l1b_m1", [P, 32], F32))
            sq = emit_sparsify_top2(pr, qtl, q_tok, absb, t2b, m1s, war_prev=v_tok)
            sk = emit_sparsify_top2(pr, ktl, k_tok, absb, t2b, m1s, war_prev=sq[-1])
            sv = emit_sparsify_top2(pr, vtl, v_tok, absb, t2b, m1s, war_prev=sk[-1])
            spar_tok = pr.tok("v")

        qT = [lctx(nc.sbuf_tensor(f"l1b_qT{c}", [P, TT], BF16)) for c in range(8)]
        kT = [lctx(nc.sbuf_tensor(f"l1b_kT{c}", [P, TT], BF16)) for c in range(8)]
        with ExitStack() as trs:
            qT_tok = emit_transpose_fm(pr, trs, qtl, sq, qT, tag=f"l1bq{uid}")
            kT_tok = emit_transpose_fm(pr, trs, ktl, sk, kT, tag=f"l1bk{uid}")

        att = qtl
        att_ps = ExitStack()
        psS = [att_ps.enter_context(nc.psum_tensor(f"l1b_psS{j}", [P, 512], F32)) for j in range(2)]
        psT = att_ps.enter_context(nc.psum_tensor("l1b_psT", [P, 512], BF16))
        psO = [att_ps.enter_context(nc.psum_tensor(f"l1b_psO{j}", [P, 64], F32)) for j in range(2)]
        prev_tile = {}
        for h in range(16):
            cchunk, coff = h // 2, (h % 2) * 64
            for isub in range(NT):
                W = (isub + 1) * 128
                ZW = min(W, 256)
                key = (h, isub)
                prev = prev_tile.get((h, isub - 1), prev_tile.get((h - 1, NT - 1)))

                ev_toks = []
                for jc in range((W + 511) // 512):
                    j0, j1 = jc * 512, min(W, (jc + 1) * 512)
                    bank = psS[jc % 2]

                    def mmsc(t, h=h, isub=isub, j0=j0, j1=j1, bank=bank, jc=jc,
                             cchunk=cchunk, coff=coff, prev=prev):
                        pr.wait(t, "t", qT_tok, kT_tok)
                        pr.wait(t, "t", ev_toks[jc - 2] if jc >= 2 else prev)
                        pr.inc(t.matmul(bank[:, 0:j1 - j0],
                                        qT[cchunk][coff:coff + 64, isub * 128:(isub + 1) * 128],
                                        kT[cchunk][coff:coff + 64, j0:j1],
                                        start=True, stop=True), "t")
                    pr.blk(t=mmsc)
                    mmtok = pr.tok("t")

                    def evex(s, j0=j0, j1=j1, bank=bank, mmtok=mmtok, prev=prev):
                        pr.wait(s, "s", mmtok)
                        pr.wait(s, "s", prev)
                        pr.inc(s.activation(wexp[:, j0:j1], bank[:, 0:j1 - j0],
                                            AF.Exp, scale=0.125), "s")
                    pr.blk(s=evex)
                    ev_toks.append(pr.tok("s"))
                exp_tok = ev_toks[-1]

                def msk(g, isub=isub, exp_tok=exp_tok):
                    pr.wait(g, "g", exp_tok)
                    pr.inc(g.affine_select(wexp[:, isub * 128:(isub + 1) * 128],
                                           wexp[:, isub * 128:(isub + 1) * 128],
                                           [[-1, 128]], ALU.is_ge, 0.0,
                                           base=0, channel_multiplier=1), "g")
                pr.blk(g=msk)
                m_tok = pr.tok("g")

                def indp(v, W=W, m_tok=m_tok):
                    pr.wait(v, "v", m_tok)
                    pr.inc(v.tensor_scalar(indf[:, 0:W], wexp[:, 0:W], 1.0, 0.0,
                                           ALU.is_gt, ALU.add, accum_out=cols[:, 0:1]), "v")
                    pr.wait(v, "v", pr.tok("v"))
                    pr.inc(v.tensor_scalar(cols[:, 1:2], cols[:, 0:1], -1.0, 64.0,
                                           ALU.mult, ALU.add), "v")
                pr.blk(v=indp)
                p_tok = pr.tok("v")

                def zrule(v, ZW=ZW, p_tok=p_tok):
                    pr.wait(v, "v", p_tok)
                    pr.inc(v.tensor_scalar(zbA[:, 0:ZW], wexp[:, 0:ZW], 1.0, None,
                                           ALU.is_equal), "v")
                    pr.wait(v, "v", pr.tok("v"))
                    src, dst = zbA, zbB
                    sh = 1
                    while sh < ZW:
                        pr.inc(v.tensor_copy(dst[:, 0:sh], src[:, 0:sh]), "v")
                        pr.inc(v.tensor_add(dst[:, sh:ZW], src[:, sh:ZW],
                                            src[:, 0:ZW - sh]), "v")
                        pr.wait(v, "v", pr.tok("v"))
                        src, dst = dst, src
                        sh *= 2
                    pr.inc(v.tensor_scalar(src[:, 0:ZW], src[:, 0:ZW], cols[:, 1:2],
                                           None, ALU.is_le), "v")
                    pr.wait(v, "v", pr.tok("v"))
                    pr.inc(v.tensor_tensor(zbA[:, 0:ZW] if src is zbB else zbB[:, 0:ZW],
                                           src[:, 0:ZW],
                                           wexp[:, 0:ZW], ALU.bypass) if False else
                           v.tensor_mul(src[:, 0:ZW], src[:, 0:ZW],
                                        indf[:, 0:ZW]) if False else
                           v.engine_nop(), "v")
                    pr.inc(v.tensor_scalar(zbA[:, 0:ZW] if src is zbB else zbB[:, 0:ZW],
                                           wexp[:, 0:ZW], 1.0, None, ALU.is_equal), "v")
                    pr.wait(v, "v", pr.tok("v"))
                    other = zbA if src is zbB else zbB
                    pr.inc(v.tensor_mul(other[:, 0:ZW], other[:, 0:ZW], src[:, 0:ZW]), "v")
                    pr.wait(v, "v", pr.tok("v"))
                    pr.inc(v.tensor_max(indf[:, 0:ZW], indf[:, 0:ZW], other[:, 0:ZW]), "v")
                pr.blk(v=zrule)
                z_tok = pr.tok("v")

                sel_tok = z_tok
                if isub >= NT - NBIS or isub == 0:
                    lo_init = 0.0 if isub == 0 else 1.0

                    def binit(g, z_tok=z_tok, lo_init=lo_init):
                        pr.wait(g, "g", z_tok)
                        pr.inc(g.memset(cols[:, 2:3], lo_init), "g")
                        pr.inc(g.memset(cols[:, 3:4], 32.0), "g")
                    pr.blk(g=binit)
                    btok = pr.tok("g")

                    for it in range(NITER):
                        def bit(v, W=W, btok=btok):
                            pr.wait(v, "v", btok)
                            pr.inc(v.tensor_add(cols[:, 4:5], cols[:, 2:3], cols[:, 3:4]), "v")
                            pr.wait(v, "v", pr.tok("v"))
                            pr.inc(v.tensor_scalar(cols[:, 4:5], cols[:, 4:5], 0.5,
                                                   None, ALU.mult), "v")
                            pr.wait(v, "v", pr.tok("v"))
                            pr.inc(v.tensor_scalar(scrap[:, 0:W], wexp[:, 0:W],
                                                   cols[:, 4:5], 0.0, ALU.is_gt,
                                                   ALU.add, accum_out=cols[:, 5:6]), "v")
                            pr.wait(v, "v", pr.tok("v"))
                            pr.inc(v.tensor_scalar(colu[:, 0:1], cols[:, 5:6], 64.0,
                                                   None, ALU.is_ge), "v")
                            pr.wait(v, "v", pr.tok("v"))
                            pr.inc(v.select(cols[:, 2:3], colu[:, 0:1], cols[:, 4:5],
                                            cols[:, 2:3]), "v")
                            pr.inc(v.select(cols[:, 3:4], colu[:, 0:1], cols[:, 3:4],
                                            cols[:, 4:5]), "v")
                        pr.blk(v=bit)
                        btok = pr.tok("v")

                    def bfin(v, W=W, ZW=ZW, isub=isub, btok=btok):
                        pr.wait(v, "v", btok)
                        pr.inc(v.tensor_scalar(scrap[:, 0:W], wexp[:, 0:W],
                                               cols[:, 2:3], None, ALU.is_gt), "v")
                        pr.inc(v.tensor_scalar(cols[:, 6:7], cols[:, 0:1], 64.0,
                                               None, ALU.is_ge), "v")
                        if isub == 0:
                            pr.inc(v.tensor_scalar(scrap[:, 1024:1024 + W], wexp[:, 0:W],
                                                   1.0, 0.0, ALU.is_ge,
                                                   ALU.add, accum_out=cols[:, 10:11]), "v")
                            pr.wait(v, "v", pr.tok("v"))
                            pr.inc(v.tensor_scalar(cols[:, 7:8], cols[:, 10:11], 64.0,
                                                   None, ALU.is_lt), "v")
                            pr.wait(v, "v", pr.tok("v"))
                            pr.inc(v.tensor_max(cols[:, 6:7], cols[:, 6:7],
                                                cols[:, 7:8]), "v")
                        pr.wait(v, "v", pr.tok("v"))
                        pr.inc(v.tensor_copy(colu[:, 1:2], cols[:, 6:7]), "v")
                        pr.wait(v, "v", pr.tok("v"))
                        pr.inc(v.select(indf[:, 0:W],
                                        colu[:, 1:2].broadcast_to([P, W]),
                                        scrap[:, 0:W], indf[:, 0:W]), "v")
                    pr.blk(v=bfin)
                    sel_tok = pr.tok("v")

                def finw(v, W=W, sel_tok=sel_tok):
                    pr.wait(v, "v", sel_tok)
                    pr.inc(v.scalar_tensor_tensor(wexp[:, 0:W], wexp[:, 0:W], 1.0,
                                                  indf[:, 0:W], ALU.mult, ALU.mult,
                                                  accum_out=cols[:, 8:9]), "v")
                    pr.wait(v, "v", pr.tok("v"))
                    pr.inc(v.reciprocal(cols[:, 9:10], cols[:, 8:9]), "v")
                pr.blk(v=finw)
                fw_tok = pr.tok("v")

                tr_toks2 = []
                for jt in range(isub + 1):
                    def trw(t, jt=jt, fw_tok=fw_tok):
                        pr.wait(t, "t", fw_tok, pr.ident_tok)
                        if tr_toks2:
                            pr.wait(t, "t", tr_toks2[-1])
                        pr.inc(t.transpose(psT[:, 0:128], wexp[:, jt * 128:(jt + 1) * 128],
                                           pr.ident[:, :]), "t")
                    pr.blk(t=trw)
                    tt = pr.tok("t")

                    def evw(s, jt=jt, tt=tt):
                        pr.wait(s, "s", tt)
                        pr.inc(s.copy(wTb[:, jt * 128:(jt + 1) * 128], psT[:, 0:128]), "s")
                    pr.blk(s=evw)
                    tr_toks2.append(pr.tok("s"))
                wT_tok = tr_toks2[-1] if tr_toks2 else fw_tok

                bank = psO[(h * 16 + isub) % 2]

                def mmav(t, h=h, isub=isub, bank=bank, wT_tok=wT_tok, prev=prev):
                    pr.wait(t, "t", wT_tok, spar_tok)
                    pr.wait(t, "t", prev)
                    last = None
                    for jt in range(isub + 1):
                        last = t.matmul(bank[:, :], wTb[:, jt * 128:(jt + 1) * 128],
                                        vtl[jt][:, h * 64:(h + 1) * 64],
                                        start=(jt == 0), stop=(jt == isub))
                    pr.inc(last, "t")
                pr.blk(t=mmav)
                av_tok = pr.tok("t")

                def evo(v, h=h, isub=isub, bank=bank, av_tok=av_tok):
                    pr.wait(v, "v", av_tok)
                    pr.inc(v.tensor_scalar(att[isub][:, h * 64:(h + 1) * 64],
                                           bank[:, :], cols[:, 9:10], None,
                                           ALU.mult), "v")
                pr.blk(v=evo)
                prev_tile[key] = pr.tok("v")
        att_tok = pr.tok("v")

        def attT_of(c, i):
            return ktl[2 * c + (1 if i >= 8 else 0)][:, (i % 8) * 128:(i % 8 + 1) * 128]

        trp = {}
        for i in range(NT):
            for c in range(8):
                def trat(t, i=i, c=c):
                    pr.wait(t, "t", att_tok, pr.ident_tok)
                    pr.wait(t, "t", trp.get((i, c - 1), trp.get((i - 1, 7))))
                    pr.inc(t.transpose(psT[:, 0:128], att[i][:, c * 128:(c + 1) * 128],
                                       pr.ident[:, :]), "t")
                pr.blk(t=trat)
                tt = pr.tok("t")

                def evat(s, i=i, c=c, tt=tt):
                    pr.wait(s, "s", tt)
                    pr.inc(s.copy(attT_of(c, i), psT[:, 0:128]), "s")
                pr.blk(s=evat)
                trp[(i, c)] = pr.tok("s")
        attT_tok = pr.tok("s")
        att_ps.close()

        with ExitStack() as fs:
            wob = fs.enter_context(nc.sbuf_tensor("l1b_wob", [P, 8 * 1024], BF16))
            fin = fs.enter_context(nc.sbuf_tensor("l1b_fin", [P, 1024], F32))
            psF = [fs.enter_context(nc.psum_tensor(f"l1b_psF{j}", [P, 1024], F32))
                   for j in range(2)]

            def ldwo(g):
                pr.wait(g, "g", att_tok)
                for c in range(8):
                    pr.inc(g.dma_start(out=wob[:, c * 1024:(c + 1) * 1024],
                                       in_=w["woT"][c]), "gd")
            pr.blk(g=ldwo)
            wo_tok = pr.tok("gd")

            out_toks = [None] * NT
            prevf = {}
            for i in range(NT):
                bank = psF[i % 2]

                def mmo(t, i=i, bank=bank):
                    pr.wait(t, "t", attT_tok, wo_tok)
                    pr.wait(t, "t", prevf.get(i - 2))
                    last = None
                    for mh in range(2):
                        for c in range(8):
                            last = t.matmul(bank[:, mh * 512:(mh + 1) * 512],
                                            attT_of(c, i),
                                            wob[:, c * 1024 + mh * 512:c * 1024 + (mh + 1) * 512],
                                            start=(c == 0), stop=(c == 7))
                    pr.inc(last, "t")
                pr.blk(t=mmo)
                mmtok = pr.tok("t")
                out_toks[i] = xio.add_store(i, bank[:, :], mmtok)
                prevf[i] = out_toks[i]
        return out_toks



SEQ = [
    ("L0_attn_a",), ("L0_ffn",), ("L0_attn_b",), ("L0_ffn",),
    ("L1_attn_a",), ("L1_ffn",), ("L1_attn_b",), ("L1_ffn",),
    ("L2_attn_a",), ("L2_ffn",), ("L2_attn_b",), ("L2_ffn",),
    ("L3_attn_a",), ("L3_ffn",), ("L3_attn_b",), ("L3_ffn",),
]

_DECLS = {
    "L0_attn_a": decl_L0_attn_a, "L0_ffn": decl_L0_ffn, "L0_attn_b": decl_L0_attn_b,
    "L1_attn_a": decl_L1_attn_a, "L1_ffn": decl_L1_ffn, "L1_attn_b": decl_L1_attn_b,
    "L2_attn_a": decl_L2_attn_a, "L2_ffn": decl_L2_ffn, "L2_attn_b": decl_L2_attn_b,
    "L3_attn_a": decl_L3_attn_a, "L3_ffn": decl_L3_ffn, "L3_attn_b": decl_L3_attn_b,
}


def stage_all(inputs, pre):
    stage_L0_attn_a(inputs, pre)
    stage_L0_ffn(inputs, pre)
    stage_L0_attn_b(inputs, pre)
    stage_L1_attn_a(inputs, pre)
    stage_L1_ffn(inputs, pre)
    stage_L1_attn_b(inputs, pre)
    stage_L2_attn_a(inputs, pre)
    stage_L2_ffn(inputs, pre)
    stage_L2_attn_b(inputs, pre)
    stage_L3_attn_a(inputs, pre)
    stage_L3_attn_b(inputs, pre)
    stage_L3_ffn(inputs, pre)
    pre["L3_ffn_input_scale"] = np.asarray(inputs["L3_ffn_input_scale"], np.float32).reshape(1, 1)
    pre["L3_ffn_kernel_scale"] = np.asarray(inputs["L3_ffn_kernel_scale"], np.float32).reshape(1, 1)
    pre["L3_ffn_bias"] = np.asarray(inputs["L3_ffn_bias"]).reshape(1, 1024)


def build_sub(name):
    import concourse.bass as bass
    nc = bass.Bass("TRN2", target_bir_lowering=False)
    x_in = nc.dram_tensor("x_in", [TT, D], F32, kind="ExternalInput").ap()
    x_out = nc.dram_tensor("x_out", [TT, D], F32, kind="ExternalOutput").ap()
    w = _DECLS[name](nc)
    pr = Prog(nc)
    with ExitStack() as stack:
        pr.open_sems(stack)
        setup_consts(pr, stack)
        setup_identity(pr, stack)
        setup_ones(pr, stack)
        xio = XIOUnfused(pr, stack, x_in, x_out)
        if name == "L2_attn_b":
            out_toks = emit_L2_attn_b(pr, stack, xio, w, rope_tables_L2(), 0)
        elif name == "L1_attn_b":
            out_toks = emit_L1_attn_b(pr, stack, xio, w, 0)
        elif name == "L2_ffn":
            lnT = [stack.enter_context(nc.sbuf_tensor(f"lnT{c}", [P, TT], BF16))
                   for c in range(NF)]
            ln = [stack.enter_context(nc.sbuf_tensor(f"ln{i}", [P, D], BF16))
                  for i in range(NT)]
            with ExitStack() as lnstack:
                ln_toks = emit_ln_from_xio(pr, lnstack, xio, ln)
                lnT_tok = emit_transpose_fm(pr, lnstack, ln, ln_toks, lnT)
            out_toks = emit_L2_ffn(pr, stack, xio, lnT, lnT_tok, ln, ln_toks, w, 0)
        else:
            lnT = [stack.enter_context(nc.sbuf_tensor(f"lnT{c}", [P, TT], BF16))
                   for c in range(NF)]
            with ExitStack() as lnstack:
                ln = [lnstack.enter_context(nc.sbuf_tensor(f"ln{i}", [P, D], BF16))
                      for i in range(NT)]
                ln_toks = emit_ln_from_xio(pr, lnstack, xio, ln)
                lnT_tok = emit_transpose_fm(pr, lnstack, ln, ln_toks, lnT)
            if name == "L0_attn_a":
                out_toks = emit_L0_attn_a(pr, stack, xio, lnT, lnT_tok, w)
            elif name == "L0_ffn":
                out_toks = emit_L0_ffn(pr, stack, xio, lnT, lnT_tok, w, 0)
            elif name == "L0_attn_b":
                out_toks = emit_L0_attn_b(pr, stack, xio, lnT, lnT_tok, w)
            elif name == "L1_attn_a":
                out_toks = emit_L1_attn_a(pr, stack, xio, lnT, lnT_tok, w,
                                          rope_tables_L1())
            elif name == "L1_ffn":
                out_toks = emit_L1_ffn(pr, stack, xio, lnT, lnT_tok, w)
            elif name == "L2_attn_a":
                out_toks = emit_L2_attn_a(pr, stack, xio, lnT, lnT_tok, w)
            elif name == "L3_attn_a":
                out_toks = emit_L3_attn_a(pr, stack, xio, lnT, lnT_tok, w)
            elif name == "L3_attn_b":
                out_toks = emit_L3_attn_b(pr, stack, xio, lnT, lnT_tok, w)
            elif name == "L3_ffn":
                out_toks = emit_L3_ffn(pr, stack, xio, lnT, lnT_tok, w)
            else:
                raise KeyError(name)

        def fin(q):
            pr.wait(q, "q", *out_toks)
        pr.blk(q=fin)
    innames = set()
    for alloc in nc.m.functions[0].allocations:
        try:
            if alloc.kind == "ExternalInput":
                nm = alloc.memorylocations[0].name
                if nm != "x_in" and "partition_id" not in nm and "dbg_addr" not in nm:
                    innames.add(nm)
        except AttributeError:
            pass
    return nc, sorted(innames)


_PROGS = {}


def _get_prog(name):
    if name not in _PROGS:
        _PROGS[name] = build_sub(name)
    return _PROGS[name]


def kernel(**inputs):
    from concourse.bass_utils import run_bass_kernel_spmd
    pre = {}
    stage_all(inputs, pre)
    x0 = np.ascontiguousarray(np.asarray(inputs["L0_attn_a_hidden_states"],
                                         np.float32)).reshape(8, TT, D)
    shards = [np.ascontiguousarray(x0[c]) for c in range(8)]
    for (name,) in SEQ:
        nc, innames = _get_prog(name)
        in_maps = []
        for c in range(8):
            im = {"x_in": shards[c]}
            for k in innames:
                im[k] = pre[k]
            in_maps.append(im)
        res = run_bass_kernel_spmd(nc, in_maps, core_ids=list(range(8)))
        shards = [np.asarray(res.results[c]["x_out"], np.float32) for c in range(8)]
    return np.stack(shards, 0).astype(np.float32)


# version 04 (final)
# speedup vs baseline: 1.0099x; 1.0099x over previous
import sys
import resource
try:
    resource.setrlimit(resource.RLIMIT_STACK,
                       (resource.RLIM_INFINITY, resource.RLIM_INFINITY))
except (ValueError, OSError):
    try:
        resource.setrlimit(resource.RLIMIT_STACK, (1 << 30, 1 << 30))
    except (ValueError, OSError):
        pass
for _p in ("/opt/trn_rl_repo",):
    if _p not in sys.path:
        sys.path.insert(0, _p)
"""Framework for raw-Bass sublayer emitters (mk_hybrid_d4).

Layout conventions:
- Residual x: token-major fp32, NT tiles [128, D] (tile i = tokens i*128..+128).
- ln: bf16 token-major tiles; lnT: bf16 feature-major tiles [128, T] (chunk c =
  features c*128..+128).
- Weights are staged in DRAM pre-transposed host-side so the device rhs is
  [feature-chunk 128 x out_dim], contraction dim on partitions.

Sync model:
- Per-engine semaphores: t, v, s, g (compute, +1 per instruction we track),
  qd, gd, sd (DMA completions on sync/gpsimd/scalar queues, +16), cc (+1).
- Builder tracks Python-side counts; barrier() makes every engine wait for
  every other sem's current count at its next emitted instruction.
"""
import numpy as np
import concourse.bass as bass
import concourse.mybir as mybir

F32 = mybir.dt.float32
BF16 = mybir.dt.bfloat16
I32 = mybir.dt.int32
U32 = mybir.dt.uint32
FP8 = mybir.dt.float8e4
AF = mybir.ActivationFunctionType
ALU = mybir.AluOpType
AX = mybir.AxisListType

P = 128
D = 1024
NT = 16
TT = 2048
NF = 8

ENGS = ("t", "v", "s", "g", "q")
SEMS = ("t", "v", "s", "g", "qd", "gd", "sd", "cc")


class Prog:
    def __init__(self, nc):
        self.nc = nc
        self.count = {s: 0 for s in SEMS}
        self.waited = {}
        self.sems = {}
        self._exit = []

    def open_sems(self, stack):
        for s in SEMS:
            self.sems[s] = stack.enter_context(self.nc.semaphore("sem_" + s))

    def inc(self, inst, sem, n=None):
        if n is None:
            n = 16 if sem in ("qd", "gd", "sd") else 1
        inst.then_inc(self.sems[sem], n)
        self.count[sem] += n
        return (sem, self.count[sem])

    def tok(self, sem):
        return (sem, self.count[sem])

    def wait(self, eng, engname, *toks):
        best = {}
        for t in toks:
            if t is None:
                continue
            s, v = t
            if v > best.get(s, -1):
                best[s] = v
        for s, v in sorted(best.items()):
            key = (engname, s)
            if self.waited.get(key, -1) >= v:
                continue
            eng.wait_ge(self.sems[s], v)
            self.waited[key] = v

    def barrier_tokens(self):
        return [(s, self.count[s]) for s in SEMS]

    def blk(self, **bodies):
        nc = self.nc
        with nc.Block() as block:
            if "q" in bodies:
                @block.sync
                def _(sync):
                    bodies["q"](sync)
            if "g" in bodies:
                @block.gpsimd
                def _(g):
                    bodies["g"](g)
            if "t" in bodies:
                @block.tensor
                def _(t):
                    bodies["t"](t)
            if "s" in bodies:
                @block.scalar
                def _(s):
                    bodies["s"](s)
            if "v" in bodies:
                @block.vector
                def _(v):
                    bodies["v"](v)


def np_dt(dtype):
    return mybir.dt.np(dtype)



def emit_layernorm(pr, x_tiles, x_toks, ln_tiles, sq_scratch, stat, eps=1e-5,
                   out_extra=None, gamma=None, beta=None):
    nc = pr.nc
    out_toks = [None] * len(x_tiles)

    for i, (xt, xtok) in enumerate(zip(x_tiles, x_toks)):
        def body_v(v, i=i, xt=xt, xtok=xtok):
            pr.wait(v, "v", xtok)
            t0 = pr.inc(v.reduce_sum(stat[:, 0:1], xt[:, :], AX.X), "v")
        def body_s(s, i=i, xt=xt, xtok=xtok):
            pr.wait(s, "s", xtok)
            pr.inc(s.activation(sq_scratch[:, :], xt[:, :], AF.Square,
                                accum_out=stat[:, 1:2]), "s")
        pr.blk(v=body_v, s=body_s)
        vtok = pr.tok("v")
        stok = pr.tok("s")

        def body_v2(v, i=i):
            pr.wait(v, "v", vtok, stok)
            pr.inc(v.tensor_scalar(stat[:, 2:4], stat[:, 0:2], 1.0 / D, None, ALU.mult), "v")
            vt = pr.tok("v")
            pr.wait(v, "v", vt)
            pr.inc(v.tensor_tensor(stat[:, 5:6], stat[:, 2:3], stat[:, 2:3], ALU.mult), "v")
            vt = pr.tok("v")
            pr.wait(v, "v", vt)
            pr.inc(v.tensor_tensor(stat[:, 4:5], stat[:, 3:4], stat[:, 5:6], ALU.subtract), "v")
        pr.blk(v=body_v2)
        vtok2 = pr.tok("v")

        def body_s2(s, i=i, xt=xt):
            pr.wait(s, "s", vtok2)
            pr.inc(s.activation(stat[:, 5:6], stat[:, 4:5], AF.Ln, bias=eps), "s")
            st = pr.tok("s")
            pr.wait(s, "s", st)
            pr.inc(s.activation(stat[:, 6:7], stat[:, 5:6], AF.Exp, scale=-0.5), "s")
            st = pr.tok("s")
            pr.wait(s, "s", st)
        pr.blk(s=body_s2)
        stok2 = pr.tok("s")

        def body_v3(v, i=i):
            pr.wait(v, "v", stok2)
            pr.inc(v.tensor_tensor(stat[:, 7:8], stat[:, 2:3], stat[:, 6:7], ALU.mult), "v")
            vt = pr.tok("v")
            pr.wait(v, "v", vt)
            pr.inc(v.tensor_scalar(stat[:, 7:8], stat[:, 7:8], -1.0, None, ALU.mult), "v")
        pr.blk(v=body_v3)
        vtok3 = pr.tok("v")

        def body_s3(s, i=i, xt=xt):
            pr.wait(s, "s", vtok3)
            pr.inc(s.activation(ln_tiles[i][:, :], xt[:, :], AF.Copy,
                                bias=stat[:, 7:8], scale=stat[:, 6:7]), "s")
            if out_extra is not None:
                st = pr.tok("s")
                pr.wait(s, "s", st)
                pr.inc(s.activation(out_extra[i][:, :], xt[:, :], AF.Copy,
                                    bias=stat[:, 7:8], scale=stat[:, 6:7]), "s")
        pr.blk(s=body_s3)
        out_toks[i] = pr.tok("s")
    return out_toks


def emit_transpose_fm(pr, src_tiles, src_toks, dst_tiles, ident_bf16, psA, psB,
                      n_src=NT, n_chunk=NF, src_col0=0):
    nc = pr.nc
    ps = [psA, psB]
    groups = []
    for c in range(n_chunk):
        for i0 in range(0, n_src, 4):
            groups.append((c, i0))
    for gi, (c, i0) in enumerate(groups):
        bank = ps[gi % 2]
        n_in_group = min(4, n_src - i0)

        def body_t(t, c=c, i0=i0, bank=bank, n_in_group=n_in_group, gi=gi):
            if gi >= 2:
                pr.wait(t, "t", evac_toks[gi - 2])
            for j in range(n_in_group):
                pr.wait(t, "t", src_toks[i0 + j])
                pr.inc(t.transpose(bank[:, j * 128:(j + 1) * 128],
                                   src_tiles[i0 + j][:, src_col0 + c * 128:src_col0 + (c + 1) * 128],
                                   ident_bf16[:, :]), "t")

        pr.blk(t=body_t)
        ttok = pr.tok("t")

        def body_s(s, c=c, i0=i0, bank=bank, n_in_group=n_in_group, ttok=ttok):
            pr.wait(s, "s", ttok)
            pr.inc(s.copy(dst_tiles[c][:, i0 * 128:(i0 + n_in_group) * 128],
                          bank[:, 0:n_in_group * 128]), "s")

        pr.blk(s=body_s)
        if gi == 0:
            evac_toks = []
        evac_toks.append(pr.tok("s"))
    return pr.tok("s")


def emit_matmul(pr, lhsT_tiles, lhsT_tok, rhs_chunks, rhs_toks, psum_pair,
                out_writer, n_k, out_dim, free_per_mm=512, t_tiles=NT,
                lhsT_of=None):
    nc = pr.nc
    nmm = (out_dim + free_per_mm - 1) // free_per_mm
    evac_toks = [None, None]
    for i in range(t_tiles):
        bank = psum_pair[i % 2]

        def body_t(t, i=i, bank=bank):
            pr.wait(t, "t", lhsT_tok)
            pr.wait(t, "t", *rhs_toks)
            if evac_toks[i % 2] is not None:
                pr.wait(t, "t", evac_toks[i % 2])
            for m in range(nmm):
                o0 = m * free_per_mm
                o1 = min(out_dim, o0 + free_per_mm)
                for c in range(n_k):
                    lhs_ap = (lhsT_of(c, i) if lhsT_of is not None
                              else lhsT_tiles[c][:, i * 128:(i + 1) * 128])
                    t.matmul(bank[:, o0:o1], lhs_ap, rhs_chunks[c][:, o0:o1],
                             start=(c == 0), stop=(c == n_k - 1))
            pr.inc(t.engine_nop(), "t")

        pr.blk(t=body_t)
        mmtok = pr.tok("t")
        evac_toks[i % 2] = out_writer(i, bank, mmtok)
    return evac_toks

"""Sublayer emitters + single-sublayer test driver.

Each sublayer computes y = x + f(LN(x)) for this core's [2048, 1024] slice.
x lives in DRAM between sublayers in unfused mode (XIO handles get/add_store).
"""
import numpy as np
from contextlib import ExitStack

import concourse.bass as bass
import concourse.mybir as mybir
from concourse.bass_utils import run_bass_kernel_spmd

CONST_VALUES = [1e-5, 1e-6, 1.0, -1.0, 0.5, -0.5, 1.0 / 127.0, 127.0,
                1e30, -1e30, 0.0, 2048.0, 240.0, 1e9, 20.0, -20.0]


def setup_consts(pr, stack):
    nc = pr.nc
    arr = np.tile(np.asarray(CONST_VALUES, np.float32)[None, :], (P, 1))
    dram = nc.inline_tensor(arr, "constpool")
    pool = stack.enter_context(nc.sbuf_tensor("constpool_sb", [P, len(CONST_VALUES)], F32))
    def body(q):
        pr.inc(q.dma_start(out=pool[:, :], in_=dram.ap()), "qd")
    pr.blk(q=body)
    pr.const_pool = pool
    pr.const_tok = pr.tok("qd")
    pr.const_idx = {v: i for i, v in enumerate(CONST_VALUES)}
    for v, i in pr.const_idx.items():
        nc.const_aps.register(F32, v, pool[:, i:i + 1])


def cval(pr, v):
    return pr.const_pool[:, pr.const_idx[v]:pr.const_idx[v] + 1]


def setup_identity(pr, stack):
    nc = pr.nc
    import ml_dtypes
    ident_np = np.eye(P, dtype=ml_dtypes.bfloat16)
    dram = nc.inline_tensor(ident_np, "identconst")
    ident = stack.enter_context(nc.sbuf_tensor("ident_sb", [P, P], BF16))
    identf_np = np.eye(P, dtype=np.float32)
    dramf = nc.inline_tensor(identf_np, "identconstf")
    identf = stack.enter_context(nc.sbuf_tensor("identf_sb", [P, P], F32))
    def body(q):
        pr.inc(q.dma_start(out=ident[:, :], in_=dram.ap()), "qd")
        pr.inc(q.dma_start(out=identf[:, :], in_=dramf.ap()), "qd")
    pr.blk(q=body)
    pr.ident = ident
    pr.identf = identf
    pr.ident_tok = pr.tok("qd")


class XIOUnfused:

    def __init__(self, pr, stack, x_in_ap, x_out_ap):
        self.pr = pr
        self.x_in = x_in_ap
        self.x_out = x_out_ap
        nc = pr.nc
        self.tmp = [stack.enter_context(nc.sbuf_tensor(f"xio_tmp{j}", [P, D], F32))
                    for j in range(2)]
        self.tmp_tok = [None] * 2
        self.tmp_load = [None] * 2
        self.rr = 0

    def get(self, i):
        pr = self.pr
        slot = self.rr % 2
        self.rr += 1
        war = self.tmp_tok[slot]

        def body(q, i=i, slot=slot, war=war):
            pr.wait(q, "q", war)
            pr.inc(q.dma_start(out=self.tmp[slot][:, :],
                               in_=self.x_in[i * 128:(i + 1) * 128, :]), "qd")
        pr.blk(q=body)
        self.tmp_load[slot] = pr.tok("qd")
        return self.tmp[slot], self.tmp_load[slot], slot

    def done(self, slot, tok):
        self.tmp_tok[slot] = tok

    def add_store(self, i, src_ap, src_tok, scale=None, scale_ap=None):
        pr = self.pr
        xap, ltok, slot = self.get(i)

        def body_v(v, i=i, xap=xap, ltok=ltok):
            pr.wait(v, "v", ltok, src_tok)
            if scale_ap is not None:
                pr.inc(v.scalar_tensor_tensor(xap[:, :], src_ap, scale_ap, xap[:, :],
                                              ALU.mult, ALU.add), "v")
            elif scale is not None:
                pr.inc(v.scalar_tensor_tensor(xap[:, :], src_ap, float(scale), xap[:, :],
                                              ALU.mult, ALU.add), "v")
            else:
                pr.inc(v.tensor_tensor(xap[:, :], xap[:, :], src_ap, ALU.add), "v")
        pr.blk(v=body_v)
        vtok = pr.tok("v")

        def body_q(q, i=i, xap=xap, vtok=vtok):
            pr.wait(q, "q", vtok)
            pr.inc(q.dma_start(out=self.x_out[i * 128:(i + 1) * 128, :],
                               in_=xap[:, :]), "qd")
        pr.blk(q=body_q)
        stok = pr.tok("qd")
        self.done(slot, stok)
        return stok


def emit_ln_from_xio(pr, stack, xio, ln_tiles, n_tiles=NT, eps=1e-5,
                     post_tile=None, tile_war=None):
    nc = pr.nc
    sq = stack.enter_context(nc.sbuf_tensor("ln_sq", [P, D], F32))
    stat = stack.enter_context(nc.sbuf_tensor("ln_stat", [P, 8 * n_tiles], F32))
    out_toks = []
    for i in range(n_tiles):
        xt, xtok, slot = xio.get(i)
        st = stat[:, 8 * i:8 * i + 8]

        def b1(v, xt=xt, xtok=xtok, st=st):
            pr.wait(v, "v", xtok)
            pr.inc(v.reduce_sum(st[:, 0:1], xt[:, :], AX.X), "v")
        def b1s(s, xt=xt, xtok=xtok, st=st, i=i):
            pr.wait(s, "s", xtok)
            if i > 0:
                pr.wait(s, "s", sq_done[0])
            pr.inc(s.activation(sq[:, :], xt[:, :], AF.Square, accum_out=st[:, 1:2]), "s")
        pr.blk(v=b1, s=b1s)
        sq_done = [pr.tok("s")]
        vtok, stok = pr.tok("v"), pr.tok("s")

        def b2(v, st=st, vtok=vtok, stok=stok):
            pr.wait(v, "v", vtok, stok)
            pr.inc(v.tensor_scalar(st[:, 2:4], st[:, 0:2], 1.0 / D, None, ALU.mult), "v")
            pr.wait(v, "v", pr.tok("v"))
            pr.inc(v.tensor_tensor(st[:, 4:5], st[:, 2:3], st[:, 2:3], ALU.mult), "v")
            pr.wait(v, "v", pr.tok("v"))
            pr.inc(v.tensor_tensor(st[:, 4:5], st[:, 3:4], st[:, 4:5], ALU.subtract), "v")
        pr.blk(v=b2)
        vtok2 = pr.tok("v")

        def b3(s, st=st, vtok2=vtok2):
            pr.wait(s, "s", vtok2, pr.const_tok)
            pr.inc(s.activation(st[:, 5:6], st[:, 4:5], AF.Ln, bias=cval(pr, eps)), "s")
            pr.wait(s, "s", pr.tok("s"))
            pr.inc(s.activation(st[:, 6:7], st[:, 5:6], AF.Exp, scale=-0.5), "s")
        pr.blk(s=b3)
        stok2 = pr.tok("s")

        def b4(v, st=st, stok2=stok2):
            pr.wait(v, "v", stok2)
            pr.inc(v.scalar_tensor_tensor(st[:, 7:8], st[:, 2:3], -1.0, st[:, 6:7],
                                          ALU.mult, ALU.mult), "v")
        pr.blk(v=b4)
        vtok3 = pr.tok("v")

        def b5(v, xt=xt, st=st, vtok3=vtok3, i=i):
            pr.wait(v, "v", vtok3)
            if tile_war is not None:
                pr.wait(v, "v", tile_war(i))
            pr.inc(v.tensor_scalar(ln_tiles[i][:, :], xt[:, :], st[:, 6:7],
                                   st[:, 7:8], ALU.mult, ALU.add), "v")
        pr.blk(v=b5)
        tok = pr.tok("v")
        if post_tile is not None:
            tok = post_tile(i, tok)
        out_toks.append(tok)
        xio.done(slot, tok)
    return out_toks


def emit_transpose_fm(pr, stack, src_tiles, src_toks, dst_tiles, n_src=NT,
                      n_chunk=NF, src_col0=0, dst_row0=0, tag="tr", dtype=BF16):
    nc = pr.nc
    psA = stack.enter_context(nc.psum_tensor(tag + "_psA", [P, 512], dtype))
    psB = stack.enter_context(nc.psum_tensor(tag + "_psB", [P, 512], dtype))
    ps = [psA, psB]
    evac_toks = []
    groups = [(c, i0) for c in range(n_chunk) for i0 in range(0, n_src, 4)]
    for gi, (c, i0) in enumerate(groups):
        bank = ps[gi % 2]
        n_in = min(4, n_src - i0)

        def body_t(t, c=c, i0=i0, bank=bank, n_in=n_in, gi=gi):
            pr.wait(t, "t", pr.ident_tok)
            if gi >= 2:
                pr.wait(t, "t", evac_toks[gi - 2])
            last = None
            idn = pr.ident if dtype == BF16 else pr.identf
            for j in range(n_in):
                pr.wait(t, "t", src_toks[i0 + j])
                last = t.transpose(bank[:, j * 128:(j + 1) * 128],
                                   src_tiles[i0 + j][:, src_col0 + c * 128:src_col0 + (c + 1) * 128],
                                   idn[:, :])
            pr.inc(last, "t")
        pr.blk(t=body_t)
        ttok = pr.tok("t")

        def body_s(s, c=c, i0=i0, bank=bank, n_in=n_in, ttok=ttok):
            pr.wait(s, "s", ttok)
            pr.inc(s.copy(dst_tiles[c][:, dst_row0 + i0 * 128:dst_row0 + (i0 + n_in) * 128],
                          bank[:, 0:n_in * 128]), "s")
        pr.blk(s=body_s)
        evac_toks.append(pr.tok("s"))
    return pr.tok("s")



def stage_L1_ffn(inputs, pre):
    wg, wu, wd = inputs["L1_ffn_w_gate"], inputs["L1_ffn_w_up"], inputs["L1_ffn_w_down"]
    pre["L1_ffn_wgT"] = np.ascontiguousarray(
        wg.reshape(24, 128, 8, 128).transpose(0, 3, 2, 1)).reshape(24, 128, 1024)
    pre["L1_ffn_wuT"] = np.ascontiguousarray(
        wu.reshape(24, 128, 8, 128).transpose(0, 3, 2, 1)).reshape(24, 128, 1024)
    pre["L1_ffn_wdT"] = np.ascontiguousarray(wd.T).reshape(24, 128, 1024)


def decl_L1_ffn(nc):
    return {
        "wgT": nc.dram_tensor("L1_ffn_wgT", [24, 1024, 128], I32, kind="ExternalInput").ap(),
        "wuT": nc.dram_tensor("L1_ffn_wuT", [24, 1024, 128], I32, kind="ExternalInput").ap(),
        "wdT": nc.dram_tensor("L1_ffn_wdT", [24, 128, 1024], I32, kind="ExternalInput").ap(),
    }


def emit_L1_ffn(pr, stack, xio, lnT_tiles, lnT_tok, w):
    nc = pr.nc
    HO = 24
    ctx = stack.enter_context
    with ExitStack() as ls:
        lctx = ls.enter_context
        hT = [lctx(nc.sbuf_tensor(f"l1f_hT{o}", [P, TT], BF16)) for o in range(HO)]
        wg_b = lctx(nc.sbuf_tensor("l1f_wgb", [P, 2 * 1024], BF16))
        wu_b = lctx(nc.sbuf_tensor("l1f_wub", [P, 2 * 1024], BF16))
        wd_b = [lctx(nc.sbuf_tensor(f"l1f_wdb{o}", [P, 1024], BF16)) for o in range(HO)]
        sg = lctx(nc.sbuf_tensor("l1f_sg", [P, 512], F32))
        psA = lctx(nc.psum_tensor("l1f_psA", [P, 512], F32))
        psB = lctx(nc.psum_tensor("l1f_psB", [P, 512], F32))
        psC = lctx(nc.psum_tensor("l1f_psC", [P, 1024], F32))
        psD = lctx(nc.psum_tensor("l1f_psD", [P, 1024], F32))

        def ldd(g):
            for o in range(HO):
                pr.inc(g.dma_start(out=wd_b[o][:, :], in_=w["wdT"][o]), "gd")
        pr.blk(g=ldd)
        wd_conv = [pr.tok("gd")]

        silu_done = {}
        mul_done = {}
        mm_done = {}
        wgu_conv = [None] * HO
        for o in range(HO):
            buf = o % 2
            wslc = slice(buf * 1024, (buf + 1) * 1024)

            def ldw(g, o=o, wslc=wslc):
                if o >= 2:
                    pr.wait(g, "g", mm_done.get(o - 2))
                pr.inc(g.dma_start(out=wg_b[:, wslc], in_=w["wgT"][o]), "gd")
                pr.inc(g.dma_start(out=wu_b[:, wslc], in_=w["wuT"][o]), "gd")
            pr.blk(g=ldw)
            wgu_conv[o] = pr.tok("gd")

            for tg in range(4):
                tsl = slice(tg * 512, (tg + 1) * 512)

                def mm(t, o=o, tg=tg, tsl=tsl, wslc=wslc):
                    pr.wait(t, "t", wgu_conv[o], lnT_tok)
                    pr.wait(t, "t", silu_done.get((o - 1, tg)))
                    pr.wait(t, "t", mul_done.get((o - 1, tg)))
                    last = None
                    for c in range(8):
                        last = t.matmul(psA[:, :], wg_b[:, wslc][:, c * 128:(c + 1) * 128],
                                        lnT_tiles[c][:, tsl], start=(c == 0), stop=(c == 7))
                    for c in range(8):
                        last = t.matmul(psB[:, :], wu_b[:, wslc][:, c * 128:(c + 1) * 128],
                                        lnT_tiles[c][:, tsl], start=(c == 0), stop=(c == 7))
                    pr.inc(last, "t")
                pr.blk(t=mm)
                mmtok = pr.tok("t")

                def ev1(s, o=o, tg=tg, mmtok=mmtok):
                    pr.wait(s, "s", mmtok)
                    pr.inc(s.activation(sg[:, :], psA[:, :], AF.Silu, scale=1.0 / 127.0), "s")
                pr.blk(s=ev1)
                silu_done[(o, tg)] = pr.tok("s")

                def ev2(v, o=o, tg=tg, tsl=tsl):
                    pr.wait(v, "v", silu_done[(o, tg)])
                    pr.inc(v.scalar_tensor_tensor(hT[o][:, tsl], psB[:, :], 1.0 / 127.0,
                                                  sg[:, :], ALU.mult, ALU.mult), "v")
                pr.blk(v=ev2)
                mul_done[(o, tg)] = pr.tok("v")
            mm_done[o] = pr.tok("v")

        hT_tok = pr.tok("v")

        out_toks = [None] * NT
        ev_done = {}
        for i in range(NT):
            bank = psC if i % 2 == 0 else psD

            def mmd(t, i=i, bank=bank):
                pr.wait(t, "t", hT_tok, *wd_conv)
                pr.wait(t, "t", ev_done.get(i - 2))
                last = None
                for o in range(HO):
                    lhs = hT[o][:, i * 128:(i + 1) * 128]
                    last = t.matmul(bank[:, 0:512], lhs, wd_b[o][:, 0:512],
                                    start=(o == 0), stop=(o == HO - 1))
                for o in range(HO):
                    lhs = hT[o][:, i * 128:(i + 1) * 128]
                    last = t.matmul(bank[:, 512:1024], lhs, wd_b[o][:, 512:1024],
                                    start=(o == 0), stop=(o == HO - 1))
                pr.inc(last, "t")
            pr.blk(t=mmd)
            mmtok = pr.tok("t")
            out_toks[i] = xio.add_store(i, bank[:, :], mmtok, scale=1.0 / 127.0)
            ev_done[i] = out_toks[i]
    return out_toks


def setup_ones(pr, stack):
    nc = pr.nc
    import ml_dtypes
    arr = np.ones((P, 128), np.float32)
    dram = nc.inline_tensor(arr, "onesf32")
    arrb = np.ones((P, 128), ml_dtypes.bfloat16)
    dramb = nc.inline_tensor(arrb, "onesbf16")
    onf = stack.enter_context(nc.sbuf_tensor("ones_f32", [P, 128], F32))
    onb = stack.enter_context(nc.sbuf_tensor("ones_bf16", [P, 128], BF16))
    def body(q):
        pr.inc(q.dma_start(out=onf[:, :], in_=dram.ap()), "qd")
        pr.inc(q.dma_start(out=onb[:, :], in_=dramb.ap()), "qd")
    pr.blk(q=body)
    pr.ones_f32 = onf
    pr.ones_bf16 = onb
    pr.ones_tok = pr.tok("qd")


def emit_row_broadcast(pr, ps_out, row_ap, row_tok, width, tag="rb"):
    def body(t):
        pr.wait(t, "t", row_tok, pr.ones_tok)
        last = None
        dt_is_bf16 = row_ap.dtype == BF16
        ones = pr.ones_bf16 if dt_is_bf16 else pr.ones_f32
        for o0 in range(0, width, 512):
            o1 = min(width, o0 + 512)
            last = t.matmul(ps_out[:, o0:o1], ones[0:1, 0:128],
                            row_ap[0:1, o0:o1], start=True, stop=True)
        pr.inc(last, "t")
    pr.blk(t=body)
    return pr.tok("t")



def stage_L3_ffn(inputs, pre):
    k = inputs["L3_ffn_kernel"]
    pre["L3_ffn_kernel_r"] = np.ascontiguousarray(np.asarray(k).reshape(8, 128, 1024))


def decl_L3_ffn(nc):
    return {
        "kernel": nc.dram_tensor("L3_ffn_kernel_r", [8, 128, 1024], BF16, kind="ExternalInput").ap(),
        "s_in": nc.dram_tensor("L3_ffn_input_scale", [1, 1], F32, kind="ExternalInput").ap(),
        "s_k": nc.dram_tensor("L3_ffn_kernel_scale", [1, 1], F32, kind="ExternalInput").ap(),
        "bias": nc.dram_tensor("L3_ffn_bias", [1, 1024], BF16, kind="ExternalInput").ap(),
    }


def emit_fp8_quant(pr, dst_bf16, src_f32, src_tok, f8a, f8b, sc_f32, mask_t):
    def body(v):
        pr.wait(v, "v", src_tok)
        pr.inc(v.tensor_copy(f8a[:, :], src_f32[:, :]), "v")
        pr.inc(v.tensor_tensor(sc_f32[:, :], src_f32[:, :], src_f32[:, :], ALU.mult), "v")
        pr.wait(v, "v", pr.tok("v"))
        pr.inc(v.tensor_scalar(mask_t[:, :], sc_f32[:, :], 61504.0, None, ALU.is_ge), "v")
        pr.inc(v.tensor_scalar(sc_f32[:, :], src_f32[:, :], 0.5, None, ALU.mult), "v")
        pr.wait(v, "v", pr.tok("v"))
        pr.inc(v.tensor_copy(f8b[:, :], sc_f32[:, :]), "v")
        pr.wait(v, "v", pr.tok("v"))
        pr.inc(v.tensor_scalar(sc_f32[:, :], f8b[:, :], 2.0, None, ALU.mult), "v")
        pr.wait(v, "v", pr.tok("v"))
        pr.inc(v.select(dst_bf16[:, :], mask_t[:, :], sc_f32[:, :], f8a[:, :]), "v")
        pr.wait(v, "v", pr.tok("v"))
    pr.blk(v=body)
    return pr.tok("v")


def emit_scalar_bcast(pr, ps, col_out, src_11, src_tok):
    def body_t(t):
        pr.wait(t, "t", src_tok, pr.ones_tok)
        pr.inc(t.matmul(ps[:, 0:1], pr.ones_f32[0:1, 0:128], src_11[0:1, 0:1],
                        start=True, stop=True), "t")
    pr.blk(t=body_t)
    ttok = pr.tok("t")

    def body_s(s):
        pr.wait(s, "s", ttok)
        pr.inc(s.copy(col_out[:, :], ps[:, 0:1]), "s")
    pr.blk(s=body_s)
    return pr.tok("s")


def emit_L3_ffn(pr, stack, xio, lnT_tiles, lnT_tok, w):
    nc = pr.nc
    with ExitStack() as ls:
        lctx = ls.enter_context
        sc = lctx(nc.sbuf_tensor("l3f_sc", [P, 8], F32))
        s11 = lctx(nc.sbuf_tensor("l3f_s11", [1, 2], F32))
        bias_b = lctx(nc.sbuf_tensor("l3f_biasb", [P, 1024], BF16))
        kq = [lctx(nc.sbuf_tensor(f"l3f_kq{c}", [P, 1024], BF16)) for c in range(8)]
        xq = [lctx(nc.sbuf_tensor(f"l3f_xq{c}", [P, TT], BF16)) for c in range(8)]
        kf = lctx(nc.sbuf_tensor("l3f_kf", [P, TT], F32))
        f8a = lctx(nc.sbuf_tensor("l3f_f8a", [P, TT], FP8))
        f8b = lctx(nc.sbuf_tensor("l3f_f8b", [P, TT], FP8))
        scf = lctx(nc.sbuf_tensor("l3f_scf", [P, TT], F32))
        mskt = lctx(nc.sbuf_tensor("l3f_msk", [P, TT], mybir.dt.uint8))
        yb = lctx(nc.sbuf_tensor("l3f_yb", [P, 1024], BF16))
        kraw = lctx(nc.sbuf_tensor("l3f_kraw", [P, 2 * 1024], BF16))
        psS = lctx(nc.psum_tensor("l3f_psS", [P, 8], F32))
        psBC = lctx(nc.psum_tensor("l3f_psBC", [P, 1024], F32))
        psC = lctx(nc.psum_tensor("l3f_psC", [P, 1024], F32))
        psD = lctx(nc.psum_tensor("l3f_psD", [P, 1024], F32))

        def ld_s(q):
            pr.inc(q.dma_start(out=s11[:, 0:1], in_=w["s_in"]), "qd")
            pr.inc(q.dma_start(out=s11[:, 1:2], in_=w["s_k"]), "qd")
        pr.blk(q=ld_s)
        s11_tok = pr.tok("qd")
        emit_scalar_bcast(pr, psS, sc[:, 0:1], s11[:, 0:1], s11_tok)
        btok = emit_scalar_bcast(pr, psS, sc[:, 1:2], s11[:, 1:2], s11_tok)

        def mk_scales(v):
            pr.wait(v, "v", btok)
            pr.inc(v.reciprocal(sc[:, 2:3], sc[:, 0:1]), "v")
            pr.inc(v.reciprocal(sc[:, 3:4], sc[:, 1:2]), "v")
            pr.inc(v.tensor_tensor(sc[:, 4:5], sc[:, 0:1], sc[:, 1:2], ALU.mult), "v")
            pr.wait(v, "v", pr.tok("v"))
        pr.blk(v=mk_scales)
        sc_tok = pr.tok("v")

        def ld_b(q):
            pr.inc(q.dma_start(out=kraw[0:1, 0:1024], in_=w["bias"]), "qd")
        pr.blk(q=ld_b)
        brow_tok = pr.tok("qd")
        bc_tok = emit_row_broadcast(pr, psBC, kraw[0:1, 0:1024], brow_tok, 1024)

        def ev_b(s):
            pr.wait(s, "s", bc_tok)
            pr.inc(s.copy(bias_b[:, :], psBC[:, :]), "s")
        pr.blk(s=ev_b)
        bias_tok = pr.tok("s")

        kq_toks = []
        for c in range(8):
            buf = c % 2
            kslc = slice(buf * 1024, (buf + 1) * 1024)

            def ldk(q, c=c, kslc=kslc):
                if c >= 2:
                    pr.wait(q, "q", kq_toks[c - 2])
                pr.inc(q.dma_start(out=kraw[:, kslc], in_=w["kernel"][c]), "qd")
            pr.blk(q=ldk)
            ktok = pr.tok("qd")

            def mulk(v, c=c, kslc=kslc, ktok=ktok):
                pr.wait(v, "v", ktok, sc_tok)
                pr.inc(v.tensor_scalar(kf[:, 0:1024], kraw[:, kslc], sc[:, 3:4],
                                       None, ALU.mult), "v")
            pr.blk(v=mulk)
            mtok = pr.tok("v")
            kq_toks.append(emit_fp8_quant(pr, kq[c], kf[:, 0:1024], mtok,
                                          f8a[:, 0:1024], f8b[:, 0:1024],
                                          scf[:, 0:1024], mskt[:, 0:1024]))

        xq_toks = []
        for c in range(8):
            def mulx(v, c=c):
                pr.wait(v, "v", lnT_tok, sc_tok)
                if c >= 1:
                    pr.wait(v, "v", xq_toks[c - 1])
                pr.inc(v.tensor_scalar(kf[:, :], lnT_tiles[c][:, :], sc[:, 2:3],
                                       None, ALU.mult), "v")
            pr.blk(v=mulx)
            mtok = pr.tok("v")
            xq_toks.append(emit_fp8_quant(pr, xq[c], kf, mtok, f8a, f8b, scf, mskt))
        xq_tok = xq_toks[-1]

        prev = {}
        for i in range(NT):
            bank = psC if i % 2 == 0 else psD

            def mmx(t, i=i, bank=bank):
                pr.wait(t, "t", xq_tok, *kq_toks)
                pr.wait(t, "t", prev.get(i - 2))
                last = None
                for mh in range(2):
                    for c in range(8):
                        last = t.matmul(bank[:, mh * 512:(mh + 1) * 512],
                                        xq[c][:, i * 128:(i + 1) * 128],
                                        kq[c][:, mh * 512:(mh + 1) * 512],
                                        start=(c == 0), stop=(c == 7))
                pr.inc(last, "t")
            pr.blk(t=mmx)
            mmtok = pr.tok("t")

            def evy(v, i=i, bank=bank, mmtok=mmtok):
                pr.wait(v, "v", mmtok, bias_tok)
                pr.inc(v.tensor_scalar(yb[:, :], bank[:, :], sc[:, 4:5], None,
                                       ALU.mult), "v")
                pr.wait(v, "v", pr.tok("v"))
                pr.inc(v.tensor_tensor(yb[:, :], yb[:, :], bias_b[:, :], ALU.add), "v")
            pr.blk(v=evy)
            ytok = pr.tok("v")
            prev[i] = xio.add_store(i, yb[:, :], ytok)
        return [prev[i] for i in range(NT)]



def stage_L0_attn_b(inputs, pre):
    WQ, WK, WV = (np.asarray(inputs["L0_attn_b_WQ"]), np.asarray(inputs["L0_attn_b_WK"]),
                  np.asarray(inputs["L0_attn_b_WV"]))
    pre["L0ab_WQt"] = np.ascontiguousarray(
        WQ.reshape(8, 128, 8, 128).transpose(0, 2, 3, 1)).reshape(8, 128 * 8, 128)
    pre["L0ab_WKt"] = np.ascontiguousarray(
        WK.reshape(8, 128, 8, 128).transpose(0, 2, 3, 1)).reshape(8, 128 * 8, 128)
    pre["L0ab_WVt"] = np.ascontiguousarray(WV.T).reshape(8, 128, 1024)


def decl_L0_attn_b(nc):
    return {
        "WQt": nc.dram_tensor("L0ab_WQt", [8, 128 * 8, 128], F32, kind="ExternalInput").ap(),
        "WKt": nc.dram_tensor("L0ab_WKt", [8, 128 * 8, 128], F32, kind="ExternalInput").ap(),
        "WVt": nc.dram_tensor("L0ab_WVt", [8, 128, 1024], F32, kind="ExternalInput").ap(),
    }


def emit_proj_T(pr, lnT_tiles, lnT_tok, wdram, wsl, qT, tag, psum, n_kt=8,
                evac_bf16=True, war_tok=None, act=None):
    nc = pr.nc
    ev_done = {}
    ld_done = {}
    for kt in range(n_kt):
        buf = kt % 2
        wslc = slice(buf * 1024, (buf + 1) * 1024)

        def ldw(g, kt=kt, wslc=wslc):
            if kt >= 2:
                pr.wait(g, "g", ev_done.get((kt - 2, 3)))
            else:
                pr.wait(g, "g", war_tok)
            pr.inc(g.dma_start(
                out=wsl[:, wslc].rearrange("p (c k) -> p c k", c=8),
                in_=wdram[kt].rearrange("(c p) k -> p c k", p=128)), "gd")
        pr.blk(g=ldw)
        ld_done[kt] = pr.tok("gd")

        for tg in range(4):
            bank = psum[tg % 2]

            def mm(t, kt=kt, tg=tg, wslc=wslc, bank=bank):
                pr.wait(t, "t", ld_done[kt], lnT_tok)
                prev = ev_done.get((kt, tg - 2)) or ev_done.get((kt - 1, tg + 2))
                pr.wait(t, "t", prev if prev is not None else war_tok)
                last = None
                for c in range(8):
                    last = t.matmul(bank[:, :], wsl[:, wslc][:, c * 128:(c + 1) * 128],
                                    lnT_tiles[c][:, tg * 512:(tg + 1) * 512],
                                    start=(c == 0), stop=(c == 7))
                pr.inc(last, "t")
            pr.blk(t=mm)
            mmtok = pr.tok("t")

            def ev(s, kt=kt, tg=tg, bank=bank, mmtok=mmtok):
                pr.wait(s, "s", mmtok)
                if act is not None:
                    pr.inc(s.activation(qT[kt][:, tg * 512:(tg + 1) * 512], bank[:, :],
                                        act), "s")
                else:
                    pr.inc(s.copy(qT[kt][:, tg * 512:(tg + 1) * 512], bank[:, :]), "s")
            pr.blk(s=ev)
            ev_done[(kt, tg)] = pr.tok("s")
    return pr.tok("s")


def emit_proj_tok(pr, lnT_tiles, lnT_tok, wdram_chunks, wbuf, out_tiles, psum,
                  tag, out_dim=1024, n_c=8, scale=None, war_tok=None,
                  bias_ap=None, bias_tok=None):
    nc = pr.nc
    def ldall(g):
        for c in range(n_c):
            pr.inc(g.dma_start(out=wbuf[:, c * out_dim:(c + 1) * out_dim],
                               in_=wdram_chunks[c]), "gd")
    pr.blk(g=ldall)
    wtok = pr.tok("gd")

    ev_done = {-2: war_tok, -1: war_tok}
    for i in range(NT):
        bank = psum[i % 2]

        def mm(t, i=i, bank=bank):
            pr.wait(t, "t", wtok, lnT_tok)
            pr.wait(t, "t", ev_done.get(i - 2))
            last = None
            for mh in range((out_dim + 511) // 512):
                o0, o1 = mh * 512, min(out_dim, (mh + 1) * 512)
                for c in range(n_c):
                    last = t.matmul(bank[:, o0:o1],
                                    lnT_tiles[c][:, i * 128:(i + 1) * 128],
                                    wbuf[:, c * out_dim + o0:c * out_dim + o1],
                                    start=(c == 0), stop=(c == n_c - 1))
            pr.inc(last, "t")
        pr.blk(t=mm)
        mmtok = pr.tok("t")

        if bias_ap is None:
            def ev(s, i=i, bank=bank, mmtok=mmtok):
                pr.wait(s, "s", mmtok)
                if scale is not None:
                    pr.inc(s.activation(out_tiles[i][:, :], bank[:, :], AF.Copy,
                                        scale=float(scale)), "s")
                else:
                    pr.inc(s.copy(out_tiles[i][:, :], bank[:, :]), "s")
            pr.blk(s=ev)
            ev_done[i] = pr.tok("s")
        else:
            def ev(v, i=i, bank=bank, mmtok=mmtok):
                pr.wait(v, "v", mmtok, bias_tok)
                pr.inc(v.tensor_add(out_tiles[i][:, :], bank[:, :], bias_ap), "v")
            pr.blk(v=ev)
            ev_done[i] = pr.tok("v")
    return ev_done[NT - 1]


def emit_L0_attn_b(pr, stack, xio, lnT_tiles, lnT_tok, w):
    nc = pr.nc
    with ExitStack() as ls:
        lctx = ls.enter_context
        qT = [lctx(nc.sbuf_tensor(f"l0b_qT{c}", [P, TT], BF16)) for c in range(8)]
        kT = [lctx(nc.sbuf_tensor(f"l0b_kT{c}", [P, TT], BF16)) for c in range(8)]
        vt = [lctx(nc.sbuf_tensor(f"l0b_v{i}", [P, 1024], BF16)) for i in range(NT)]
        wsl = lctx(nc.sbuf_tensor("l0b_wsl", [P, 2048], BF16))
        wv = lctx(nc.sbuf_tensor("l0b_wv", [P, 8 * 1024], BF16))
        ws = lctx(nc.sbuf_tensor("l0b_ws", [P, 2 * 256], BF16))
        rz = lctx(nc.sbuf_tensor("l0b_rz", [P, 4], F32))
        with ExitStack() as ps_stack:
            psA = ps_stack.enter_context(nc.psum_tensor("l0b_psA", [P, 512], F32))
            psB = ps_stack.enter_context(nc.psum_tensor("l0b_psB", [P, 512], F32))
            qT_tok = emit_proj_T(pr, lnT_tiles, lnT_tok, w["WQt"], wsl, qT, "q", [psA, psB])
            kT_tok = emit_proj_T(pr, lnT_tiles, lnT_tok, w["WKt"], wsl, kT, "k", [psA, psB],
                                 war_tok=qT_tok)
            psV = [ps_stack.enter_context(nc.psum_tensor(f"l0b_psV{j}", [P, 1024], F32))
                   for j in range(2)]
            v_tok = emit_proj_tok(pr, lnT_tiles, lnT_tok,
                                  [w["WVt"][c] for c in range(8)], wv, vt, psV, "v",
                                  war_tok=kT_tok)
        psS = [lctx(nc.psum_tensor(f"l0b_psS{j}", [P, 256], F32)) for j in range(2)]
        psO = [lctx(nc.psum_tensor(f"l0b_psO{j}", [P, 1536], F32)) for j in range(2)]

        out_toks = [None] * NT
        prev_group = {}
        prev_last_exp = {}
        for ig in range(8):
            i0 = ig * 256
            jt_max = (i0 + 255) // 128
            exp_done = {}
            attn_done = {-2: prev_group.get(ig - 2), -1: prev_group.get(ig - 2)}
            for jt in range(jt_max + 1):
                sbank = psS[jt % 2]
                wslc = slice((jt % 2) * 256, (jt % 2 + 1) * 256)

                def mmsc(t, ig=ig, jt=jt, sbank=sbank, i0=i0):
                    pr.wait(t, "t", qT_tok, kT_tok)
                    pr.wait(t, "t", exp_done.get(jt - 2) if jt >= 2
                            else prev_last_exp.get(ig - 1))
                    last = None
                    for c in range(8):
                        last = t.matmul(sbank[:, :], kT[c][:, jt * 128:(jt + 1) * 128],
                                        qT[c][:, i0:i0 + 256], start=(c == 0), stop=(c == 7))
                    pr.inc(last, "t")
                pr.blk(t=mmsc)
                sctok = pr.tok("t")

                def evexp(s, jt=jt, sbank=sbank, wslc=wslc, sctok=sctok, ig=ig):
                    pr.wait(s, "s", sctok)
                    pr.wait(s, "s", attn_done.get(jt - 2))
                    pr.inc(s.activation(ws[:, wslc], sbank[:, :], AF.Exp,
                                        scale=1.0 / 32.0), "s")
                pr.blk(s=evexp)
                etok = pr.tok("s")

                if jt * 128 + 127 > i0:
                    def msk(g, jt=jt, wslc=wslc, etok=etok, i0=i0):
                        pr.wait(g, "g", etok)
                        pr.inc(g.affine_select(ws[:, wslc], ws[:, wslc], [[1, 256]],
                                               ALU.is_ge, 0.0, base=i0 - jt * 128,
                                               channel_multiplier=-1), "g")
                    pr.blk(g=msk)
                    etok = pr.tok("g")

                def mmav(t, ig=ig, jt=jt, wslc=wslc, etok=etok, jt_max=jt_max):
                    pr.wait(t, "t", etok, v_tok)
                    if jt == 0:
                        pr.wait(t, "t", prev_group.get(ig - 2))
                    last = None
                    for isub in range(2):
                        lhs = ws[:, wslc][:, isub * 128:(isub + 1) * 128]
                        for mh in range(2):
                            last = t.matmul(psO[isub][:, mh * 512:(mh + 1) * 512],
                                            lhs, vt[jt][:, mh * 512:(mh + 1) * 512],
                                            start=(jt == 0), stop=(jt == jt_max))
                        last = t.matmul(psO[isub][:, 1024:1025],
                                        lhs, pr.ones_bf16[:, 0:1],
                                        start=(jt == 0), stop=(jt == jt_max))
                    pr.inc(last, "t")
                pr.blk(t=mmav)
                attn_done[jt] = pr.tok("t")
                exp_done[jt] = attn_done[jt]

            avtok = pr.tok("t")
            prev_last_exp[ig] = pr.tok("s")

            def rzc(v, ig=ig, avtok=avtok):
                pr.wait(v, "v", avtok)
                pr.inc(v.reciprocal(rz[:, (ig % 2) * 2:(ig % 2) * 2 + 1],
                                    psO[0][:, 1024:1025]), "v")
                pr.inc(v.reciprocal(rz[:, (ig % 2) * 2 + 1:(ig % 2) * 2 + 2],
                                    psO[1][:, 1024:1025]), "v")
                pr.wait(v, "v", pr.tok("v"))
            pr.blk(v=rzc)
            rztok = pr.tok("v")

            for isub in range(2):
                i = ig * 2 + isub
                out_toks[i] = xio.add_store(
                    i, psO[isub][:, 0:1024], rztok,
                    scale_ap=rz[:, (ig % 2) * 2 + isub:(ig % 2) * 2 + isub + 1])
            prev_group[ig] = out_toks[ig * 2 + 1]
        return out_toks



def stage_L1_attn_a(inputs, pre):
    for nm in ("wq", "wk", "wv"):
        W = np.asarray(inputs[f"L1_attn_a_{nm}"])
        pre[f"L1aa_{nm}T"] = np.ascontiguousarray(W.T).reshape(8, 128, 1024)


def rope_tables_L1():
    HD = 64
    inv = 1.0 / (10000.0 ** (np.arange(0, HD, 2, dtype=np.float64) / HD))
    t = np.arange(TT, dtype=np.float64)
    fr = np.outer(t, inv)
    cos = np.cos(fr); sin = np.sin(fr)
    cos2 = np.repeat(cos, 2, axis=1)
    sin2 = np.repeat(sin, 2, axis=1)
    return (np.ascontiguousarray(cos2.reshape(16, 128, 64).astype(np.float32)),
            np.ascontiguousarray(sin2.reshape(16, 128, 64).astype(np.float32)))


def decl_L1_attn_a(nc):
    return {
        "wqT": nc.dram_tensor("L1aa_wqT", [8, 128, 1024], F32, kind="ExternalInput").ap(),
        "wkT": nc.dram_tensor("L1aa_wkT", [8, 128, 1024], F32, kind="ExternalInput").ap(),
        "wvT": nc.dram_tensor("L1aa_wvT", [8, 128, 1024], F32, kind="ExternalInput").ap(),
    }


def emit_rope_inplace(pr, tiles, toks, cs_sb, cs_tok, rot, t1):
    new_toks = []
    for i, (q, qtok) in enumerate(zip(tiles, toks)):
        cos2 = cs_sb[:, i * 128:i * 128 + 64]
        sin2 = cs_sb[:, i * 128 + 64:i * 128 + 128]

        def body(v, q=q, qtok=qtok, cos2=cos2, sin2=sin2):
            pr.wait(v, "v", qtok, cs_tok)
            qv = q[:, :].rearrange("p (x two) -> p x two", two=2)
            rv = rot[:, :].rearrange("p (x two) -> p x two", two=2)
            pr.inc(v.tensor_scalar(rv[:, :, 0:1], qv[:, :, 1:2], -1.0, None, ALU.mult), "v")
            pr.inc(v.tensor_copy(rv[:, :, 1:2], qv[:, :, 0:1]), "v")
            pr.wait(v, "v", pr.tok("v"))
            q3 = q[:, :].rearrange("p (h d) -> p h d", d=64)
            r3 = rot[:, :].rearrange("p (h d) -> p h d", d=64)
            t3 = t1[:, :].rearrange("p (h d) -> p h d", d=64)
            cb = cos2.unsqueeze(1).broadcast_to([P, 16, 64])
            sb = sin2.unsqueeze(1).broadcast_to([P, 16, 64])
            pr.inc(v.tensor_tensor(t3, q3, cb, ALU.mult), "v")
            pr.inc(v.tensor_tensor(r3, r3, sb, ALU.mult), "v")
            pr.wait(v, "v", pr.tok("v"))
            pr.inc(v.tensor_tensor(q[:, :], t1[:, :], rot[:, :], ALU.add), "v")
        pr.blk(v=body)
        new_toks.append(pr.tok("v"))
    return new_toks


def emit_L1_attn_a(pr, stack, xio, lnT_tiles, lnT_tok, w, rope_consts):
    nc = pr.nc
    cos2_np, sin2_np = rope_consts
    cs_np = np.concatenate([cos2_np, sin2_np], axis=2).reshape(16, 128, 128)
    cs_np = np.ascontiguousarray(cs_np.transpose(1, 0, 2)).reshape(128, 16 * 128)
    cs_dram = nc.inline_tensor(cs_np, "l1aa_cs")

    def ptok(dct, ig, h, back=2, nh=16, default=None):
        hh = h - back
        igg = ig
        while hh < 0:
            hh += nh
            igg -= 1
        return dct.get((igg, hh), default)

    with ExitStack() as ls:
        lctx = ls.enter_context
        qT = [lctx(nc.sbuf_tensor(f"l1aa_qT{c}", [P, TT], BF16)) for c in range(8)]
        kT = [lctx(nc.sbuf_tensor(f"l1aa_kT{c}", [P, TT], BF16)) for c in range(8)]
        wbuf = lctx(nc.sbuf_tensor("l1aa_wbuf", [P, 8 * 1024], BF16))

        with ExitStack() as qk_stack:
            qctx = qk_stack.enter_context
            cs_sb = qctx(nc.sbuf_tensor("l1aa_cs_sb", [P, 16 * 128], F32))
            rot = qctx(nc.sbuf_tensor("l1aa_rot", [P, 1024], F32))
            t1s = qctx(nc.sbuf_tensor("l1aa_t1", [P, 1024], F32))

            def ldcs(q):
                pr.inc(q.dma_start(out=cs_sb[:, :], in_=cs_dram.ap()), "qd")
            pr.blk(q=ldcs)
            cs_tok = pr.tok("qd")
            qtl = [qctx(nc.sbuf_tensor(f"l1aa_q{i}", [P, D], BF16)) for i in range(NT)]
            ktl = [qctx(nc.sbuf_tensor(f"l1aa_k{i}", [P, D], BF16)) for i in range(NT)]
            psP = [qctx(nc.psum_tensor(f"l1aa_psP{j}", [P, 1024], F32)) for j in range(2)]
            q_tok = emit_proj_tok(pr, lnT_tiles, lnT_tok, [w["wqT"][c] for c in range(8)],
                                  wbuf, qtl, psP, "q")
            k_tok = emit_proj_tok(pr, lnT_tiles, lnT_tok, [w["wkT"][c] for c in range(8)],
                                  wbuf, ktl, psP, "k", war_tok=q_tok)
            q_toks = emit_rope_inplace(pr, qtl, [q_tok] * NT, cs_sb, cs_tok, rot, t1s)
            k_toks = emit_rope_inplace(pr, ktl, [k_tok] * NT, cs_sb, cs_tok, rot, t1s)
            qT_tok = emit_transpose_fm(pr, qk_stack, qtl, q_toks, qT, tag="l1aaq")
            kT_tok = emit_transpose_fm(pr, qk_stack, ktl, k_toks, kT, tag="l1aak")

        vx = [lctx(nc.sbuf_tensor(f"l1aa_vx{i}", [P, 16 * 65], BF16)) for i in range(NT)]
        onT = [lctx(nc.sbuf_tensor(f"l1aa_onT{c}", [P, TT], BF16)) for c in range(8)]
        vp_stack = ExitStack()
        psP2 = [vp_stack.enter_context(nc.psum_tensor(f"l1aa_psP2{j}", [P, 1024], F32)) for j in range(2)]

        def ldvw(g):
            pr.wait(g, "g", kT_tok)
            for c in range(8):
                pr.inc(g.dma_start(out=wbuf[:, c * 1024:(c + 1) * 1024],
                                   in_=w["wvT"][c]), "gd")
        pr.blk(g=ldvw)
        vw_tok = pr.tok("gd")

        def vones(v):
            for i in range(NT):
                pr.inc(v.memset(vx[i][:, :].rearrange("p (h e) -> p h e", e=65)[:, :, 64:65],
                                1.0), "v")
        pr.blk(v=vones)
        vones_tok = pr.tok("v")

        ev_done = {}
        for i in range(NT):
            bank = psP2[i % 2]

            def mmv(t, i=i, bank=bank):
                pr.wait(t, "t", lnT_tok, vw_tok)
                pr.wait(t, "t", ev_done.get(i - 2, kT_tok))
                last = None
                for mh in range(2):
                    for c in range(8):
                        last = t.matmul(bank[:, mh * 512:(mh + 1) * 512],
                                        lnT_tiles[c][:, i * 128:(i + 1) * 128],
                                        wbuf[:, c * 1024 + mh * 512:c * 1024 + (mh + 1) * 512],
                                        start=(c == 0), stop=(c == 7))
                pr.inc(last, "t")
            pr.blk(t=mmv)
            mmtok = pr.tok("t")

            def evv(s, i=i, bank=bank, mmtok=mmtok):
                pr.wait(s, "s", mmtok, vones_tok)
                dst = vx[i][:, :].rearrange("p (h e) -> p h e", e=65)[:, :, 0:64]
                pr.inc(s.copy(dst, bank[:, :].rearrange("p (h d) -> p h d", d=64)), "s")
            pr.blk(s=evv)
            ev_done[i] = pr.tok("s")
        v_tok = pr.tok("s")
        vp_stack.close()

        ws = lctx(nc.sbuf_tensor("l1aa_ws", [P, 2 * 512], BF16))
        rzrow = lctx(nc.sbuf_tensor("l1aa_rzrow", [1, 512], F32))
        rzb = lctx(nc.sbuf_tensor("l1aa_rzb", [P, 512], F32))
        att_stack = ExitStack()
        psS = [att_stack.enter_context(nc.psum_tensor(f"l1aa_psS{j}", [P, 512], F32)) for j in range(2)]
        psO = [att_stack.enter_context(nc.psum_tensor(f"l1aa_psO{j}", [P, 512], F32)) for j in range(2)]
        psB2 = att_stack.enter_context(nc.psum_tensor("l1aa_psB2", [P, 512], F32))

        last_exp = {}
        norm_done = {}
        rz_done = {}
        for ig in range(4):
            i0 = ig * 512
            for h in range(16):
                cchunk, coff = h // 2, (h % 2) * 64
                for jt in range(NT):
                    sb = psS[jt % 2]
                    wslc = slice((jt % 2) * 512, (jt % 2 + 1) * 512)

                    def mmsc(t, ig=ig, h=h, jt=jt, sb=sb, i0=i0, cchunk=cchunk, coff=coff):
                        pr.wait(t, "t", qT_tok, kT_tok)
                        if jt < 2:
                            pr.wait(t, "t", ptok(last_exp, ig, h, 1))
                        else:
                            pr.wait(t, "t", exp_done[jt - 2])
                        pr.inc(t.matmul(sb[:, :],
                                        kT[cchunk][coff:coff + 64, jt * 128:(jt + 1) * 128],
                                        qT[cchunk][coff:coff + 64, i0:i0 + 512],
                                        start=True, stop=True), "t")
                    pr.blk(t=mmsc)
                    sctok = pr.tok("t")

                    def evexp(s, jt=jt, sb=sb, wslc=wslc, sctok=sctok, ig=ig, h=h):
                        pr.wait(s, "s", sctok)
                        if jt >= 2:
                            pr.wait(s, "s", attn_done[jt - 2])
                        elif jt < 2 and (ig + h) > 0:
                            pr.wait(s, "s", ptok(last_attn, ig, h, 1))
                        pr.inc(s.activation(ws[:, wslc], sb[:, :], AF.Exp,
                                            scale=0.125), "s")
                    pr.blk(s=evexp)
                    if jt == 0:
                        exp_done = {}
                        attn_done = {}
                    exp_done[jt] = pr.tok("s")

                    def mmav(t, ig=ig, h=h, jt=jt, wslc=wslc):
                        pr.wait(t, "t", exp_done[jt], v_tok)
                        if jt == 0:
                            pr.wait(t, "t", ptok(norm_done, ig, h, 2))
                        pr.inc(t.matmul(psO[h % 2][0:65, :],
                                        vx[jt][:, h * 65:(h + 1) * 65],
                                        ws[:, wslc],
                                        start=(jt == 0), stop=(jt == NT - 1)), "t")
                    pr.blk(t=mmav)
                    attn_done[jt] = pr.tok("t")
                last_exp[(ig, h)] = exp_done[NT - 1]
                last_attn = dict() if (ig == 0 and h == 0) else last_attn
                last_attn[(ig, h)] = attn_done[NT - 1]
                avtok = attn_done[NT - 1]

                def rzc(v, h=h, avtok=avtok, ig=ig):
                    pr.wait(v, "v", avtok)
                    pr.wait(v, "v", rz_done.get("bc"))
                    pr.inc(v.reciprocal(rzrow[0:1, :], psO[h % 2][64:65, :]), "v")
                pr.blk(v=rzc)
                rztok = pr.tok("v")

                def bcz(t, rztok=rztok):
                    pr.wait(t, "t", rztok, pr.ones_tok)
                    pr.wait(t, "t", rz_done.get("cp"))
                    pr.inc(t.matmul(psB2[:, :], pr.ones_f32[0:1, 0:128],
                                    rzrow[0:1, :], start=True, stop=True), "t")
                pr.blk(t=bcz)
                rz_done["bc"] = pr.tok("t")

                def cpz(s):
                    pr.wait(s, "s", rz_done["bc"])
                    pr.wait(s, "s", rz_done.get("mul"))
                    pr.inc(s.copy(rzb[:, :], psB2[:, :]), "s")
                pr.blk(s=cpz)
                rz_done["cp"] = pr.tok("s")

                def nrm(v, ig=ig, h=h, cchunk=cchunk, coff=coff):
                    pr.wait(v, "v", rz_done["cp"])
                    pr.inc(v.tensor_tensor(onT[cchunk][coff:coff + 64, i0:i0 + 512],
                                           psO[h % 2][0:64, :], rzb[0:64, :],
                                           ALU.mult), "v")
                pr.blk(v=nrm)
                norm_done[(ig, h)] = pr.tok("v")
                rz_done["mul"] = norm_done[(ig, h)]

        onT_tok = pr.tok("v")
        att_stack.close()

        ytl = [lctx(nc.sbuf_tensor(f"l1aa_y{j}", [P, D], BF16)) for j in range(2)]
        out_toks = [None] * NT
        with ExitStack() as trs:
            psT = [trs.enter_context(nc.psum_tensor(f"l1aa_psT{j}", [P, 1024], BF16))
                   for j in range(2)]
            ytok = {}
            for i in range(NT):
                bank = psT[i % 2]
                yt = ytl[i % 2]

                def trb(t, i=i, bank=bank):
                    pr.wait(t, "t", onT_tok, pr.ident_tok)
                    pr.wait(t, "t", ytok.get(i - 2))
                    last = None
                    for c in range(8):
                        last = t.transpose(bank[:, c * 128:(c + 1) * 128],
                                           onT[c][:, i * 128:(i + 1) * 128],
                                           pr.ident[:, :])
                    pr.inc(last, "t")
                pr.blk(t=trb)
                ttok = pr.tok("t")

                def evy(s, i=i, bank=bank, yt=yt, ttok=ttok):
                    pr.wait(s, "s", ttok)
                    pr.inc(s.copy(yt[:, :], bank[:, :]), "s")
                pr.blk(s=evy)
                sy = pr.tok("s")
                out_toks[i] = xio.add_store(i, yt[:, :], sy)
                ytok[i] = out_toks[i]
        return out_toks



def stage_L0_attn_a(inputs, pre):
    for nm in ("Wq", "Wk", "Wv", "Wo"):
        W = np.asarray(inputs[f"L0_attn_a_{nm}"])
        pre[f"L0aa_{nm}T"] = np.ascontiguousarray(W.T).reshape(8, 128, 1024)
    wfT = np.ascontiguousarray(np.asarray(inputs["L0_attn_a_Wf"]).T)
    pre["L0aa_WfT"] = np.ascontiguousarray(np.concatenate([wfT, wfT], axis=0))


def decl_L0_attn_a(nc):
    d = {}
    for nm in ("WqT", "WkT", "WvT", "WoT"):
        d[nm] = nc.dram_tensor(f"L0aa_{nm}", [8, 128, 1024], F32, kind="ExternalInput").ap()
    d["WfT"] = nc.dram_tensor("L0aa_WfT", [128, 256], F32, kind="ExternalInput").ap()
    return d


def emit_elu1_evac(pr, dst, psrc, mmtok, tmp_r, tmp_m, war=None):
    def b1(s):
        pr.wait(s, "s", mmtok)
        if war is not None:
            pr.wait(s, "s", war)
        pr.inc(s.activation(tmp_r[:, :], psrc, AF.Relu), "s")
    pr.blk(s=b1)
    rtok = pr.tok("s")

    def b2(v):
        pr.wait(v, "v", rtok)
        pr.inc(v.tensor_tensor(tmp_m[:, :], psrc, tmp_r[:, :], ALU.subtract), "v")
    pr.blk(v=b2)
    vtok = pr.tok("v")

    def b3(s):
        pr.wait(s, "s", vtok)
        pr.inc(s.activation(tmp_m[:, :], tmp_m[:, :], AF.Exp), "s")
    pr.blk(s=b3)
    etok = pr.tok("s")

    def b4(v):
        pr.wait(v, "v", etok)
        pr.inc(v.tensor_add(dst, tmp_r[:, :], tmp_m[:, :]), "v")
    pr.blk(v=b4)
    return pr.tok("v")


def emit_L0_attn_a(pr, stack, xio, lnT_tiles, lnT_tok, w):
    nc = pr.nc
    with ExitStack() as ls:
        lctx = ls.enter_context
        qT = [lctx(nc.sbuf_tensor(f"l0aa_qT{c}", [P, TT], BF16)) for c in range(8)]
        kT = [lctx(nc.sbuf_tensor(f"l0aa_kT{c}", [P, TT], BF16)) for c in range(8)]
        wbuf = lctx(nc.sbuf_tensor("l0aa_wbuf", [P, 8 * 1024], BF16))
        wfT = lctx(nc.sbuf_tensor("l0aa_wfT", [128, 256], BF16))

        def ldwf(g):
            pr.inc(g.dma_start(out=wfT[:, :], in_=w["WfT"]), "gd")
        pr.blk(g=ldwf)
        wf_tok = pr.tok("gd")

        with ExitStack() as qk_stack:
            qctx = qk_stack.enter_context
            qtl = [qctx(nc.sbuf_tensor(f"l0aa_q{i}", [P, D], BF16)) for i in range(NT)]
            ktl = [qctx(nc.sbuf_tensor(f"l0aa_k{i}", [P, D], BF16)) for i in range(NT)]
            psP = [qctx(nc.psum_tensor(f"l0aa_psP{j}", [P, 1024], F32)) for j in range(2)]
            q_tok = emit_proj_tok(pr, lnT_tiles, lnT_tok, [w["WqT"][c] for c in range(8)],
                                  wbuf, qtl, psP, "q")
            k_tok = emit_proj_tok(pr, lnT_tiles, lnT_tok, [w["WkT"][c] for c in range(8)],
                                  wbuf, ktl, psP, "k", war_tok=q_tok)
            qT_tok = emit_transpose_fm(pr, qk_stack, qtl, [q_tok] * NT, qT, tag="l0aaq")
            kT_tok = emit_transpose_fm(pr, qk_stack, ktl, [k_tok] * NT, kT, tag="l0aak")

        vx = [lctx(nc.sbuf_tensor(f"l0aa_vx{i}", [P, 16 * 65], BF16)) for i in range(NT)]
        att = [lctx(nc.sbuf_tensor(f"l0aa_att{i}", [P, 1024], BF16)) for i in range(NT)]
        vp_stack = ExitStack()
        psP2 = [vp_stack.enter_context(nc.psum_tensor(f"l0aa_psP2{j}", [P, 1024], F32))
                for j in range(2)]

        def ldvw(g):
            pr.wait(g, "g", kT_tok)
            for c in range(8):
                pr.inc(g.dma_start(out=wbuf[:, c * 1024:(c + 1) * 1024],
                                   in_=w["WvT"][c]), "gd")
        pr.blk(g=ldvw)
        vw_tok = pr.tok("gd")

        def vones(v):
            for i in range(NT):
                pr.inc(v.memset(vx[i][:, :].rearrange("p (h e) -> p h e", e=65)[:, :, 64:65],
                                1.0), "v")
        pr.blk(v=vones)
        vones_tok = pr.tok("v")

        ev_done = {}
        for i in range(NT):
            bank = psP2[i % 2]

            def mmv(t, i=i, bank=bank):
                pr.wait(t, "t", lnT_tok, vw_tok)
                pr.wait(t, "t", ev_done.get(i - 2, kT_tok))
                last = None
                for mh in range(2):
                    for c in range(8):
                        last = t.matmul(bank[:, mh * 512:(mh + 1) * 512],
                                        lnT_tiles[c][:, i * 128:(i + 1) * 128],
                                        wbuf[:, c * 1024 + mh * 512:c * 1024 + (mh + 1) * 512],
                                        start=(c == 0), stop=(c == 7))
                pr.inc(last, "t")
            pr.blk(t=mmv)
            mmtok = pr.tok("t")

            def evv(s, i=i, bank=bank, mmtok=mmtok):
                pr.wait(s, "s", mmtok, vones_tok)
                dst = vx[i][:, :].rearrange("p (h e) -> p h e", e=65)[:, :, 0:64]
                pr.inc(s.copy(dst, bank[:, :].rearrange("p (h d) -> p h d", d=64)), "s")
            pr.blk(s=evv)
            ev_done[i] = pr.tok("s")
        v_tok = pr.tok("s")
        vp_stack.close()

        hd_stack = ExitStack()
        hctx = hd_stack.enter_context
        kf1 = hctx(nc.sbuf_tensor("l0aa_kf1", [P, 16 * 256], BF16))
        qf1 = [hctx(nc.sbuf_tensor(f"l0aa_qf1{j}", [P, TT], BF16)) for j in range(2)]
        kvb = hctx(nc.sbuf_tensor("l0aa_kvb", [P, 2 * 65], BF16))
        tr1 = hctx(nc.sbuf_tensor("l0aa_tr1", [P, 512], F32))
        tr2 = hctx(nc.sbuf_tensor("l0aa_tr2", [P, 512], F32))
        rn = hctx(nc.sbuf_tensor("l0aa_rn", [P, 2], F32))
        psF = [hctx(nc.psum_tensor(f"l0aa_psF{j}", [P, 512], F32)) for j in range(2)]
        psKV = hctx(nc.psum_tensor("l0aa_psKV", [P, 130], F32))
        psA = [hctx(nc.psum_tensor(f"l0aa_psA{j}", [P, 65], F32)) for j in range(2)]

        prev_head = {}
        att_done = {}
        for h in range(16):
            cchunk, coff = h // 2, (h % 2) * 64
            kf_toks = []
            for i in range(NT):
                bank = psF[i % 2]

                def mmkf(t, h=h, i=i, bank=bank, cchunk=cchunk, coff=coff):
                    pr.wait(t, "t", kT_tok, wf_tok)
                    pr.wait(t, "t", kf_toks[i - 2] if i >= 2 else prev_head.get(h - 1))
                    pr.inc(t.matmul(bank[:, 0:256],
                                    kT[cchunk][coff:coff + 64, i * 128:(i + 1) * 128],
                                    wfT[coff:coff + 64, :], start=True, stop=True), "t")
                pr.blk(t=mmkf)
                mmtok = pr.tok("t")
                kf_toks.append(emit_elu1_evac(
                    pr, kf1[:, i * 256:(i + 1) * 256], bank[:, 0:256], mmtok,
                    tr1[:, 0:256], tr2[:, 0:256],
                    war=kf_toks[i - 1] if i >= 1 else prev_head.get(h - 1)))
            kf_tok = kf_toks[-1]

            def mmkv(t, h=h, kf_tok=kf_tok):
                pr.wait(t, "t", kf_tok, v_tok)
                pr.wait(t, "t", prev_head.get(h - 1))
                last = None
                for fh in range(2):
                    for i in range(NT):
                        last = t.matmul(psKV[:, fh * 65:(fh + 1) * 65],
                                        kf1[:, i * 256 + fh * 128:i * 256 + (fh + 1) * 128],
                                        vx[i][:, h * 65:(h + 1) * 65],
                                        start=(i == 0), stop=(i == NT - 1))
                pr.inc(last, "t")
            pr.blk(t=mmkv)
            kv_mm = pr.tok("t")

            def evkv(s, kv_mm=kv_mm):
                pr.wait(s, "s", kv_mm)
                pr.inc(s.copy(kvb[:, :], psKV[:, :]), "s")
            pr.blk(s=evkv)
            kv_tok = pr.tok("s")

            qf_toks = []
            for fh in range(2):
                for tg in range(4):
                    bank = psF[tg % 2]

                    def mmqf(t, h=h, fh=fh, tg=tg, bank=bank, cchunk=cchunk, coff=coff):
                        pr.wait(t, "t", qT_tok, wf_tok)
                        pr.wait(t, "t", qf_toks[-2] if len(qf_toks) >= 2 else kv_tok)
                        pr.inc(t.matmul(bank[:, :],
                                        wfT[coff:coff + 64, fh * 128:(fh + 1) * 128],
                                        qT[cchunk][coff:coff + 64, tg * 512:(tg + 1) * 512],
                                        start=True, stop=True), "t")
                    pr.blk(t=mmqf)
                    mmtok = pr.tok("t")
                    qf_toks.append(emit_elu1_evac(
                        pr, qf1[fh][:, tg * 512:(tg + 1) * 512], bank[:, :], mmtok,
                        tr1[:, :], tr2[:, :],
                        war=qf_toks[-1] if qf_toks else kf_tok))
            qf_tok = qf_toks[-1]

            for i in range(NT):
                bank = psA[i % 2]

                def mmo(t, h=h, i=i, bank=bank, qf_tok=qf_tok, kv_tok=kv_tok):
                    pr.wait(t, "t", qf_tok, kv_tok)
                    pr.wait(t, "t", att_done.get((h, i - 2), prev_head.get(h - 1)))
                    last = None
                    for fh in range(2):
                        last = t.matmul(bank[:, :],
                                        qf1[fh][:, i * 128:(i + 1) * 128],
                                        kvb[:, fh * 65:(fh + 1) * 65],
                                        start=(fh == 0), stop=(fh == 1))
                    pr.inc(last, "t")
                pr.blk(t=mmo)
                mmtok = pr.tok("t")

                def evo(v, h=h, i=i, bank=bank, mmtok=mmtok):
                    pr.wait(v, "v", mmtok)
                    pr.inc(v.tensor_scalar(rn[:, i % 2:i % 2 + 1], bank[:, 64:65],
                                           1e-6, None, ALU.add), "v")
                    pr.wait(v, "v", pr.tok("v"))
                    pr.inc(v.reciprocal(rn[:, i % 2:i % 2 + 1], rn[:, i % 2:i % 2 + 1]), "v")
                    pr.wait(v, "v", pr.tok("v"))
                    pr.inc(v.tensor_scalar(att[i][:, h * 64:(h + 1) * 64],
                                           bank[:, 0:64], rn[:, i % 2:i % 2 + 1],
                                           None, ALU.mult), "v")
                pr.blk(v=evo)
                att_done[(h, i)] = pr.tok("v")
            prev_head[h] = pr.tok("v")
        att_tok = pr.tok("v")
        hd_stack.close()

        attT = lnT_tiles
        with ExitStack() as fs:
            attT_tok = emit_transpose_fm(pr, fs, att, [att_tok] * NT, attT, tag="l0aat")

            def ldow(g):
                pr.wait(g, "g", v_tok)
                for c in range(8):
                    pr.inc(g.dma_start(out=wbuf[:, c * 1024:(c + 1) * 1024],
                                       in_=w["WoT"][c]), "gd")
            pr.blk(g=ldow)
            ow_tok = pr.tok("gd")

            psO = [fs.enter_context(nc.psum_tensor(f"l0aa_psO{j}", [P, 1024], F32))
                   for j in range(2)]
            out_toks = [None] * NT
            ev2 = {}
            for i in range(NT):
                bank = psO[i % 2]

                def mmo2(t, i=i, bank=bank):
                    pr.wait(t, "t", attT_tok, ow_tok)
                    pr.wait(t, "t", ev2.get(i - 2))
                    last = None
                    for mh in range(2):
                        for c in range(8):
                            last = t.matmul(bank[:, mh * 512:(mh + 1) * 512],
                                            attT[c][:, i * 128:(i + 1) * 128],
                                            wbuf[:, c * 1024 + mh * 512:c * 1024 + (mh + 1) * 512],
                                            start=(c == 0), stop=(c == 7))
                    pr.inc(last, "t")
                pr.blk(t=mmo2)
                mmtok = pr.tok("t")
                out_toks[i] = xio.add_store(i, bank[:, :], mmtok)
                ev2[i] = out_toks[i]
        return out_toks



def stage_L3_attn_b(inputs, pre):
    for nm in ("Wq", "Wk", "Wv"):
        W = np.asarray(inputs[f"L3_attn_b_{nm}"])
        pre[f"L3ab_{nm}T"] = np.ascontiguousarray(W.T).reshape(8, 128, 1024)
    for nm in ("bq", "bk", "bv"):
        pre[f"L3ab_{nm}"] = np.asarray(inputs[f"L3_attn_b_{nm}"]).reshape(1, 1024)
    pre["L3ab_gamma"] = np.asarray(inputs["L3_attn_b_gamma"]).reshape(1, 1)


def decl_L3_attn_b(nc):
    d = {}
    for nm in ("WqT", "WkT", "WvT"):
        d[nm] = nc.dram_tensor(f"L3ab_{nm}", [8, 128, 1024], F32, kind="ExternalInput").ap()
    for nm in ("bq", "bk", "bv"):
        d[nm] = nc.dram_tensor(f"L3ab_{nm}", [1, 1024], F32, kind="ExternalInput").ap()
    d["gamma"] = nc.dram_tensor("L3ab_gamma", [1, 1], F32, kind="ExternalInput").ap()
    return d


def emit_colnorm_inv(pr, tiles, tok, q2, psN, invn, sq_is_sum=False, eps_add=None):
    sq_toks = []
    for i in range(NT):
        def bsq(s, i=i):
            pr.wait(s, "s", tok)
            if i > 0:
                pr.wait(s, "s", mm_tok[0])
            pr.inc(s.activation(q2[:, :], tiles[i][:, :], AF.Square), "s")
        pr.blk(s=bsq)
        sqtok = pr.tok("s")

        def bmm(t, i=i, sqtok=sqtok):
            pr.wait(t, "t", sqtok, pr.ones_tok)
            last = None
            for sl in range(8):
                last = t.matmul(psN[:, sl:sl + 1], q2[:, sl * 128:(sl + 1) * 128],
                                pr.ones_f32[:, 0:1], start=(i == 0), stop=(i == NT - 1))
            pr.inc(last, "t")
        pr.blk(t=bmm)
        if i == 0:
            mm_tok = [None]
        mm_tok[0] = pr.tok("t")
    mmtok = pr.tok("t")

    def binv(s, mmtok=mmtok):
        pr.wait(s, "s", mmtok, pr.const_tok)
        pr.inc(s.activation(invn[:, :], psN[:, :], AF.Ln), "s")
        pr.wait(s, "s", pr.tok("s"))
        pr.inc(s.activation(invn[:, :], invn[:, :], AF.Exp, scale=-0.5), "s")
        pr.wait(s, "s", pr.tok("s"))
    pr.blk(s=binv)
    return pr.tok("s")


def emit_col_bcast(pr, stack, col8, col_tok, dram_bounce, psB, out_b, tag):
    nc = pr.nc
    with nc.sbuf_tensor(tag + "_t8", [8, 128], F32) as t8, \
         nc.sbuf_tensor(tag + "_row", [1, 1024], F32) as row, \
         nc.psum_tensor(tag + "_pst", [P, 128], F32) as pst:
        def btr(t):
            pr.wait(t, "t", col_tok, pr.ident_tok)
            pr.inc(t.transpose(pst[0:8, 0:128], col8[:, 0:8], pr.identf[:, :]), "t")
        pr.blk(t=btr)
        ttok = pr.tok("t")

        def bev(s, ttok=ttok):
            pr.wait(s, "s", ttok)
            pr.inc(s.copy(t8[:, :], pst[0:8, 0:128]), "s")
        pr.blk(s=bev)
        etok = pr.tok("s")

        def bdma(g, etok=etok):
            pr.wait(g, "g", etok)
            pr.inc(g.dma_start(out=dram_bounce, in_=t8[:, :]), "gd")
        pr.blk(g=bdma)
        d1 = pr.tok("gd")

        def bdma2(g, d1=d1):
            pr.wait(g, "g", d1)
            pr.inc(g.dma_start(out=row[:, :], in_=dram_bounce.rearrange("a b -> (a b)").unsqueeze(0)), "gd")
        pr.blk(g=bdma2)
        d2 = pr.tok("gd")

        rb_tok = emit_row_broadcast(pr, psB, row, d2, 1024)

        def bev2(s, rb_tok=rb_tok):
            pr.wait(s, "s", rb_tok)
            pr.inc(s.copy(out_b[:, :], psB[:, :]), "s")
        pr.blk(s=bev2)
    return pr.tok("s")


def emit_L3_attn_b(pr, stack, xio, lnT_tiles, lnT_tok, w):
    nc = pr.nc
    L = 1024.0
    with ExitStack() as ls:
        lctx = ls.enter_context
        Q = [lctx(nc.sbuf_tensor(f"l3b_Q{i}", [P, D], BF16)) for i in range(NT)]
        K = [lctx(nc.sbuf_tensor(f"l3b_K{i}", [P, D], BF16)) for i in range(NT)]
        V = [lctx(nc.sbuf_tensor(f"l3b_V{i}", [P, D], BF16)) for i in range(NT)]
        VT = [lctx(nc.sbuf_tensor(f"l3b_VT{c}", [P, TT], BF16)) for c in range(8)]
        q2 = lctx(nc.sbuf_tensor("l3b_q2", [P, D], F32))
        invq = lctx(nc.sbuf_tensor("l3b_invq", [P, 8], F32))
        invk = lctx(nc.sbuf_tensor("l3b_invk", [P, 8], F32))
        ksums = lctx(nc.sbuf_tensor("l3b_ksums", [P, 16], BF16))
        vsums = lctx(nc.sbuf_tensor("l3b_vsums", [P, 16], F32))
        ts8 = lctx(nc.sbuf_tensor("l3b_ts8", [P, 8], F32))
        gcol = lctx(nc.sbuf_tensor("l3b_gcol", [P, 1], F32))
        g11 = lctx(nc.sbuf_tensor("l3b_g11", [1, 1], F32))
        bounce = nc.dram_tensor("l3b_bounce", [8, 128], F32, kind="Internal").ap()
        psW = [lctx(nc.psum_tensor(f"l3b_psW{j}", [P, 1024], F32)) for j in range(2)]
        bc_stack = ExitStack()
        psN = bc_stack.enter_context(nc.psum_tensor("l3b_psN", [P, 8], F32))
        psB = bc_stack.enter_context(nc.psum_tensor("l3b_psB", [P, 1024], F32))

        def ldg(q):
            pr.inc(q.dma_start(out=g11[:, :], in_=w["gamma"]), "qd")
        pr.blk(q=ldg)
        g_tok = emit_scalar_bcast(pr, psN, gcol, g11, pr.tok("qd"))

        proj_stack = ExitStack()
        wbuf = proj_stack.enter_context(nc.sbuf_tensor("l3b_wbuf", [P, 8 * 1024], BF16))
        bias_b = proj_stack.enter_context(nc.sbuf_tensor("l3b_biasb", [P, 1024], F32))
        row1 = proj_stack.enter_context(nc.sbuf_tensor("l3b_row1", [1, 1024], F32))

        def load_bias(nm):
            def bb(q, nm=nm):
                pr.inc(q.dma_start(out=row1[:, :], in_=w[nm]), "qd")
            pr.blk(q=bb)
            rtok = pr.tok("qd")
            rb = emit_row_broadcast(pr, psB, row1, rtok, 1024)

            def ev(s, rb=rb):
                pr.wait(s, "s", rb)
                pr.inc(s.copy(bias_b[:, :], psB[:, :]), "s")
            pr.blk(s=ev)
            return pr.tok("s")

        btok = load_bias("bq")
        q_tok = emit_proj_tok(pr, lnT_tiles, lnT_tok, [w["WqT"][c] for c in range(8)],
                              wbuf, Q, psW, "q", war_tok=g_tok,
                              bias_ap=bias_b[:, :], bias_tok=btok)
        btok = load_bias("bk")
        k_tok = emit_proj_tok(pr, lnT_tiles, lnT_tok, [w["WkT"][c] for c in range(8)],
                              wbuf, K, psW, "k", war_tok=q_tok,
                              bias_ap=bias_b[:, :], bias_tok=btok)
        btok = load_bias("bv")
        v_tok = emit_proj_tok(pr, lnT_tiles, lnT_tok, [w["WvT"][c] for c in range(8)],
                              wbuf, V, psW, "v", war_tok=k_tok,
                              bias_ap=bias_b[:, :], bias_tok=btok)
        proj_stack.close()
        invqb = lctx(nc.sbuf_tensor("l3b_invqb", [P, 1024], F32))
        invkb = lctx(nc.sbuf_tensor("l3b_invkb", [P, 1024], F32))
        tsb = lctx(nc.sbuf_tensor("l3b_tsb", [P, 1024], F32))

        invq_tok = emit_colnorm_inv(pr, Q, q_tok, q2, psN, invq)
        invqb_tok = emit_col_bcast(pr, ls, invq, invq_tok, bounce, psB, invqb, "l3bq")
        invk_tok = emit_colnorm_inv(pr, K, k_tok, q2, psN, invk)
        invkb_tok = emit_col_bcast(pr, ls, invk, invk_tok, bounce, psB, invkb, "l3bk")

        def normqk(v):
            pr.wait(v, "v", invqb_tok, invkb_tok)
            for i in range(NT):
                pr.inc(v.tensor_tensor(Q[i][:, :], Q[i][:, :], invqb[:, :], ALU.mult), "v")
                pr.inc(v.tensor_tensor(K[i][:, :], K[i][:, :], invkb[:, :], ALU.mult), "v")
        pr.blk(v=normqk)
        qk_tok = pr.tok("v")

        def sums(v):
            pr.wait(v, "v", qk_tok, v_tok)
            for i in range(NT):
                pr.inc(v.reduce_sum(vsums[:, i:i + 1], V[i][:, :], AX.X), "v")
                pr.inc(v.reduce_sum(q2[:, i:i + 1], K[i][:, :], AX.X), "v")
            pr.wait(v, "v", pr.tok("v"))
            pr.inc(v.tensor_scalar(ksums[:, :], q2[:, 0:16], 1e-6, None, ALU.add), "v")
            pr.wait(v, "v", pr.tok("v"))
        pr.blk(v=sums)
        sums_tok = pr.tok("v")

        def mmts(t):
            pr.wait(t, "t", sums_tok)
            last = None
            for i in range(NT):
                for sl in range(8):
                    last = t.matmul(psN[:, sl:sl + 1], Q[i][:, sl * 128:(sl + 1) * 128],
                                    ksums[:, i:i + 1], start=(i == 0), stop=(i == NT - 1))
            pr.inc(last, "t")
        pr.blk(t=mmts)
        ts_mm = pr.tok("t")

        def tsinv(v, ts_mm=ts_mm):
            pr.wait(v, "v", ts_mm)
            pr.inc(v.tensor_scalar(ts8[:, :], psN[:, :], L, None, ALU.add), "v")
            pr.wait(v, "v", pr.tok("v"))
            pr.inc(v.reciprocal(ts8[:, :], ts8[:, :]), "v")
            pr.wait(v, "v", pr.tok("v"))
        pr.blk(v=tsinv)
        ts_tok = pr.tok("v")
        tsb_tok = emit_col_bcast(pr, ls, ts8, ts_tok, bounce, psB, tsb, "l3bt")
        bc_stack.close()

        with ExitStack() as vts:
            vt_tok = emit_transpose_fm(pr, vts, V, [v_tok] * NT, VT, tag="l3bvt")

        GT = lnT_tiles
        gt_toks = []
        for c in range(8):
            def mmg(t, c=c):
                pr.wait(t, "t", qk_tok)
                if len(gt_toks) >= 2:
                    pr.wait(t, "t", gt_toks[-2])
                last = None
                for mh in range(2):
                    for i in range(NT):
                        last = t.matmul(psW[c % 2][:, mh * 512:(mh + 1) * 512],
                                        K[i][:, c * 128:(c + 1) * 128],
                                        Q[i][:, mh * 512:(mh + 1) * 512],
                                        start=(i == 0), stop=(i == NT - 1))
                pr.inc(last, "t")
            pr.blk(t=mmg)
            gmm = pr.tok("t")

            def evg(s, c=c, gmm=gmm):
                pr.wait(s, "s", gmm)
                pr.inc(s.copy(GT[c][:, 0:1024], psW[c % 2][:, :]), "s")
            pr.blk(s=evg)
            gt_toks.append(pr.tok("s"))
        gt_tok = gt_toks[-1]

        o2 = q2
        out_toks = [None] * NT
        prev = {}
        for i in range(NT):
            bank = psW[i % 2]

            def mmo(t, i=i, bank=bank):
                pr.wait(t, "t", gt_tok, vt_tok)
                pr.wait(t, "t", prev.get(i - 2, gt_tok))
                last = None
                for mh in range(2):
                    for c in range(8):
                        last = t.matmul(bank[:, mh * 512:(mh + 1) * 512],
                                        VT[c][:, i * 128:(i + 1) * 128],
                                        GT[c][:, mh * 512:(mh + 1) * 512],
                                        start=(c == 0), stop=(c == 7))
                pr.inc(last, "t")
            pr.blk(t=mmo)
            mmtok = pr.tok("t")

            def evo(v, i=i, bank=bank, mmtok=mmtok):
                pr.wait(v, "v", mmtok, sums_tok, tsb_tok, g_tok)
                pr.inc(v.tensor_scalar(o2[:, :], bank[:, :], vsums[:, i:i + 1],
                                       None, ALU.add), "v")
                pr.wait(v, "v", pr.tok("v"))
                pr.inc(v.tensor_tensor(o2[:, :], o2[:, :], tsb[:, :], ALU.mult), "v")
            pr.blk(v=evo)
            otok = pr.tok("v")
            out_toks[i] = xio.add_store(i, o2[:, :], otok, scale_ap=gcol[:, 0:1])
            prev[i] = out_toks[i]
        return out_toks



def stage_L2_attn_a(inputs, pre):
    for nm in ("Wq", "Wk", "Wv"):
        pre[f"L2aa_{nm}"] = np.ascontiguousarray(np.asarray(inputs[f"L2_attn_a_{nm}"])).reshape(8, 128, 512)
    for nm in ("bq", "bk", "bv"):
        pre[f"L2aa_{nm}"] = np.asarray(inputs[f"L2_attn_a_{nm}"]).reshape(1, 512)
    pre["L2aa_Wo"] = np.ascontiguousarray(np.asarray(inputs["L2_attn_a_Wo"])).reshape(4, 128, 1024)
    pre["L2aa_bo"] = np.asarray(inputs["L2_attn_a_bo"]).reshape(1, 1024)
    al = np.asarray(inputs["L2_attn_a_alpha"]).reshape(512)
    pre["L2aa_alpha_row"] = al.reshape(1, 512)
    pre["L2aa_alpha_col"] = np.ascontiguousarray(al.reshape(4, 128).T)
    ag = np.asarray(inputs["L2_attn_a_aug_param"]).reshape(512)
    pre["L2aa_aug_col"] = np.ascontiguousarray(ag.reshape(4, 128).T)
    pre["L2aa_ln_scale"] = np.asarray(inputs["L2_attn_a_ln_scale"]).reshape(1, 1024)
    pre["L2aa_ln_bias"] = np.asarray(inputs["L2_attn_a_ln_bias"]).reshape(1, 1024)


def decl_L2_attn_a(nc):
    d = {}
    for nm in ("Wq", "Wk", "Wv"):
        d[nm] = nc.dram_tensor(f"L2aa_{nm}", [8, 128, 512], F32, kind="ExternalInput").ap()
    for nm in ("bq", "bk", "bv"):
        d[nm] = nc.dram_tensor(f"L2aa_{nm}", [1, 512], F32, kind="ExternalInput").ap()
    d["Wo"] = nc.dram_tensor("L2aa_Wo", [4, 128, 1024], F32, kind="ExternalInput").ap()
    d["bo"] = nc.dram_tensor("L2aa_bo", [1, 1024], F32, kind="ExternalInput").ap()
    d["alpha_row"] = nc.dram_tensor("L2aa_alpha_row", [1, 512], F32, kind="ExternalInput").ap()
    d["alpha_col"] = nc.dram_tensor("L2aa_alpha_col", [128, 4], F32, kind="ExternalInput").ap()
    d["aug_col"] = nc.dram_tensor("L2aa_aug_col", [128, 4], F32, kind="ExternalInput").ap()
    d["ln_scale"] = nc.dram_tensor("L2aa_ln_scale", [1, 1024], F32, kind="ExternalInput").ap()
    d["ln_bias"] = nc.dram_tensor("L2aa_ln_bias", [1, 1024], F32, kind="ExternalInput").ap()
    return d


def emit_L2_attn_a(pr, stack, xio, lnT_tiles, lnT_tok, w):
    nc = pr.nc
    with ExitStack() as ls:
        lctx = ls.enter_context
        qt = [lctx(nc.sbuf_tensor(f"l2aa_q{i}", [P, 512], BF16)) for i in range(NT)]
        kt = [lctx(nc.sbuf_tensor(f"l2aa_k{i}", [P, 512], BF16)) for i in range(NT)]
        vt = [lctx(nc.sbuf_tensor(f"l2aa_v{i}", [P, 512], BF16)) for i in range(NT)]
        ut = [lctx(nc.sbuf_tensor(f"l2aa_u{i}", [P, 512], BF16)) for i in range(NT)]
        uT = [lctx(nc.sbuf_tensor(f"l2aa_uT{c}", [P, TT], BF16)) for c in range(4)]
        qT = [lctx(nc.sbuf_tensor(f"l2aa_qT{c}", [P, TT], BF16)) for c in range(4)]
        stT = [lctx(nc.sbuf_tensor(f"l2aa_stT{c}", [P, TT], F32)) for c in range(4)]
        wbuf = lctx(nc.sbuf_tensor("l2aa_wbuf", [P, 8 * 512], BF16))
        row1 = lctx(nc.sbuf_tensor("l2aa_row1", [1, 1024], F32))
        bias_b = lctx(nc.sbuf_tensor("l2aa_biasb", [P, 512], F32))
        dec_b = lctx(nc.sbuf_tensor("l2aa_decb", [P, 512], F32))
        dcol = lctx(nc.sbuf_tensor("l2aa_dcol", [P, 4], F32))
        acol = lctx(nc.sbuf_tensor("l2aa_acol", [P, 4], F32))
        zt = lctx(nc.sbuf_tensor("l2aa_zt", [P, 8], F32))
        zs = lctx(nc.sbuf_tensor("l2aa_zs", [P, 4], F32))
        prod = lctx(nc.sbuf_tensor("l2aa_prod", [P, 1024], F32))
        psW = [lctx(nc.psum_tensor(f"l2aa_psW{j}", [P, 512], F32)) for j in range(2)]
        psB = lctx(nc.psum_tensor("l2aa_psB", [P, 1024], F32))

        def ldc(q):
            pr.inc(q.dma_start(out=dcol[:, 0:4], in_=w["alpha_col"]), "qd")
            pr.inc(q.dma_start(out=acol[:, 0:4], in_=w["aug_col"]), "qd")
            pr.inc(q.dma_start(out=row1[:, 0:512], in_=w["alpha_row"]), "qd")
        pr.blk(q=ldc)
        c_tok = pr.tok("qd")

        def sgc(s):
            pr.wait(s, "s", c_tok)
            pr.inc(s.activation(dcol[:, 0:4], dcol[:, 0:4], AF.Sigmoid), "s")
        pr.blk(s=sgc)
        dcol_tok = pr.tok("s")

        rb = emit_row_broadcast(pr, psB, row1[:, 0:512], c_tok, 512)

        def sgb(s, rb=rb):
            pr.wait(s, "s", rb)
            pr.inc(s.activation(dec_b[:, :], psB[:, 0:512], AF.Sigmoid), "s")
        pr.blk(s=sgb)

        def omd(v):
            pr.wait(v, "v", pr.tok("s"))
            pr.inc(v.tensor_scalar(dec_b[:, :], dec_b[:, :], -1.0, 1.0, ALU.mult, ALU.add), "v")
        pr.blk(v=omd)
        decb_tok = pr.tok("v")

        def load_bias(nm):
            def bb(q, nm=nm):
                pr.inc(q.dma_start(out=row1[:, 0:512], in_=w[nm]), "qd")
            pr.blk(q=bb)
            rtok = pr.tok("qd")
            rbl = emit_row_broadcast(pr, psB, row1[:, 0:512], rtok, 512)

            def ev(s, rbl=rbl):
                pr.wait(s, "s", rbl)
                pr.inc(s.copy(bias_b[:, :], psB[:, 0:512]), "s")
            pr.blk(s=ev)
            return pr.tok("s")

        btok = load_bias("bq")
        q_tok = emit_proj_tok(pr, lnT_tiles, lnT_tok, [w["Wq"][c] for c in range(8)],
                              wbuf, qt, psW, "q", out_dim=512, war_tok=decb_tok,
                              bias_ap=bias_b[:, :], bias_tok=btok)
        btok = load_bias("bk")
        k_tok = emit_proj_tok(pr, lnT_tiles, lnT_tok, [w["Wk"][c] for c in range(8)],
                              wbuf, kt, psW, "k", out_dim=512, war_tok=q_tok,
                              bias_ap=bias_b[:, :], bias_tok=btok)
        btok = load_bias("bv")
        v_tok = emit_proj_tok(pr, lnT_tiles, lnT_tok, [w["Wv"][c] for c in range(8)],
                              wbuf, vt, psW, "v", out_dim=512, war_tok=k_tok,
                              bias_ap=bias_b[:, :], bias_tok=btok)

        u_toks = []
        be_toks = []
        for i in range(NT):
            def bu(v, i=i):
                pr.wait(v, "v", q_tok, k_tok, v_tok, decb_tok)
                if i > 0:
                    pr.wait(v, "v", be_toks[i - 1])
                pr.inc(v.tensor_mul(prod[:, 0:512], qt[i][:, :], kt[i][:, :]), "v")
                pr.wait(v, "v", pr.tok("v"))
                p3 = prod[:, 0:512].rearrange("p (h d) -> p h d", d=64)
                pr.inc(v.tensor_reduce(zt[:, :], p3, AX.X, ALU.add), "v")
                pr.wait(v, "v", pr.tok("v"))
                pr.inc(v.tensor_scalar(zt[:, :], zt[:, :], 0.125, None, ALU.mult), "v")
                pr.wait(v, "v", pr.tok("v"))
                pr.inc(v.reduce_max(zs[:, 0:1], zt[:, :], AX.X, negate=True), "v")
                pr.wait(v, "v", pr.tok("v"))
            pr.blk(v=bu)
            vt1 = pr.tok("v")

            def be(s, i=i, vt1=vt1):
                pr.wait(s, "s", vt1)
                pr.inc(s.activation(zt[:, :], zt[:, :], AF.Exp, bias=zs[:, 0:1],
                                    accum_out=zs[:, 1:2]), "s")
            pr.blk(s=be)
            st1 = pr.tok("s")
            be_toks.append(st1)

            def bn(v, i=i, st1=st1):
                pr.wait(v, "v", st1)
                pr.inc(v.reciprocal(zs[:, 2:3], zs[:, 1:2]), "v")
                pr.wait(v, "v", pr.tok("v"))
                pr.inc(v.tensor_scalar(zt[:, :], zt[:, :], zs[:, 2:3], None, ALU.mult), "v")
                pr.wait(v, "v", pr.tok("v"))
                a3 = zt[:, :].unsqueeze(2).broadcast_to([P, 8, 64])
                v3 = vt[i][:, :].rearrange("p (h d) -> p h d", d=64)
                u3 = ut[i][:, :].rearrange("p (h d) -> p h d", d=64)
                pr.inc(v.tensor_tensor(u3, v3, a3, ALU.mult), "v")
                pr.wait(v, "v", pr.tok("v"))
                pr.inc(v.tensor_mul(ut[i][:, :], ut[i][:, :], dec_b[:, :]), "v")
            pr.blk(v=bn)
            u_toks.append(pr.tok("v"))

        with ExitStack() as ts1:
            psTa = ts1.enter_context(nc.psum_tensor("l2aa_psTa", [P, 512], BF16))
            psTb = ts1.enter_context(nc.psum_tensor("l2aa_psTb", [P, 512], BF16))
            evt = []
            gi = 0
            for c in range(4):
                for i0 in range(0, NT, 4):
                    bank = psTa if gi % 2 == 0 else psTb

                    def btr(t, c=c, i0=i0, bank=bank, gi=gi):
                        pr.wait(t, "t", pr.ident_tok)
                        if gi >= 2:
                            pr.wait(t, "t", evt[gi - 2])
                        last = None
                        for j in range(4):
                            pr.wait(t, "t", u_toks[i0 + j])
                            last = t.transpose(bank[:, j * 128:(j + 1) * 128],
                                               ut[i0 + j][:, c * 128:(c + 1) * 128],
                                               pr.ident[:, :])
                        pr.inc(last, "t")
                    pr.blk(t=btr)
                    ttok = pr.tok("t")

                    def bev(s, c=c, i0=i0, bank=bank, ttok=ttok):
                        pr.wait(s, "s", ttok)
                        pr.inc(s.copy(uT[c][:, i0 * 128:(i0 + 4) * 128], bank[:, :]), "s")
                    pr.blk(s=bev)
                    evt.append(pr.tok("s"))
                    gi += 1
            uT_tok = pr.tok("s")
        with ExitStack() as ts2:
            qT_tok = emit_transpose_fm(pr, ts2, qt, [q_tok] * NT, qT, n_chunk=4,
                                       tag="l2aaqt")

        scan_toks = []
        for c in range(4):
            def bscan(v, c=c):
                pr.wait(v, "v", uT_tok, dcol_tok)
                pr.inc(v.tensor_tensor_scan(stT[c][:, :],
                                            dcol[:, c:c + 1].broadcast_to([P, TT]),
                                            uT[c][:, :], 0.0,
                                            ALU.mult, ALU.add), "v")
            pr.blk(v=bscan)
            scan_toks.append(pr.tok("v"))

        outT = uT
        o_toks = []
        for c in range(4):
            def bo1(s, c=c):
                pr.wait(s, "s", scan_toks[c], qT_tok, c_tok)
                pr.inc(s.activation(prod[:, 0:TT] if False else stT[c][:, :],
                                    stT[c][:, :], AF.Copy) if False else None, "s") \
                    if False else None
                pr.inc(s.activation(outT[c][:, :], qT[c][:, :], AF.Sigmoid,
                                    scale=acol[:, c:c + 1]), "s")
            pr.blk(s=bo1)
            s1 = pr.tok("s")

            def bo2(v, c=c, s1=s1):
                pr.wait(v, "v", s1)
                pr.inc(v.scalar_tensor_tensor(stT[c][:, :], qT[c][:, :], 1.0,
                                              stT[c][:, :], ALU.mult, ALU.mult), "v")
                pr.wait(v, "v", pr.tok("v"))
                pr.inc(v.tensor_add(outT[c][:, :], outT[c][:, :], stT[c][:, :]), "v")
            pr.blk(v=bo2)
            o_toks.append(pr.tok("v"))
        obT = qT
        ob_toks = []
        for c in range(4):
            def bcv(v, c=c):
                pr.wait(v, "v", o_toks[c])
                pr.inc(v.tensor_copy(obT[c][:, :], outT[c][:, :]), "v")
            pr.blk(v=bcv)
            ob_toks.append(pr.tok("v"))
        obT_tok = pr.tok("v")

        def ldow(g):
            pr.wait(g, "g", v_tok)
            for c in range(4):
                pr.inc(g.dma_start(out=wbuf[:, c * 1024:(c + 1) * 1024],
                                   in_=w["Wo"][c]), "gd")
        pr.blk(g=ldow)
        ow_tok = pr.tok("gd")

        def ldlb(q):
            pr.inc(q.dma_start(out=row1[:, :], in_=w["bo"]), "qd")
        pr.blk(q=ldlb)
        bo_rtok = pr.tok("qd")
        bo_rb = emit_row_broadcast(pr, psB, row1, bo_rtok, 1024)
        bo_b = lctx(nc.sbuf_tensor("l2aa_bob", [P, 1024], F32))

        def evbo(s):
            pr.wait(s, "s", bo_rb)
            pr.inc(s.copy(bo_b[:, :], psB[:, :]), "s")
        pr.blk(s=evbo)
        bo_tok = pr.tok("s")

        def ldls(q):
            pr.wait(q, "q", bo_rb)
            pr.inc(q.dma_start(out=row1[:, :], in_=w["ln_scale"]), "qd")
        pr.blk(q=ldls)
        ls_rtok = pr.tok("qd")
        ls_rb = emit_row_broadcast(pr, psB, row1, ls_rtok, 1024)
        lsc_b = lctx(nc.sbuf_tensor("l2aa_lscb", [P, 1024], F32))

        def evls(s):
            pr.wait(s, "s", ls_rb)
            pr.inc(s.copy(lsc_b[:, :], psB[:, :]), "s")
        pr.blk(s=evls)
        lsc_tok = pr.tok("s")

        def ldlb2(q):
            pr.wait(q, "q", ls_rb)
            pr.inc(q.dma_start(out=row1[:, :], in_=w["ln_bias"]), "qd")
        pr.blk(q=ldlb2)
        lb_rtok = pr.tok("qd")
        lb_rb = emit_row_broadcast(pr, psB, row1, lb_rtok, 1024)
        lnb_b = lctx(nc.sbuf_tensor("l2aa_lnbb", [P, 1024], F32))

        def evlb(s):
            pr.wait(s, "s", lb_rb)
            pr.inc(s.copy(lnb_b[:, :], psB[:, :]), "s")
        pr.blk(s=evlb)
        lnb_tok = pr.tok("s")

        yout = lctx(nc.sbuf_tensor("l2aa_yout", [P, 1024], F32))
        stt = lctx(nc.sbuf_tensor("l2aa_stt", [P, 8], F32))
        psO = [lctx(nc.psum_tensor(f"l2aa_psO{j}", [P, 1024], F32)) for j in range(2)]
        out_toks = [None] * NT
        prev = {}
        for i in range(NT):
            bank = psO[i % 2]

            def mmo(t, i=i, bank=bank):
                pr.wait(t, "t", obT_tok, ow_tok)
                pr.wait(t, "t", prev.get(i - 2))
                last = None
                for mh in range(2):
                    for c in range(4):
                        last = t.matmul(bank[:, mh * 512:(mh + 1) * 512],
                                        obT[c][:, i * 128:(i + 1) * 128],
                                        wbuf[:, c * 1024 + mh * 512:c * 1024 + (mh + 1) * 512],
                                        start=(c == 0), stop=(c == 3))
                pr.inc(last, "t")
            pr.blk(t=mmo)
            mmtok = pr.tok("t")

            def ev1(v, i=i, bank=bank, mmtok=mmtok):
                pr.wait(v, "v", mmtok, bo_tok)
                pr.inc(v.tensor_add(yout[:, :], bank[:, :], bo_b[:, :]), "v")
                pr.wait(v, "v", pr.tok("v"))
                pr.inc(v.reduce_sum(stt[:, 0:1], yout[:, :], AX.X), "v")
            pr.blk(v=ev1)
            v1 = pr.tok("v")

            def ev2(s, i=i, v1=v1):
                pr.wait(s, "s", v1)
                pr.inc(s.activation(prod[:, 0:1024], yout[:, :], AF.Square,
                                    accum_out=stt[:, 1:2]), "s")
            pr.blk(s=ev2)
            s1 = pr.tok("s")

            def ev3(v, i=i, s1=s1):
                pr.wait(v, "v", s1)
                pr.inc(v.tensor_scalar(stt[:, 2:4], stt[:, 0:2], 1.0 / 1024, None, ALU.mult), "v")
                pr.wait(v, "v", pr.tok("v"))
                pr.inc(v.tensor_tensor(stt[:, 4:5], stt[:, 2:3], stt[:, 2:3], ALU.mult), "v")
                pr.wait(v, "v", pr.tok("v"))
                pr.inc(v.tensor_tensor(stt[:, 4:5], stt[:, 3:4], stt[:, 4:5], ALU.subtract), "v")
            pr.blk(v=ev3)
            v2 = pr.tok("v")

            def ev4(s, i=i, v2=v2):
                pr.wait(s, "s", v2, pr.const_tok)
                pr.inc(s.activation(stt[:, 5:6], stt[:, 4:5], AF.Ln, bias=cval(pr, 1e-6)), "s")
                pr.wait(s, "s", pr.tok("s"))
                pr.inc(s.activation(stt[:, 6:7], stt[:, 5:6], AF.Exp, scale=-0.5), "s")
            pr.blk(s=ev4)
            s2 = pr.tok("s")

            def ev5(v, i=i, s2=s2):
                pr.wait(v, "v", s2, lsc_tok, lnb_tok)
                pr.inc(v.scalar_tensor_tensor(stt[:, 7:8], stt[:, 2:3], -1.0,
                                              stt[:, 6:7], ALU.mult, ALU.mult), "v")
                pr.wait(v, "v", pr.tok("v"))
                pr.inc(v.tensor_scalar(yout[:, :], yout[:, :], stt[:, 6:7],
                                       stt[:, 7:8], ALU.mult, ALU.add), "v")
                pr.wait(v, "v", pr.tok("v"))
                pr.inc(v.tensor_mul(yout[:, :], yout[:, :], lsc_b[:, :]), "v")
                pr.wait(v, "v", pr.tok("v"))
                pr.inc(v.tensor_add(yout[:, :], yout[:, :], lnb_b[:, :]), "v")
            pr.blk(v=ev5)
            ytok = pr.tok("v")
            out_toks[i] = xio.add_store(i, yout[:, :], ytok)
            prev[i] = out_toks[i]
        return out_toks



def stage_L3_attn_a(inputs, pre):
    Wxp = np.asarray(inputs["L3_attn_a_W_xproj"])
    pre["L3aa2_WxpT"] = np.ascontiguousarray(Wxp.T).reshape(8, 128, 96)
    Wdt = np.asarray(inputs["L3_attn_a_W_dt"])
    pre["L3aa2_WdtT"] = np.ascontiguousarray(Wdt.T)
    pre["L3aa2_bdt"] = np.ascontiguousarray(
        np.asarray(inputs["L3_attn_a_b_dt"]).reshape(8, 128).T)
    Alog = np.asarray(inputs["L3_attn_a_A_log"])
    pre["L3aa2_AlogT"] = np.ascontiguousarray(Alog.reshape(8, 128, 16)).reshape(8, 128 * 16) \
        .reshape(8, 128, 16).transpose(1, 0, 2).reshape(128, 128)
    pre["L3aa2_Dcol"] = np.ascontiguousarray(
        np.asarray(inputs["L3_attn_a_D"]).reshape(8, 128).T)


def decl_L3_attn_a(nc):
    return {
        "WxpT": nc.dram_tensor("L3aa2_WxpT", [8, 128, 96], F32, kind="ExternalInput").ap(),
        "WdtT": nc.dram_tensor("L3aa2_WdtT", [64, 1024], F32, kind="ExternalInput").ap(),
        "bdt": nc.dram_tensor("L3aa2_bdt", [128, 8], F32, kind="ExternalInput").ap(),
        "AlogT": nc.dram_tensor("L3aa2_AlogT", [128, 128], F32, kind="ExternalInput").ap(),
        "Dcol": nc.dram_tensor("L3aa2_Dcol", [128, 8], F32, kind="ExternalInput").ap(),
    }


def emit_L3_attn_a(pr, stack, xio, lnT_tiles, lnT_tok, w):
    nc = pr.nc
    import ml_dtypes
    sel_np = np.zeros((16, 16 * 128), ml_dtypes.bfloat16)
    for n in range(16):
        sel_np[n, n * 128:(n + 1) * 128] = 1.0
    sel_dram = nc.inline_tensor(sel_np, "l3aa2_sel")
    with ExitStack() as ls:
        lctx = ls.enter_context
        xp = [lctx(nc.sbuf_tensor(f"m_xp{i}", [P, 96], BF16)) for i in range(NT)]
        xpTdt = lctx(nc.sbuf_tensor("m_xpTdt", [64, TT], BF16))
        Bt = lctx(nc.sbuf_tensor("m_Bt", [16, TT], BF16))
        Ct = lctx(nc.sbuf_tensor("m_Ct", [16, TT], BF16))
        sel = lctx(nc.sbuf_tensor("m_sel", [16, 16 * 128], BF16))
        wdt = lctx(nc.sbuf_tensor("m_wdt", [64, 1024], BF16))
        wxp = lctx(nc.sbuf_tensor("m_wxp", [P, 8 * 96], BF16))
        bdt = lctxnc = lctx(nc.sbuf_tensor("m_bdt", [P, 8], F32))
        Acols = lctx(nc.sbuf_tensor("m_Acol", [P, 128], F32))
        Dcol = lctx(nc.sbuf_tensor("m_Dcol", [P, 8], F32))
        dT = lctx(nc.sbuf_tensor("m_dT", [P, TT], BF16))
        dxT = lctx(nc.sbuf_tensor("m_dxT", [P, TT], BF16))
        yTc = lctx(nc.sbuf_tensor("m_yTc", [P, TT], F32))
        da = lctx(nc.sbuf_tensor("m_da", [P, TT], F32))
        ub = lctx(nc.sbuf_tensor("m_ub", [P, TT], F32))
        hb = lctx(nc.sbuf_tensor("m_hb", [P, TT], BF16))
        ytok = [lctx(nc.sbuf_tensor(f"m_y{i}", [P, D], BF16)) for i in range(NT)]
        sp1 = lctx(nc.sbuf_tensor("m_sp1", [P, 512], F32))
        sp2 = lctx(nc.sbuf_tensor("m_sp2", [P, 512], F32))
        psA = [lctx(nc.psum_tensor(f"m_psA{j}", [P, 512], F32)) for j in range(2)]
        psTB = lctx(nc.psum_tensor("m_psTB", [P, 384], BF16))
        psR = lctx(nc.psum_tensor("m_psR", [P, TT], F32))

        def ldc(q):
            pr.inc(q.dma_start(out=sel[:, :], in_=sel_dram.ap()), "qd")
            pr.inc(q.dma_start(out=bdt[:, :], in_=w["bdt"]), "qd")
            pr.inc(q.dma_start(out=Acols[:, :], in_=w["AlogT"]), "qd")
            pr.inc(q.dma_start(out=Dcol[:, :], in_=w["Dcol"]), "qd")
        pr.blk(q=ldc)
        c_tok = pr.tok("qd")

        def ldw(g):
            pr.inc(g.dma_start(out=wdt[:, :], in_=w["WdtT"]), "gd")
            for c in range(8):
                pr.inc(g.dma_start(out=wxp[:, c * 96:(c + 1) * 96], in_=w["WxpT"][c]), "gd")
        pr.blk(g=ldw)
        w_tok = pr.tok("gd")

        def mkA(s):
            pr.wait(s, "s", c_tok)
            pr.inc(s.activation(Acols[:, :], Acols[:, :], AF.Exp), "s")
        pr.blk(s=mkA)
        sA = pr.tok("s")

        def mkA2(v):
            pr.wait(v, "v", sA)
            pr.inc(v.tensor_scalar(Acols[:, :], Acols[:, :], -1.0, None, ALU.mult), "v")
        pr.blk(v=mkA2)
        A_tok = pr.tok("v")

        xp_done = {}
        for i in range(NT):
            bank = psA[i % 2]

            def mmxp(t, i=i, bank=bank):
                pr.wait(t, "t", lnT_tok, w_tok)
                pr.wait(t, "t", xp_done.get(i - 2))
                last = None
                for c in range(8):
                    last = t.matmul(bank[:, 0:96],
                                    lnT_tiles[c][:, i * 128:(i + 1) * 128],
                                    wxp[:, c * 96:(c + 1) * 96],
                                    start=(c == 0), stop=(c == 7))
                pr.inc(last, "t")
            pr.blk(t=mmxp)
            mmtok = pr.tok("t")

            def evxp(s, i=i, bank=bank, mmtok=mmtok):
                pr.wait(s, "s", mmtok)
                pr.inc(s.copy(xp[i][:, :], bank[:, 0:96]), "s")
            pr.blk(s=evxp)
            xp_done[i] = pr.tok("s")
        xp_tok = pr.tok("s")

        tr_done = {}
        for i in range(NT):
            bank = psA[i % 2]

            def btr(t, i=i):
                pr.wait(t, "t", xp_done[i], pr.ident_tok)
                pr.wait(t, "t", tr_done.get(i - 1))
                t.transpose(psTB[0:64, 0:128], xp[i][:, 0:64], pr.ident[:, :])
                t.transpose(psTB[0:16, 128:256], xp[i][:, 64:80], pr.ident[:, :])
                last = t.transpose(psTB[0:16, 256:384], xp[i][:, 80:96], pr.ident[:, :])
                pr.inc(last, "t")
            pr.blk(t=btr)
            ttok = pr.tok("t")

            def bev(s, i=i, ttok=ttok):
                pr.wait(s, "s", ttok)
                pr.inc(s.copy(xpTdt[:, i * 128:(i + 1) * 128], psTB[0:64, 0:128]), "s")
                pr.inc(s.copy(Bt[:, i * 128:(i + 1) * 128], psTB[0:16, 128:256]), "s")
                pr.inc(s.copy(Ct[:, i * 128:(i + 1) * 128], psTB[0:16, 256:384]), "s")
            pr.blk(s=bev)
            tr_done[i] = pr.tok("s")
        tr_tok = pr.tok("s")

        out_toks = [None] * NT
        psY = [lctx(nc.psum_tensor("m_psY0", [P, 128], BF16))] * 2
        prev_c = {}
        ytok_done = {}
        for c in range(8):
            dsp_done = {}
            for tg in range(4):
                bank = psA[tg % 2]

                def mmd(t, c=c, tg=tg, bank=bank):
                    pr.wait(t, "t", tr_tok, w_tok)
                    pr.wait(t, "t", dsp_done.get(tg - 2, prev_c.get(c - 1)))
                    pr.inc(t.matmul(bank[:, :], wdt[:, c * 128:(c + 1) * 128],
                                    xpTdt[:, tg * 512:(tg + 1) * 512],
                                    start=True, stop=True), "t")
                pr.blk(t=mmd)
                mmtok = pr.tok("t")

                def sp_a(v, c=c, tg=tg, bank=bank, mmtok=mmtok):
                    pr.wait(v, "v", mmtok)
                    pr.inc(v.tensor_scalar(sp1[:, :], bank[:, :], bdt[:, c:c + 1],
                                           None, ALU.add), "v")
                pr.blk(v=sp_a)
                v1 = pr.tok("v")

                def sp_b(s, v1=v1):
                    pr.wait(s, "s", v1)
                    pr.inc(s.activation(sp2[:, :], sp1[:, :], AF.Abs), "s")
                    pr.wait(s, "s", pr.tok("s"))
                    pr.inc(s.activation(sp2[:, :], sp2[:, :], AF.Exp, scale=-1.0), "s")
                    pr.wait(s, "s", pr.tok("s"))
                    pr.inc(s.activation(sp2[:, :], sp2[:, :], AF.Ln, bias=cval(pr, 1.0)), "s")
                    pr.wait(s, "s", pr.tok("s"))
                    pr.inc(s.activation(sp1[:, :], sp1[:, :], AF.Relu), "s")
                pr.blk(s=sp_b)
                s1 = pr.tok("s")

                def sp_c(v, c=c, tg=tg, s1=s1):
                    pr.wait(v, "v", s1)
                    pr.inc(v.tensor_add(dT[:, tg * 512:(tg + 1) * 512], sp1[:, :],
                                        sp2[:, :]), "v")
                pr.blk(v=sp_c)
                dsp_done[tg] = pr.tok("v")
            dT_tok = pr.tok("v")

            def mkdx(v, c=c, dT_tok=dT_tok):
                pr.wait(v, "v", dT_tok, lnT_tok)
                pr.inc(v.tensor_mul(dxT[:, :], dT[:, :], lnT_tiles[c][:, :]), "v")
                pr.wait(v, "v", pr.tok("v"))
                pr.inc(v.tensor_scalar(yTc[:, :], lnT_tiles[c][:, :], Dcol[:, c:c + 1],
                                       None, ALU.mult), "v")
            pr.blk(v=mkdx)
            dx_tok = pr.tok("v")

            for n in range(16):
                def mmb(t, n=n, dx_tok=dx_tok):
                    pr.wait(t, "t", tr_tok, dx_tok)
                    pr.wait(t, "t", crep_used.get((c, n - 1)) if n > 0 else dx_tok)
                    last = None
                    for tg in range(4):
                        last = t.matmul(psR[:, tg * 512:(tg + 1) * 512],
                                        sel[:, n * 128:(n + 1) * 128],
                                        Bt[:, tg * 512:(tg + 1) * 512],
                                        start=True, stop=True)
                    pr.inc(last, "t")
                pr.blk(t=mmb)
                if c == 0 and n == 0:
                    crep_used = {}
                brep_tok = pr.tok("t")

                def mku(v, c=c, n=n, brep_tok=brep_tok):
                    pr.wait(v, "v", brep_tok)
                    pr.inc(v.tensor_mul(ub[:, :], dxT[:, :], psR[:, :]), "v")
                    pr.wait(v, "v", pr.tok("v"))
                    pr.inc(v.tensor_scalar(da[:, :], dT[:, :], Acols[:, c * 16 + n:c * 16 + n + 1],
                                           None, ALU.mult), "v")
                pr.blk(v=mku)
                u_tok = pr.tok("v")

                def mkda(s, u_tok=u_tok):
                    pr.wait(s, "s", u_tok)
                    pr.inc(s.activation(da[:, :], da[:, :], AF.Exp), "s")
                pr.blk(s=mkda)
                da_tok = pr.tok("s")

                def mkscan(v, da_tok=da_tok):
                    pr.wait(v, "v", da_tok)
                    pr.inc(v.tensor_tensor_scan(hb[:, :], da[:, :], ub[:, :], 0.0,
                                                ALU.mult, ALU.add), "v")
                pr.blk(v=mkscan)
                h_tok = pr.tok("v")

                def mmc(t, n=n, h_tok=h_tok):
                    pr.wait(t, "t", u_tok)
                    last = None
                    for tg in range(4):
                        last = t.matmul(psR[:, tg * 512:(tg + 1) * 512],
                                        sel[:, n * 128:(n + 1) * 128],
                                        Ct[:, tg * 512:(tg + 1) * 512],
                                        start=True, stop=True)
                    pr.inc(last, "t")
                pr.blk(t=mmc)
                crep_tok = pr.tok("t")

                def acc(v, c=c, n=n, crep_tok=crep_tok, h_tok=h_tok):
                    pr.wait(v, "v", crep_tok, h_tok)
                    pr.inc(v.tensor_mul(da[:, :], hb[:, :], psR[:, :]), "v")
                    pr.wait(v, "v", pr.tok("v"))
                    pr.inc(v.tensor_add(yTc[:, :], yTc[:, :], da[:, :]), "v")
                pr.blk(v=acc)
                crep_used[(c, n)] = pr.tok("v")
            yc_tok = pr.tok("v")

            def ycast(v, c=c, yc_tok=yc_tok):
                pr.wait(v, "v", yc_tok)
                pr.inc(v.tensor_copy(dxT[:, :], yTc[:, :]), "v")
            pr.blk(v=ycast)
            ycb_tok = pr.tok("v")

            for i in range(NT):
                bank = psY[i % 2]

                def ytr(t, c=c, i=i, bank=bank, ycb_tok=ycb_tok):
                    pr.wait(t, "t", ycb_tok, pr.ident_tok)
                    pr.wait(t, "t", ytok_done.get((c, i - 1), ytok_done.get((c - 1, NT - 1))))
                    pr.inc(t.transpose(bank[:, :], dxT[:, i * 128:(i + 1) * 128],
                                       pr.ident[:, :]), "t")
                pr.blk(t=ytr)
                ttok = pr.tok("t")

                def yev(s, c=c, i=i, bank=bank, ttok=ttok):
                    pr.wait(s, "s", ttok)
                    pr.inc(s.copy(ytok[i][:, c * 128:(c + 1) * 128], bank[:, :]), "s")
                pr.blk(s=yev)
                ytok_done[(c, i)] = pr.tok("s")
            prev_c[c] = pr.tok("s")
        y_tok = pr.tok("s")

        for i in range(NT):
            out_toks[i] = xio.add_store(i, ytok[i][:, :], y_tok)
        return out_toks



def stage_L2_ffn(inputs, pre):
    wg = np.asarray(inputs["L2_ffn_w_gate"])
    wu = np.asarray(inputs["L2_ffn_w_up"])
    wd = np.asarray(inputs["L2_ffn_w_down"])
    pre["L2f_wgT"] = np.ascontiguousarray(wg.T).reshape(8, 128, 4096)
    pre["L2f_wuT"] = np.ascontiguousarray(wu.T).reshape(8, 128, 4096)
    pre["L2f_wdT"] = np.ascontiguousarray(wd.T).reshape(32, 128, 1024)


def decl_L2_ffn(nc):
    return {
        "wgT": nc.dram_tensor("L2f_wgT", [8, 128, 4096], F32, kind="ExternalInput").ap(),
        "wuT": nc.dram_tensor("L2f_wuT", [8, 128, 4096], F32, kind="ExternalInput").ap(),
        "wdT": nc.dram_tensor("L2f_wdT", [32, 128, 1024], F32, kind="ExternalInput").ap(),
    }


def emit_ternarize(pr, stack, wdram, n_chunks, width, outs_dram, mwcol, tag, uid=[0]):
    nc = pr.nc
    uid[0] += 1
    u = f"{tag}{uid[0]}"
    with ExitStack() as ls:
        lctx = ls.enter_context
        wbufA = lctx(nc.sbuf_tensor(u + "_wA", [P, width], F32))
        wbufB = lctx(nc.sbuf_tensor(u + "_wB", [P, width], F32))
        absb = lctx(nc.sbuf_tensor(u + "_abs", [P, width], BF16))
        acc = lctx(nc.sbuf_tensor(u + "_acc", [P, n_chunks], F32))
        tot = lctx(nc.sbuf_tensor(u + "_tot", [P, 2], F32))
        sgn = lctx(nc.sbuf_tensor(u + "_sgn", [P, width], BF16))
        msk = lctx(nc.sbuf_tensor(u + "_msk", [P, width], BF16))
        ps1 = lctx(nc.psum_tensor(u + "_ps1", [P, 8], F32))
        bufs = [wbufA, wbufB]
        ld = {}
        ab = {}
        for c in range(n_chunks):
            b = bufs[c % 2]

            def bl(g, c=c, b=b):
                pr.wait(g, "g", ab.get(c - 2))
                pr.inc(g.dma_start(out=b[:, :], in_=wdram[c]), "gd")
            pr.blk(g=bl)
            ld[c] = pr.tok("gd")

            def ba(s, c=c, b=b):
                pr.wait(s, "s", ld[c])
                pr.inc(s.activation(absb[:, :], b[:, :], AF.Abs,
                                    accum_out=acc[:, c:c + 1]), "s")
            pr.blk(s=ba)
            ab[c] = pr.tok("s")
        abs_tok = pr.tok("s")

        def red(v):
            pr.wait(v, "v", abs_tok)
            pr.inc(v.reduce_sum(tot[:, 0:1], acc[:, 0:n_chunks], AX.X), "v")
        pr.blk(v=red)
        rtok = pr.tok("v")

        def mmt(t, rtok=rtok):
            pr.wait(t, "t", rtok, pr.ones_tok)
            pr.inc(t.matmul(ps1[0:1, 0:1], tot[:, 0:1], pr.ones_f32[:, 0:1],
                            start=True, stop=True), "t")
        pr.blk(t=mmt)
        mtok = pr.tok("t")

        def evt(s, mtok=mtok):
            pr.wait(s, "s", mtok)
            pr.inc(s.copy(tot[0:1, 1:2], ps1[0:1, 0:1]), "s")
        pr.blk(s=evt)
        etok = pr.tok("s")
        rb = emit_row_broadcast(pr, ps1, tot[0:1, 1:2], etok, 1)

        def mkmw(v, rb=rb):
            pr.wait(v, "v", rb)
            pr.inc(v.tensor_scalar(mwcol[:, 0:1], ps1[:, 0:1],
                                   1.0 / (n_chunks * 128 * width), 1e-5,
                                   ALU.mult, ALU.max), "v")
            pr.wait(v, "v", pr.tok("v"))
            pr.inc(v.tensor_scalar(mwcol[:, 1:2], mwcol[:, 0:1], 0.5, None, ALU.mult), "v")
        pr.blk(v=mkmw)
        mw_tok = pr.tok("v")

        st = {}
        for c in range(n_chunks):
            b = bufs[c % 2]

            def bl2(g, c=c, b=b):
                pr.wait(g, "g", st.get(c - 2, mw_tok))
                pr.inc(g.dma_start(out=b[:, :], in_=wdram[c]), "gd")
            pr.blk(g=bl2)
            l2 = pr.tok("gd")

            def bs(s, c=c, b=b, l2=l2):
                pr.wait(s, "s", l2)
                pr.inc(s.activation(sgn[:, :], b[:, :], AF.Sign), "s")
                pr.wait(s, "s", pr.tok("s"))
                pr.inc(s.activation(absb[:, :], b[:, :], AF.Abs), "s")
            pr.blk(s=bs)
            stok = pr.tok("s")

            def bm(v, c=c, stok=stok):
                pr.wait(v, "v", stok, mw_tok)
                pr.inc(v.tensor_scalar(msk[:, :], absb[:, :], mwcol[:, 1:2],
                                       None, ALU.is_gt), "v")
                pr.wait(v, "v", pr.tok("v"))
                pr.inc(v.tensor_mul(msk[:, :], msk[:, :], sgn[:, :]), "v")
            pr.blk(v=bm)
            vtok = pr.tok("v")

            def bst(g, c=c, vtok=vtok):
                pr.wait(g, "g", vtok)
                pr.inc(g.dma_start(out=outs_dram[c], in_=msk[:, :]), "gd")
            pr.blk(g=bst)
            st[c] = pr.tok("gd")
    return pr.tok("gd"), mw_tok


def emit_L2_ffn(pr, stack, xio, lnT_tiles, lnT_tok, ln_tiles, ln_toks, w, uid):
    nc = pr.nc
    wgq_d = nc.dram_tensor(f"l2f_wgq{uid}", [8, 128, 4096], BF16, kind="Internal").ap()
    wuq_d = nc.dram_tensor(f"l2f_wuq{uid}", [8, 128, 4096], BF16, kind="Internal").ap()
    wdq_d = nc.dram_tensor(f"l2f_wdq{uid}", [32, 128, 1024], BF16, kind="Internal").ap()
    with ExitStack() as ls:
        lctx = ls.enter_context
        mws = lctx(nc.sbuf_tensor("l2f_mws", [P, 6], F32))
        t1, _ = emit_ternarize(pr, ls, [w["wgT"][c] for c in range(8)], 8, 4096,
                               [wgq_d[c] for c in range(8)], mws[:, 0:2], "l2f_tg")
        t2, _ = emit_ternarize(pr, ls, [w["wuT"][c] for c in range(8)], 8, 4096,
                               [wuq_d[c] for c in range(8)], mws[:, 2:4], "l2f_tu")
        t3, _ = emit_ternarize(pr, ls, [w["wdT"][c] for c in range(32)], 32, 1024,
                               [wdq_d[c] for c in range(32)], mws[:, 4:6], "l2f_td")

        xqT = [lctx(nc.sbuf_tensor(f"l2f_xqT{c}", [P, TT], BF16)) for c in range(8)]
        scol = lctx(nc.sbuf_tensor("l2f_scol", [P, 64], F32))
        qs = lctx(nc.sbuf_tensor("l2f_qs", [P, 1024], F32))
        qi = lctx(nc.sbuf_tensor("l2f_qi", [P, 1024], I32))

        with ExitStack() as xs:
            xq = [xs.enter_context(nc.sbuf_tensor(f"l2f_xq{i}", [P, D], BF16))
                  for i in range(NT)]
            xq_toks = []
            for i in range(NT):
                def aq(v, i=i):
                    pr.wait(v, "v", ln_toks[i])
                    if xq_toks:
                        pr.wait(v, "v", xq_toks[-1])
                    pr.inc(v.tensor_reduce(scol[:, i:i + 1], ln_tiles[i][:, :], AX.X,
                                           ALU.max, apply_absolute_value=True), "v")
                    pr.wait(v, "v", pr.tok("v"))
                    pr.inc(v.tensor_scalar(scol[:, i:i + 1], scol[:, i:i + 1], 1e-5,
                                           None, ALU.max), "v")
                    pr.wait(v, "v", pr.tok("v"))
                    pr.inc(v.tensor_scalar(scol[:, 16 + i:17 + i], scol[:, i:i + 1],
                                           1.0 / 127.0, None, ALU.mult), "v")
                    pr.inc(v.reciprocal(scol[:, 32 + i:33 + i], scol[:, i:i + 1]), "v")
                    pr.wait(v, "v", pr.tok("v"))
                    pr.inc(v.tensor_scalar(scol[:, 32 + i:33 + i], scol[:, 32 + i:33 + i],
                                           127.0, None, ALU.mult), "v")
                    pr.wait(v, "v", pr.tok("v"))
                    pr.inc(v.tensor_scalar(qs[:, 0:D], ln_tiles[i][:, :],
                                           scol[:, 32 + i:33 + i], None, ALU.mult), "v")
                    pr.wait(v, "v", pr.tok("v"))
                    pr.inc(v.tensor_copy(qi[:, 0:D], qs[:, 0:D]), "v")
                    pr.wait(v, "v", pr.tok("v"))
                    pr.inc(v.tensor_copy(xq[i][:, :], qi[:, 0:D]), "v")
                pr.blk(v=aq)
                xq_toks.append(pr.tok("v"))
            xqT_tok = emit_transpose_fm(pr, xs, xq, xq_toks, xqT, tag="l2fxq" + str(uid))

        def mkc(v):
            pr.wait(v, "v", xq_toks[-1], t1, t2, t3)
            pr.inc(v.tensor_scalar(scol[:, 48:64], scol[:, 16:32], mws[:, 0:1],
                                   None, ALU.mult), "v")
        pr.blk(v=mkc)
        cg_tok = pr.tok("v")

        hq = [lctx(nc.sbuf_tensor(f"l2f_h{j}", [P, 4096], BF16)) for j in range(4)]
        hqT = [lctx(nc.sbuf_tensor(f"l2f_hqT{o}", [P, 512], BF16)) for o in range(32)]
        wsl = lctx(nc.sbuf_tensor("l2f_wsl", [P, 8192], BF16))
        wdsl = lctx(nc.sbuf_tensor("l2f_wdsl", [P, 2 * 1024], BF16))
        sgb = lctx(nc.sbuf_tensor("l2f_sgb", [P, 512], F32))
        hs = lctx(nc.sbuf_tensor("l2f_hs", [P, 16], F32))
        psA = lctx(nc.psum_tensor("l2f_psA", [P, 512], F32))
        psB = lctx(nc.psum_tensor("l2f_psB", [P, 512], F32))
        psT = lctx(nc.psum_tensor("l2f_psT", [P, 512], BF16))
        psD = [lctx(nc.psum_tensor(f"l2f_psD{j}", [P, 1024], F32)) for j in range(2)]

        out_toks = [None] * NT
        prev_q = {}
        dprev = {}
        for qt in range(4):
            hdone = {}
            for og in range(8):
                w0 = 0

                def lw(g, qt=qt, og=og, w0=w0):
                    if og == 0 and qt == 0:
                        pr.wait(g, "g", t1, t2)
                    pr.wait(g, "g", hdone.get(og - 1, prev_q.get(qt - 1)))
                    for c in range(8):
                        pr.inc(g.dma_start(out=wsl[:, w0 + c * 512:w0 + (c + 1) * 512],
                                           in_=wgq_d[c][:, og * 512:(og + 1) * 512]), "gd")
                        pr.inc(g.dma_start(out=wsl[:, w0 + 4096 + c * 512:w0 + 4096 + (c + 1) * 512],
                                           in_=wuq_d[c][:, og * 512:(og + 1) * 512]), "gd")
                pr.blk(g=lw)
                wtok = pr.tok("gd")

                for j in range(4):
                    i = qt * 4 + j

                    def mmgu(t, i=i, og=og, w0=w0, wtok=wtok, j=j):
                        pr.wait(t, "t", wtok, xqT_tok)
                        pr.wait(t, "t", hdone.get((og - 1, j)))
                        last = None
                        for c in range(8):
                            last = t.matmul(psA[:, :], xqT[c][:, i * 128:(i + 1) * 128],
                                            wsl[:, w0 + c * 512:w0 + (c + 1) * 512],
                                            start=(c == 0), stop=(c == 7))
                        for c in range(8):
                            last = t.matmul(psB[:, :], xqT[c][:, i * 128:(i + 1) * 128],
                                            wsl[:, w0 + 4096 + c * 512:w0 + 4096 + (c + 1) * 512],
                                            start=(c == 0), stop=(c == 7))
                        pr.inc(last, "t")
                    pr.blk(t=mmgu)
                    mmtok = pr.tok("t")

                    def evs(s, i=i, mmtok=mmtok):
                        pr.wait(s, "s", mmtok, cg_tok)
                        pr.inc(s.activation(sgb[:, :], psA[:, :], AF.Silu,
                                            scale=scol[:, 48 + i:49 + i]), "s")
                    pr.blk(s=evs)
                    sil = pr.tok("s")

                    def evh(v, i=i, j=j, og=og, sil=sil):
                        pr.wait(v, "v", sil)
                        pr.inc(v.scalar_tensor_tensor(qs[:, 0:512], psB[:, :],
                                                      scol[:, 16 + i:17 + i], sgb[:, :],
                                                      ALU.mult, ALU.mult), "v")
                        pr.wait(v, "v", pr.tok("v"))
                        pr.inc(v.tensor_scalar(hq[j][:, og * 512:(og + 1) * 512],
                                               qs[:, 0:512], mws[:, 2:3], None,
                                               ALU.mult), "v")
                    pr.blk(v=evh)
                    hdone[(og, j)] = pr.tok("v")
                hdone[og] = pr.tok("v")

            trq = {}
            for j in range(4):
                i = qt * 4 + j

                def qh(v, i=i, j=j):
                    pr.wait(v, "v", hdone[(7, j)])
                    if trq:
                        pr.wait(v, "v", list(trq.values())[-1])
                    pr.inc(v.tensor_reduce(hs[:, j:j + 1], hq[j][:, :], AX.X,
                                           ALU.max, apply_absolute_value=True), "v")
                    pr.wait(v, "v", pr.tok("v"))
                    pr.inc(v.tensor_scalar(hs[:, j:j + 1], hs[:, j:j + 1], 1e-5,
                                           None, ALU.max), "v")
                    pr.wait(v, "v", pr.tok("v"))
                    pr.inc(v.tensor_scalar(hs[:, 8 + j:9 + j], hs[:, j:j + 1],
                                           1.0 / 127.0, None, ALU.mult), "v")
                    pr.inc(v.reciprocal(hs[:, 4 + j:5 + j], hs[:, j:j + 1]), "v")
                    pr.wait(v, "v", pr.tok("v"))
                    pr.inc(v.tensor_scalar(hs[:, 4 + j:5 + j], hs[:, 4 + j:5 + j],
                                           127.0, None, ALU.mult), "v")
                    pr.wait(v, "v", pr.tok("v"))
                    for rr in range(4):
                        sl = slice(rr * 1024, (rr + 1) * 1024)
                        pr.inc(v.tensor_scalar(qs[:, :], hq[j][:, sl], hs[:, 4 + j:5 + j],
                                               None, ALU.mult), "v")
                        pr.wait(v, "v", pr.tok("v"))
                        pr.inc(v.tensor_copy(qi[:, :], qs[:, :]), "v")
                        pr.wait(v, "v", pr.tok("v"))
                        pr.inc(v.tensor_copy(hq[j][:, sl], qi[:, :]), "v")
                        pr.wait(v, "v", pr.tok("v"))
                pr.blk(v=qh)
                qtok = pr.tok("v")

                for o in range(32):
                    def htr(t, j=j, o=o, qtok=qtok):
                        pr.wait(t, "t", qtok, pr.ident_tok)
                        pr.wait(t, "t", trq.get((j, o - 1), trq.get((j - 1, 31))))
                        pr.inc(t.transpose(psT[:, 0:128],
                                           hq[j][:, o * 128:(o + 1) * 128],
                                           pr.ident[:, :]), "t")
                    pr.blk(t=htr)
                    ttk = pr.tok("t")

                    def hev(s, j=j, o=o, ttk=ttk):
                        pr.wait(s, "s", ttk)
                        pr.inc(s.copy(hqT[o // 4 * 4 + o % 4][:, j * 128:(j + 1) * 128]
                                      if False else hqT[o][:, j * 128:(j + 1) * 128],
                                      psT[:, 0:128]), "s")
                    pr.blk(s=hev)
                    trq[(j, o)] = pr.tok("s")
            hqT_tok = pr.tok("s")

            for hp in range(2):
                jj = [hp * 2, hp * 2 + 1]
                ddone = {}
                for o in range(32):
                    buf = o % 2
                    wslc = slice(buf * 1024, (buf + 1) * 1024)

                    def ldd(g, o=o, qt=qt, hp=hp, wslc=wslc):
                        if o == 0 and qt == 0 and hp == 0:
                            pr.wait(g, "g", t3)
                        pr.wait(g, "g", ddone.get(o - 2, dprev.get((qt, hp - 1), prev_q.get(qt - 1))))
                        pr.inc(g.dma_start(out=wdsl[:, wslc], in_=wdq_d[o]), "gd")
                    pr.blk(g=ldd)
                    dtok = pr.tok("gd")

                    def mmd(t, o=o, qt=qt, hp=hp, wslc=wslc, dtok=dtok):
                        pr.wait(t, "t", dtok, hqT_tok)
                        if o == 0:
                            pr.wait(t, "t", dprev.get((qt, hp - 1), prev_q.get(qt - 1)))
                        last = None
                        for jx in range(2):
                            j = hp * 2 + jx
                            for mh in range(2):
                                last = t.matmul(psD[jx][:, mh * 512:(mh + 1) * 512],
                                                hqT[o][:, j * 128:(j + 1) * 128],
                                                wdsl[:, wslc][:, mh * 512:(mh + 1) * 512],
                                                start=(o == 0), stop=(o == 31))
                        pr.inc(last, "t")
                    pr.blk(t=mmd)
                    ddone[o] = pr.tok("t")
                dmm = pr.tok("t")

                for jx in range(2):
                    j = hp * 2 + jx
                    i = qt * 4 + j

                    def mkcd(v, j=j, dmm=dmm):
                        pr.wait(v, "v", dmm)
                        pr.inc(v.tensor_scalar(hs[:, 12 + j:13 + j], hs[:, 8 + j:9 + j],
                                               mws[:, 4:5], None, ALU.mult), "v")
                    pr.blk(v=mkcd)
                    cdk = pr.tok("v")
                    out_toks[i] = xio.add_store(i, psD[jx][:, :], cdk,
                                                scale_ap=hs[:, 12 + j:13 + j])
                dprev[(qt, hp)] = out_toks[qt * 4 + hp * 2 + 1]
            prev_q[qt] = out_toks[qt * 4 + 3]
        return out_toks



def stage_L0_ffn(inputs, pre):
    gw = np.asarray(inputs["L0_ffn_gate_w"])
    pre["L0f_gwT"] = np.ascontiguousarray(gw.T).reshape(8, 128, 8)
    w1 = np.asarray(inputs["L0_ffn_w1"])
    pre["L0f_w1s"] = np.ascontiguousarray(
        w1.reshape(8, 8, 128, 8, 128).transpose(0, 1, 3, 4, 2)).reshape(8, 8, 1024, 128)
    w2 = np.asarray(inputs["L0_ffn_w2"])
    pre["L0f_w2s"] = np.ascontiguousarray(
        w2.transpose(0, 2, 1)).reshape(8, 8, 128, 1024)


def decl_L0_ffn(nc):
    return {
        "gwT": nc.dram_tensor("L0f_gwT", [8, 128, 8], F32, kind="ExternalInput").ap(),
        "w1s": nc.dram_tensor("L0f_w1s", [8, 8, 1024, 128], F32, kind="ExternalInput").ap(),
        "w2s": nc.dram_tensor("L0f_w2s", [8, 8, 128, 1024], F32, kind="ExternalInput").ap(),
    }


def emit_L0_ffn(pr, stack, xio, lnT_tiles, lnT_tok, w, uid):
    nc = pr.nc
    import ml_dtypes
    NTOK_G = 16384
    CAP = 2048
    sel64 = np.zeros((64, 8), np.float32)
    for p_ in range(64):
        sel64[p_, p_ % 8] = 1.0
    sel64_d = nc.inline_tensor(sel64, f"l0f_sel64_{uid}")
    sel8x64 = np.zeros((8, 64), np.float32)
    for p_ in range(64):
        sel8x64[p_ % 8, p_] = 1.0
    sel8x64_d = nc.inline_tensor(sel8x64, f"l0f_sel8x64_{uid}")
    cc_in = nc.dram_tensor(f"l0f_ccin{uid}", [8, TT], F32, kind="Internal")
    cc_out = nc.dram_tensor(f"l0f_ccout{uid}", [64, TT], F32, kind="Internal")

    with ExitStack() as ls:
        lctx = ls.enter_context
        probs = lctx(nc.sbuf_tensor("l0f_probs", [P, NT * 8], F32))
        Rcol = lctx(nc.sbuf_tensor("l0f_R", [P, NT * 8], F32))
        scr = lctx(nc.sbuf_tensor("l0f_scr", [P, NT * 8], F32))
        cnt = lctx(nc.sbuf_tensor("l0f_cnt", [P, 8], F32))
        ssum = lctx(nc.sbuf_tensor("l0f_ssum", [P, 8], F32))
        selm = lctx(nc.sbuf_tensor("l0f_selm", [P, NT * 8], mybir.dt.uint8))
        neg9 = lctx(nc.sbuf_tensor("l0f_neg9", [P, 8], F32))
        tmp8 = lctx(nc.sbuf_tensor("l0f_tmp8", [P, 8], F32))
        gw_sb = lctx(nc.sbuf_tensor("l0f_gw", [P, 64], BF16))
        sel64_sb = lctx(nc.sbuf_tensor("l0f_s64", [64, 8], F32))
        sel8_sb = lctx(nc.sbuf_tensor("l0f_s8", [8, 64], F32))
        Rg = lctx(nc.sbuf_tensor("l0f_Rg", [64, TT], F32))
        lo = lctx(nc.sbuf_tensor("l0f_lo", [64, 8], F32))
        lou = lctx(nc.sbuf_tensor("l0f_lou", [64, 1], mybir.dt.uint8))
        ind = lctx(nc.sbuf_tensor("l0f_ind", [64, TT], BF16))
        c8 = lctx(nc.sbuf_tensor("l0f_c8", [8, 1], F32))
        lob = lctx(nc.sbuf_tensor("l0f_lob", [P, 8], F32))
        psG = lctx(nc.psum_tensor("l0f_psG", [P, 8], F32))
        psT8 = lctx(nc.psum_tensor("l0f_psT8", [P, 128], F32))

        def ldg(g):
            for c in range(8):
                pr.inc(g.dma_start(out=gw_sb[:, c * 8:(c + 1) * 8], in_=w["gwT"][c]), "gd")
        pr.blk(g=ldg)
        def ldg2(q):
            pr.inc(q.dma_start(out=sel64_sb[:, :], in_=sel64_d.ap()), "qd")
            pr.inc(q.dma_start(out=sel8_sb[:, :], in_=sel8x64_d.ap()), "qd")
        pr.blk(q=ldg2)
        gw_tok = pr.tok("gd")
        sel_tok = pr.tok("qd")

        prb_toks = []
        for i in range(NT):
            def mmg(t, i=i):
                pr.wait(t, "t", gw_tok, lnT_tok)
                if prb_toks:
                    pr.wait(t, "t", prb_toks[-1])
                last = None
                for c in range(8):
                    last = t.matmul(psG[:, :], lnT_tiles[c][:, i * 128:(i + 1) * 128],
                                    gw_sb[:, c * 8:(c + 1) * 8],
                                    start=(c == 0), stop=(c == 7))
                pr.inc(last, "t")
            pr.blk(t=mmg)
            gmt = pr.tok("t")

            def sfm(v, i=i, gmt=gmt):
                pr.wait(v, "v", gmt)
                pr.inc(v.reduce_max(tmp8[:, 0:1], psG[:, :], AX.X, negate=True), "v")
            pr.blk(v=sfm)
            v1 = pr.tok("v")

            def sfe(s, i=i, v1=v1):
                pr.wait(s, "s", v1)
                pr.inc(s.activation(probs[:, i * 8:(i + 1) * 8], psG[:, :], AF.Exp,
                                    bias=tmp8[:, 0:1], accum_out=tmp8[:, 1:2]), "s")
            pr.blk(s=sfe)
            s1 = pr.tok("s")

            def sfn(v, i=i, s1=s1):
                pr.wait(v, "v", s1)
                pr.inc(v.reciprocal(tmp8[:, 2:3], tmp8[:, 1:2]), "v")
                pr.wait(v, "v", pr.tok("v"))
                pr.inc(v.tensor_scalar(probs[:, i * 8:(i + 1) * 8],
                                       probs[:, i * 8:(i + 1) * 8],
                                       tmp8[:, 2:3], None, ALU.mult), "v")
            pr.blk(v=sfn)
            prb_toks.append(pr.tok("v"))

        def mkneg(g):
            pr.inc(g.memset(neg9[:, :], -1e9), "g")
        pr.blk(g=mkneg)
        n9_tok = pr.tok("g")

        r_toks = []
        for i in range(NT):
            sl = slice(i * 8, (i + 1) * 8)

            def mkr(v, i=i, sl=sl):
                pr.wait(v, "v", prb_toks[i], n9_tok)
                if r_toks:
                    pr.wait(v, "v", r_toks[-1])
                pr.inc(v.memset(cnt[:, :], 0.0), "v")
                pr.inc(v.memset(ssum[:, :], 0.0), "v")
                pr.wait(v, "v", pr.tok("v"))
                for e2 in range(8):
                    pcol = probs[:, i * 8 + e2:i * 8 + e2 + 1]
                    pr.inc(v.tensor_scalar(tmp8[:, :], probs[:, sl], pcol, None,
                                           ALU.is_lt), "v")
                    pr.wait(v, "v", pr.tok("v"))
                    pr.inc(v.tensor_add(cnt[:, :], cnt[:, :], tmp8[:, :]), "v")
                    pr.inc(v.scalar_tensor_tensor(ssum[:, :], tmp8[:, :], pcol,
                                                  ssum[:, :], ALU.mult, ALU.add), "v")
                    pr.wait(v, "v", pr.tok("v"))
                pr.inc(v.tensor_scalar(selm[:, sl], ssum[:, :], 0.9, None, ALU.is_lt), "v")
                pr.inc(v.tensor_sub(tmp8[:, :], probs[:, sl], cnt[:, :]), "v")
                pr.wait(v, "v", pr.tok("v"))
                pr.inc(v.tensor_scalar(tmp8[:, :], tmp8[:, :], 1.0, None, ALU.subtract), "v")
                pr.wait(v, "v", pr.tok("v"))
                pr.inc(v.select(Rcol[:, sl], selm[:, sl], tmp8[:, :], neg9[:, :]), "v")
            pr.blk(v=mkr)
            r_toks.append(pr.tok("v"))

        rt_done = {}
        for i in range(NT):
            def rtr(t, i=i):
                pr.wait(t, "t", r_toks[i], pr.ident_tok)
                pr.wait(t, "t", rt_done.get(i - 1))
                pr.inc(t.transpose(psT8[0:8, 0:128], Rcol[:, i * 8:(i + 1) * 8],
                                   pr.identf[:, :]), "t")
            pr.blk(t=rtr)
            tt = pr.tok("t")

            def rev(s, i=i, tt=tt):
                pr.wait(s, "s", tt)
                pr.inc(s.copy(Rg[0:8, i * 128:(i + 1) * 128], psT8[0:8, 0:128]), "s")
            pr.blk(s=rev)
            rt_done[i] = pr.tok("s")
        rt_tok = pr.tok("s")

        def dmain(g):
            pr.wait(g, "g", rt_tok)
            pr.inc(g.dma_start(out=cc_in.ap(), in_=Rg[0:8, :]), "gd")
        pr.blk(g=dmain)
        ccin_tok = pr.tok("gd")

        def docc(g):
            pr.wait(g, "g", ccin_tok)
            pr.inc(g.collective_compute(
                "AllGather", ALU.bypass,
                replica_groups=[list(range(8))],
                ins=[cc_in.ap().opt()],
                outs=[cc_out.ap().opt()],
            ), "cc")
        pr.blk(g=docc)
        cc_tok = pr.tok("cc")

        def ldrg(g):
            pr.wait(g, "g", cc_tok)
            pr.inc(g.dma_start(out=Rg[:, :], in_=cc_out.ap()), "gd")
        pr.blk(g=ldrg)
        rg_tok = pr.tok("gd")

        def binit(g):
            pr.wait(g, "g", rg_tok)
            pr.inc(g.memset(lo[:, 0:1], -9.5), "g")
            pr.inc(g.memset(lo[:, 1:2], 1.0), "g")
        pr.blk(g=binit)
        btok = pr.tok("g")

        for it in range(30):
            def bmid(v, it=it, btok=btok):
                pr.wait(v, "v", btok)
                pr.inc(v.tensor_add(lo[:, 2:3], lo[:, 0:1], lo[:, 1:2]), "v")
                pr.wait(v, "v", pr.tok("v"))
                pr.inc(v.tensor_scalar(lo[:, 2:3], lo[:, 2:3], 0.5, None, ALU.mult), "v")
                pr.wait(v, "v", pr.tok("v"))
                pr.inc(v.tensor_scalar(ind[:, :], Rg[:, :], lo[:, 2:3], None,
                                       ALU.is_gt), "v")
                pr.wait(v, "v", pr.tok("v"))
                pr.inc(v.reduce_sum(lo[:, 3:4], ind[:, :], AX.X), "v")
            pr.blk(v=bmid)
            v1 = pr.tok("v")

            def bmm(t, v1=v1):
                pr.wait(t, "t", v1, sel_tok)
                pr.inc(t.matmul(psT8[0:8, 0:1], sel64_sb[:, :], lo[0:64, 3:4],
                                start=True, stop=True), "t")
            pr.blk(t=bmm)
            t1 = pr.tok("t")

            def bev(s, t1=t1):
                pr.wait(s, "s", t1)
                pr.inc(s.copy(c8[:, :], psT8[0:8, 0:1]), "s")
            pr.blk(s=bev)
            s1 = pr.tok("s")

            def bmm2(t, s1=s1):
                pr.wait(t, "t", s1)
                pr.inc(t.matmul(psT8[0:64, 1:2], sel8_sb[:, :], c8[:, :],
                                start=True, stop=True), "t")
            pr.blk(t=bmm2)
            t2 = pr.tok("t")

            def bupd(v, t2=t2):
                pr.wait(v, "v", t2)
                pr.inc(v.tensor_scalar(lou[:, 0:1], psT8[0:64, 1:2], float(CAP),
                                       None, ALU.is_ge), "v")
                pr.wait(v, "v", pr.tok("v"))
                pr.inc(v.select(lo[:, 0:1], lou[:, 0:1], lo[:, 2:3], lo[:, 0:1]), "v")
                pr.inc(v.select(lo[:, 1:2], lou[:, 0:1], lo[:, 1:2], lo[:, 2:3]), "v")
            pr.blk(v=bupd)
            btok = pr.tok("v")

        def lotr(t, btok=btok):
            pr.wait(t, "t", btok)
            pr.inc(t.transpose(psT8[0:1, 0:8], lo[0:8, 0:1], pr.identf[0:8, 0:8]), "t")
        pr.blk(t=lotr)
        lt = pr.tok("t")

        def loev(s, lt=lt):
            pr.wait(s, "s", lt)
            pr.inc(s.copy(tmp8[0:1, 0:8], psT8[0:1, 0:8]), "s")
        pr.blk(s=loev)
        le = pr.tok("s")
        lob_tok = emit_row_broadcast(pr, psT8, tmp8[0:1, 0:8], le, 8)

        def loev2(s, lob_tok=lob_tok):
            pr.wait(s, "s", lob_tok)
            pr.inc(s.copy(lob[:, :], psT8[:, 0:8]), "s")
        pr.blk(s=loev2)
        lob_tok = pr.tok("s")

        scr_toks = []
        for i in range(NT):
            sl = slice(i * 8, (i + 1) * 8)

            def mks(v, i=i, sl=sl):
                pr.wait(v, "v", lob_tok, r_toks[i])
                if scr_toks:
                    pr.wait(v, "v", scr_toks[-1])
                pr.inc(v.tensor_tensor(tmp8[:, :], Rcol[:, sl], lob[:, :], ALU.is_gt), "v")
                pr.wait(v, "v", pr.tok("v"))
                pr.inc(v.tensor_mul(scr[:, sl], tmp8[:, :], probs[:, sl]), "v")
            pr.blk(v=mks)
            scr_toks.append(pr.tok("v"))
        scr_tok = scr_toks[-1]

        hT = [lctx(nc.sbuf_tensor(f"l0f_hT{c}", [P, TT], BF16)) for c in range(8)]
        yt = [lctx(nc.sbuf_tensor(f"l0f_y{i}", [P, D], F32)) for i in range(NT)]
        wsl = lctx(nc.sbuf_tensor("l0f_wsl", [P, 2048], BF16))
        w2b = lctx(nc.sbuf_tensor("l0f_w2b", [P, 8 * 1024], BF16))
        psE = [lctx(nc.psum_tensor(f"l0f_psE{j}", [P, 512], F32)) for j in range(2)]
        psY = [lctx(nc.psum_tensor(f"l0f_psY{j}", [P, 1024], F32)) for j in range(2)]

        def zy(v):
            for i in range(NT):
                pr.inc(v.memset(yt[i][:, :], 0.0), "v")
        pr.blk(v=zy)
        y0_tok = pr.tok("v")

        out_toks = [None] * NT
        prev_e = {}
        for e in range(8):
            hT_tok = emit_proj_T(pr, lnT_tiles, lnT_tok, w["w1s"][e], wsl, hT,
                                 f"l0fh{e}", psE, war_tok=prev_e.get(e - 1, y0_tok),
                                 act=AF.Gelu)
            yacc = {}

            def ldw2(g, e=e):
                pr.wait(g, "g", prev_e.get(e - 1))
                for c in range(8):
                    pr.inc(g.dma_start(out=w2b[:, c * 1024:(c + 1) * 1024],
                                       in_=w["w2s"][e][c]), "gd")
            pr.blk(g=ldw2)
            w2_tok = pr.tok("gd")
            for i in range(NT):
                bank = psY[i % 2]

                def mmy(t, e=e, i=i, bank=bank, w2_tok=w2_tok):
                    pr.wait(t, "t", hT_tok, w2_tok)
                    pr.wait(t, "t", yacc.get(i - 2))
                    last = None
                    for mh in range(2):
                        for c in range(8):
                            last = t.matmul(bank[:, mh * 512:(mh + 1) * 512],
                                            hT[c][:, i * 128:(i + 1) * 128],
                                            w2b[:, c * 1024 + mh * 512:c * 1024 + (mh + 1) * 512],
                                            start=(c == 0), stop=(c == 7))
                    pr.inc(last, "t")
                pr.blk(t=mmy)
                mmtok = pr.tok("t")

                def evy(v, e=e, i=i, bank=bank, mmtok=mmtok):
                    pr.wait(v, "v", mmtok, scr_tok, y0_tok)
                    pr.inc(v.scalar_tensor_tensor(yt[i][:, :], bank[:, :],
                                                  scr[:, i * 8 + e:i * 8 + e + 1],
                                                  yt[i][:, :], ALU.mult, ALU.add), "v")
                pr.blk(v=evy)
                yacc[i] = pr.tok("v")
            prev_e[e] = pr.tok("v")

        if getattr(pr, "l0f_debug", None) is not None:
            dbg = pr.l0f_debug

            def dmp(g):
                pr.wait(g, "g", scr_tok, pr.tok("v"))
                pr.inc(g.dma_start(out=dbg["scr"], in_=scr[:, :]), "gd")
                pr.inc(g.dma_start(out=dbg["lo"], in_=lo[:, :]), "gd")
                pr.inc(g.dma_start(out=dbg["Rg"], in_=Rg[:, 0:256]), "gd")
                pr.inc(g.dma_start(out=dbg["probs"], in_=probs[:, :]), "gd")
            pr.blk(g=dmp)
        for i in range(NT):
            out_toks[i] = xio.add_store(i, yt[i][:, :], prev_e[7])
        return out_toks



def stage_L2_attn_b(inputs, pre):
    W = np.asarray(inputs["L2_attn_b_W_attn"])
    pre["L2ab_W"] = np.ascontiguousarray(W).reshape(8, 128, 3072)
    pre["L2ab_b"] = np.asarray(inputs["L2_attn_b_b_attn"]).reshape(1, 3072)
    Wp = np.asarray(inputs["L2_attn_b_W_proj"])
    pre["L2ab_Wp"] = np.ascontiguousarray(Wp).reshape(8, 128, 1024)
    pre["L2ab_bp"] = np.asarray(inputs["L2_attn_b_b_proj"]).reshape(1, 1024)
    pre["L2ab_R0"] = np.ascontiguousarray(np.asarray(inputs["L2_attn_b_R"])[:, 0, :])


def rope_tables_L2():
    HD = 64
    inv = 1.0 / (10000.0 ** (np.arange(0, HD, 2, dtype=np.float64) / HD))
    t = np.arange(TT, dtype=np.float64)
    emb = np.concatenate([np.outer(t, inv), np.outer(t, inv)], 1)
    cos = np.cos(emb); sin = np.sin(emb)
    return (np.ascontiguousarray(cos.reshape(16, 128, 64).astype(np.float32)),
            np.ascontiguousarray(sin.reshape(16, 128, 64).astype(np.float32)))


def decl_L2_attn_b(nc):
    return {
        "W": nc.dram_tensor("L2ab_W", [8, 128, 3072], F32, kind="ExternalInput").ap(),
        "b": nc.dram_tensor("L2ab_b", [1, 3072], F32, kind="ExternalInput").ap(),
        "Wp": nc.dram_tensor("L2ab_Wp", [8, 128, 1024], F32, kind="ExternalInput").ap(),
        "bp": nc.dram_tensor("L2ab_bp", [1, 1024], F32, kind="ExternalInput").ap(),
        "R0": nc.dram_tensor("L2ab_R0", [64, 32], F32, kind="ExternalInput").ap(),
    }


def _l2b_ldcs(pr, cs2, cs_used, cs_dram, i, key):
    sl = 0 if key == "q" else 1

    def body(q, i=i, sl=sl):
        pr.wait(q, "q", cs_used.get((key, i - 1)))
        pr.inc(q.dma_start(out=cs2[:, sl * 128:(sl + 1) * 128],
                           in_=cs_dram.ap().rearrange("p (i c) -> p i c", c=128)[:, i, :]), "qd")
    pr.blk(q=body)
    return pr.tok("qd"), sl


def emit_L2_attn_b(pr, stack, xio, w, rope_consts, uid):
    nc = pr.nc
    import ml_dtypes
    cos2_np, sin2_np = rope_consts
    cs_np = np.concatenate([cos2_np, sin2_np], axis=2).reshape(16, 128, 128)
    cs_np = np.ascontiguousarray(cs_np.transpose(1, 0, 2)).reshape(128, 16 * 128)
    cs_dram = nc.inline_tensor(cs_np, f"l2ab_cs{uid}")
    iota64_np = np.tile(np.arange(64, dtype=np.float32)[None, :], (P, 1))
    iota64_d = nc.inline_tensor(iota64_np, f"l2ab_io64{uid}")
    iota32_np = np.tile(np.tile(np.arange(32, dtype=np.float32), 16)[None, :], (P, 1))
    iota32_d = nc.inline_tensor(iota32_np, f"l2ab_io32{uid}")
    u128 = np.ascontiguousarray(np.triu(np.ones((128, 128)), 1)).astype(ml_dtypes.bfloat16)
    u128_d = nc.inline_tensor(u128, f"l2ab_u128{uid}")
    u64 = np.ascontiguousarray(np.triu(np.ones((64, 64), np.float32), 1))
    u64_d = nc.inline_tensor(u64, f"l2ab_u64{uid}")
    qsq_d = nc.dram_tensor(f"l2ab_qsq{uid}", [16 * TT, 64], BF16, kind="Internal").ap()
    qsk_d = nc.dram_tensor(f"l2ab_qsk{uid}", [16 * TT, 64], BF16, kind="Internal").ap()
    qsv_d = nc.dram_tensor(f"l2ab_qsv{uid}", [16 * TT, 64], BF16, kind="Internal").ap()
    outl_d = nc.dram_tensor(f"l2ab_ol{uid}", [16 * TT, 64], BF16, kind="Internal").ap()

    with ExitStack() as ls:
        lctx = ls.enter_context
        lnT16 = [lctx(nc.sbuf_tensor(f"l2b_lnT{c}", [P, TT], BF16)) for c in range(8)]
        poscol = lctx(nc.sbuf_tensor("l2b_pos", [P, 16 * 16], I32))
        bcol = lctx(nc.sbuf_tensor("l2b_bcol", [P, 16 * 16], F32))
        cs2 = lctx(nc.sbuf_tensor("l2b_cs2", [P, 2 * 128], F32))
        cs_used = {}
        io64 = lctx(nc.sbuf_tensor("l2b_io64", [P, 64], F32))
        io32 = lctx(nc.sbuf_tensor("l2b_io32", [P, 512], F32))
        u128s = lctx(nc.sbuf_tensor("l2b_u128", [P, 128], BF16))
        u64s = lctx(nc.sbuf_tensor("l2b_u64", [64, 64], F32))
        r0 = lctx(nc.sbuf_tensor("l2b_r0", [64, 32], F32))
        bias1 = lctx(nc.sbuf_tensor("l2b_bias1", [P, 1024], F32))
        row1 = lctx(nc.sbuf_tensor("l2b_row1", [1, 1024], F32))
        qr = lctx(nc.sbuf_tensor("l2b_qr", [P, 1024], F32))
        rot = lctx(nc.sbuf_tensor("l2b_rot", [P, 1024], F32))
        t1b = lctx(nc.sbuf_tensor("l2b_t1", [P, 1024], F32))

        def ldc(q):
            pr.inc(q.dma_start(out=io64[:, :], in_=iota64_d.ap()), "qd")
            pr.inc(q.dma_start(out=io32[:, :], in_=iota32_d.ap()), "qd")
            pr.inc(q.dma_start(out=u128s[:, :], in_=u128_d.ap()), "qd")
            pr.inc(q.dma_start(out=u64s[:, :], in_=u64_d.ap()), "qd")
            pr.inc(q.dma_start(out=r0[:, :], in_=w["R0"]), "qd")
        pr.blk(q=ldc)
        c_tok = pr.tok("qd")

        def load_bias_seg(seg, psB, war):
            def ldb(q, seg=seg):
                pr.wait(q, "q", war)
                pr.inc(q.dma_start(out=row1[:, :],
                                   in_=w["b"][0:1, seg * 1024:(seg + 1) * 1024]), "qd")
            pr.blk(q=ldb)
            rt = pr.tok("qd")
            rb = emit_row_broadcast(pr, psB, row1, rt, 1024)

            def evb(s, rb=rb):
                pr.wait(s, "s", rb)
                pr.inc(s.copy(bias1[:, :], psB[:, :]), "s")
            pr.blk(s=evb)
            return pr.tok("s")

        with ExitStack() as phq:
            pctx = phq.enter_context
            lnTlo = [pctx(nc.sbuf_tensor(f"l2b_lnTlo{c}", [P, TT], BF16)) for c in range(8)]
            tmpln = pctx(nc.sbuf_tensor("l2b_tmpln0", [P, D], F32))
            tmphl = pctx(nc.sbuf_tensor("l2b_tmphl0", [P, 2 * D], BF16))
            psLT = pctx(nc.psum_tensor("l2b_psLT0", [P, 1024], BF16))
            tr_toks = {}

            def post_tile(i, tok):
                def hilo(v, i=i, tok=tok):
                    pr.wait(v, "v", tok)
                    pr.inc(v.tensor_copy(tmphl[:, 0:D], tmpln[:, :]), "v")
                    pr.wait(v, "v", pr.tok("v"))
                    pr.inc(v.tensor_sub(tmphl[:, D:2 * D], tmpln[:, :], tmphl[:, 0:D]), "v")
                pr.blk(v=hilo)
                hl_tok = pr.tok("v")

                for grp in range(2):
                    dstset = lnT16 if grp == 0 else lnTlo

                    def trt(t, i=i, grp=grp, hl_tok=hl_tok):
                        pr.wait(t, "t", hl_tok, pr.ident_tok)
                        pr.wait(t, "t", tr_toks.get((i, grp - 1), tr_toks.get((i - 1, 1))))
                        last = None
                        for c in range(8):
                            last = t.transpose(psLT[:, c * 128:(c + 1) * 128],
                                               tmphl[:, grp * D + c * 128:grp * D + (c + 1) * 128],
                                               pr.ident[:, :])
                        pr.inc(last, "t")
                    pr.blk(t=trt)
                    ttk = pr.tok("t")

                    def evt(s, i=i, grp=grp, ttk=ttk, dstset=dstset):
                        pr.wait(s, "s", ttk)
                        for c in range(8):
                            pr.inc(s.copy(dstset[c][:, i * 128:(i + 1) * 128],
                                          psLT[:, c * 128:(c + 1) * 128]), "s")
                    pr.blk(s=evt)
                    tr_toks[(i, grp)] = pr.tok("s")
                return tr_toks[(i, 1)]

            ln_list = [tmpln] * NT
            emit_ln_from_xio(pr, phq, xio, ln_list, post_tile=post_tile,
                             tile_war=lambda i: tr_toks.get((i - 1, 1)))
            lnT16_tok = pr.tok("s")

            wqh = pctx(nc.sbuf_tensor("l2b_wqh", [P, 8 * 1024], BF16))
            wql = pctx(nc.sbuf_tensor("l2b_wql", [P, 8 * 1024], BF16))
            wtmp = pctx(nc.sbuf_tensor("l2b_wtmp", [P, 1024], F32))
            qtl = [pctx(nc.sbuf_tensor(f"l2b_q{i}", [P, D], BF16)) for i in range(NT)]
            qrT = pctx(nc.sbuf_tensor("l2b_qrT", [64, 2048], F32))
            ptile = pctx(nc.sbuf_tensor("l2b_pt", [P, 512], F32))
            mx16 = pctx(nc.sbuf_tensor("l2b_mx", [P, 16 * 4], F32))
            psQ = [pctx(nc.psum_tensor(f"l2b_psQ{j}", [P, 1024], F32)) for j in range(2)]
            psP = pctx(nc.psum_tensor("l2b_psP", [P, 512], F32))
            psB3 = pctx(nc.psum_tensor("l2b_psB3", [P, 1024], F32))
            psTq = psB3

            wq_cv = []
            for c in range(8):
                def ldwq(g, c=c):
                    if wq_cv:
                        pr.wait(g, "g", wq_cv[-1])
                    pr.inc(g.dma_start(out=wtmp[:, :], in_=w["W"][c][:, 0:1024]), "gd")
                pr.blk(g=ldwq)
                lt = pr.tok("gd")

                def cvw(v, c=c, lt=lt):
                    pr.wait(v, "v", lt)
                    pr.inc(v.tensor_copy(wqh[:, c * 1024:(c + 1) * 1024], wtmp[:, :]), "v")
                    pr.wait(v, "v", pr.tok("v"))
                    pr.inc(v.tensor_sub(wql[:, c * 1024:(c + 1) * 1024], wtmp[:, :],
                                        wqh[:, c * 1024:(c + 1) * 1024]), "v")
                pr.blk(v=cvw)
                wq_cv.append(pr.tok("v"))
            wq_tok = wq_cv[-1]
            bias_q = load_bias_seg(0, psB3, None)

            b_done = {}
            bk_done = {}
            for i in range(NT):
                bank = psQ[i % 2]

                def mmq(t, i=i, bank=bank):
                    pr.wait(t, "t", wq_tok, lnT16_tok)
                    pr.wait(t, "t", b_done.get(i - 2))
                    last = None
                    for mh in range(2):
                        ng = 0
                        for aa, bb in ((lnT16, wqh), (lnT16, wql), (lnTlo, wqh)):
                            for c in range(8):
                                last = t.matmul(bank[:, mh * 512:(mh + 1) * 512],
                                                aa[c][:, i * 128:(i + 1) * 128],
                                                bb[:, c * 1024 + mh * 512:c * 1024 + (mh + 1) * 512],
                                                start=(ng == 0), stop=(ng == 23))
                                ng += 1
                    pr.inc(last, "t")
                pr.blk(t=mmq)
                mmtok = pr.tok("t")
                cstok, csl = _l2b_ldcs(pr, cs2, cs_used, cs_dram, i, "q")

                def rope_q(v, i=i, bank=bank, mmtok=mmtok, cstok=cstok, csl=csl):
                    pr.wait(v, "v", mmtok, bias_q, cstok)
                    pr.inc(v.tensor_add(qr[:, :], bank[:, :], bias1[:, :]), "v")
                    pr.wait(v, "v", pr.tok("v"))
                    qv = qr[:, :].rearrange("p (x two) -> p x two", two=2)
                    rv = rot[:, :].rearrange("p (x two) -> p x two", two=2)
                    pr.inc(v.tensor_scalar(rv[:, :, 0:1], qv[:, :, 1:2], -1.0, None, ALU.mult), "v")
                    pr.inc(v.tensor_copy(rv[:, :, 1:2], qv[:, :, 0:1]), "v")
                    pr.wait(v, "v", pr.tok("v"))
                    cos2 = cs2[:, csl * 128:csl * 128 + 64]
                    sin2 = cs2[:, csl * 128 + 64:csl * 128 + 128]
                    q3 = qr[:, :].rearrange("p (h d) -> p h d", d=64)
                    r3 = rot[:, :].rearrange("p (h d) -> p h d", d=64)
                    t3 = t1b[:, :].rearrange("p (h d) -> p h d", d=64)
                    cb = cos2.unsqueeze(1).broadcast_to([P, 16, 64])
                    sbx = sin2.unsqueeze(1).broadcast_to([P, 16, 64])
                    pr.inc(v.tensor_tensor(t3, q3, cb, ALU.mult), "v")
                    pr.inc(v.tensor_tensor(r3, r3, sbx, ALU.mult), "v")
                    pr.wait(v, "v", pr.tok("v"))
                    pr.inc(v.tensor_add(qr[:, :], t1b[:, :], rot[:, :]), "v")
                    pr.wait(v, "v", pr.tok("v"))
                    pr.inc(v.tensor_copy(qtl[i][:, :], qr[:, :]), "v")
                pr.blk(v=rope_q)
                vtok = pr.tok("v")
                cs_used[("q", i)] = vtok

                def trq(t, i=i, vtok=vtok, half=0):
                    pass
                for hh in range(2):
                    def trqh(t, i=i, vtok=vtok, hh=hh):
                        pr.wait(t, "t", vtok, pr.ident_tok)
                        if hh == 0:
                            pr.wait(t, "t", bk_done.get(i - 1))
                        else:
                            pr.wait(t, "t", ev_tr[0])
                        last = None
                        for hx in range(8):
                            h = hh * 8 + hx
                            last = t.transpose(psTq[0:64, hx * 128:(hx + 1) * 128],
                                               qr[:, h * 64:(h + 1) * 64], pr.identf[:, :])
                        pr.inc(last, "t")
                    pr.blk(t=trqh)
                    ttk = pr.tok("t")

                    def evqh(s, hh=hh, ttk=ttk):
                        pr.wait(s, "s", ttk)
                        pr.inc(s.copy(qrT[0:64, hh * 1024:(hh + 1) * 1024],
                                      psTq[0:64, 0:1024]), "s")
                    pr.blk(s=evqh)
                    if hh == 0:
                        ev_tr = [pr.tok("s")]
                    else:
                        ev_tr[0] = pr.tok("s")
                etok = pr.tok("s")

                def mmp(t, i=i, etok=etok):
                    pr.wait(t, "t", etok, c_tok)
                    pr.wait(t, "t", bk_done.get(i - 1))
                    last = None
                    for h in range(16):
                        last = t.matmul(psP[:, h * 32:(h + 1) * 32],
                                        qrT[0:64, h * 128:(h + 1) * 128],
                                        r0[:, :], start=True, stop=True)
                    pr.inc(last, "t")
                pr.blk(t=mmp)
                ptok = pr.tok("t")

                def bkt(v, i=i, ptok=ptok):
                    pr.wait(v, "v", ptok)
                    pr.inc(v.tensor_copy(ptile[:, :], psP[:, :]), "v")
                    pr.wait(v, "v", pr.tok("v"))
                    p3 = ptile[:, :].rearrange("p (h m) -> p h m", m=32)
                    pr.inc(v.tensor_reduce(mx16[:, 0:16], p3, AX.X, ALU.max), "v")
                    pr.inc(v.tensor_reduce(mx16[:, 16:32], p3, AX.X, ALU.min), "v")
                    pr.wait(v, "v", pr.tok("v"))
                    mxb = mx16[:, 0:16].unsqueeze(2).broadcast_to([P, 16, 32])
                    mnb = mx16[:, 16:32].unsqueeze(2).broadcast_to([P, 16, 32])
                    t3 = t1b[:, 0:512].rearrange("p (h m) -> p h m", m=32)
                    pr.inc(v.tensor_tensor(t3, p3, mxb, ALU.is_lt), "v")
                    r3 = rot[:, 0:512].rearrange("p (h m) -> p h m", m=32)
                    pr.inc(v.tensor_tensor(r3, p3, mnb, ALU.is_gt), "v")
                    pr.wait(v, "v", pr.tok("v"))
                    pr.inc(v.scalar_tensor_tensor(t1b[:, 0:512], t1b[:, 0:512], 64.0,
                                                  io32[:, :], ALU.mult, ALU.add), "v")
                    pr.inc(v.scalar_tensor_tensor(rot[:, 0:512], rot[:, 0:512], 64.0,
                                                  io32[:, :], ALU.mult, ALU.add), "v")
                    pr.wait(v, "v", pr.tok("v"))
                    t3b = t1b[:, 0:512].rearrange("p (h m) -> p h m", m=32)
                    r3b = rot[:, 0:512].rearrange("p (h m) -> p h m", m=32)
                    pr.inc(v.tensor_reduce(mx16[:, 32:48], t3b, AX.X, ALU.min), "v")
                    pr.inc(v.tensor_reduce(mx16[:, 48:64], r3b, AX.X, ALU.min), "v")
                    pr.wait(v, "v", pr.tok("v"))
                    pr.inc(v.tensor_scalar(mx16[:, 16:32], mx16[:, 16:32], -1.0, None,
                                           ALU.mult), "v")
                    pr.inc(v.tensor_scalar(mx16[:, 48:64], mx16[:, 48:64], 32.0, None,
                                           ALU.add), "v")
                    pr.wait(v, "v", pr.tok("v"))
                    pr.inc(v.tensor_tensor(t1b[:, 512:528], mx16[:, 0:16], mx16[:, 16:32],
                                           ALU.is_ge), "v")
                    pr.wait(v, "v", pr.tok("v"))
                    pr.inc(v.tensor_copy(t1b[:, 528:544].bitcast(mybir.dt.uint8)[:, 0:16]
                                         if False else rot[:, 512:528], t1b[:, 512:528]), "v")
                    pr.inc(v.tensor_tensor(rot[:, 528:544], mx16[:, 32:48], mx16[:, 48:64],
                                           ALU.subtract), "v")
                    pr.wait(v, "v", pr.tok("v"))
                    pr.inc(v.scalar_tensor_tensor(bcol[:, i * 16:(i + 1) * 16],
                                                  t1b[:, 512:528], 1.0, rot[:, 528:544],
                                                  ALU.mult, ALU.mult), "v")
                    pr.wait(v, "v", pr.tok("v"))
                    pr.inc(v.tensor_add(bcol[:, i * 16:(i + 1) * 16],
                                        bcol[:, i * 16:(i + 1) * 16], mx16[:, 48:64]), "v")
                pr.blk(v=bkt)
                b_done[i] = pr.tok("v")
                bk_done[i] = b_done[i]
            q_phase_tok = pr.tok("v")
            sort_done = {}

            Ot = pctx(nc.sbuf_tensor("l2b_O", [P, 64], BF16))
            hrow = pctx(nc.sbuf_tensor("l2b_hrow", [1, 64], F32))
            pfrow = pctx(nc.sbuf_tensor("l2b_pfrow", [1, 64], F32))
            hcol = pctx(nc.sbuf_tensor("l2b_hcol", [64, 1], F32))
            rsum = pctx(nc.sbuf_tensor("l2b_rsum", [P, 8], F32))
            psH = psQ[0][:, 0:64]
            psC = psQ[0][:, 512:576]
            psHB = psQ[1][:, 0:64]

            for h in range(16):
                o_done = []
                for i in range(NT):
                    def mko(v, h=h, i=i):
                        pr.wait(v, "v", b_done[i] if h == 0 else q_phase_tok)
                        if o_done:
                            pr.wait(v, "v", o_done[-1])
                        elif h > 0:
                            pr.wait(v, "v", sort_done[(h - 1, NT - 1)])
                        pr.inc(v.tensor_scalar(Ot[:, :], io64[:, :],
                                               bcol[:, i * 16 + h:i * 16 + h + 1],
                                               None, ALU.is_equal), "v")
                    pr.blk(v=mko)
                    otok = pr.tok("v")

                    def mmh(t, h=h, i=i, otok=otok):
                        pr.wait(t, "t", otok, pr.ones_tok)
                        pr.inc(t.matmul(psH[0:64, 0:1], Ot[:, :], pr.ones_bf16[:, 0:1],
                                        start=(i == 0), stop=(i == NT - 1)), "t")
                    pr.blk(t=mmh)
                    o_done.append(pr.tok("t"))
                h_tok = pr.tok("t")

                def evh(s, h_tok=h_tok):
                    pr.wait(s, "s", h_tok)
                    pr.inc(s.copy(hcol[:, :], psH[0:64, 0:1]), "s")
                pr.blk(s=evh)
                hc_tok = pr.tok("s")

                def mmpf(t, hc_tok=hc_tok):
                    pr.wait(t, "t", hc_tok)
                    pr.inc(t.matmul(psH[0:64, 1:2], u64s[:, :], hcol[:, :],
                                    start=True, stop=True), "t")
                pr.blk(t=mmpf)
                pf_tok = pr.tok("t")

                def trpf(t, pf_tok=pf_tok):
                    pr.wait(t, "t", pf_tok)
                    pr.inc(t.transpose(psH[0:1, 2:66], psH[0:64, 1:2], pr.identf[0:64, 0:64])
                           if False else
                           t.transpose(psC[0:1, 0:64], hcol[:, :], pr.identf[0:64, 0:64]), "t")
                pr.blk(t=trpf)
                def evpf(s):
                    pr.wait(s, "s", pr.tok("t"))
                    pr.inc(s.copy(hrow[0:1, :], psC[0:1, 0:64]), "s")
                    pr.inc(s.copy(rsum[0:64, 0:1], psH[0:64, 1:2]), "s")
                pr.blk(s=evpf)
                ev1 = pr.tok("s")

                def trpf2(t, ev1=ev1):
                    pr.wait(t, "t", ev1)
                    pr.inc(t.transpose(psC[0:1, 0:64], rsum[0:64, 0:1],
                                       pr.identf[0:64, 0:64]), "t")
                pr.blk(t=trpf2)
                tr2 = pr.tok("t")

                def evpf2(s, tr2=tr2):
                    pr.wait(s, "s", tr2)
                    pr.inc(s.copy(pfrow[0:1, :], psC[0:1, 0:64]), "s")
                pr.blk(s=evpf2)
                pf_row_tok = pr.tok("s")

                for i in range(NT):
                    def mko2(v, h=h, i=i, pf_row_tok=pf_row_tok):
                        pr.wait(v, "v", pf_row_tok)
                        if sort_done and (h, i - 1) in sort_done:
                            pr.wait(v, "v", sort_done[(h, i - 1)])
                        pr.inc(v.tensor_scalar(Ot[:, :], io64[:, :],
                                               bcol[:, i * 16 + h:i * 16 + h + 1],
                                               None, ALU.is_equal), "v")
                    pr.blk(v=mko2)
                    otok = pr.tok("v")

                    def mmc(t, otok=otok, i=i, h=h):
                        pr.wait(t, "t", otok)
                        pr.inc(t.matmul(psC[:, 0:64], u128s[:, :], Ot[:, :],
                                        start=True, stop=True), "t")
                        pr.inc(t.matmul(psHB[:, 0:64], pr.ones_f32[0:1, 0:128],
                                        pfrow[0:1, :], start=True, stop=True), "t")
                    pr.blk(t=mmc)
                    ctok = pr.tok("t")

                    def mkpos(v, i=i, h=h, ctok=ctok):
                        pr.wait(v, "v", ctok)
                        pr.inc(v.tensor_mul(t1b[:, 0:64], Ot[:, :], psC[:, 0:64]), "v")
                        pr.inc(v.tensor_mul(rot[:, 0:64], Ot[:, :], psHB[:, 0:64]), "v")
                        pr.wait(v, "v", pr.tok("v"))
                        pr.inc(v.reduce_sum(rsum[:, 0:1], t1b[:, 0:64], AX.X), "v")
                        pr.inc(v.reduce_sum(rsum[:, 1:2], rot[:, 0:64], AX.X), "v")
                        pr.wait(v, "v", pr.tok("v"))
                        pr.inc(v.tensor_add(rsum[:, 2:3], rsum[:, 0:1], rsum[:, 1:2]), "v")
                        pr.wait(v, "v", pr.tok("v"))
                        pr.inc(v.tensor_scalar(rsum[:, 2:3], rsum[:, 2:3],
                                               float(h * TT), None, ALU.add), "v")
                        pr.wait(v, "v", pr.tok("v"))
                        pr.inc(v.tensor_copy(poscol[:, i * 16 + h:i * 16 + h + 1],
                                             rsum[:, 2:3]), "v")
                    pr.blk(v=mkpos)
                    postok = pr.tok("v")

                    def mmu(t, otok=otok, postok=postok):
                        pr.wait(t, "t", postok, pr.ones_tok)
                        pr.inc(t.matmul(psC[0:1, 0:64], pr.ones_bf16[:, 0:1], Ot[:, :],
                                        start=True, stop=True), "t")
                    pr.blk(t=mmu)
                    utok = pr.tok("t")

                    def evu(v, utok=utok):
                        pr.wait(v, "v", utok)
                        pr.inc(v.tensor_add(pfrow[0:1, :], pfrow[0:1, :], psC[0:1, 0:64]), "v")
                    pr.blk(v=evu)
                    uvtok = pr.tok("v")

                    def sct(g, h=h, i=i, postok=postok):
                        pr.wait(g, "g", postok)
                        pr.inc(g.indirect_dma_start(
                            qsq_d, bass.IndirectOffsetOnAxis(
                                ap=poscol[:, i * 16 + h:i * 16 + h + 1], axis=0),
                            qtl[i][:, h * 64:(h + 1) * 64], None), "gd")
                    pr.blk(g=sct)
                    sort_done[(h, i)] = pr.tok("gd")
            sort_tok = pr.tok("gd")
        with ExitStack() as phkv:
            kctx = phkv.enter_context
            kvtl = [kctx(nc.sbuf_tensor(f"l2b_kv{i}", [P, 2048], BF16)) for i in range(NT)]
            wkv = kctx(nc.sbuf_tensor("l2b_wkv", [P, 8 * 2048], BF16))
            psKV = [kctx(nc.psum_tensor(f"l2b_psKV{j}", [P, 1024], F32)) for j in range(2)]
            psB4 = kctx(nc.psum_tensor("l2b_psB4", [P, 1024], F32))
            bias_kv = kctx(nc.sbuf_tensor("l2b_biaskv", [P, 2048], F32))
            bias_k = load_bias_seg(1, psB4, None)

            def cpk(v):
                pr.wait(v, "v", bias_k)
                pr.inc(v.tensor_copy(bias_kv[:, 0:1024], bias1[:, :]), "v")
            pr.blk(v=cpk)
            bias_k = pr.tok("v")
            bias_v0 = load_bias_seg(2, psB4, bias_k)

            def cpv(v):
                pr.wait(v, "v", bias_v0)
                pr.inc(v.tensor_copy(bias_kv[:, 1024:2048], bias1[:, :]), "v")
            pr.blk(v=cpv)
            bias_v = pr.tok("v")

            def ldwkv(g):
                pr.wait(g, "g", sort_tok)
                for c in range(8):
                    pr.inc(g.dma_start(out=wkv[:, c * 2048:(c + 1) * 2048],
                                       in_=w["W"][c][:, 1024:3072]), "gd")
            pr.blk(g=ldwkv)
            wkv_tok = pr.tok("gd")

            kv_done = {}
            for i in range(NT):
                for half in range(2):
                    bank = psKV[(i * 2 + half) % 2]

                    def mmkv(t, i=i, half=half, bank=bank):
                        pr.wait(t, "t", wkv_tok, lnT16_tok)
                        pr.wait(t, "t", kv_done.get((i, half - 2), kv_done.get((i - 1, half))))
                        last = None
                        for mh in range(2):
                            for c in range(8):
                                last = t.matmul(bank[:, mh * 512:(mh + 1) * 512],
                                                lnT16[c][:, i * 128:(i + 1) * 128],
                                                wkv[:, c * 2048 + half * 1024 + mh * 512:
                                                    c * 2048 + half * 1024 + (mh + 1) * 512],
                                                start=(c == 0), stop=(c == 7))
                        pr.inc(last, "t")
                    pr.blk(t=mmkv)
                    mmtok = pr.tok("t")
                    cstok, csl = (None, 1)
                    if half == 0:
                        cstok, csl = _l2b_ldcs(pr, cs2, cs_used, cs_dram, i, "k")

                    def evkv(v, i=i, half=half, bank=bank, mmtok=mmtok, cstok=cstok, csl=csl):
                        pr.wait(v, "v", mmtok, bias_v, cstok)
                        dst = kvtl[i][:, half * 1024:(half + 1) * 1024]
                        pr.inc(v.tensor_add(qr[:, :], bank[:, :],
                                            bias_kv[:, half * 1024:(half + 1) * 1024]), "v")
                        pr.wait(v, "v", pr.tok("v"))
                        if half == 0:
                            qv = qr[:, :].rearrange("p (x two) -> p x two", two=2)
                            rv = rot[:, :].rearrange("p (x two) -> p x two", two=2)
                            pr.inc(v.tensor_scalar(rv[:, :, 0:1], qv[:, :, 1:2], -1.0, None, ALU.mult), "v")
                            pr.inc(v.tensor_copy(rv[:, :, 1:2], qv[:, :, 0:1]), "v")
                            pr.wait(v, "v", pr.tok("v"))
                            cos2 = cs2[:, csl * 128:csl * 128 + 64]
                            sin2 = cs2[:, csl * 128 + 64:csl * 128 + 128]
                            q3 = qr[:, :].rearrange("p (h d) -> p h d", d=64)
                            r3 = rot[:, :].rearrange("p (h d) -> p h d", d=64)
                            t3 = t1b[:, :].rearrange("p (h d) -> p h d", d=64)
                            cb = cos2.unsqueeze(1).broadcast_to([P, 16, 64])
                            sbx = sin2.unsqueeze(1).broadcast_to([P, 16, 64])
                            pr.inc(v.tensor_tensor(t3, q3, cb, ALU.mult), "v")
                            pr.inc(v.tensor_tensor(r3, r3, sbx, ALU.mult), "v")
                            pr.wait(v, "v", pr.tok("v"))
                            pr.inc(v.tensor_add(dst, t1b[:, :], rot[:, :]), "v")
                        else:
                            pr.inc(v.tensor_copy(dst, qr[:, :]), "v")
                    pr.blk(v=evkv)
                    kv_done[(i, half)] = pr.tok("v")
                    if half == 0:
                        cs_used[("k", i)] = kv_done[(i, half)]

            kvs_done = {}
            for h in range(16):
                for i in range(NT):
                    def skv(g, h=h, i=i):
                        pr.wait(g, "g", kv_done[(i, 1)])
                        pr.inc(g.indirect_dma_start(
                            qsk_d, bass.IndirectOffsetOnAxis(
                                ap=poscol[:, i * 16 + h:i * 16 + h + 1], axis=0),
                            kvtl[i][:, h * 64:(h + 1) * 64], None), "gd")
                        pr.inc(g.indirect_dma_start(
                            qsv_d, bass.IndirectOffsetOnAxis(
                                ap=poscol[:, i * 16 + h:i * 16 + h + 1], axis=0),
                            kvtl[i][:, 1024 + h * 64:1024 + (h + 1) * 64], None), "gd")
                    pr.blk(g=skv)
                    kvs_done[(h, i)] = pr.tok("gd")
            sort_tok = pr.tok("gd")

        att = [lctx(nc.sbuf_tensor(f"l2b_att{i}", [P, 1024], BF16)) for i in range(NT)]
        latt = ExitStack()
        chnk = [latt.enter_context(nc.sbuf_tensor(f"l2b_ch{j}", [P, 192], BF16)) for j in range(2)]
        kqT = latt.enter_context(nc.sbuf_tensor("l2b_kqT", [64, 2 * 256], BF16))
        wsx = latt.enter_context(nc.sbuf_tensor("l2b_wsx", [P, 2 * 128], BF16))
        olz = latt.enter_context(nc.sbuf_tensor("l2b_olz", [P, 2 * 66], F32))
        psS2 = [latt.enter_context(nc.psum_tensor(f"l2b_psS2{j}", [P, 128], F32)) for j in range(2)]
        psO2 = [latt.enter_context(nc.psum_tensor(f"l2b_psO2{j}", [P, 66], F32)) for j in range(2)]
        psT2 = [latt.enter_context(nc.psum_tensor(f"l2b_psT2{j}", [64, 256], BF16)) for j in range(2)]

        la_done = {}
        ld_done = {}
        tr_done2 = {}
        for h in range(16):
            for ck in range(16):
                sl2 = (h * 16 + ck) % 2
                c0 = sl2 * 0

                def ldch(g, h=h, ck=ck, sl2=sl2):
                    pr.wait(g, "g", sort_tok)
                    pr.wait(g, "g", la_done.get((h, ck - 2), la_done.get((h - 1, 14 + ck))))
                    r0_ = h * TT + ck * 128
                    pr.inc(g.dma_start(out=chnk[sl2][:, 0:64],
                                       in_=qsq_d[r0_:r0_ + 128, :]), "gd")
                    pr.inc(g.dma_start(out=chnk[sl2][:, 64:128],
                                       in_=qsk_d[r0_:r0_ + 128, :]), "gd")
                    pr.inc(g.dma_start(out=chnk[sl2][:, 128:192],
                                       in_=qsv_d[r0_:r0_ + 128, :]), "gd")
                pr.blk(g=ldch)
                ld_done[(h, ck)] = pr.tok("gd")

                def trc(t, h=h, ck=ck, sl2=sl2):
                    pr.wait(t, "t", ld_done[(h, ck)], pr.ident_tok)
                    pr.wait(t, "t", la_done.get((h, ck - 2), la_done.get((h - 1, 14 + ck))))
                    t.transpose(psT2[sl2][0:64, 0:128], chnk[sl2][:, 0:64], pr.ident[:, :])
                    pr.inc(t.transpose(psT2[sl2][0:64, 128:256], chnk[sl2][:, 64:128],
                                       pr.ident[:, :]), "t")
                pr.blk(t=trc)
                ttok = pr.tok("t")

                def evc(s, sl2=sl2, ttok=ttok):
                    pr.wait(s, "s", ttok)
                    pr.inc(s.copy(kqT[0:64, sl2 * 256:(sl2 + 1) * 256],
                                  psT2[sl2][0:64, :]), "s")
                pr.blk(s=evc)
                etok = pr.tok("s")
                tr_done2[(h, ck)] = etok

                def mms(t, sl2=sl2, etok=etok):
                    pr.wait(t, "t", etok)
                    pr.inc(t.matmul(psS2[sl2][:, :],
                                    kqT[0:64, sl2 * 256 + 128:sl2 * 256 + 256],
                                    kqT[0:64, sl2 * 256:sl2 * 256 + 128],
                                    start=True, stop=True), "t")
                pr.blk(t=mms)
                stok = pr.tok("t")

                def eve(s, sl2=sl2, stok=stok):
                    pr.wait(s, "s", stok)
                    pr.inc(s.activation(wsx[:, sl2 * 128:(sl2 + 1) * 128], psS2[sl2][:, :],
                                        AF.Exp, scale=0.125), "s")
                pr.blk(s=eve)
                extok = pr.tok("s")

                def mmav(t, h=h, ck=ck, sl2=sl2, extok=extok):
                    pr.wait(t, "t", extok)
                    last = t.matmul(psO2[sl2][:, 0:64],
                                    wsx[:, sl2 * 128:(sl2 + 1) * 128],
                                    chnk[sl2][:, 128:192], start=True, stop=True)
                    pr.inc(t.matmul(psO2[sl2][:, 64:65],
                                    wsx[:, sl2 * 128:(sl2 + 1) * 128],
                                    pr.ones_bf16[:, 0:1], start=True, stop=True), "t")
                pr.blk(t=mmav)
                avtok = pr.tok("t")

                def evo(v, h=h, ck=ck, sl2=sl2, avtok=avtok):
                    pr.wait(v, "v", avtok)
                    pr.inc(v.reciprocal(olz[:, sl2 * 66 + 65:sl2 * 66 + 66],
                                        psO2[sl2][:, 64:65]), "v")
                    pr.wait(v, "v", pr.tok("v"))
                    pr.inc(v.tensor_scalar(olz[:, sl2 * 66:sl2 * 66 + 64],
                                           psO2[sl2][:, 0:64],
                                           olz[:, sl2 * 66 + 65:sl2 * 66 + 66],
                                           None, ALU.mult), "v")
                pr.blk(v=evo)
                otok = pr.tok("v")

                def sto(g, h=h, ck=ck, sl2=sl2, otok=otok):
                    pr.wait(g, "g", otok)
                    r0_ = h * TT + ck * 128
                    pr.inc(g.dma_start(out=outl_d[r0_:r0_ + 128, :],
                                       in_=olz[:, sl2 * 66:sl2 * 66 + 64]), "gd")
                pr.blk(g=sto)
                la_done[(h, ck)] = pr.tok("gd")
        la_tok = pr.tok("gd")

        ug_done = {}
        for h in range(16):
            for i in range(NT):
                def ug(g, h=h, i=i):
                    pr.wait(g, "g", la_tok)
                    pr.inc(g.indirect_dma_start(
                        att[i][:, h * 64:(h + 1) * 64], None,
                        outl_d, bass.IndirectOffsetOnAxis(
                            ap=poscol[:, i * 16 + h:i * 16 + h + 1], axis=0)), "gd")
                pr.blk(g=ug)
                ug_done[(h, i)] = pr.tok("gd")
        att_tok = pr.tok("gd")
        latt.close()
        if getattr(pr, "l2b_debug", None) is not None:
            dbg = pr.l2b_debug

            def dmp(g):
                pr.wait(g, "g", att_tok)
                pr.inc(g.dma_start(out=dbg["pos"], in_=poscol[:, :]), "gd")
                pr.inc(g.dma_start(out=dbg["bc"], in_=bcol[:, :]), "gd")
                pr.inc(g.dma_start(out=dbg["att"], in_=att[0][:, :]), "gd")
                pr.inc(g.dma_start(out=dbg["qsq"], in_=qsq_d[0:128, :]), "gd")
                pr.inc(g.dma_start(out=dbg["ol"], in_=outl_d[0:128, :]), "gd")
                pr.inc(g.dma_start(out=dbg["kv"][:, 0:64], in_=qsk_d[0:128, :]), "gd")
                pr.inc(g.dma_start(out=dbg["kv"][:, 64:128], in_=qsv_d[0:128, :]), "gd")
            pr.blk(g=dmp)

        attT = lnT16
        with ExitStack() as fs:
            attT_tok = emit_transpose_fm(pr, fs, att, [att_tok] * NT, attT,
                                         tag=f"l2bat{uid}")
            fin = fs.enter_context(nc.sbuf_tensor("l2b_fin", [P, 1024], F32))
            wpb = fs.enter_context(nc.sbuf_tensor("l2b_wpb", [P, 8 * 1024], BF16))
            bpb = fs.enter_context(nc.sbuf_tensor("l2b_bpb", [P, 1024], F32))
            rowp = fs.enter_context(nc.sbuf_tensor("l2b_rowp", [1, 1024], F32))
            psF = [fs.enter_context(nc.psum_tensor(f"l2b_psF{j}", [P, 1024], F32))
                   for j in range(2)]

            def ldwp(g):
                pr.wait(g, "g", att_tok)
                for c in range(8):
                    pr.inc(g.dma_start(out=wpb[:, c * 1024:(c + 1) * 1024],
                                       in_=w["Wp"][c]), "gd")
            pr.blk(g=ldwp)
            wp_tok = pr.tok("gd")

            def ldbp(q):
                pr.inc(q.dma_start(out=rowp[:, :], in_=w["bp"]), "qd")
            pr.blk(q=ldbp)
            bp_rtok = pr.tok("qd")
            bp_rb = emit_row_broadcast(pr, psF[0], rowp, bp_rtok, 1024)

            def evbp(s):
                pr.wait(s, "s", bp_rb)
                pr.inc(s.copy(bpb[:, :], psF[0][:, :]), "s")
            pr.blk(s=evbp)
            bp_tok = pr.tok("s")

            out_toks = [None] * NT
            prev = {}
            for i in range(NT):
                bank = psF[i % 2]

                def mmo(t, i=i, bank=bank):
                    pr.wait(t, "t", attT_tok, wp_tok, bp_tok)
                    pr.wait(t, "t", prev.get(i - 2, bp_tok))
                    last = None
                    for mh in range(2):
                        for c in range(8):
                            last = t.matmul(bank[:, mh * 512:(mh + 1) * 512],
                                            attT[c][:, i * 128:(i + 1) * 128],
                                            wpb[:, c * 1024 + mh * 512:c * 1024 + (mh + 1) * 512],
                                            start=(c == 0), stop=(c == 7))
                    pr.inc(last, "t")
                pr.blk(t=mmo)
                mmtok = pr.tok("t")

                def evf(v, i=i, bank=bank, mmtok=mmtok):
                    pr.wait(v, "v", mmtok)
                    pr.inc(v.tensor_add(fin[:, :], bank[:, :], bpb[:, :]), "v")
                pr.blk(v=evf)
                ftok = pr.tok("v")
                out_toks[i] = xio.add_store(i, fin[:, :], ftok)
                prev[i] = out_toks[i]
        return out_toks



def stage_L1_attn_b(inputs, pre):
    for nm in ("wq", "wk", "wv", "wo"):
        W = np.asarray(inputs[f"L1_attn_b_{nm}"])
        pre[f"L1ab_{nm}T"] = np.ascontiguousarray(W.T).reshape(8, 128, 1024)


def decl_L1_attn_b(nc):
    d = {}
    for nm in ("wqT", "wkT", "wvT", "woT"):
        d[nm] = nc.dram_tensor(f"L1ab_{nm}", [8, 128, 1024], F32, kind="ExternalInput").ap()
    return d


def emit_sparsify_top2(pr, tiles, tok0, absb, t2b, m1, war_prev=None):
    toks = []
    for i, q in enumerate(tiles):
        def body_s(s, q=q, i=i):
            pr.wait(s, "s", tok0)
            pr.wait(s, "s", toks[-1] if toks else war_prev)
            pr.inc(s.activation(absb[:, :], q[:, :], AF.Abs), "s")
        pr.blk(s=body_s)
        stok = pr.tok("s")

        def body_v(v, q=q, i=i, stok=stok):
            pr.wait(v, "v", stok)
            a3 = absb[:, :].rearrange("p (h d) -> p h d", d=64)
            t3 = t2b[:, :].rearrange("p (h d) -> p h d", d=64)
            pr.inc(v.tensor_reduce(m1[:, 0:16], a3, AX.X, ALU.max), "v")
            pr.wait(v, "v", pr.tok("v"))
            m1b = m1[:, 0:16].unsqueeze(2).broadcast_to([P, 16, 64])
            pr.inc(v.tensor_tensor(t3, a3, m1b, ALU.is_lt), "v")
            pr.wait(v, "v", pr.tok("v"))
            pr.inc(v.tensor_mul(t2b[:, :], t2b[:, :], absb[:, :]), "v")
            pr.wait(v, "v", pr.tok("v"))
            pr.inc(v.tensor_reduce(m1[:, 16:32], t3, AX.X, ALU.max), "v")
            pr.wait(v, "v", pr.tok("v"))
            m2b = m1[:, 16:32].unsqueeze(2).broadcast_to([P, 16, 64])
            pr.inc(v.tensor_tensor(t3, a3, m2b, ALU.is_ge), "v")
            pr.wait(v, "v", pr.tok("v"))
            pr.inc(v.tensor_mul(q[:, :], q[:, :], t2b[:, :]), "v")
        pr.blk(v=body_v)
        toks.append(pr.tok("v"))
    return toks


def emit_L1_attn_b(pr, stack, xio, w, uid):
    nc = pr.nc
    NBIS = 3
    NITER = 16
    with ExitStack() as ls:
        lctx = ls.enter_context
        qtl = [lctx(nc.sbuf_tensor(f"l1b_q{i}", [P, D], BF16)) for i in range(NT)]
        ktl = [lctx(nc.sbuf_tensor(f"l1b_k{i}", [P, D], BF16)) for i in range(NT)]
        vtl = [lctx(nc.sbuf_tensor(f"l1b_v{i}", [P, D], BF16)) for i in range(NT)]
        wexp = lctx(nc.sbuf_tensor("l1b_wexp", [P, TT], BF16))
        wTb = lctx(nc.sbuf_tensor("l1b_wT", [P, TT], BF16))
        indf = lctx(nc.sbuf_tensor("l1b_indf", [P, TT], BF16))
        scrap = lctx(nc.sbuf_tensor("l1b_scrap", [P, TT], BF16))
        zbA = lctx(nc.sbuf_tensor("l1b_zbA", [P, 256], F32))
        zbB = lctx(nc.sbuf_tensor("l1b_zbB", [P, 256], F32))
        cols = lctx(nc.sbuf_tensor("l1b_cols", [P, 16], F32))
        colu = lctx(nc.sbuf_tensor("l1b_colu", [P, 2], mybir.dt.uint8))

        with ExitStack() as ph1:
            pctx = ph1.enter_context
            lnT = [pctx(nc.sbuf_tensor(f"l1b_lnT{c}", [P, TT], BF16)) for c in range(8)]
            lntmp = [pctx(nc.sbuf_tensor(f"l1b_ln{j}", [P, D], BF16)) for j in range(2)]
            psLT = pctx(nc.psum_tensor("l1b_psLT", [P, 1024], BF16))
            tr_toks = {}

            def post_tile(i, tok):
                def trt(t, i=i, tok=tok):
                    pr.wait(t, "t", tok, pr.ident_tok)
                    pr.wait(t, "t", tr_toks.get(i - 2))
                    last = None
                    for c in range(8):
                        last = t.transpose(psLT[:, c * 128:(c + 1) * 128],
                                           lntmp[i % 2][:, c * 128:(c + 1) * 128],
                                           pr.ident[:, :])
                    pr.inc(last, "t")
                pr.blk(t=trt)
                ttk = pr.tok("t")

                def evt(s, i=i, ttk=ttk):
                    pr.wait(s, "s", ttk)
                    for c in range(8):
                        pr.inc(s.copy(lnT[c][:, i * 128:(i + 1) * 128],
                                      psLT[:, c * 128:(c + 1) * 128]), "s")
                pr.blk(s=evt)
                tr_toks[i] = pr.tok("s")
                return tr_toks[i]

            emit_ln_from_xio(pr, ph1, xio, [lntmp[i % 2] for i in range(NT)],
                             post_tile=post_tile,
                             tile_war=lambda i: tr_toks.get(i - 2))
            lnT_tok = pr.tok("s")

            wbuf = pctx(nc.sbuf_tensor("l1b_wbuf", [P, 8 * 1024], BF16))
            psW = [pctx(nc.psum_tensor(f"l1b_psW{j}", [P, 1024], F32)) for j in range(2)]
            q_tok = emit_proj_tok(pr, lnT, lnT_tok, [w["wqT"][c] for c in range(8)],
                                  wbuf, qtl, psW, "q")
            k_tok = emit_proj_tok(pr, lnT, lnT_tok, [w["wkT"][c] for c in range(8)],
                                  wbuf, ktl, psW, "k", war_tok=q_tok)
            v_tok = emit_proj_tok(pr, lnT, lnT_tok, [w["wvT"][c] for c in range(8)],
                                  wbuf, vtl, psW, "v", war_tok=k_tok)

        with ExitStack() as sps:
            absb = sps.enter_context(nc.sbuf_tensor("l1b_absb", [P, 1024], F32))
            t2b = sps.enter_context(nc.sbuf_tensor("l1b_t2b", [P, 1024], F32))
            m1s = sps.enter_context(nc.sbuf_tensor("l1b_m1", [P, 32], F32))
            sq = emit_sparsify_top2(pr, qtl, q_tok, absb, t2b, m1s, war_prev=v_tok)
            sk = emit_sparsify_top2(pr, ktl, k_tok, absb, t2b, m1s, war_prev=sq[-1])
            sv = emit_sparsify_top2(pr, vtl, v_tok, absb, t2b, m1s, war_prev=sk[-1])
            spar_tok = pr.tok("v")

        qT = [lctx(nc.sbuf_tensor(f"l1b_qT{c}", [P, TT], BF16)) for c in range(8)]
        kT = [lctx(nc.sbuf_tensor(f"l1b_kT{c}", [P, TT], BF16)) for c in range(8)]
        with ExitStack() as trs:
            qT_tok = emit_transpose_fm(pr, trs, qtl, sq, qT, tag=f"l1bq{uid}")
            kT_tok = emit_transpose_fm(pr, trs, ktl, sk, kT, tag=f"l1bk{uid}")

        att = qtl
        att_ps = ExitStack()
        psS = [att_ps.enter_context(nc.psum_tensor(f"l1b_psS{j}", [P, 512], F32)) for j in range(2)]
        psT = att_ps.enter_context(nc.psum_tensor("l1b_psT", [P, 512], BF16))
        psO = [att_ps.enter_context(nc.psum_tensor(f"l1b_psO{j}", [P, 64], F32)) for j in range(2)]
        prev_tile = {}
        for h in range(16):
            cchunk, coff = h // 2, (h % 2) * 64
            for isub in range(NT):
                W = (isub + 1) * 128
                ZW = min(W, 256)
                key = (h, isub)
                prev = prev_tile.get((h, isub - 1), prev_tile.get((h - 1, NT - 1)))

                ev_toks = []
                for jc in range((W + 511) // 512):
                    j0, j1 = jc * 512, min(W, (jc + 1) * 512)
                    bank = psS[jc % 2]

                    def mmsc(t, h=h, isub=isub, j0=j0, j1=j1, bank=bank, jc=jc,
                             cchunk=cchunk, coff=coff, prev=prev):
                        pr.wait(t, "t", qT_tok, kT_tok)
                        pr.wait(t, "t", ev_toks[jc - 2] if jc >= 2 else prev)
                        pr.inc(t.matmul(bank[:, 0:j1 - j0],
                                        qT[cchunk][coff:coff + 64, isub * 128:(isub + 1) * 128],
                                        kT[cchunk][coff:coff + 64, j0:j1],
                                        start=True, stop=True), "t")
                    pr.blk(t=mmsc)
                    mmtok = pr.tok("t")

                    def evex(s, j0=j0, j1=j1, bank=bank, mmtok=mmtok, prev=prev):
                        pr.wait(s, "s", mmtok)
                        pr.wait(s, "s", prev)
                        pr.inc(s.activation(wexp[:, j0:j1], bank[:, 0:j1 - j0],
                                            AF.Exp, scale=0.125), "s")
                    pr.blk(s=evex)
                    ev_toks.append(pr.tok("s"))
                exp_tok = ev_toks[-1]

                def msk(g, isub=isub, exp_tok=exp_tok):
                    pr.wait(g, "g", exp_tok)
                    pr.inc(g.affine_select(wexp[:, isub * 128:(isub + 1) * 128],
                                           wexp[:, isub * 128:(isub + 1) * 128],
                                           [[-1, 128]], ALU.is_ge, 0.0,
                                           base=0, channel_multiplier=1), "g")
                pr.blk(g=msk)
                m_tok = pr.tok("g")

                def indp(v, W=W, m_tok=m_tok):
                    pr.wait(v, "v", m_tok)
                    pr.inc(v.tensor_scalar(indf[:, 0:W], wexp[:, 0:W], 1.0, 0.0,
                                           ALU.is_gt, ALU.add, accum_out=cols[:, 0:1]), "v")
                    pr.wait(v, "v", pr.tok("v"))
                    pr.inc(v.tensor_scalar(cols[:, 1:2], cols[:, 0:1], -1.0, 64.0,
                                           ALU.mult, ALU.add), "v")
                pr.blk(v=indp)
                p_tok = pr.tok("v")

                def zrule(v, ZW=ZW, p_tok=p_tok):
                    pr.wait(v, "v", p_tok)
                    pr.inc(v.tensor_scalar(zbA[:, 0:ZW], wexp[:, 0:ZW], 1.0, None,
                                           ALU.is_equal), "v")
                    pr.wait(v, "v", pr.tok("v"))
                    src, dst = zbA, zbB
                    sh = 1
                    while sh < ZW:
                        pr.inc(v.tensor_copy(dst[:, 0:sh], src[:, 0:sh]), "v")
                        pr.inc(v.tensor_add(dst[:, sh:ZW], src[:, sh:ZW],
                                            src[:, 0:ZW - sh]), "v")
                        pr.wait(v, "v", pr.tok("v"))
                        src, dst = dst, src
                        sh *= 2
                    pr.inc(v.tensor_scalar(src[:, 0:ZW], src[:, 0:ZW], cols[:, 1:2],
                                           None, ALU.is_le), "v")
                    pr.wait(v, "v", pr.tok("v"))
                    pr.inc(v.tensor_tensor(zbA[:, 0:ZW] if src is zbB else zbB[:, 0:ZW],
                                           src[:, 0:ZW],
                                           wexp[:, 0:ZW], ALU.bypass) if False else
                           v.tensor_mul(src[:, 0:ZW], src[:, 0:ZW],
                                        indf[:, 0:ZW]) if False else
                           v.engine_nop(), "v")
                    pr.inc(v.tensor_scalar(zbA[:, 0:ZW] if src is zbB else zbB[:, 0:ZW],
                                           wexp[:, 0:ZW], 1.0, None, ALU.is_equal), "v")
                    pr.wait(v, "v", pr.tok("v"))
                    other = zbA if src is zbB else zbB
                    pr.inc(v.tensor_mul(other[:, 0:ZW], other[:, 0:ZW], src[:, 0:ZW]), "v")
                    pr.wait(v, "v", pr.tok("v"))
                    pr.inc(v.tensor_max(indf[:, 0:ZW], indf[:, 0:ZW], other[:, 0:ZW]), "v")
                pr.blk(v=zrule)
                z_tok = pr.tok("v")

                sel_tok = z_tok
                if isub >= NT - NBIS or isub == 0:
                    lo_init = 0.0 if isub == 0 else 1.0

                    def binit(g, z_tok=z_tok, lo_init=lo_init):
                        pr.wait(g, "g", z_tok)
                        pr.inc(g.memset(cols[:, 2:3], lo_init), "g")
                        pr.inc(g.memset(cols[:, 3:4], 32.0), "g")
                    pr.blk(g=binit)
                    btok = pr.tok("g")

                    for it in range(NITER):
                        def bit(v, W=W, btok=btok):
                            pr.wait(v, "v", btok)
                            pr.inc(v.tensor_add(cols[:, 4:5], cols[:, 2:3], cols[:, 3:4]), "v")
                            pr.wait(v, "v", pr.tok("v"))
                            pr.inc(v.tensor_scalar(cols[:, 4:5], cols[:, 4:5], 0.5,
                                                   None, ALU.mult), "v")
                            pr.wait(v, "v", pr.tok("v"))
                            pr.inc(v.tensor_scalar(scrap[:, 0:W], wexp[:, 0:W],
                                                   cols[:, 4:5], 0.0, ALU.is_gt,
                                                   ALU.add, accum_out=cols[:, 5:6]), "v")
                            pr.wait(v, "v", pr.tok("v"))
                            pr.inc(v.tensor_scalar(colu[:, 0:1], cols[:, 5:6], 64.0,
                                                   None, ALU.is_ge), "v")
                            pr.wait(v, "v", pr.tok("v"))
                            pr.inc(v.select(cols[:, 2:3], colu[:, 0:1], cols[:, 4:5],
                                            cols[:, 2:3]), "v")
                            pr.inc(v.select(cols[:, 3:4], colu[:, 0:1], cols[:, 3:4],
                                            cols[:, 4:5]), "v")
                        pr.blk(v=bit)
                        btok = pr.tok("v")

                    def bfin(v, W=W, ZW=ZW, isub=isub, btok=btok):
                        pr.wait(v, "v", btok)
                        pr.inc(v.tensor_scalar(scrap[:, 0:W], wexp[:, 0:W],
                                               cols[:, 2:3], None, ALU.is_gt), "v")
                        pr.inc(v.tensor_scalar(cols[:, 6:7], cols[:, 0:1], 64.0,
                                               None, ALU.is_ge), "v")
                        if isub == 0:
                            pr.inc(v.tensor_scalar(scrap[:, 1024:1024 + W], wexp[:, 0:W],
                                                   1.0, 0.0, ALU.is_ge,
                                                   ALU.add, accum_out=cols[:, 10:11]), "v")
                            pr.wait(v, "v", pr.tok("v"))
                            pr.inc(v.tensor_scalar(cols[:, 7:8], cols[:, 10:11], 64.0,
                                                   None, ALU.is_lt), "v")
                            pr.wait(v, "v", pr.tok("v"))
                            pr.inc(v.tensor_max(cols[:, 6:7], cols[:, 6:7],
                                                cols[:, 7:8]), "v")
                        pr.wait(v, "v", pr.tok("v"))
                        pr.inc(v.tensor_copy(colu[:, 1:2], cols[:, 6:7]), "v")
                        pr.wait(v, "v", pr.tok("v"))
                        pr.inc(v.select(indf[:, 0:W],
                                        colu[:, 1:2].broadcast_to([P, W]),
                                        scrap[:, 0:W], indf[:, 0:W]), "v")
                    pr.blk(v=bfin)
                    sel_tok = pr.tok("v")

                def finw(v, W=W, sel_tok=sel_tok):
                    pr.wait(v, "v", sel_tok)
                    pr.inc(v.scalar_tensor_tensor(wexp[:, 0:W], wexp[:, 0:W], 1.0,
                                                  indf[:, 0:W], ALU.mult, ALU.mult,
                                                  accum_out=cols[:, 8:9]), "v")
                    pr.wait(v, "v", pr.tok("v"))
                    pr.inc(v.reciprocal(cols[:, 9:10], cols[:, 8:9]), "v")
                pr.blk(v=finw)
                fw_tok = pr.tok("v")

                tr_toks2 = []
                for jt in range(isub + 1):
                    def trw(t, jt=jt, fw_tok=fw_tok):
                        pr.wait(t, "t", fw_tok, pr.ident_tok)
                        if tr_toks2:
                            pr.wait(t, "t", tr_toks2[-1])
                        pr.inc(t.transpose(psT[:, 0:128], wexp[:, jt * 128:(jt + 1) * 128],
                                           pr.ident[:, :]), "t")
                    pr.blk(t=trw)
                    tt = pr.tok("t")

                    def evw(s, jt=jt, tt=tt):
                        pr.wait(s, "s", tt)
                        pr.inc(s.copy(wTb[:, jt * 128:(jt + 1) * 128], psT[:, 0:128]), "s")
                    pr.blk(s=evw)
                    tr_toks2.append(pr.tok("s"))
                wT_tok = tr_toks2[-1] if tr_toks2 else fw_tok

                bank = psO[(h * 16 + isub) % 2]

                def mmav(t, h=h, isub=isub, bank=bank, wT_tok=wT_tok, prev=prev):
                    pr.wait(t, "t", wT_tok, spar_tok)
                    pr.wait(t, "t", prev)
                    last = None
                    for jt in range(isub + 1):
                        last = t.matmul(bank[:, :], wTb[:, jt * 128:(jt + 1) * 128],
                                        vtl[jt][:, h * 64:(h + 1) * 64],
                                        start=(jt == 0), stop=(jt == isub))
                    pr.inc(last, "t")
                pr.blk(t=mmav)
                av_tok = pr.tok("t")

                def evo(v, h=h, isub=isub, bank=bank, av_tok=av_tok):
                    pr.wait(v, "v", av_tok)
                    pr.inc(v.tensor_scalar(att[isub][:, h * 64:(h + 1) * 64],
                                           bank[:, :], cols[:, 9:10], None,
                                           ALU.mult), "v")
                pr.blk(v=evo)
                prev_tile[key] = pr.tok("v")
        att_tok = pr.tok("v")

        def attT_of(c, i):
            return ktl[2 * c + (1 if i >= 8 else 0)][:, (i % 8) * 128:(i % 8 + 1) * 128]

        trp = {}
        for i in range(NT):
            for c in range(8):
                def trat(t, i=i, c=c):
                    pr.wait(t, "t", att_tok, pr.ident_tok)
                    pr.wait(t, "t", trp.get((i, c - 1), trp.get((i - 1, 7))))
                    pr.inc(t.transpose(psT[:, 0:128], att[i][:, c * 128:(c + 1) * 128],
                                       pr.ident[:, :]), "t")
                pr.blk(t=trat)
                tt = pr.tok("t")

                def evat(s, i=i, c=c, tt=tt):
                    pr.wait(s, "s", tt)
                    pr.inc(s.copy(attT_of(c, i), psT[:, 0:128]), "s")
                pr.blk(s=evat)
                trp[(i, c)] = pr.tok("s")
        attT_tok = pr.tok("s")
        att_ps.close()

        with ExitStack() as fs:
            wob = fs.enter_context(nc.sbuf_tensor("l1b_wob", [P, 8 * 1024], BF16))
            fin = fs.enter_context(nc.sbuf_tensor("l1b_fin", [P, 1024], F32))
            psF = [fs.enter_context(nc.psum_tensor(f"l1b_psF{j}", [P, 1024], F32))
                   for j in range(2)]

            def ldwo(g):
                pr.wait(g, "g", att_tok)
                for c in range(8):
                    pr.inc(g.dma_start(out=wob[:, c * 1024:(c + 1) * 1024],
                                       in_=w["woT"][c]), "gd")
            pr.blk(g=ldwo)
            wo_tok = pr.tok("gd")

            out_toks = [None] * NT
            prevf = {}
            for i in range(NT):
                bank = psF[i % 2]

                def mmo(t, i=i, bank=bank):
                    pr.wait(t, "t", attT_tok, wo_tok)
                    pr.wait(t, "t", prevf.get(i - 2))
                    last = None
                    for mh in range(2):
                        for c in range(8):
                            last = t.matmul(bank[:, mh * 512:(mh + 1) * 512],
                                            attT_of(c, i),
                                            wob[:, c * 1024 + mh * 512:c * 1024 + (mh + 1) * 512],
                                            start=(c == 0), stop=(c == 7))
                    pr.inc(last, "t")
                pr.blk(t=mmo)
                mmtok = pr.tok("t")
                out_toks[i] = xio.add_store(i, bank[:, :], mmtok)
                prevf[i] = out_toks[i]
        return out_toks



SEQ = [
    ("L0_attn_a",), ("L0_ffn",), ("L0_attn_b",), ("L0_ffn",),
    ("L1_attn_a",), ("L1_ffn",), ("L1_attn_b",), ("L1_ffn",),
    ("L2_attn_a",), ("L2_ffn",), ("L2_attn_b",), ("L2_ffn",),
    ("L3_attn_a",), ("L3_ffn",), ("L3_attn_b",), ("L3_ffn",),
]

_DECLS = {
    "L0_attn_a": decl_L0_attn_a, "L0_ffn": decl_L0_ffn, "L0_attn_b": decl_L0_attn_b,
    "L1_attn_a": decl_L1_attn_a, "L1_ffn": decl_L1_ffn, "L1_attn_b": decl_L1_attn_b,
    "L2_attn_a": decl_L2_attn_a, "L2_ffn": decl_L2_ffn, "L2_attn_b": decl_L2_attn_b,
    "L3_attn_a": decl_L3_attn_a, "L3_ffn": decl_L3_ffn, "L3_attn_b": decl_L3_attn_b,
}


def stage_all(inputs, pre):
    stage_L0_attn_a(inputs, pre)
    stage_L0_ffn(inputs, pre)
    stage_L0_attn_b(inputs, pre)
    stage_L1_attn_a(inputs, pre)
    stage_L1_ffn(inputs, pre)
    stage_L1_attn_b(inputs, pre)
    stage_L2_attn_a(inputs, pre)
    stage_L2_ffn(inputs, pre)
    stage_L2_attn_b(inputs, pre)
    stage_L3_attn_a(inputs, pre)
    stage_L3_attn_b(inputs, pre)
    stage_L3_ffn(inputs, pre)
    pre["L3_ffn_input_scale"] = np.asarray(inputs["L3_ffn_input_scale"], np.float32).reshape(1, 1)
    pre["L3_ffn_kernel_scale"] = np.asarray(inputs["L3_ffn_kernel_scale"], np.float32).reshape(1, 1)
    pre["L3_ffn_bias"] = np.asarray(inputs["L3_ffn_bias"]).reshape(1, 1024)


def build_sub(name):
    import concourse.bass as bass
    nc = bass.Bass("TRN2", target_bir_lowering=False)
    x_in = nc.dram_tensor("x_in", [TT, D], F32, kind="ExternalInput").ap()
    x_out = nc.dram_tensor("x_out", [TT, D], F32, kind="ExternalOutput").ap()
    w = _DECLS[name](nc)
    pr = Prog(nc)
    with ExitStack() as stack:
        pr.open_sems(stack)
        setup_consts(pr, stack)
        setup_identity(pr, stack)
        setup_ones(pr, stack)
        xio = XIOUnfused(pr, stack, x_in, x_out)
        if name == "L2_attn_b":
            out_toks = emit_L2_attn_b(pr, stack, xio, w, rope_tables_L2(), 0)
        elif name == "L1_attn_b":
            out_toks = emit_L1_attn_b(pr, stack, xio, w, 0)
        elif name == "L2_ffn":
            lnT = [stack.enter_context(nc.sbuf_tensor(f"lnT{c}", [P, TT], BF16))
                   for c in range(NF)]
            ln = [stack.enter_context(nc.sbuf_tensor(f"ln{i}", [P, D], BF16))
                  for i in range(NT)]
            with ExitStack() as lnstack:
                ln_toks = emit_ln_from_xio(pr, lnstack, xio, ln)
                lnT_tok = emit_transpose_fm(pr, lnstack, ln, ln_toks, lnT)
            out_toks = emit_L2_ffn(pr, stack, xio, lnT, lnT_tok, ln, ln_toks, w, 0)
        else:
            lnT = [stack.enter_context(nc.sbuf_tensor(f"lnT{c}", [P, TT], BF16))
                   for c in range(NF)]
            with ExitStack() as lnstack:
                ln = [lnstack.enter_context(nc.sbuf_tensor(f"ln{i}", [P, D], BF16))
                      for i in range(NT)]
                ln_toks = emit_ln_from_xio(pr, lnstack, xio, ln)
                lnT_tok = emit_transpose_fm(pr, lnstack, ln, ln_toks, lnT)
            if name == "L0_attn_a":
                out_toks = emit_L0_attn_a(pr, stack, xio, lnT, lnT_tok, w)
            elif name == "L0_ffn":
                out_toks = emit_L0_ffn(pr, stack, xio, lnT, lnT_tok, w, 0)
            elif name == "L0_attn_b":
                out_toks = emit_L0_attn_b(pr, stack, xio, lnT, lnT_tok, w)
            elif name == "L1_attn_a":
                out_toks = emit_L1_attn_a(pr, stack, xio, lnT, lnT_tok, w,
                                          rope_tables_L1())
            elif name == "L1_ffn":
                out_toks = emit_L1_ffn(pr, stack, xio, lnT, lnT_tok, w)
            elif name == "L2_attn_a":
                out_toks = emit_L2_attn_a(pr, stack, xio, lnT, lnT_tok, w)
            elif name == "L3_attn_a":
                out_toks = emit_L3_attn_a(pr, stack, xio, lnT, lnT_tok, w)
            elif name == "L3_attn_b":
                out_toks = emit_L3_attn_b(pr, stack, xio, lnT, lnT_tok, w)
            elif name == "L3_ffn":
                out_toks = emit_L3_ffn(pr, stack, xio, lnT, lnT_tok, w)
            else:
                raise KeyError(name)

        def fin(q):
            pr.wait(q, "q", *out_toks)
        pr.blk(q=fin)
    innames = set()
    for alloc in nc.m.functions[0].allocations:
        try:
            if alloc.kind == "ExternalInput":
                nm = alloc.memorylocations[0].name
                if nm != "x_in":
                    innames.add(nm)
        except AttributeError:
            pass
    return nc, sorted(innames)


_PROGS = {}


def _get_prog(name):
    if name not in _PROGS:
        _PROGS[name] = build_sub(name)
    return _PROGS[name]


def kernel_unfused(**inputs):
    from concourse.bass_utils import run_bass_kernel_spmd
    pre = {}
    stage_all(inputs, pre)
    x0 = np.ascontiguousarray(np.asarray(inputs["L0_attn_a_hidden_states"],
                                         np.float32)).reshape(8, TT, D)
    shards = [np.ascontiguousarray(x0[c]) for c in range(8)]
    for (name,) in SEQ:
        nc, innames = _get_prog(name)
        in_maps = []
        for c in range(8):
            im = {"x_in": shards[c]}
            for k in innames:
                im[k] = pre[k]
            in_maps.append(im)
        res = run_bass_kernel_spmd(nc, in_maps, core_ids=list(range(8)))
        shards = [np.asarray(res.results[c]["x_out"], np.float32) for c in range(8)]
    return np.stack(shards, 0).astype(np.float32)



class NcProxy:

    def __init__(self, nc):
        object.__setattr__(self, "_nc", nc)
        object.__setattr__(self, "_ctr", [0])

    def _mangle(self, name):
        self._ctr[0] += 1
        return f"{name}_m{self._ctr[0]}"

    def sbuf_tensor(self, *args, **kw):
        if args and isinstance(args[0], str):
            args = (self._mangle(args[0]),) + args[1:]
        return self._nc.sbuf_tensor(*args, **kw)

    def psum_tensor(self, *args, **kw):
        if args and isinstance(args[0], str):
            args = (self._mangle(args[0]),) + args[1:]
        return self._nc.psum_tensor(*args, **kw)

    def dram_tensor(self, *args, **kw):
        if kw.get("kind", "Internal") == "Internal" and args and isinstance(args[0], str):
            args = (self._mangle(args[0]),) + args[1:]
        return self._nc.dram_tensor(*args, **kw)

    def inline_tensor(self, data, name=None):
        return self._nc.inline_tensor(data, self._mangle(name) if name else None)

    def __getattr__(self, name):
        return getattr(object.__getattribute__(self, "_nc"), name)


class XIOChain:

    def __init__(self, pr, stack, x_in_ap, x_out_ap, xa_ap, xb_ap):
        self.pr = pr
        self.x_out_final = x_out_ap
        self.xa = xa_ap
        self.xb = xb_ap
        self.cur_read = None
        self.cur_write = x_in_ap
        nc = pr.nc
        self.tmp = [stack.enter_context(nc.sbuf_tensor(f"xioc_tmp{j}", [P, D], F32))
                    for j in range(2)]
        self.tmp_tok = [None] * 2
        self.rr = 0
        self.store_toks = [None] * NT
        self.pending = [None] * NT
        self.reads_cur = [None] * NT
        self.war_writes = [None] * NT

    def next_sublayer(self, last=False):
        self.store_toks = self.pending
        self.pending = [None] * NT
        self.war_writes = self.reads_cur
        self.reads_cur = [None] * NT
        self.cur_read = self.cur_write
        if last:
            self.cur_write = self.x_out_final
        else:
            self.cur_write = self.xb if self.cur_read is self.xa else self.xa

    def get(self, i):
        pr = self.pr
        slot = self.rr % 2
        self.rr += 1
        war = self.tmp_tok[slot]
        stok = self.store_toks[i]

        def body(q, i=i, slot=slot, war=war, stok=stok):
            pr.wait(q, "q", war, stok)
            pr.inc(q.dma_start(out=self.tmp[slot][:, :],
                               in_=self.cur_read[i * 128:(i + 1) * 128, :]), "qd")
        pr.blk(q=body)
        ltok = pr.tok("qd")
        t0 = self.reads_cur[i]
        if t0 is None or ltok[1] > t0[1]:
            self.reads_cur[i] = ltok
        return self.tmp[slot], ltok, slot

    def done(self, slot, tok):
        self.tmp_tok[slot] = tok

    def add_store(self, i, src_ap, src_tok, scale=None, scale_ap=None):
        pr = self.pr
        xap, ltok, slot = self.get(i)

        def body_v(v, i=i, xap=xap, ltok=ltok):
            pr.wait(v, "v", ltok, src_tok)
            if scale_ap is not None:
                pr.inc(v.scalar_tensor_tensor(xap[:, :], src_ap, scale_ap, xap[:, :],
                                              ALU.mult, ALU.add), "v")
            elif scale is not None:
                pr.inc(v.scalar_tensor_tensor(xap[:, :], src_ap, float(scale), xap[:, :],
                                              ALU.mult, ALU.add), "v")
            else:
                pr.inc(v.tensor_tensor(xap[:, :], xap[:, :], src_ap, ALU.add), "v")
        pr.blk(v=body_v)
        vtok = pr.tok("v")

        def body_q(q, i=i, xap=xap, vtok=vtok):
            pr.wait(q, "q", vtok, self.war_writes[i])
            pr.inc(q.dma_start(out=self.cur_write[i * 128:(i + 1) * 128, :],
                               in_=xap[:, :]), "qd")
        pr.blk(q=body_q)
        stok = pr.tok("qd")
        self.done(slot, stok)
        self.pending[i] = stok
        return stok


def emit_barrier(pr):
    toks = pr.barrier_tokens()

    def bq(q):
        pr.wait(q, "q", *toks)

    def bg(g):
        pr.wait(g, "g", *toks)

    def bt(t):
        pr.wait(t, "t", *toks)

    def bs(s):
        pr.wait(s, "s", *toks)

    def bv(v):
        pr.wait(v, "v", *toks)

    pr.blk(q=bq, g=bg, t=bt, s=bs, v=bv)


def build_fused(seq=None):
    import concourse.bass as bass
    if seq is None:
        seq = SEQ
    nc0 = bass.Bass("TRN2", target_bir_lowering=False)
    x_in = nc0.dram_tensor("x_in", [TT, D], F32, kind="ExternalInput").ap()
    x_out = nc0.dram_tensor("x_out", [TT, D], F32, kind="ExternalOutput").ap()
    xa = nc0.dram_tensor("x_scrA", [TT, D], F32, kind="Internal").ap()
    xb = nc0.dram_tensor("x_scrB", [TT, D], F32, kind="Internal").ap()
    decls = {nm: fn(nc0) for nm, fn in {n: _DECLS[n] for (n,) in seq}.items()}
    nc = NcProxy(nc0)
    pr = Prog(nc)
    with ExitStack() as stack:
        pr.open_sems(stack)
        setup_consts(pr, stack)
        setup_identity(pr, stack)
        setup_ones(pr, stack)
        xio = XIOChain(pr, stack, x_in, x_out, xa, xb)
        for k, (name,) in enumerate(seq):
            xio.next_sublayer(last=(k == len(seq) - 1))
            w = decls[name]
            with ExitStack() as sstack:
                if name == "L2_attn_b":
                    emit_L2_attn_b(pr, sstack, xio, w, rope_tables_L2(), k)
                elif name == "L1_attn_b":
                    emit_L1_attn_b(pr, sstack, xio, w, k)
                elif name == "L2_ffn":
                    lnT = [sstack.enter_context(nc.sbuf_tensor(f"lnT{c}", [P, TT], BF16))
                           for c in range(NF)]
                    ln = [sstack.enter_context(nc.sbuf_tensor(f"ln{i}", [P, D], BF16))
                          for i in range(NT)]
                    with ExitStack() as lnstack:
                        ln_toks = emit_ln_from_xio(pr, lnstack, xio, ln)
                        lnT_tok = emit_transpose_fm(pr, lnstack, ln, ln_toks, lnT)
                    emit_L2_ffn(pr, sstack, xio, lnT, lnT_tok, ln, ln_toks, w, k)
                else:
                    lnT = [sstack.enter_context(nc.sbuf_tensor(f"lnT{c}", [P, TT], BF16))
                           for c in range(NF)]
                    with ExitStack() as lnstack:
                        ln = [lnstack.enter_context(nc.sbuf_tensor(f"ln{i}", [P, D], BF16))
                              for i in range(NT)]
                        ln_toks = emit_ln_from_xio(pr, lnstack, xio, ln)
                        lnT_tok = emit_transpose_fm(pr, lnstack, ln, ln_toks, lnT)
                    if name == "L0_attn_a":
                        emit_L0_attn_a(pr, sstack, xio, lnT, lnT_tok, w)
                    elif name == "L0_ffn":
                        emit_L0_ffn(pr, sstack, xio, lnT, lnT_tok, w, k)
                    elif name == "L0_attn_b":
                        emit_L0_attn_b(pr, sstack, xio, lnT, lnT_tok, w)
                    elif name == "L1_attn_a":
                        emit_L1_attn_a(pr, sstack, xio, lnT, lnT_tok, w,
                                       rope_tables_L1())
                    elif name == "L1_ffn":
                        emit_L1_ffn(pr, sstack, xio, lnT, lnT_tok, w)
                    elif name == "L2_attn_a":
                        emit_L2_attn_a(pr, sstack, xio, lnT, lnT_tok, w)
                    elif name == "L3_attn_a":
                        emit_L3_attn_a(pr, sstack, xio, lnT, lnT_tok, w)
                    elif name == "L3_attn_b":
                        emit_L3_attn_b(pr, sstack, xio, lnT, lnT_tok, w)
                    elif name == "L3_ffn":
                        emit_L3_ffn(pr, sstack, xio, lnT, lnT_tok, w)
                    else:
                        raise KeyError(name)
                emit_barrier(pr)
    innames = set()
    for alloc in nc0.m.functions[0].allocations:
        try:
            if alloc.kind == "ExternalInput":
                nm2 = alloc.memorylocations[0].name
                if nm2 != "x_in" and "partition_id" not in nm2 and "dbg_addr" not in nm2:
                    innames.add(nm2)
        except AttributeError:
            pass
    return nc0, sorted(innames)


_FPROG = []


def kernel(**inputs):
    from concourse.bass_utils import run_bass_kernel_spmd
    pre = {}
    stage_all(inputs, pre)
    x0 = np.ascontiguousarray(np.asarray(inputs["L0_attn_a_hidden_states"],
                                         np.float32)).reshape(8, TT, D)
    if not _FPROG:
        _FPROG.append(build_fused())
    nc, innames = _FPROG[0]
    in_maps = []
    for c in range(8):
        im = {"x_in": np.ascontiguousarray(x0[c])}
        for k in innames:
            im[k] = pre[k]
        in_maps.append(im)
    res = run_bass_kernel_spmd(nc, in_maps, core_ids=list(range(8)))
    return np.stack([np.asarray(res.results[c]["x_out"], np.float32)
                     for c in range(8)], 0).astype(np.float32)
